# version 09
# baseline (speedup 1.0000x reference)
import numpy as np
import ml_dtypes
from contextlib import ExitStack
import concourse.bass as bass
import concourse.mybir as mybir
from concourse.bass_utils import run_bass_kernel_spmd

F32 = mybir.dt.float32
BF16 = mybir.dt.bfloat16
I32 = mybir.dt.int32
AF = mybir.ActivationFunctionType
ALU = mybir.AluOpType
AX = mybir.AxisListType
BF16NP = ml_dtypes.bfloat16

NCORES = 8
D = 1024
L = 2048
DKT = D // 128
NTG = L // 512
EPS_LN = 1e-5


class K:
    ENGINES = ("sync", "tensor", "vector", "scalar", "gpsimd")
    SERIAL = ("vector", "scalar", "gpsimd")

    def __init__(self, nc, es):
        self.nc = nc
        self.es = es
        self.q = {e: [] for e in self.ENGINES}
        self._semcnt = {}
        self.serial = {e: self.sem("ser_" + e) for e in self.SERIAL}

    def sem(self, name):
        h = self.es.enter_context(self.nc.semaphore(name))
        self._semcnt[id(h)] = 0
        return h

    def bump(self, sem, amt):
        self._semcnt[id(sem)] += amt
        return self._semcnt[id(sem)]

    def val(self, sem):
        return self._semcnt[id(sem)]

    def emit(self, eng, fn):
        self.q[eng].append(fn)

    def op(self, eng, meth, *args, wait=None, inc=None, **kw):
        waits = [(s, v) for (s, v) in (wait or [])]
        incs = []
        marker = None
        if inc is not None and meth.startswith("dma_start"):
            prev = self.val(inc[0])
            if prev > 0 and not any(s is inc[0] and v >= prev for (s, v) in waits):
                waits.append((inc[0], prev))
        if inc is not None:
            self.bump(inc[0], inc[1])
            incs.append(inc)
            marker = (inc[0], self.val(inc[0]))
        if (eng in self.SERIAL and meth not in ("nop", "dma_start", "dma_start_transpose")
                and inc is None):
            ser = self.serial[eng]
            waits.append((ser, self.val(ser)))
            self.bump(ser, 1)
            incs.append((ser, 1))
            marker = (ser, self.val(ser))
        elif eng in self.SERIAL and meth not in ("nop", "dma_start", "dma_start_transpose"):
            raise AssertionError("one sem update per instruction: no user inc on serial engines")
        def f(e, meth=meth, args=args, kw=kw, waits=waits, incs=incs):
            for s, v in waits:
                e.wait_ge(s, v)
            ins = getattr(e, meth)(*args, **kw)
            for s, v in incs:
                ins = ins.then_inc(s, v)
        self.emit(eng, f)
        return marker

    def tr(self, out_ap, in_ap, ident_ap, wait=None, inc=None):
        return self.op("tensor", "transpose", out_ap, in_ap, ident_ap, wait=wait, inc=inc)

    def wait(self, eng, sem, val=None):
        v = self.val(sem) if val is None else val
        self.emit(eng, lambda e, s=sem, v=v: e.wait_ge(s, v))

    def waitm(self, eng, marker):
        self.emit(eng, lambda e, s=marker[0], v=marker[1]: e.wait_ge(s, v))

    def mark(self, eng):
        ser = self.serial[eng]
        return (ser, self.val(ser))

    def finalize(self):
        blk = self.es.enter_context(self.nc.Block())
        for name in self.ENGINES:
            ops = self.q[name]
            def runner(e, ops=ops):
                for f in ops:
                    f(e)
            getattr(blk, name)(runner)


class DPool:

    def __init__(self, k, name, queue="sync", n=4):
        self.k = k
        self.queue = queue
        self.sems = [k.sem(f"{name}{i}") for i in range(n)]
        self.i = 0

    def dma(self, out, in_, wait=None, queue=None):
        s = self.sems[self.i % len(self.sems)]
        self.i += 1
        return self.k.op(queue or self.queue, "dma_start", out=out, in_=in_,
                         wait=wait, inc=(s, 16))

    def all(self):
        return [(s, self.k.val(s)) for s in self.sems if self.k.val(s) > 0]


def sbuf(k, name, shape, dt=F32):
    return k.es.enter_context(k.nc.sbuf_tensor("sb_" + name, list(shape), dt))


def psum(k, name, shape=(128, 512), dt=F32):
    return k.es.enter_context(k.nc.psum_tensor("pb_" + name, list(shape), dt))


FVALS = [EPS_LN, 1.0 / D, -1.0, float(D), -1e30, -0.5, 0.5, 2.0, 1e-6, 1e-12, 384.0, 1.0 / 384.0, 1.0 / 192.0]


def make_consts():
    c = {}
    c["FCONSTS"] = np.tile(np.asarray(FVALS, np.float32)[None, :], (128, 1))
    c["IDENT"] = np.eye(128, dtype=np.float32)
    c["ONESCOL"] = np.ones((128, 1), np.float32)
    c["ONESROW"] = np.ones((1, 128), np.float32)
    c["ONESROW512"] = np.ones((1, 512), np.float32)
    c["CUMUL"] = np.triu(np.ones((128, 128), np.float32))
    c["MASK_STRICT"] = np.where(np.arange(128)[:, None] < np.arange(128)[None, :], 0.0, -1e9).astype(np.float32)
    c["MASK_INCL"] = np.where(np.arange(128)[:, None] <= np.arange(128)[None, :], 0.0, -1e9).astype(np.float32)
    c["MASK_STRICT_T"] = np.ascontiguousarray(c["MASK_STRICT"].T)
    c["IDENT4"] = np.tile(np.eye(128, dtype=np.float32), (1, 4))
    sel = np.zeros((4, 4 * 128), np.float32)
    for h in range(4):
        sel[h, h * 128:(h + 1) * 128] = 1.0
    c["SEL4"] = sel
    sel8 = np.zeros((8, 8 * 128), np.float32)
    for e in range(8):
        sel8[e, e * 128:(e + 1) * 128] = 1.0
    c["SEL8"] = sel8
    return c


_CONSTS = make_consts()
CONST_SPECS = {k: v.shape for k, v in _CONSTS.items()}


def declare_consts(nc):
    return {name: nc.dram_tensor("C_" + name, list(shape), F32, kind="ExternalInput").ap()
            for name, shape in CONST_SPECS.items()}


def const_inputs():
    return {"C_" + name: v for name, v in _CONSTS.items()}


def load_const(k, aps, name, ld):
    t = sbuf(k, "c_" + name, CONST_SPECS[name], F32)
    k.op("sync", "dma_start", out=t[:], in_=aps[name], inc=(ld, 16))
    return t


def register_fconsts(k, caps, ld):
    t = load_const(k, caps, "FCONSTS", ld)
    for j, v in enumerate(FVALS):
        k.nc.const_aps.register(F32, float(v), t[:, j:j + 1])
    return t


def build_ln0(stop=9):
    nc = bass.Bass("TRN2", target_bir_lowering=False)
    xT = nc.dram_tensor("xT", [D, L], F32, kind="ExternalInput").ap()
    cT = nc.dram_tensor("cT", [D, 8], F32, kind="ExternalInput").ap()
    wrT = nc.dram_tensor("wrT", [D, 8], F32, kind="ExternalInput").ap()
    caps = declare_consts(nc)
    o_xnhi = nc.dram_tensor("xnhi", [D, L], BF16, kind="ExternalOutput").ap()
    o_xnlo = nc.dram_tensor("xnlo", [D, L], BF16, kind="ExternalOutput").ap()
    o_gatesT = nc.dram_tensor("gatesT", [8, L], F32, kind="ExternalOutput").ap()

    es = ExitStack()
    with es:
        k = K(nc, es)
        ld = k.sem("ld")
        st = k.sem("stn")
        so = k.sem("so")

        register_fconsts(k, caps, ld)
        ident = load_const(k, caps, "IDENT", ld)
        onescol = load_const(k, caps, "ONESCOL", ld)
        onesrow = load_const(k, caps, "ONESROW", ld)
        onesrow512 = load_const(k, caps, "ONESROW512", ld)

        xg = sbuf(k, "xg", [128, DKT * 512], F32)
        xsq = sbuf(k, "xsq", [128, 2 * 512], F32)
        scw_all = sbuf(k, "scw_all", [128, DKT * 16], F32)
        scw_hi = sbuf(k, "scw_hi", [128, DKT * 16], BF16)
        scw_lo = sbuf(k, "scw_lo", [128, DKT * 16], BF16)
        csq = sbuf(k, "csq", [128, 8], F32)
        rowsg = sbuf(k, "rowsg", [1, 8 * 512], F32)
        rn = sbuf(k, "rn", [1, L], F32)
        cnrow = sbuf(k, "cnrow", [1, 16], F32)
        bcA = sbuf(k, "bcA", [128, 512], F32)
        bcB = sbuf(k, "bcB", [128, 512], F32)
        xn32 = sbuf(k, "xn32", [128, 512], F32)
        tsc = sbuf(k, "tsc", [128, 512], F32)
        xnhi = sbuf(k, "xnhi", [128, DKT * L], BF16)
        xnlo = sbuf(k, "xnlo", [128, DKT * L], BF16)
        lg = sbuf(k, "lg", [16, L], F32)
        bccn = sbuf(k, "bccn", [8, 512], F32)
        bcrn = sbuf(k, "bcrn", [8, 512], F32)
        ltm = sbuf(k, "ltm", [128, 128], F32)
        gtm = sbuf(k, "gtm", [128, 128], F32)
        m1 = sbuf(k, "m1", [128, 16], F32)
        m2 = sbuf(k, "m2", [128, 16], F32)
        ssum = sbuf(k, "ssum", [128, 16], F32)
        eq = sbuf(k, "eq", [128, 128], F32)
        keep = sbuf(k, "keep", [128, 128], F32)
        etm = sbuf(k, "etm", [128, 128], F32)
        gatesT = sbuf(k, "gatesT", [8, L], F32)

        ps_s = psum(k, "ps_s", (128, 1024))
        ps_bc = psum(k, "ps_bc", (128, 1024))
        ps_route = psum(k, "ps_route", (128, 1024))
        ps_tr = psum(k, "ps_tr")

        for kt in range(DKT):
            k.op("sync", "dma_start", out=scw_all[:, kt * 16:kt * 16 + 8],
                 in_=cT[kt * 128:(kt + 1) * 128, :], inc=(ld, 16))
            k.op("sync", "dma_start", out=scw_all[:, kt * 16 + 8:kt * 16 + 16],
                 in_=wrT[kt * 128:(kt + 1) * 128, :], inc=(ld, 16))
        M_LDC = (ld, k.val(ld))

        if stop <= 1:
            k.wait("gpsimd", ld, k.val(ld))
            k.op("gpsimd", "dma_start", out=o_xnhi[0:128, :], in_=xnhi[:, 0:L], inc=(so, 16))
            k.op("gpsimd", "dma_start", out=o_gatesT, in_=gatesT[:], inc=(so, 16))
            k.wait("gpsimd", so, k.val(so))
            k.finalize()
            return nc

        prev_mm = None
        for kt in range(DKT):
            w = [M_LDC] if kt == 0 else [prev_mm]
            msq = k.op("scalar", "activation", csq[:, 0:8], scw_all[:, kt * 16:kt * 16 + 8],
                       AF.Square, wait=w)
            prev_mm = k.op("tensor", "matmul", ps_bc[0:1, 0:8], onescol[:], csq[:, 0:8],
                           start=(kt == 0), stop=(kt == DKT - 1),
                           wait=[msq, M_LDC] if kt == 0 else [msq], inc=(st, 1))
        k.op("scalar", "activation", cnrow[0:1, 0:8], ps_bc[0:1, 0:8], AF.Ln, wait=[prev_mm])
        m_cn = k.op("scalar", "activation", cnrow[0:1, 0:8], cnrow[0:1, 0:8], AF.Exp, scale=-0.5)
        k.op("vector", "tensor_copy", scw_hi[:], scw_all[:], wait=[M_LDC])
        m_scw = k.op("vector", "tensor_tensor", scw_lo[:], scw_all[:], scw_hi[:], ALU.subtract)

        if stop <= 2:
            k.op("gpsimd", "dma_start", out=o_xnhi[0:128, :], in_=xnhi[:, 0:L], wait=[m_scw, m_cn], inc=(so, 16))
            k.op("gpsimd", "dma_start", out=o_gatesT, in_=gatesT[:], inc=(so, 16))
            k.wait("gpsimd", so, k.val(so))
            k.finalize()
            return nc

        Rg = lambda i: rowsg[0:1, i * 512:(i + 1) * 512]
        m_xn_done = None
        m_lg_done = None
        m_ev = None
        m_sq_prev = [None, None]
        nsq = 0
        for g in range(NTG):
            gs = slice(g * 512, (g + 1) * 512)
            ldw = []
            if m_xn_done is not None:
                ldw = [m_xn_done]
            for kt in range(DKT):
                k.op("sync", "dma_start", out=xg[:, kt * 512:(kt + 1) * 512],
                     in_=xT[kt * 128:(kt + 1) * 128, gs],
                     wait=ldw if kt == 0 else None, inc=(ld, 16))
            M_LDG = (ld, k.val(ld))

            for kt in range(DKT):
                buf = xsq[:, (nsq % 2) * 512:(nsq % 2) * 512 + 512]
                w = [M_LDG] if kt == 0 else []
                if m_sq_prev[nsq % 2] is not None:
                    w = w + [m_sq_prev[nsq % 2]]
                msq = k.op("scalar", "activation", buf, xg[:, kt * 512:(kt + 1) * 512],
                           AF.Square, wait=w)
                wmm = ([M_LDG] + ([m_ev] if m_ev is not None else [])) if kt == 0 else None
                k.op("tensor", "matmul", ps_s[0:1, 0:512], onescol[:], xg[:, kt * 512:(kt + 1) * 512],
                     start=(kt == 0), stop=(kt == DKT - 1),
                     wait=wmm)
                m_sq_prev[nsq % 2] = k.op("tensor", "matmul", ps_s[0:1, 512:1024], onescol[:], buf,
                                          start=(kt == 0), stop=(kt == DKT - 1),
                                          wait=[msq], inc=(st, 1))
                nsq += 1
            m_stats = m_sq_prev[(nsq - 1) % 2]
            k.op("vector", "tensor_copy", Rg(0), ps_s[0:1, 0:512], wait=[m_stats])
            m_ev = k.op("vector", "tensor_copy", Rg(1), ps_s[0:1, 512:1024])
            k.op("vector", "tensor_scalar", Rg(2), Rg(0), 1.0 / D, None, ALU.mult)
            k.op("vector", "tensor_scalar", Rg(3), Rg(1), 1.0 / D, None, ALU.mult)
            k.op("vector", "tensor_tensor", Rg(6), Rg(2), Rg(2), ALU.mult)
            m_var = k.op("vector", "tensor_tensor", Rg(3), Rg(3), Rg(6), ALU.subtract)
            k.op("scalar", "activation", Rg(6), Rg(3), AF.Ln, bias=EPS_LN, wait=[m_var])
            m_rstd = k.op("scalar", "activation", Rg(4), Rg(6), AF.Exp, scale=-0.5)
            k.op("vector", "tensor_tensor", Rg(5), Rg(2), Rg(4), ALU.mult, wait=[m_rstd])
            k.op("vector", "tensor_scalar", Rg(5), Rg(5), -1.0, None, ALU.mult)
            m_nm = k.op("vector", "tensor_scalar", Rg(7), Rg(3), float(D), None, ALU.mult)
            m_ln2 = k.op("scalar", "activation", Rg(7), Rg(7), AF.Ln, wait=[m_nm])
            m_sub = k.op("vector", "tensor_tensor", Rg(7), Rg(7), Rg(6), ALU.subtract, wait=[m_ln2])
            m_rn = k.op("scalar", "activation", rn[0:1, gs], Rg(7), AF.Exp, scale=-0.5, wait=[m_sub])
            if stop <= 3:
                m_xn_done = m_ev
                m_lg_done = m_ev
                continue
            wbc1 = [m_nm] + ([m_cn] if g == 0 else [])
            m_bc1 = k.op("tensor", "matmul", ps_bc[0:128, 0:512], onesrow[:], Rg(4),
                         start=True, stop=True, wait=wbc1, inc=(st, 1))
            m_cA = k.op("vector", "tensor_copy", bcA[:], ps_bc[0:128, 0:512], wait=[m_bc1])
            m_bc2 = k.op("tensor", "matmul", ps_bc[0:128, 0:512], onesrow[:], Rg(5),
                         start=True, stop=True, wait=[m_cA], inc=(st, 1))
            m_cB = k.op("vector", "tensor_copy", bcB[:], ps_bc[0:128, 0:512], wait=[m_bc2])
            for kt in range(DKT):
                xsl = slice(kt * 512, (kt + 1) * 512)
                osl = slice(kt * L + g * 512, kt * L + g * 512 + 512)
                k.op("vector", "tensor_tensor", tsc[:], xg[:, xsl], bcA[:], ALU.mult)
                k.op("vector", "tensor_tensor", xn32[:], tsc[:], bcB[:], ALU.add)
                k.op("vector", "tensor_copy", xnhi[:, osl], xn32[:])
                m_xn_done = k.op("vector", "tensor_tensor", xnlo[:, osl], xn32[:], xnhi[:, osl], ALU.subtract)
            if stop <= 4:
                m_lg_done = m_xn_done
                continue
            rw = [m_xn_done, m_scw]
            if m_lg_done is not None:
                rw.append(m_lg_done)
            for kt in range(DKT):
                osl = slice(kt * L + g * 512, kt * L + g * 512 + 512)
                chi = scw_hi[:, kt * 16:kt * 16 + 8]
                clo = scw_lo[:, kt * 16:kt * 16 + 8]
                whi = scw_hi[:, kt * 16 + 8:kt * 16 + 16]
                wlo = scw_lo[:, kt * 16 + 8:kt * 16 + 16]
                first, last = kt == 0, kt == DKT - 1
                k.op("tensor", "matmul", ps_route[0:8, 0:512], chi, xnhi[:, osl],
                     start=first, stop=False, wait=rw if first else None)
                k.op("tensor", "matmul", ps_route[0:8, 0:512], chi, xnlo[:, osl], start=False, stop=False)
                k.op("tensor", "matmul", ps_route[0:8, 0:512], clo, xnhi[:, osl], start=False, stop=last)
                k.op("tensor", "matmul", ps_route[0:8, 512:1024], whi, xnhi[:, osl], start=first, stop=False)
                k.op("tensor", "matmul", ps_route[0:8, 512:1024], whi, xnlo[:, osl], start=False, stop=False)
                mr = k.op("tensor", "matmul", ps_route[0:8, 512:1024], wlo, xnhi[:, osl],
                          start=False, stop=last, inc=(st, 1) if last else None)
            m_bcn = k.op("tensor", "matmul", ps_bc[0:8, 0:512], cnrow[0:1, 0:8], onesrow512[:],
                         start=True, stop=True, wait=[m_cn, m_cB], inc=(st, 1))
            m_brn = k.op("tensor", "matmul", ps_bc[0:8, 512:1024], onesrow[0:1, 0:8], rn[0:1, gs],
                         start=True, stop=True, wait=[m_rn], inc=(st, 1))
            k.op("vector", "tensor_copy", bccn[:], ps_bc[0:8, 0:512], wait=[m_bcn, m_brn, mr])
            k.op("vector", "tensor_copy", bcrn[:], ps_bc[0:8, 512:1024])
            k.op("vector", "tensor_tensor", lg[0:8, gs], ps_route[0:8, 0:512], bccn[:], ALU.mult)
            k.op("vector", "tensor_tensor", lg[0:8, gs], lg[0:8, gs], bcrn[:], ALU.mult)
            m_lg_done = k.op("vector", "tensor_tensor", lg[0:8, gs], lg[0:8, gs],
                             ps_route[0:8, 512:1024], ALU.add)
        if stop <= 5:
            k.op("gpsimd", "dma_start", out=o_xnhi[0:128, :], in_=xnhi[:, 0:L], wait=[m_lg_done], inc=(so, 16))
            k.op("gpsimd", "dma_start", out=o_gatesT, in_=gatesT[:], inc=(so, 16))
            k.wait("gpsimd", so, k.val(so))
            k.finalize()
            return nc
        m_trf = None
        for c in range(16):
            m_trf = k.tr(ps_tr[0:128, c * 8:(c + 1) * 8],
                 lg[0:8, c * 128:(c + 1) * 128], ident[0:8, 0:8],
                 wait=[m_lg_done, M_LDC] if c == 0 else None,
                 inc=(st, 1) if c == 15 else None)
        m_ltm = k.op("vector", "tensor_copy", ltm[:], ps_tr[0:128, 0:128], wait=[m_trf])
        if stop <= 6:
            k.op("gpsimd", "dma_start", out=o_xnhi[0:128, :], in_=xnhi[:, 0:L], wait=[m_ltm], inc=(so, 16))
            k.op("gpsimd", "dma_start", out=o_gatesT, in_=gatesT[:], inc=(so, 16))
            k.wait("gpsimd", so, k.val(so))
            k.finalize()
            return nc
        ltm3 = ltm[:].rearrange("p (c e) -> p c e", e=8)
        bview = lambda t: t[:].unsqueeze(2).broadcast_to([128, 16, 8])
        k.op("vector", "tensor_reduce", m1[:], ltm3, AX.X, ALU.max)
        k.op("vector", "tensor_tensor", eq[:].rearrange("p (c e) -> p c e", e=8), ltm3, bview(m1), ALU.is_ge)
        k.op("vector", "scalar_tensor_tensor", etm[:], eq[:], -1e30, ltm[:], ALU.mult, ALU.add)
        k.op("vector", "tensor_reduce", m2[:], etm[:].rearrange("p (c e) -> p c e", e=8), AX.X, ALU.max)
        k.op("vector", "tensor_tensor", keep[:].rearrange("p (c e) -> p c e", e=8), ltm3, bview(m2), ALU.is_ge)
        m_e = k.op("vector", "tensor_tensor", etm[:].rearrange("p (c e) -> p c e", e=8), ltm3, bview(m1), ALU.subtract)
        m_ex = k.op("scalar", "activation", etm[:], etm[:], AF.Exp, wait=[m_e])
        k.op("vector", "tensor_tensor", etm[:], etm[:], keep[:], ALU.mult, wait=[m_ex])
        k.op("vector", "tensor_reduce", ssum[:], etm[:].rearrange("p (c e) -> p c e", e=8), AX.X, ALU.add)
        k.op("vector", "reciprocal", ssum[:], ssum[:])
        m_gtm = k.op("vector", "tensor_tensor", gtm[:].rearrange("p (c e) -> p c e", e=8),
                     etm[:].rearrange("p (c e) -> p c e", e=8), bview(ssum), ALU.mult)
        if stop <= 7:
            k.op("gpsimd", "dma_start", out=o_xnhi[0:128, :], in_=xnhi[:, 0:L], wait=[m_gtm], inc=(so, 16))
            k.op("gpsimd", "dma_start", out=o_gatesT, in_=gatesT[:], inc=(so, 16))
            k.wait("gpsimd", so, k.val(so))
            k.finalize()
            return nc
        m_gt_ev = None
        for g in range(NTG):
            m_trb = None
            for j in range(4):
                c = g * 4 + j
                w = [m_gtm] if (g == 0 and j == 0) else ([m_gt_ev] if j == 0 else None)
                m_trb = k.tr(ps_tr[0:8, j * 128:(j + 1) * 128],
                     gtm[:, c * 8:(c + 1) * 8], ident[:], wait=w,
                     inc=(st, 1) if j == 3 else None)
            m_gt_ev = k.op("vector", "tensor_copy", gatesT[:, g * 512:(g + 1) * 512],
                           ps_tr[0:8, 0:512], wait=[m_trb])

        if stop <= 4:
            k.op("gpsimd", "dma_start", out=o_xnhi[0:128, :], in_=xnhi[:, 0:L], wait=[m_xn_done], inc=(so, 16))
            k.op("gpsimd", "dma_start", out=o_gatesT, in_=gatesT[:], inc=(so, 16))
            k.wait("gpsimd", so, k.val(so))
            k.finalize()
            return nc
        for kt in range(DKT):
            k.op("gpsimd", "dma_start", out=o_xnhi[kt * 128:(kt + 1) * 128, :],
                 in_=xnhi[:, kt * L:(kt + 1) * L],
                 wait=[m_xn_done] if kt == 0 else None, inc=(so, 16))
            k.op("gpsimd", "dma_start", out=o_xnlo[kt * 128:(kt + 1) * 128, :],
                 in_=xnlo[:, kt * L:(kt + 1) * L], inc=(so, 16))
        k.op("gpsimd", "dma_start", out=o_gatesT, in_=gatesT[:], wait=[m_gt_ev], inc=(so, 16))
        k.wait("gpsimd", so, k.val(so))
        k.finalize()
    return nc


def build_conv(nmt=16, notaps=False, nomm=False, noevac=False, serial_evac=False, ngroups=4, nomemset=False, evac_eng='scalar', evac_dst='xzrow1', rawxz=False, rawacc=False):
    nc = bass.Bass("TRN2", target_bir_lowering=False)
    xn = nc.dram_tensor("xn", [D, L], BF16, kind="ExternalInput").ap()
    W_in = nc.dram_tensor("W_in", [D, 2 * D], F32, kind="ExternalInput").ap()
    cwx = nc.dram_tensor("cwx", [D, 4], F32, kind="ExternalInput").ap()
    cbx = nc.dram_tensor("cbx", [D, 1], F32, kind="ExternalInput").ap()
    cwz = nc.dram_tensor("cwz", [D, 4], F32, kind="ExternalInput").ap()
    cbz = nc.dram_tensor("cbz", [D, 1], F32, kind="ExternalInput").ap()
    caps = declare_consts(nc)
    o_xs = nc.dram_tensor("xs", [D, L], BF16, kind="ExternalOutput").ap()
    o_zs = nc.dram_tensor("zs", [D, L], BF16, kind="ExternalOutput").ap()

    es = ExitStack()
    with es:
        k = K(nc, es)
        ld = k.sem("ld")
        ldw = k.sem("ldw")
        st = k.sem("stn")
        so = k.sem("so")
        pin = DPool(k, "pin", "sync", 4)
        register_fconsts(k, caps, ld)

        xnb = sbuf(k, "xnb", [128, DKT * L], BF16)
        w32 = sbuf(k, "w32", [128, 2 * D], F32)
        w16 = sbuf(k, "w16", [128, 2 * D], BF16)
        cw = sbuf(k, "cw", [128, 2 * 4 * DKT], F32)
        cb = sbuf(k, "cb", [128, 2 * DKT], F32)
        xzrow = sbuf(k, "xzrow", [128, 2 * 2051], F32)
        fresh = sbuf(k, "fresh", [128, 512], F32)
        acc1 = sbuf(k, "acc1", [128, L], F32)
        acc2 = sbuf(k, "acc2", [128, L], F32)
        outt = sbuf(k, "outt", [128, L], BF16)
        ps = [psum(k, "ps0"), psum(k, "ps1")]

        for kt in range(DKT):
            pin.dma(xnb[:, kt * L:(kt + 1) * L], xn[kt * 128:(kt + 1) * 128, :])
        for kt in range(DKT):
            k.op("sync", "dma_start", out=cw[:, kt * 4:kt * 4 + 4],
                 in_=cwx[kt * 128:(kt + 1) * 128, :], inc=(ld, 16))
            k.op("sync", "dma_start", out=cw[:, 32 + kt * 4:32 + kt * 4 + 4],
                 in_=cwz[kt * 128:(kt + 1) * 128, :], inc=(ld, 16))
        for kt in range(DKT):
            k.op("sync", "dma_start", out=cb[:, kt:kt + 1], in_=cbx[kt * 128:(kt + 1) * 128, :], inc=(ld, 16))
            k.op("sync", "dma_start", out=cb[:, DKT + kt:DKT + kt + 1], in_=cbz[kt * 128:(kt + 1) * 128, :], inc=(ld, 16))
        M_LD = (ld, k.val(ld))

        m_zero = None
        if not nomemset:
            for bi in range(2):
                base = bi * 2051
                k.op("vector", "memset", xzrow[:, base:base + 1], 0.0)
                m_zero = k.op("vector", "memset", xzrow[:, base + 2049:base + 2051], 0.0)

        m_cast = [None, None]
        m_conv = [None, None]
        m_evac = [None, None]
        m_evac_last = None
        m_mmdone = [None, None]
        m_silu_prev = None
        for mt in range(nmt):
            bi = mt % 2
            wlw = [m_cast[bi]] if m_cast[bi] is not None else []
            if m_mmdone[bi] is not None:
                pass
            mws = []
            for kt in range(DKT):
                mws.append(pin.dma(w32[:, bi * D + kt * 128:bi * D + (kt + 1) * 128],
                                   W_in[kt * 128:(kt + 1) * 128, mt * 128:(mt + 1) * 128],
                                   wait=wlw if kt == 0 else None))
            M_W = mws
            mc = None
            for kt in range(DKT):
                w = None
                if kt == 0:
                    w = list(M_W)
                    if m_mmdone[bi] is not None:
                        w.append(m_mmdone[bi])
                mc = k.op("gpsimd", "tensor_copy", w16[:, bi * D + kt * 128:bi * D + (kt + 1) * 128],
                          w32[:, bi * D + kt * 128:bi * D + (kt + 1) * 128],
                          wait=w)
            m_cast[bi] = mc
            base = bi * 2051
            if nomm:
                continue
            for g in range(ngroups):
                pbank = (mt * NTG + g) % 2
                for kt in range(DKT):
                    w = []
                    if kt == 0:
                        if g == 0:
                            w = [mc, M_LD]
                            if m_conv[bi] is not None:
                                w.append(m_conv[bi])
                        if m_evac[pbank] is not None:
                            w.append(m_evac[pbank])
                        if serial_evac and m_evac_last is not None:
                            w.append(m_evac_last)
                    m_mm = k.op("tensor", "matmul", ps[pbank][0:128, 0:512],
                         w16[:, bi * D + kt * 128:bi * D + (kt + 1) * 128],
                         xnb[:, kt * L + g * 512:kt * L + g * 512 + 512],
                         start=(kt == 0), stop=(kt == DKT - 1),
                         wait=w if w else None,
                         inc=(st, 1) if kt == DKT - 1 else None)
                m_mmdone[bi] = m_mm
                if noevac:
                    m_evac[pbank] = m_mm
                    m_evac_last = m_mm
                    continue
                if evac_dst == "fresh":
                    dstap = fresh[:, 0:512]
                elif evac_dst == "xzrow0":
                    dstap = xzrow[:, base + g * 512:base + g * 512 + 512]
                else:
                    dstap = xzrow[:, base + 1 + g * 512:base + 1 + g * 512 + 512]
                m_evac[pbank] = k.op(evac_eng, "copy" if evac_eng == "scalar" else "tensor_copy",
                                  dstap,
                                  ps[pbank][0:128, 0:512],
                                  wait=[m_mm] + ([m_zero] if (mt == 0 and g == 0 and m_zero is not None) else []))
                m_evac_last = m_evac[pbank]
            if rawxz:
                mr = k.op("vector", "tensor_copy", outt[:], xzrow[:, base + 1:base + 2049], wait=[m_evac_last])
                dst = o_xs if mt < 8 else o_zs
                row = (mt % 8) * 128
                k.op("gpsimd", "dma_start", out=dst[row:row + 128, :], in_=outt[:], wait=[mr], inc=(so, 16))
                k.wait("vector", so, k.val(so))
                continue
            if notaps:
                continue
            cwcol = (0 if mt < 8 else 32) + (mt % 8) * 4
            cbcol = (0 if mt < 8 else DKT) + (mt % 8)
            wfirst = [m_evac_last] + ([m_silu_prev] if m_silu_prev is not None else [])
            k.op("vector", "tensor_scalar", acc1[:], xzrow[:, base + 0:base + 2048],
                 cw[:, cwcol:cwcol + 1], None, ALU.mult, wait=wfirst)
            k.op("vector", "scalar_tensor_tensor", acc2[:], xzrow[:, base + 1:base + 1 + 2048],
                 cw[:, cwcol + 1:cwcol + 2], acc1[:], ALU.mult, ALU.add)
            k.op("vector", "scalar_tensor_tensor", acc1[:], xzrow[:, base + 2:base + 2 + 2048],
                 cw[:, cwcol + 2:cwcol + 3], acc2[:], ALU.mult, ALU.add)
            k.op("vector", "scalar_tensor_tensor", acc2[:], xzrow[:, base + 3:base + 3 + 2048],
                 cw[:, cwcol + 3:cwcol + 4], acc1[:], ALU.mult, ALU.add)
            m_conv[bi] = k.mark("vector")
            if rawacc:
                mr = k.op("vector", "tensor_copy", outt[:], acc2[:])
                dst = o_xs if mt < 8 else o_zs
                row = (mt % 8) * 128
                k.op("gpsimd", "dma_start", out=dst[row:row + 128, :], in_=outt[:], wait=[mr], inc=(so, 16))
                k.wait("vector", so, k.val(so))
                m_silu_prev = mr
                continue
            m_act = k.op("scalar", "activation", outt[:], acc2[:], AF.Silu,
                         bias=cb[:, cbcol:cbcol + 1], wait=[m_conv[bi]])
            m_silu_prev = m_act
            dst = o_xs if mt < 8 else o_zs
            row = (mt % 8) * 128
            k.op("scalar", "dma_start", out=dst[row:row + 128, :], in_=outt[:], wait=[m_act], inc=(so, 16))
            k.wait("scalar", so, k.val(so))
        k.wait("gpsimd", so, k.val(so))
        k.finalize()
    return nc


def build_proj(stop=9, qstop=9):
    nc = bass.Bass("TRN2", target_bir_lowering=False)
    xs = nc.dram_tensor("xs", [D, L], BF16, kind="ExternalInput").ap()
    W_q = nc.dram_tensor("W_q", [D, 768], F32, kind="ExternalInput").ap()
    W_k = nc.dram_tensor("W_k", [D, 768], F32, kind="ExternalInput").ap()
    W_v = nc.dram_tensor("W_v", [D, 1536], F32, kind="ExternalInput").ap()
    W_g = nc.dram_tensor("W_g", [D, 1536], F32, kind="ExternalInput").ap()
    W_gkb = nc.dram_tensor("W_gkb", [D, 8], F32, kind="ExternalInput").ap()
    dtb_bb = nc.dram_tensor("dtb_bb", [1, 8], F32, kind="ExternalInput").ap()
    A_log = nc.dram_tensor("A_log", [4, 1], F32, kind="ExternalInput").ap()
    HSELK = nc.dram_tensor("HSELK", [6 * 128, 4], F32, kind="ExternalInput").ap()
    HSELKT = nc.dram_tensor("HSELKT", [4, 6 * 128], F32, kind="ExternalInput").ap()
    caps = declare_consts(nc)
    o_qdm = nc.dram_tensor("qdm", [768, L], BF16, kind="ExternalOutput").ap()
    o_kdm = nc.dram_tensor("kdm", [768, L], BF16, kind="ExternalOutput").ap()
    o_ktm = nc.dram_tensor("ktm", [L, 768], BF16, kind="ExternalOutput").ap()
    o_vtm = nc.dram_tensor("vtm", [L, 1536], BF16, kind="ExternalOutput").ap()
    o_gtm = nc.dram_tensor("gtm", [L, 1536], BF16, kind="ExternalOutput").ap()
    o_G = nc.dram_tensor("G", [4, L], F32, kind="ExternalOutput").ap()
    o_beta = nc.dram_tensor("beta", [L, 4], F32, kind="ExternalOutput").ap()

    es = ExitStack()
    with es:
        k = K(nc, es)
        ld = k.sem("ld")
        st = k.sem("stn")
        pa = DPool(k, "pa", "sync", 4)
        pw = DPool(k, "pw", "sync", 4)
        po = DPool(k, "po", "gpsimd", 4)
        register_fconsts(k, caps, ld)
        ident = load_const(k, caps, "IDENT", ld)
        onesrow = load_const(k, caps, "ONESROW", ld)
        cumul = load_const(k, caps, "CUMUL", ld)
        M_CONST = (ld, k.val(ld))

        xsb = sbuf(k, "xsb", [128, DKT * L], BF16)
        wq16 = sbuf(k, "wq16", [128, DKT * 768], BF16)
        wk16 = sbuf(k, "wk16", [128, DKT * 768], BF16)
        wv16 = sbuf(k, "wv16", [128, DKT * 1536], BF16)
        wg16 = sbuf(k, "wg16", [128, DKT * 1536], BF16)
        wgkb16 = sbuf(k, "wgkb16", [128, DKT * 8], BF16)
        wld4 = sbuf(k, "wld4", [128, 4 * 1536], F32)
        hselk = sbuf(k, "hselk", [128, 6 * 4], F32)
        hselkt = sbuf(k, "hselkt", [4, 6 * 128], F32)
        dtbb = sbuf(k, "dtbb", [1, 8], F32)
        negA = sbuf(k, "negA", [4, 1], F32)
        qf = sbuf(k, "qf", [128, 6 * 512], F32)
        qsq = sbuf(k, "qsq", [128, 512], F32)
        rstdq = sbuf(k, "rstdq", [4, L], F32)
        rstdk = sbuf(k, "rstdk", [4, L], F32)
        rowsc = sbuf(k, "rowsc", [4, 512], F32)
        outb = sbuf(k, "outb", [128, 2 * 512], BF16)
        rstdTall = sbuf(k, "rstdTall", [128, 64], F32)
        gkball = sbuf(k, "gkball", [128, 256], F32)
        Grow = sbuf(k, "Grow", [4, L], F32)

        ps_a = psum(k, "ps_a", (128, 1024))
        ps_s = psum(k, "ps_s", (128, 512))
        ps_m = psum(k, "ps_m", (128, 512))

        for kt in range(DKT):
            pa.dma(xsb[:, kt * L:(kt + 1) * L], xs[kt * 128:(kt + 1) * 128, :])
        for t in range(6):
            pa.dma(hselk[:, t * 4:(t + 1) * 4], HSELK[t * 128:(t + 1) * 128, :])
        pa.dma(hselkt[0:4, :], HSELKT)
        pa.dma(dtbb[:], dtb_bb)
        pa.dma(negA[:], A_log)
        M_LDL = pa.all()
        M_LD = M_LDL[0]
        m_nA = k.op("scalar", "activation", negA[:], negA[:], AF.Exp, wait=M_LDL)
        m_nA = k.op("vector", "tensor_scalar", negA[:], negA[:], -1.0, None, ALU.mult, wait=[m_nA])

        wcasts = []

        def load_w16(dst, src_ap, width):
            for kt in range(DKT):
                i = len(wcasts)
                buf = wld4[:, (i % 4) * 1536:(i % 4) * 1536 + width]
                mdl = pw.dma(buf, src_ap[kt * 128:(kt + 1) * 128, :],
                             wait=[wcasts[-4]] if i >= 4 else None)
                wcasts.append(k.op("scalar", "activation",
                                   dst[:, kt * width:(kt + 1) * width], buf, AF.Copy,
                                   wait=[mdl]))
            return wcasts[-1]

        m_wq = load_w16(wq16, W_q, 768)
        m_wk = load_w16(wk16, W_k, 768)
        m_wgkb = load_w16(wgkb16, W_gkb, 8)
        m_wv = load_w16(wv16, W_v, 1536)
        m_wg = load_w16(wg16, W_g, 1536)

        m_bank = [None, None]
        m_pss = None
        m_psm = None
        m_outb = [None, None]

        def qk_dm(w16t, m_w, o_dm, rstd_rows):
            nonlocal m_pss
            m_bcmul = None
            for g in range(NTG):
                gs = slice(g * 512, (g + 1) * 512)
                for mt in range(6):
                    bank = mt % 2
                    m = None
                    for kt in range(DKT):
                        w = []
                        if kt == 0:
                            if g == 0 and mt == 0:
                                w += [m_w] + M_LDL
                            if m_bank[bank] is not None:
                                w += (m_bank[bank] if isinstance(m_bank[bank], list) else [m_bank[bank]])
                        m = k.op("tensor", "matmul", ps_a[0:128, bank * 512:bank * 512 + 512],
                                 w16t[:, kt * 768 + mt * 128:kt * 768 + (mt + 1) * 128],
                                 xsb[:, kt * L + g * 512:kt * L + g * 512 + 512],
                                 start=(kt == 0), stop=(kt == DKT - 1),
                                 wait=w or None, inc=(st, 1) if kt == DKT - 1 else None)
                    m_cp = k.op("vector", "tensor_copy", qf[:, mt * 512:(mt + 1) * 512],
                                ps_a[0:128, bank * 512:bank * 512 + 512], wait=[m])
                    m_bank[bank] = [m_cp]
                    if qstop <= 2:
                        continue
                    m_sq = k.op("scalar", "activation", qsq[:],
                                qf[:, mt * 512:(mt + 1) * 512], AF.Square, wait=[m_cp])
                    wsum = [m_sq]
                    if mt == 0 and m_pss is not None:
                        wsum.append(m_pss)
                    k.op("tensor", "matmul", ps_s[0:4, 0:512], hselk[:, mt * 4:(mt + 1) * 4], qsq[:],
                         start=(mt == 0), stop=(mt == 5), wait=wsum,
                         inc=(st, 1) if mt == 5 else None)
                if qstop <= 3:
                    continue
                m_srow = (st, k.val(st))
                k.op("scalar", "activation", rowsc[0:4, :], ps_s[0:4, 0:512], AF.Ln,
                     bias=1e-6, wait=[m_srow])
                m_r = k.op("scalar", "activation", rstd_rows[0:4, gs], rowsc[0:4, :],
                           AF.Exp, scale=-0.5)
                m_pss = m_r
                if qstop <= 4:
                    continue
                for mt in range(6):
                    wbc = [m_r]
                    if m_bcmul is not None:
                        wbc.append(m_bcmul)
                    m_bc = k.op("tensor", "matmul", ps_s[0:128, 0:512],
                                hselkt[0:4, mt * 128:(mt + 1) * 128], rstd_rows[0:4, gs],
                                start=True, stop=True, wait=wbc, inc=(st, 1))
                    par = mt % 2
                    ob = outb[:, par * 512:par * 512 + 512]
                    wmul = [m_bc]
                    if m_outb[par] is not None:
                        wmul.append(m_outb[par])
                    m_bcmul = k.op("vector", "tensor_tensor", ob, qf[:, mt * 512:(mt + 1) * 512],
                                   ps_s[0:128, 0:512], ALU.mult, wait=wmul)
                    m_outb[par] = po.dma(o_dm[mt * 128:(mt + 1) * 128, gs], ob,
                                         wait=[m_bcmul])
                m_pss = m_bcmul

        if stop >= 2:
            qk_dm(wq16, m_wq, o_qdm, rstdq)
        if stop >= 3:
            qk_dm(wk16, m_wk, o_kdm, rstdk)

        m = None
        for c in range(16):
            cs = slice(c * 128, (c + 1) * 128)
            for kt in range(DKT):
                stat = xsb[:, kt * L + c * 128:kt * L + (c + 1) * 128]
                w = []
                if c == 0 and kt == 0:
                    w = [m_wgkb, M_CONST] + ([m_psm] if m_psm is not None else [])
                k.op("tensor", "matmul", ps_m[0:128, c * 8:c * 8 + 8], stat,
                     wgkb16[:, kt * 8:(kt + 1) * 8],
                     start=(kt == 0), stop=False, wait=w or None)
            m = k.op("tensor", "matmul", ps_m[0:128, c * 8:c * 8 + 8], onesrow[0:1, :],
                     dtbb[:], start=False, stop=True,
                     wait=M_LDL if c == 0 else None, inc=(st, 1) if c == 15 else None)
        m_gkev = k.op("vector", "tensor_copy", gkball[:, 0:128], ps_m[0:128, 0:128], wait=[m])
        m_psm = m_gkev
        gkv = gkball[:, 0:128].rearrange("p (c x) -> p c x", x=8)
        gkc = gkball[:, 128:192]
        btc_ = gkball[:, 192:256]
        k.op("vector", "tensor_copy", gkc.rearrange("p (c x) -> p c x", x=4),
             gkv[:, :, 0:4])
        m_cpb = k.op("vector", "tensor_copy", btc_.rearrange("p (c x) -> p c x", x=4),
                     gkv[:, :, 4:8])
        m_ab = k.op("scalar", "activation", gkball[:, 0:64], gkc, AF.Abs, wait=[m_cpb])
        m_ab = k.op("scalar", "activation", gkball[:, 0:64], gkball[:, 0:64], AF.Exp,
                    scale=-1.0)
        m_ab = k.op("scalar", "activation", gkball[:, 0:64], gkball[:, 0:64], AF.Ln,
                    bias=1.0)
        m_mx = k.op("vector", "tensor_scalar", gkc, gkc, 0.0, None, ALU.max, wait=[m_ab])
        m_sp = k.op("vector", "tensor_tensor", gkc, gkc, gkball[:, 0:64], ALU.add)
        m_bt = k.op("scalar", "activation", btc_, btc_, AF.Sigmoid, wait=[m_sp])
        for c in range(16):
            po.dma(o_beta[c * 128:(c + 1) * 128, :], btc_[:, c * 4:c * 4 + 4],
                   wait=[m_bt] if c == 0 else None)
        m_Gs = None
        for cb in range(4):
            m_G = None
            for cc in range(4):
                c = cb * 4 + cc
                w = []
                if cc == 0:
                    w = [m_sp] + ([m_Gs] if m_Gs is not None else
                                  ([m_pss] if m_pss is not None else []))
                m_G = k.op("tensor", "matmul", ps_s[0:4, cc * 128:(cc + 1) * 128],
                           gkc[:, c * 4:c * 4 + 4], cumul[:],
                           start=True, stop=True, wait=w or None,
                           inc=(st, 1) if cc == 3 else None)
            m_Gs = k.op("vector", "tensor_scalar", Grow[0:4, cb * 512:(cb + 1) * 512],
                        ps_s[0:4, 0:512], negA[:, 0:1], None, ALU.mult,
                        wait=[m_G, m_nA] if cb == 0 else [m_G])
        m_pss = m_Gs

        m_trB = None
        for c in range(16):
            wtr = []
            if c == 0:
                wtr = [m_psm]
            m_trB = k.tr(ps_m[0:128, c * 4:c * 4 + 4], rstdk[0:4, c * 128:(c + 1) * 128],
                         ident[0:4, 0:4], wait=wtr or None, inc=(st, 1))
        m_rT = k.op("vector", "tensor_copy", rstdTall[:], ps_m[0:128, 0:64], wait=[m_trB])
        m_psm = m_rT

        for c in range(16 if stop >= 5 else 0):
            cs = slice(c * 128, (c + 1) * 128)
            for wt16, o_t in [(wv16, o_vtm), (wg16, o_gtm)]:
                for j in range(3):
                    bank = j % 2
                    m = None
                    for kt in range(DKT):
                        stat = xsb[:, kt * L + c * 128:kt * L + (c + 1) * 128]
                        w = []
                        if kt == 0:
                            if c == 0 and j == 0 and wt16 is wv16:
                                w += [m_wv, m_wg]
                            if m_bank[bank] is not None:
                                w += (m_bank[bank] if isinstance(m_bank[bank], list) else [m_bank[bank]])
                        m = k.op("tensor", "matmul", ps_a[0:128, bank * 512:bank * 512 + 512],
                                 stat, wt16[:, kt * 1536 + j * 512:kt * 1536 + (j + 1) * 512],
                                 start=(kt == 0), stop=(kt == DKT - 1),
                                 wait=w or None, inc=(st, 1) if kt == DKT - 1 else None)
                    par = j % 2
                    ob = outb[:, par * 512:par * 512 + 512]
                    w = [m]
                    if m_outb[par] is not None:
                        w.append(m_outb[par])
                    m_ev = k.op("vector", "tensor_copy", ob,
                                ps_a[0:128, bank * 512:bank * 512 + 512], wait=w)
                    m_bank[bank] = m_ev
                    m_outb[par] = po.dma(o_t[cs, j * 512:(j + 1) * 512], ob, wait=[m_ev])
            for j in range(2):
                bank = j % 2
                m = None
                for kt in range(DKT):
                    stat = xsb[:, kt * L + c * 128:kt * L + (c + 1) * 128]
                    w = []
                    if kt == 0 and m_bank[bank] is not None:
                        w += (m_bank[bank] if isinstance(m_bank[bank], list) else [m_bank[bank]])
                    if kt == 0 and c == 0 and j == 0:
                        w.append(m_rT)
                    m = k.op("tensor", "matmul", ps_a[0:128, bank * 512:bank * 512 + 384],
                             stat, wk16[:, kt * 768 + j * 384:kt * 768 + (j + 1) * 384],
                             start=(kt == 0), stop=(kt == DKT - 1),
                             wait=w or None, inc=(st, 1) if kt == DKT - 1 else None)
                par = j % 2
                ob = outb[:, par * 512:par * 512 + 384]
                w = [m]
                if m_outb[par] is not None:
                    w.append(m_outb[par])
                bview = rstdTall[:, c * 4 + 2 * j:c * 4 + 2 * j + 2].unsqueeze(2).broadcast_to([128, 2, 192])
                m_ev = k.op("vector", "tensor_tensor",
                            ob.rearrange("p (h d) -> p h d", d=192),
                            ps_a[0:128, bank * 512:bank * 512 + 384].rearrange("p (h d) -> p h d", d=192),
                            bview, ALU.mult, wait=w)
                m_bank[bank] = m_ev
                m_outb[par] = po.dma(o_ktm[cs, j * 384:(j + 1) * 384], ob, wait=[m_ev])
        if stop >= 7:
            po.dma(o_G, Grow[0:4, :], wait=[m_Gs])
        for pp in (po, pa, pw):
            for s in pp.sems:
                k.wait("gpsimd", s, k.val(s))
        k.wait("gpsimd", ld, k.val(ld))
        k.finalize()
    return nc


_HSLC = {0: [(0, 0, 128), (1, 0, 64)], 1: [(1, 64, 64), (2, 0, 128)],
         2: [(3, 0, 128), (4, 0, 64)], 3: [(4, 64, 64), (5, 0, 128)]}


def build_delta():
    nc = bass.Bass("TRN2", target_bir_lowering=False)
    qdm = nc.dram_tensor("qdm", [768, L], BF16, kind="ExternalInput").ap()
    kdm = nc.dram_tensor("kdm", [768, L], BF16, kind="ExternalInput").ap()
    ktm = nc.dram_tensor("ktm", [L, 768], BF16, kind="ExternalInput").ap()
    vtm = nc.dram_tensor("vtm", [L, 1536], BF16, kind="ExternalInput").ap()
    gtm = nc.dram_tensor("gtm", [L, 1536], BF16, kind="ExternalInput").ap()
    Gin = nc.dram_tensor("G", [4, L], F32, kind="ExternalInput").ap()
    betain = nc.dram_tensor("beta", [L, 4], F32, kind="ExternalInput").ap()
    gnwbc = nc.dram_tensor("gnwbc", [128, 384], F32, kind="ExternalInput").ap()
    caps = declare_consts(nc)
    o_og = nc.dram_tensor("ogdm", [1536, L], BF16, kind="ExternalOutput").ap()

    es = ExitStack()
    with es:
        k = K(nc, es)
        ldc = k.sem("ldc")
        st = k.sem("stn")
        pl = DPool(k, "pl", "sync", 4)
        po = DPool(k, "po", "gpsimd", 4)
        register_fconsts(k, caps, ldc)
        ident = load_const(k, caps, "IDENT", ldc)
        ident4 = load_const(k, caps, "IDENT4", ldc)
        onesrow = load_const(k, caps, "ONESROW", ldc)
        mstrict = load_const(k, caps, "MASK_STRICT", ldc)
        mstrictT = load_const(k, caps, "MASK_STRICT_T", ldc)
        mincl = load_const(k, caps, "MASK_INCL", ldc)
        sel4 = load_const(k, caps, "SEL4", ldc)

        qc = sbuf(k, "qc", [128, 2 * 768], BF16)
        kc = sbuf(k, "kc", [128, 2 * 768], BF16)
        ktc = sbuf(k, "ktc", [128, 2 * 768], BF16)
        vc = sbuf(k, "vc", [128, 2 * 1536], BF16)
        gc = sbuf(k, "gc", [128, 2 * 1536], BF16)
        btc = sbuf(k, "btc", [128, 2 * 4], F32)
        Gr = sbuf(k, "Gr", [4, L], F32)
        GBr = sbuf(k, "GBr", [4, L], F32)
        brow = sbuf(k, "brow", [4, L], F32)
        gnw = sbuf(k, "gnw", [128, 384], F32)
        id16 = sbuf(k, "id16", [128, 128], BF16)
        Gtm = sbuf(k, "Gtm", [128, 2 * 4], F32)
        GBtm = sbuf(k, "GBtm", [128, 2 * 4], F32)
        gamtm = sbuf(k, "gamtm", [128, 2 * 4], F32)
        gTrow = sbuf(k, "gTrow", [1, 2 * 4], F32)
        gTbc = sbuf(k, "gTbc", [128, 2 * 4], F32)
        egti = sbuf(k, "egti", [128, 2 * 4], F32)
        bgam = sbuf(k, "bgam", [128, 2 * 4], F32)
        nbgam = sbuf(k, "nbgam", [128, 2 * 4], F32)
        Mts16 = sbuf(k, "Mts16", [128, 2 * 512], BF16)
        Ms16 = sbuf(k, "Ms16", [128, 2 * 512], BF16)
        At16 = sbuf(k, "At16", [128, 2 * 512], BF16)
        Xb = sbuf(k, "Xb", [128, 2 * 512], BF16)
        Yb = sbuf(k, "Yb", [128, 2 * 512], BF16)
        Rb = sbuf(k, "Rb", [128, 512], BF16)
        esc = sbuf(k, "esc", [128, 512], F32)
        gq16 = sbuf(k, "gq16", [128, 2 * 768], BF16)
        bv16 = sbuf(k, "bv16", [128, 2 * 4 * 384], BF16)
        nbgk16 = sbuf(k, "nbgk16", [128, 2 * 4 * 192], BF16)
        ktg16 = sbuf(k, "ktg16", [128, 2 * 4 * 192], BF16)
        sg32 = sbuf(k, "sg32", [128, 2 * 4 * 384], BF16)
        nw16 = sbuf(k, "nw16", [128, 4 * 256], BF16)
        u16 = sbuf(k, "u16", [128, 4 * 384], BF16)
        S0f = sbuf(k, "S0f", [128, 4 * 768], F32)
        S016 = sbuf(k, "S016", [128, 4 * 768], BF16)
        of32 = sbuf(k, "of32", [128, 4 * 384], F32)
        osq = sbuf(k, "osq", [128, 4 * 384], F32)
        orst = sbuf(k, "orst", [128, 8], F32)
        og16 = sbuf(k, "og16", [128, 4 * 384], BF16)
        ogst = sbuf(k, "ogst", [128, 4 * 384], BF16)

        psKK = psum(k, "psKK", (128, 512))
        psN1 = psum(k, "psN1", (128, 512))
        psN2 = psum(k, "psN2", (128, 512))
        psN3 = psum(k, "psN3", (128, 512))
        psU = psum(k, "psU", (128, 512))
        psO = psum(k, "psO", (128, 512))
        psM = psum(k, "psM", (128, 512))
        psT16 = psum(k, "psT16", (128, 512), BF16)

        k.op("sync", "dma_start", out=Gr[0:4, :], in_=Gin, inc=(ldc, 16))
        k.op("sync", "dma_start", out=gnw[:], in_=gnwbc, inc=(ldc, 16))
        M_LDC = (ldc, k.val(ldc))
        m_id16 = k.op("vector", "tensor_copy", id16[:], ident[:], wait=[M_LDC])
        m_tr = None
        for c in range(16):
            mb = pl.dma(btc[:, 0:4], betain[c * 128:(c + 1) * 128, :],
                        wait=[m_tr] if m_tr is not None else None)
            m_t = k.tr(psM[0:4, 0:128], btc[:, 0:4], ident[:], wait=[mb], inc=(st, 1))
            m_tr = k.op("vector", "tensor_copy", brow[0:4, c * 128:(c + 1) * 128],
                        psM[0:4, 0:128], wait=[m_t])
        m_ln = k.op("scalar", "activation", GBr[0:4, :], brow[0:4, :], AF.Ln, wait=[m_tr])
        m_gb = k.op("vector", "tensor_tensor", GBr[0:4, :], GBr[0:4, :], Gr[0:4, :], ALU.add,
                    wait=[m_ln])
        m_s0a = k.op("vector", "memset", S0f[:], 0.0)
        m_s0 = k.op("vector", "memset", S016[:], 0.0)

        def load_chunk(c, waitfree):
            p = c % 2
            cs = slice(c * 128, (c + 1) * 128)
            first = True
            for t in range(6):
                pl.dma(qc[:, p * 768 + t * 128:p * 768 + (t + 1) * 128],
                       qdm[t * 128:(t + 1) * 128, cs], wait=waitfree if first else None)
                first = False
                pl.dma(kc[:, p * 768 + t * 128:p * 768 + (t + 1) * 128],
                       kdm[t * 128:(t + 1) * 128, cs])
            pl.dma(ktc[:, p * 768:(p + 1) * 768], ktm[cs, :])
            pl.dma(vc[:, p * 1536:(p + 1) * 1536], vtm[cs, :])
            pl.dma(gc[:, p * 1536:(p + 1) * 1536], gtm[cs, :])
            pl.dma(btc[:, p * 4:(p + 1) * 4], betain[cs, :])
            return pl.all()

        def hsl(buf, p, h):
            return [(buf[r0:r0 + rows, p * 768 + t * 128:p * 768 + (t + 1) * 128], rows)
                    for (t, r0, rows) in _HSLC[h]]

        HR = {h: (_HSLC[h][0][2], _HSLC[h][1][2]) for h in range(4)}
        HB = {h: (_HSLC[h][0][1], _HSLC[h][1][1]) for h in range(4)}
        SA_f = lambda h: S0f[HB[h][0]:HB[h][0] + HR[h][0], h * 768:h * 768 + 384]
        SB_f = lambda h: S0f[HB[h][1]:HB[h][1] + HR[h][1], h * 768 + 384:h * 768 + 768]
        SA = lambda h: S016[HB[h][0]:HB[h][0] + HR[h][0], h * 768:h * 768 + 384]
        SB = lambda h: S016[HB[h][1]:HB[h][1] + HR[h][1], h * 768 + 384:h * 768 + 768]

        e3 = esc[:].rearrange("p (h t) -> p h t", t=128)
        ps3 = lambda ps: ps[0:128, 0:512].rearrange("p (h t) -> p h t", t=128)
        mrep = lambda t: t[:].unsqueeze(1).broadcast_to([128, 4, 128])

        m_KK_free = None
        m_N1_free = None
        m_N2_free = None
        m_N3_free = None
        m_psU_free = None
        m_psO_free = None
        m_psM_free = None
        m_psT_free = None
        m_free = None
        m_chain_mm = [None, None]
        m_chain_S = [None, None]
        m_seqdone = {h: None for h in range(4)}
        m_of_h = {h: None for h in range(4)}
        m_ogst = [None, None, None]
        m_sq_h = {h: None for h in range(4)}
        m_trog_h = {h: None for h in range(4)}
        m_S16 = {h: m_s0 for h in range(4)}
        m_u16_h = {h: None for h in range(4)}

        M_CH = [None, None]
        PM = [None, None]

        def prep(c, M_L):
            p = c % 2
            cs = slice(c * 128, (c + 1) * 128)
            P4 = slice(p * 4, (p + 1) * 4)
            wch = [m_chain_S[p]] if m_chain_S[p] is not None else []
            wmm = [m_chain_mm[p]] if m_chain_mm[p] is not None else []
            nonlocal m_N3_free, m_KK_free, m_N1_free, m_N2_free
            wps = [m_N3_free] if m_N3_free is not None else []
            m_t1 = k.tr(psN3[0:128, 0:4], Gr[0:4, cs], ident[0:4, 0:4],
                        wait=[m_gb] + wps + wch, inc=(st, 1))
            m_c1 = k.op("vector", "tensor_copy", Gtm[:, P4], psN3[0:128, 0:4], wait=[m_t1])
            m_t2 = k.tr(psN3[0:128, 4:8], GBr[0:4, cs], ident[0:4, 0:4], inc=(st, 1))
            m_c2 = k.op("vector", "tensor_copy", GBtm[:, P4], psN3[0:128, 4:8], wait=[m_t2])
            m_t3 = k.tr(psN3[0:1, 8:12], Gr[0:4, c * 128 + 127:c * 128 + 128],
                        ident[0:4, 0:4], inc=(st, 1))
            m_c3 = k.op("vector", "tensor_copy", gTrow[0:1, P4], psN3[0:1, 8:12], wait=[m_t3])
            m_t4 = k.op("tensor", "matmul", psN3[0:128, 12:16], onesrow[:], gTrow[0:1, P4],
                        start=True, stop=True, wait=[m_c3], inc=(st, 1))
            m_c4 = k.op("vector", "tensor_copy", gTbc[:, P4], psN3[0:128, 12:16], wait=[m_t4])
            m_gam = k.op("scalar", "activation", gamtm[:, P4], Gtm[:, P4], AF.Exp, wait=[m_c1])
            m_eg = k.op("vector", "tensor_tensor", egti[:, P4], gTbc[:, P4], Gtm[:, P4],
                        ALU.subtract, wait=[m_c4])
            m_eg = k.op("scalar", "activation", egti[:, P4], egti[:, P4], AF.Exp, wait=[m_eg])
            m_gT = k.op("scalar", "activation", gTbc[:, P4], gTbc[:, P4], AF.Exp)
            m_bg = k.op("vector", "tensor_tensor", bgam[:, P4], btc[:, P4], gamtm[:, P4],
                        ALU.mult, wait=M_L + [m_gam])
            m_nbg = k.op("vector", "tensor_scalar", nbgam[:, P4], bgam[:, P4], -1.0, None,
                         ALU.mult)
            m_kk = None
            m_kq = None
            for h in range(4):
                ks_ = hsl(kc, p, h)
                qs_ = hsl(qc, p, h)
                hb = slice(h * 128, (h + 1) * 128)
                for i, (ap, rows) in enumerate(ks_):
                    w = []
                    if h == 0 and i == 0:
                        w += M_L + wmm
                        if m_KK_free is not None:
                            w.append(m_KK_free)
                    m_kk = k.op("tensor", "matmul", psKK[0:128, hb], ap, ap,
                                start=(i == 0), stop=(i == 1), wait=w or None,
                                inc=(st, 1) if (h == 3 and i == 1) else None)
                for i, ((kap, kr), (qap, qr)) in enumerate(zip(ks_, qs_)):
                    w = [m_c4] if (h == 0 and i == 0) else []
                    m_kq = k.op("tensor", "matmul", psN3[0:128, hb], kap, qap,
                                start=(i == 0), stop=(i == 1), wait=w or None,
                                inc=(st, 1) if (h == 3 and i == 1) else None)
            bGB = GBtm[:, P4].unsqueeze(2).broadcast_to([128, 4, 128])
            bG = Gtm[:, P4].unsqueeze(2).broadcast_to([128, 4, 128])
            S5 = slice(p * 512, (p + 1) * 512)
            m_b2 = None
            for h in range(4):
                w = [m_N2_free] if (h == 0 and m_N2_free is not None) else None
                m_b2 = k.op("tensor", "matmul", psN2[0:128, h * 128:(h + 1) * 128],
                            sel4[0:4, h * 128:(h + 1) * 128], GBr[0:4, cs],
                            start=True, stop=True, wait=w,
                            inc=(st, 1) if h == 3 else None)
            m_e = k.op("vector", "scalar_tensor_tensor", e3, ps3(psN2), 1.0, mrep(mstrict),
                       ALU.mult, ALU.add, wait=[m_b2, m_c1, m_c2])
            m_e = k.op("vector", "tensor_tensor", e3, e3, bG, ALU.subtract)
            m_e = k.op("scalar", "activation", esc[:], esc[:], AF.Exp, wait=[m_e])
            wv = [m_e, m_kk] + wmm
            m_mts = k.op("vector", "tensor_tensor", Mts16[:, S5], esc[:], psKK[0:128, 0:512],
                         ALU.mult, wait=wv)
            m_y0 = k.op("gpsimd", "tensor_tensor",
                        Yb[:, S5].rearrange("p (h t) -> p h t", t=128),
                        mrep(ident), Mts16[:, S5].rearrange("p (h t) -> p h t", t=128),
                        ALU.subtract, wait=[M_LDC, m_mts] + wmm)
            m_N2_free = m_e
            m_b1 = None
            for h in range(4):
                w = [m_N1_free] if (h == 0 and m_N1_free is not None) else None
                m_b1 = k.op("tensor", "matmul", psN1[0:128, h * 128:(h + 1) * 128],
                            sel4[0:4, h * 128:(h + 1) * 128], Gr[0:4, cs],
                            start=True, stop=True, wait=w,
                            inc=(st, 1) if h == 3 else None)
            m_e = k.op("vector", "scalar_tensor_tensor", e3, ps3(psN1), -1.0, bGB,
                       ALU.mult, ALU.add, wait=[m_b1])
            m_e = k.op("vector", "tensor_tensor", e3, e3, mrep(mstrictT), ALU.add)
            m_e = k.op("scalar", "activation", esc[:], esc[:], AF.Exp, wait=[m_e])
            m_ms = k.op("vector", "tensor_tensor", Ms16[:, S5], esc[:], psKK[0:128, 0:512],
                        ALU.mult, wait=[m_e])
            m_KK_free = m_ms
            m_x0 = k.op("gpsimd", "tensor_tensor",
                        Xb[:, S5].rearrange("p (h t) -> p h t", t=128),
                        mrep(ident), Ms16[:, S5].rearrange("p (h t) -> p h t", t=128),
                        ALU.subtract, wait=[m_ms, m_y0])
            m_N1_free = m_e
            m_b3 = None
            for h in range(4):
                w = [m_N2_free] if h == 0 else None
                m_b3 = k.op("tensor", "matmul", psN2[0:128, h * 128:(h + 1) * 128],
                            sel4[0:4, h * 128:(h + 1) * 128], Gr[0:4, cs],
                            start=True, stop=True, wait=w,
                            inc=(st, 1) if h == 3 else None)
            m_e = k.op("vector", "scalar_tensor_tensor", e3, ps3(psN2), 1.0, mrep(mincl),
                       ALU.mult, ALU.add, wait=[m_b3])
            m_e = k.op("vector", "tensor_tensor", e3, e3, bG, ALU.subtract)
            m_e = k.op("scalar", "activation", esc[:], esc[:], AF.Exp, wait=[m_e])
            m_at = k.op("vector", "tensor_tensor", At16[:, S5], esc[:], psN3[0:128, 0:512],
                        ALU.mult, wait=[m_e, m_kq] + wmm)
            m_N2_free = m_e
            m_N3_free = m_at
            m_b4 = None
            for h in range(4):
                w = [m_N1_free] if h == 0 else None
                m_b4 = k.op("tensor", "matmul", psN1[0:128, h * 128:(h + 1) * 128],
                            sel4[0:4, h * 128:(h + 1) * 128], Gr[0:4, cs],
                            start=True, stop=True, wait=w,
                            inc=(st, 1) if h == 3 else None)
            m_ge = k.op("vector", "tensor_copy", esc[:], psN1[0:128, 0:512], wait=[m_b4, m_at])
            m_N1_free = m_ge
            m_ge = k.op("scalar", "activation", esc[:], esc[:], AF.Exp, wait=[m_ge])
            m_gq = None
            PB = p * 768
            for t, hh in [(0, 0), (2, 1), (3, 2), (5, 3)]:
                m_gq = k.op("gpsimd", "tensor_tensor", gq16[:, PB + t * 128:PB + (t + 1) * 128],
                            qc[:, PB + t * 128:PB + (t + 1) * 128],
                            esc[:, hh * 128:(hh + 1) * 128], ALU.mult,
                            wait=[m_ge] + wmm if t == 0 else None)
            m_gq = k.op("gpsimd", "tensor_tensor", gq16[0:64, PB + 128:PB + 256],
                        qc[0:64, PB + 128:PB + 256], esc[0:64, 0:128], ALU.mult)
            m_gq = k.op("gpsimd", "tensor_tensor", gq16[64:128, PB + 128:PB + 256],
                        qc[64:128, PB + 128:PB + 256], esc[64:128, 128:256], ALU.mult)
            m_gq = k.op("gpsimd", "tensor_tensor", gq16[0:64, PB + 512:PB + 640],
                        qc[0:64, PB + 512:PB + 640], esc[0:64, 256:384], ALU.mult)
            m_gq = k.op("gpsimd", "tensor_tensor", gq16[64:128, PB + 512:PB + 640],
                        qc[64:128, PB + 512:PB + 640], esc[64:128, 384:512], ALU.mult)
            S5r = lambda t: t[:, S5]
            m_xa, m_ya = m_x0, m_x0
            for itn in range(6):
                m_ax = None
                for h in range(4):
                    hb = slice(p * 512 + h * 128, p * 512 + (h + 1) * 128)
                    hbr = slice(h * 128, (h + 1) * 128)
                    w = []
                    if h == 0:
                        w = [m_x0, m_id16] if itn == 0 else [m_xa]
                        if itn == 0 and m_KK_free is not None:
                            w.append(m_KK_free)
                    k.op("tensor", "matmul", psKK[0:128, hbr], Mts16[:, hb], Xb[:, hb],
                         start=True, stop=False, wait=w or None)
                    m_ax = k.op("tensor", "matmul", psKK[0:128, hbr], id16[:], Xb[:, hb],
                                start=False, stop=True,
                                inc=(st, 1) if h == 3 else None)
                m_r = k.op("vector", "scalar_tensor_tensor", Rb[:], psKK[0:128, 0:512], -1.0,
                           ident4[:], ALU.mult, ALU.add, wait=[m_ax])
                m_xr = None
                for h in range(4):
                    hb = slice(p * 512 + h * 128, p * 512 + (h + 1) * 128)
                    hbr = slice(h * 128, (h + 1) * 128)
                    k.op("tensor", "matmul", psN1[0:128, hbr], Yb[:, hb], Rb[:, hbr],
                         start=True, stop=True,
                         wait=[m_r, m_ya, m_N1_free] if h == 0 else None)
                    m_xr = k.op("tensor", "matmul", psN2[0:128, hbr], Rb[:, hbr], Yb[:, hb],
                                start=True, stop=True,
                                wait=[m_N2_free] if h == 0 else None,
                                inc=(st, 1) if h == 3 else None)
                m_xa = k.op("vector", "tensor_tensor", Xb[:, S5], Xb[:, S5],
                            psN1[0:128, 0:512], ALU.add, wait=[m_xr])
                m_ya = k.op("vector", "tensor_tensor", Yb[:, S5], Yb[:, S5],
                            psN2[0:128, 0:512], ALU.add)
                m_N1_free = m_xa
                m_N2_free = m_ya
            m_KK_free = m_r
            m_bv = m_nk = m_kg = m_sg = None
            for h in range(4):
                vslc = vc[:, p * 1536 + h * 384:p * 1536 + (h + 1) * 384]
                gslc = gc[:, p * 1536 + h * 384:p * 1536 + (h + 1) * 384]
                ktslc = ktc[:, p * 768 + h * 192:p * 768 + (h + 1) * 192]
                w0 = M_L + wmm if h == 0 else []
                m_bv = k.op("scalar", "activation", bv16[:, p * 1536 + h * 384:p * 1536 + (h + 1) * 384],
                            vslc, AF.Copy, scale=btc[:, p * 4 + h:p * 4 + h + 1],
                            wait=w0 + [m_bg])
                m_nk = k.op("scalar", "activation", nbgk16[:, p * 768 + h * 192:p * 768 + (h + 1) * 192],
                            ktslc, AF.Copy, scale=nbgam[:, p * 4 + h:p * 4 + h + 1],
                            wait=[m_nbg] if h == 0 else None)
                m_kg = k.op("scalar", "activation", ktg16[:, p * 768 + h * 192:p * 768 + (h + 1) * 192],
                            ktslc, AF.Copy, scale=egti[:, p * 4 + h:p * 4 + h + 1],
                            wait=[m_eg] if h == 0 else None)
                m_sg = k.op("scalar", "activation", sg32[:, p * 1536 + h * 384:p * 1536 + (h + 1) * 384],
                            gslc, AF.Silu, wait=None)
            return dict(T=m_ya, at=m_at, gq=m_gq, bv=m_bv, nk=m_nk, kg=m_kg, sg=m_sg,
                        gT=m_gT, kk=m_kk, kq=m_kq)

        def chain(c, pm):
            p = c % 2
            cs = slice(c * 128, (c + 1) * 128)
            S5 = slice(p * 512, (p + 1) * 512)
            nonlocal m_psU_free, m_psO_free, m_psM_free, m_psT_free
            m_lastmm = None
            m_lastS = None
            for h in range(4):
                hb = slice(p * 512 + h * 128, p * 512 + (h + 1) * 128)
                R1, R2 = HR[h]
                B1, B2 = HB[h]
                BV = bv16[:, p * 1536 + h * 384:p * 1536 + (h + 1) * 384]
                NK = nbgk16[:, p * 768 + h * 192:p * 768 + (h + 1) * 192]
                KG = ktg16[:, p * 768 + h * 192:p * 768 + (h + 1) * 192]
                SG = sg32[:, p * 1536 + h * 384:p * 1536 + (h + 1) * 384]
                NW = nw16[:, h * 256:(h + 1) * 256]
                U = u16[:, h * 384:(h + 1) * 384]
                OF = of32[:, h * 384:(h + 1) * 384]
                OG = og16[:, h * 384:(h + 1) * 384]
                OGS = ogst[:, h * 384:(h + 1) * 384]
                wW = [pm["T"], pm["nk"]]
                if m_psM_free is not None:
                    wW.append(m_psM_free)
                if m_seqdone[h] is not None:
                    wW.append(m_seqdone[h])
                m_w1 = k.op("tensor", "matmul", psM[B1:B1 + R1, 0:128], NK[:, 0:R1],
                            Yb[:, hb], start=True, stop=True, wait=wW, inc=(st, 1))
                m_w2 = k.op("tensor", "matmul", psM[B2:B2 + R2, 128:256], NK[:, R1:192],
                            Yb[:, hb], start=True, stop=True, inc=(st, 1))
                m_nw = k.op("vector", "tensor_copy", NW[B1:B1 + R1, 0:128],
                            psM[B1:B1 + R1, 0:128], wait=[m_w1, m_w2])
                m_nw = k.op("vector", "tensor_copy", NW[B2:B2 + R2, 128:256],
                            psM[B2:B2 + R2, 128:256])
                wu = [pm["bv"], m_nw]
                if m_psU_free is not None:
                    wu.append(m_psU_free)
                k.op("tensor", "matmul", psU[0:128, 0:384], Yb[:, hb], BV,
                     start=True, stop=False, wait=wu)
                k.op("tensor", "matmul", psU[0:128, 0:384], NW[B1:B1 + R1, 0:128], SA(h),
                     start=False, stop=False, wait=[m_S16[h]])
                m_u = k.op("tensor", "matmul", psU[0:128, 0:384], NW[B2:B2 + R2, 128:256],
                           SB(h), start=False, stop=True, inc=(st, 1))
                m_u16 = k.op("scalar", "copy", U, psU[0:128, 0:384], wait=[m_u])
                m_u16_h[h] = m_u16
                (q1, r1), (q2, r2) = hsl(gq16, p, h)
                wo = [pm["gq"]]
                if m_psO_free is not None:
                    wo.append(m_psO_free)
                k.op("tensor", "matmul", psO[0:128, 0:384], q1, SA(h),
                     start=True, stop=False, wait=wo)
                k.op("tensor", "matmul", psO[0:128, 0:384], q2, SB(h),
                     start=False, stop=False)
                m_o = k.op("tensor", "matmul", psO[0:128, 0:384], At16[:, hb], U,
                           start=False, stop=True, wait=[m_u16], inc=(st, 1))
                k.op("tensor", "matmul", psU[B1:B1 + R1, 0:384], KG[:, 0:R1], U,
                     start=True, stop=True, wait=[pm["kg"], m_u16])
                m_s2 = k.op("tensor", "matmul", psM[B2:B2 + R2, 0:384], KG[:, R1:192], U,
                            start=True, stop=True, inc=(st, 1))
                m_sh = k.op("vector", "scalar_tensor_tensor", SA_f(h), SA_f(h),
                            gTbc[B1:B1 + R1, p * 4 + h:p * 4 + h + 1], psU[B1:B1 + R1, 0:384],
                            ALU.mult, ALU.add, wait=[m_s2, pm["gT"], m_s0a])
                m_sl = k.op("vector", "scalar_tensor_tensor", SB_f(h), SB_f(h),
                            gTbc[B2:B2 + R2, p * 4 + h:p * 4 + h + 1], psM[B2:B2 + R2, 0:384],
                            ALU.mult, ALU.add)
                m_psU_free = m_sh
                m_psM_free = m_sl
                m_sc = k.op("gpsimd", "tensor_copy", SA(h), SA_f(h), wait=[m_sh])
                m_sc = k.op("gpsimd", "tensor_copy", SB(h), SB_f(h), wait=[m_sl])
                m_S16[h] = m_sc
                wof = [m_o]
                if m_sq_h[0] is not None:
                    wof.append(m_sq_h[0])
                m_of = k.op("scalar", "activation", OF, psO[0:128, 0:384], AF.Copy,
                            wait=wof)
                m_psO_free = m_of
                m_of_h[h] = m_of
                m_seqdone[h] = m_o
                m_lastmm = m_s2
                m_lastS = m_sl
            p15 = slice(p * 1536, (p + 1) * 1536)
            m_sqB = k.op("scalar", "activation", osq[:], of32[:], AF.Square,
                         wait=[m_of_h[3]])
            m_rd = k.op("vector", "tensor_reduce", orst[:, 0:4],
                        osq[:].rearrange("p (h d) -> p h d", d=384), AX.X, ALU.add,
                        wait=[m_sqB])
            m_r1 = k.op("scalar", "activation", orst[:, 4:8], orst[:, 0:4],
                        AF.Ln, scale=1.0 / 384.0, bias=EPS_LN, wait=[m_rd])
            m_r2 = k.op("scalar", "activation", orst[:, 4:8], orst[:, 4:8],
                        AF.Exp, scale=-0.5)
            m_t1_ = None
            for h in range(4):
                m_t1_ = k.op("vector", "tensor_scalar", of32[:, h * 384:(h + 1) * 384],
                             of32[:, h * 384:(h + 1) * 384], orst[:, 4 + h:5 + h], None,
                             ALU.mult, wait=[m_r2] if h == 0 else None)
            m_t2_ = k.op("gpsimd", "tensor_tensor", of32[:], of32[:],
                         sg32[:, p15], ALU.mult, wait=[m_t1_, pm["sg"]])
            wt3 = [M_LDC, m_t2_]
            if m_trog_h[0] is not None:
                wt3.append(m_trog_h[0])
            m_t3_ = k.op("gpsimd", "tensor_tensor",
                         og16[:].rearrange("p (h d) -> p h d", d=384),
                         of32[:].rearrange("p (h d) -> p h d", d=384),
                         gnw[:].unsqueeze(1).broadcast_to([128, 4, 384]), ALU.mult,
                         wait=wt3)
            m_sq_h[0] = m_t3_
            for blk in range(3):
                m_trog = None
                for j in range(4):
                    col = blk * 512 + j * 128
                    w = [m_t3_]
                    if j == 0 and m_psT_free is not None:
                        w.append(m_psT_free)
                    m_trog = k.tr(psT16[0:128, j * 128:(j + 1) * 128],
                                  og16[:, col:col + 128], id16[:],
                                  wait=w if j == 0 else None,
                                  inc=(st, 1) if j == 3 else None)
                if blk == 2:
                    m_trog_h[0] = m_trog
                wog = [m_trog]
                if m_ogst[blk] is not None:
                    wog.append(m_ogst[blk])
                m_ogev = k.op("vector", "tensor_copy", ogst[:, blk * 512:(blk + 1) * 512],
                              psT16[0:128, 0:512], wait=wog)
                m_psT_free = m_ogev
                for j in range(4):
                    col = blk * 512 + j * 128
                    m_ogst[blk] = po.dma(o_og[col:col + 128, cs],
                                         ogst[:, col:col + 128],
                                         wait=[m_ogev] if j == 0 else None)
            return m_lastmm, m_lastS

        M_CH[0] = load_chunk(0, [m_gb])
        PM[0] = prep(0, M_CH[0])
        for c in range(16):
            p = c % 2
            if c + 1 < 16:
                M_CH[1 - p] = load_chunk(c + 1, m_free)
                PM[1 - p] = prep(c + 1, M_CH[1 - p])
            mm, mS = chain(c, PM[p])
            m_chain_mm[p] = mm
            m_chain_S[p] = mS
            m_free = [PM[p]["kk"], PM[p]["kq"], PM[p]["gq"], PM[p]["kg"], PM[p]["sg"],
                      PM[p]["bv"]]
        for s in po.sems:
            k.wait("gpsimd", s, k.val(s))
        k.finalize()
    return nc


def build_ffnh():
    nc = bass.Bass("TRN2", target_bir_lowering=False)
    xn = nc.dram_tensor("xn", [D, L], BF16, kind="ExternalInput").ap()
    og = nc.dram_tensor("og", [1536, L], BF16, kind="ExternalInput").ap()
    gatesT = nc.dram_tensor("gatesT", [8, L], F32, kind="ExternalInput").ap()
    w1 = nc.dram_tensor("w1", [8 * D, 512], F32, kind="ExternalInput").ap()
    b1T = nc.dram_tensor("b1T", [8 * 128, 4], F32, kind="ExternalInput").ap()
    W_so = nc.dram_tensor("W_so", [1536, D], F32, kind="ExternalInput").ap()
    caps = declare_consts(nc)
    o_ht = nc.dram_tensor("ht", [4096, L], BF16, kind="ExternalOutput").ap()
    o_yst = nc.dram_tensor("yst", [D, L], BF16, kind="ExternalOutput").ap()

    es = ExitStack()
    with es:
        k = K(nc, es)
        ld = k.sem("ld")
        st = k.sem("stn")
        pa = DPool(k, "pa", "sync", 4)
        pw = DPool(k, "pw", "sync", 4)
        po = DPool(k, "po", "gpsimd", 4)
        register_fconsts(k, caps, ld)
        sel8 = load_const(k, caps, "SEL8", ld)

        xnb = sbuf(k, "xnb", [128, DKT * L], BF16)
        ogb = sbuf(k, "ogb", [128, 12 * L], BF16)
        gts = sbuf(k, "gts", [8, L], F32)
        wso16 = sbuf(k, "wso16", [128, 12 * D], BF16)
        wstg = sbuf(k, "wstg", [128, 4 * D], F32)
        w116 = sbuf(k, "w116", [128, 2 * DKT * 512], BF16)
        b1s = sbuf(k, "b1s", [128, 8 * 4], F32)
        bcg = sbuf(k, "bcg", [128, 2 * 512], F32)
        hstage = sbuf(k, "hstage", [128, 2 * 512], F32)
        outb = sbuf(k, "outb", [128, 2 * 512], BF16)
        ps = [psum(k, "psA", (128, 512)), psum(k, "psB", (128, 512))]
        psg = psum(k, "psg", (128, 512))

        casts = []
        for kt in range(12):
            sl = len(casts) % 4
            wl = [casts[-4]] if len(casts) >= 4 else None
            mdl = pw.dma(wstg[:, sl * D:(sl + 1) * D], W_so[kt * 128:(kt + 1) * 128, :],
                         wait=wl)
            casts.append(k.op("scalar", "activation", wso16[:, kt * D:(kt + 1) * D],
                              wstg[:, sl * D:(sl + 1) * D], AF.Copy, wait=[mdl]))
        m_wso = casts[-1]

        for kt in range(12):
            pa.dma(ogb[:, kt * L:(kt + 1) * L], og[kt * 128:(kt + 1) * 128, :])
        M_OG = pa.all()
        for kt in range(DKT):
            pa.dma(xnb[:, kt * L:(kt + 1) * L], xn[kt * 128:(kt + 1) * 128, :])
        pa.dma(gts[0:8, :], gatesT)
        for e in range(8):
            pa.dma(b1s[:, e * 4:(e + 1) * 4], b1T[e * 128:(e + 1) * 128, :])
        M_LD = pa.all()

        m_bank = [None, None]
        m_outb = [None, None]
        for g in range(NTG):
            gs = slice(g * 512, (g + 1) * 512)
            for mt in range(DKT):
                bank = mt % 2
                m = None
                for kt in range(12):
                    w = []
                    if kt == 0:
                        if g == 0 and mt == 0:
                            w += [m_wso] + M_OG
                        if m_bank[bank] is not None:
                            w.append(m_bank[bank])
                    m = k.op("tensor", "matmul", ps[bank][0:128, 0:512],
                             wso16[:, kt * D + mt * 128:kt * D + (mt + 1) * 128],
                             ogb[:, kt * L + g * 512:kt * L + g * 512 + 512],
                             start=(kt == 0), stop=(kt == 11),
                             wait=w or None, inc=(st, 1) if kt == 11 else None)
                par = mt % 2
                ob = outb[:, par * 512:par * 512 + 512]
                w = [m]
                if m_outb[par] is not None:
                    w.append(m_outb[par])
                m_ev = k.op("vector", "tensor_copy", ob, ps[bank][0:128, 0:512], wait=w)
                m_bank[bank] = m_ev
                m_outb[par] = po.dma(o_yst[mt * 128:(mt + 1) * 128, gs], ob, wait=[m_ev])

        w1casts = []
        m_bcg = [None, None]
        m_hfree = [None, None]
        mm_used = {}

        def w1slot(i):
            return wstg[:, (i % 8) * 512:(i % 8) * 512 + 512]

        for e in range(8):
            pwp = e % 2
            for kt in range(DKT):
                i = len(w1casts)
                wld = []
                if i >= 8:
                    wld.append(w1casts[-8])
                if i == 0:
                    wld.append(m_wso)
                if e >= 2 and kt == 0:
                    wld.append(mm_used[e - 2])
                mdl = pw.dma(w1slot(i), w1[e * D + kt * 128:e * D + (kt + 1) * 128, :],
                             wait=wld or None)
                w1casts.append(k.op("scalar", "activation",
                                    w116[:, pwp * DKT * 512 + kt * 512:pwp * DKT * 512 + (kt + 1) * 512],
                                    w1slot(i), AF.Copy, wait=[mdl]))
            m_w1c = w1casts[-1]
            for g in range(NTG):
                gs = slice(g * 512, (g + 1) * 512)
                gp = g % 2
                m_gb = k.op("tensor", "matmul", psg[0:128, 0:512],
                            sel8[0:8, e * 128:(e + 1) * 128], gts[0:8, gs],
                            start=True, stop=True,
                            wait=([m_bcg[gp]] if m_bcg[gp] is not None else M_LD),
                            inc=(st, 1))
                m_bc = k.op("vector", "tensor_copy", bcg[:, gp * 512:gp * 512 + 512],
                            psg[0:128, 0:512], wait=[m_gb])
                m_bcg[gp] = m_bc
                for mt in range(4):
                    bank = mt % 2
                    m = None
                    for kt in range(DKT):
                        w = []
                        if kt == 0:
                            w.append(m_w1c)
                            if m_bank[bank] is not None:
                                w.append(m_bank[bank])
                        m = k.op("tensor", "matmul", ps[bank][0:128, 0:512],
                                 w116[:, pwp * DKT * 512 + kt * 512 + mt * 128:
                                       pwp * DKT * 512 + kt * 512 + (mt + 1) * 128],
                                 xnb[:, kt * L + g * 512:kt * L + g * 512 + 512],
                                 start=(kt == 0), stop=(kt == DKT - 1),
                                 wait=w or None, inc=(st, 1) if kt == DKT - 1 else None)
                    mm_used[e] = m
                    hp = mt % 2
                    hst = hstage[:, hp * 512:hp * 512 + 512]
                    m_h = k.op("scalar", "activation", hst, ps[bank][0:128, 0:512],
                               AF.Gelu, bias=b1s[:, e * 4 + mt:e * 4 + mt + 1],
                               wait=[m] + ([m_hfree[hp]] if m_hfree[hp] is not None else []))
                    m_bank[bank] = m_h
                    par = mt % 2
                    ob = outb[:, par * 512:par * 512 + 512]
                    w = [m_h, m_bc]
                    if m_outb[par] is not None:
                        w.append(m_outb[par])
                    m_ht = k.op("vector", "tensor_tensor", ob, hst,
                                bcg[:, gp * 512:gp * 512 + 512], ALU.mult, wait=w)
                    m_hfree[hp] = m_ht
                    row = e * 512 + mt * 128
                    m_outb[par] = po.dma(o_ht[row:row + 128, gs], ob, wait=[m_ht])
        for s in po.sems:
            k.wait("gpsimd", s, k.val(s))
        k.finalize()
    return nc


def build_resid():
    nc = bass.Bass("TRN2", target_bir_lowering=False)
    yst = nc.dram_tensor("yst", [D, L], BF16, kind="ExternalInput").ap()
    zs = nc.dram_tensor("zs", [D, L], BF16, kind="ExternalInput").ap()
    ht = nc.dram_tensor("ht", [4096, L], BF16, kind="ExternalInput").ap()
    xT = nc.dram_tensor("xT", [D, L], F32, kind="ExternalInput").ap()
    W_out = nc.dram_tensor("W_out", [2 * D, D], F32, kind="ExternalInput").ap()
    w2 = nc.dram_tensor("w2", [8 * 512, D], F32, kind="ExternalInput").ap()
    b2 = nc.dram_tensor("b2", [8, D], F32, kind="ExternalInput").ap()
    gatesT = nc.dram_tensor("gatesT", [8, L], F32, kind="ExternalInput").ap()
    caps = declare_consts(nc)
    o_x1 = nc.dram_tensor("x1T", [D, L], F32, kind="ExternalOutput").ap()
    o_xn1 = nc.dram_tensor("xn1", [D, L], BF16, kind="ExternalOutput").ap()

    es = ExitStack()
    with es:
        k = K(nc, es)
        ld = k.sem("ld")
        st = k.sem("stn")
        so = k.sem("so")
        pw = DPool(k, "pw", "sync", 4)
        pg = DPool(k, "pg", "sync", 4)
        po = DPool(k, "po", "gpsimd", 4)
        register_fconsts(k, caps, ld)
        onescol = load_const(k, caps, "ONESCOL", ld)
        onesrow = load_const(k, caps, "ONESROW", ld)

        wout16 = sbuf(k, "wout16", [128, 16 * D], BF16)
        w216 = sbuf(k, "w216", [128, 32 * D], BF16)
        wstg = sbuf(k, "wstg", [128, 2 * D], F32)
        b2s = sbuf(k, "b2s", [8, D], F32)
        gts = sbuf(k, "gts", [8, L], F32)
        ystg = sbuf(k, "ystg", [128, DKT * 512], BF16)
        zsg = sbuf(k, "zsg", [128, DKT * 512], BF16)
        htg = sbuf(k, "htg", [128, 32 * 512], BF16)
        xg = sbuf(k, "xg", [128, DKT * 512], F32)
        x1s = xg
        xsq = sbuf(k, "xsq", [128, 2 * 512], F32)
        rowsg = sbuf(k, "rowsg", [1, 4 * 512], F32)
        bcA = sbuf(k, "bcA", [128, 512], F32)
        bcB = sbuf(k, "bcB", [128, 512], F32)
        xnt = sbuf(k, "xnt", [128, 2 * 512], BF16)
        ps = [psum(k, "psA", (128, 512)), psum(k, "psB", (128, 512))]
        ps_s = psum(k, "ps_s", (128, 1024))
        ps_bc = psum(k, "ps_bc", (128, 512))

        k.op("sync", "dma_start", out=gts[0:8, :], in_=gatesT, inc=(ld, 16))
        k.op("sync", "dma_start", out=b2s[0:8, :], in_=b2, inc=(ld, 16))
        M_LD = (ld, k.val(ld))
        m_c = None
        casts = []
        for j, (wsrc, n, wdst) in enumerate([(W_out, 16, wout16), (w2, 32, w216)]):
            for kt in range(n):
                parw = len(casts) % 2
                wl = [casts[-2]] if len(casts) >= 2 else None
                mdl = pw.dma(wstg[:, parw * D:(parw + 1) * D],
                             wsrc[kt * 128:(kt + 1) * 128, :], wait=wl)
                m_c = k.op("scalar", "activation", wdst[:, kt * D:(kt + 1) * D],
                           wstg[:, parw * D:(parw + 1) * D], AF.Copy, wait=[mdl])
                casts.append(m_c)
        m_wc = m_c

        m_bank = [None, None]
        m_xn = {}
        m_x1done = {}
        m_x1store = {}
        m_mmlast = {}
        m_xnst = [None, None]
        m_ev = None
        nsq = 0
        for g in range(NTG):
            gs = slice(g * 512, (g + 1) * 512)
            if g == 0:
                for kt in range(DKT):
                    pg.dma(ystg[:, kt * 512:(kt + 1) * 512], yst[kt * 128:(kt + 1) * 128, gs])
                    pg.dma(zsg[:, kt * 512:(kt + 1) * 512], zs[kt * 128:(kt + 1) * 128, gs])
                for kt in range(32):
                    pg.dma(htg[:, kt * 512:(kt + 1) * 512], ht[kt * 128:(kt + 1) * 128, gs])
            for kt in range(DKT):
                wx = None
                if g > 0:
                    wx = [m_xn[(g - 1, kt)], m_x1store[(g - 1, kt)]]
                pg.dma(xg[:, kt * 512:(kt + 1) * 512], xT[kt * 128:(kt + 1) * 128, gs],
                       wait=wx)
            M_G = pg.all()
            for mt in range(DKT):
                bank = mt % 2
                wmm = list(M_G) if mt < 2 else []
                if g == 0 and mt == 0:
                    wmm += [m_wc, M_LD]
                if m_bank[bank] is not None:
                    wmm.append(m_bank[bank])
                first = True
                for kt in range(DKT):
                    k.op("tensor", "matmul", ps[bank][0:128, 0:512],
                         wout16[:, kt * D + mt * 128:kt * D + (mt + 1) * 128],
                         ystg[:, kt * 512:(kt + 1) * 512],
                         start=first, stop=False, wait=wmm if first else None)
                    first = False
                for kt in range(DKT):
                    k.op("tensor", "matmul", ps[bank][0:128, 0:512],
                         wout16[:, (8 + kt) * D + mt * 128:(8 + kt) * D + (mt + 1) * 128],
                         zsg[:, kt * 512:(kt + 1) * 512], start=False, stop=False)
                for kt in range(32):
                    k.op("tensor", "matmul", ps[bank][0:128, 0:512],
                         w216[:, kt * D + mt * 128:kt * D + (mt + 1) * 128],
                         htg[:, kt * 512:(kt + 1) * 512], start=False, stop=False)
                m = k.op("tensor", "matmul", ps[bank][0:128, 0:512],
                         b2s[0:8, mt * 128:(mt + 1) * 128], gts[0:8, gs],
                         start=False, stop=True, inc=(st, 1))
                m_mmlast[(g, mt)] = m
                m_x1t = k.op("vector", "tensor_tensor", x1s[:, mt * 512:(mt + 1) * 512],
                             ps[bank][0:128, 0:512], xg[:, mt * 512:(mt + 1) * 512], ALU.add,
                             wait=[m])
                m_bank[bank] = m_x1t
                m_x1done[(g, mt)] = m_x1t
                m_x1store[(g, mt)] = po.dma(o_x1[mt * 128:(mt + 1) * 128, gs],
                                            x1s[:, mt * 512:(mt + 1) * 512], wait=[m_x1t])
                buf = xsq[:, (nsq % 2) * 512:(nsq % 2) * 512 + 512]
                wsq = [m_x1t]
                if nsq >= 2:
                    wsq.append((st, k.val(st)))
                m_sq = k.op("scalar", "activation", buf, x1s[:, mt * 512:(mt + 1) * 512],
                            AF.Square, wait=wsq)
                wst1 = [m_x1t]
                if mt == 0 and m_ev is not None:
                    wst1.append(m_ev)
                k.op("tensor", "matmul", ps_s[0:1, 0:512], onescol[:],
                     x1s[:, mt * 512:(mt + 1) * 512],
                     start=(mt == 0), stop=(mt == DKT - 1), wait=wst1)
                k.op("tensor", "matmul", ps_s[0:1, 512:1024], onescol[:], buf,
                     start=(mt == 0), stop=(mt == DKT - 1), wait=[m_sq],
                     inc=(st, 1) if mt == DKT - 1 else None)
                nsq += 1
            m_stats = (st, k.val(st))
            if g + 1 < NTG:
                gs2 = slice((g + 1) * 512, (g + 2) * 512)
                mlast = m_mmlast[(g, DKT - 1)]
                for kt in range(DKT):
                    pg.dma(ystg[:, kt * 512:(kt + 1) * 512],
                           yst[kt * 128:(kt + 1) * 128, gs2], wait=[mlast] if kt == 0 else None)
                    pg.dma(zsg[:, kt * 512:(kt + 1) * 512], zs[kt * 128:(kt + 1) * 128, gs2])
                for kt in range(32):
                    pg.dma(htg[:, kt * 512:(kt + 1) * 512], ht[kt * 128:(kt + 1) * 128, gs2])
            Rg = lambda i: rowsg[0:1, i * 512:(i + 1) * 512]
            k.op("vector", "tensor_copy", Rg(0), ps_s[0:1, 0:512], wait=[m_stats])
            m_ev = k.op("vector", "tensor_copy", Rg(1), ps_s[0:1, 512:1024])
            k.op("vector", "tensor_scalar", Rg(0), Rg(0), 1.0 / D, None, ALU.mult)
            k.op("vector", "tensor_scalar", Rg(1), Rg(1), 1.0 / D, None, ALU.mult)
            k.op("vector", "tensor_tensor", Rg(2), Rg(0), Rg(0), ALU.mult)
            m_var = k.op("vector", "tensor_tensor", Rg(1), Rg(1), Rg(2), ALU.subtract)
            k.op("scalar", "activation", Rg(2), Rg(1), AF.Ln, bias=EPS_LN, wait=[m_var])
            m_rstd = k.op("scalar", "activation", Rg(3), Rg(2), AF.Exp, scale=-0.5)
            k.op("vector", "tensor_tensor", Rg(2), Rg(0), Rg(3), ALU.mult, wait=[m_rstd])
            m_nm = k.op("vector", "tensor_scalar", Rg(2), Rg(2), -1.0, None, ALU.mult)
            m_bc1 = k.op("tensor", "matmul", ps_bc[0:128, 0:512], onesrow[:], Rg(3),
                         start=True, stop=True, wait=[m_nm], inc=(st, 1))
            m_cA = k.op("vector", "tensor_copy", bcA[:], ps_bc[0:128, 0:512], wait=[m_bc1])
            m_bc2 = k.op("tensor", "matmul", ps_bc[0:128, 0:512], onesrow[:], Rg(2),
                         start=True, stop=True, wait=[m_cA], inc=(st, 1))
            m_cB = k.op("vector", "tensor_copy", bcB[:], ps_bc[0:128, 0:512], wait=[m_bc2])
            for mt in range(DKT):
                par = mt % 2
                ob = xnt[:, par * 512:par * 512 + 512]
                w = [m_cB]
                if m_xnst[par] is not None:
                    w.append(m_xnst[par])
                k.op("vector", "tensor_tensor", xsq[:, 0:512], x1s[:, mt * 512:(mt + 1) * 512],
                     bcA[:], ALU.mult, wait=w)
                m_x = k.op("vector", "tensor_tensor", ob, xsq[:, 0:512], bcB[:], ALU.add)
                m_xn[(g, mt)] = m_x
                po.dma(o_xn1[mt * 128:(mt + 1) * 128, gs], ob, wait=[m_x])
                m_xnst[par] = (po.sems[(po.i - 1) % 4], k.val(po.sems[(po.i - 1) % 4]))
        for s in po.sems:
            k.wait("gpsimd", s, k.val(s))
        k.finalize()
    return nc


def build_l1attn():
    nc = bass.Bass("TRN2", target_bir_lowering=False)
    xn1 = nc.dram_tensor("xn1", [D, L], BF16, kind="ExternalInput").ap()
    x1T = nc.dram_tensor("x1T", [D, L], F32, kind="ExternalInput").ap()
    WkT = nc.dram_tensor("WkT", [D, D], F32, kind="ExternalInput").ap()
    WvT = nc.dram_tensor("WvT", [D, D], F32, kind="ExternalInput").ap()
    WrT = nc.dram_tensor("WrT", [D, D], F32, kind="ExternalInput").ap()
    WoT = nc.dram_tensor("WoT", [D, D], F32, kind="ExternalInput").ap()
    wcol = nc.dram_tensor("wcol", [D, 1], F32, kind="ExternalInput").ap()
    ucol = nc.dram_tensor("ucol", [D, 1], F32, kind="ExternalInput").ap()
    mixk = nc.dram_tensor("mixk", [D, 1], F32, kind="ExternalInput").ap()
    mixv = nc.dram_tensor("mixv", [D, 1], F32, kind="ExternalInput").ap()
    mixr = nc.dram_tensor("mixr", [D, 1], F32, kind="ExternalInput").ap()
    caps = declare_consts(nc)
    o_pres = nc.dram_tensor("pres", [D, L], F32, kind="ExternalOutput").ap()

    es = ExitStack()
    with es:
        k = K(nc, es)
        ld = k.sem("ld")
        st = k.sem("stn")
        pa = DPool(k, "pa", "sync", 4)
        pw = DPool(k, "pw", "sync", 4)
        po = DPool(k, "po", "gpsimd", 4)
        register_fconsts(k, caps, ld)
        id4 = load_const(k, caps, "IDENT4", ld)
        M_ID = (ld, k.val(ld))

        xn1p = sbuf(k, "xn1p", [128, DKT * 2049], BF16)
        w16a = sbuf(k, "w16a", [128, DKT * D], BF16)
        w16b = sbuf(k, "w16b", [128, DKT * D], BF16)
        w16c = sbuf(k, "w16c", [128, DKT * D], BF16)
        wstg = sbuf(k, "wstg", [128, 2 * D], F32)
        rwkvb = sbuf(k, "rwkvb", [128, DKT * L], BF16)
        xkg = sbuf(k, "xkg", [128, DKT * 512], BF16)
        xvg = sbuf(k, "xvg", [128, DKT * 512], BF16)
        xrg = sbuf(k, "xrg", [128, DKT * 512], BF16)
        cols = sbuf(k, "cols", [128, 8 * 8], F32)
        mixd = sbuf(k, "mixd", [128, 512], BF16)
        albc8 = sbuf(k, "albc8", [128, DKT * 513], F32)
        ekg = [sbuf(k, "ekg0", [128, 513], F32), sbuf(k, "ekg1", [128, 513], F32)]
        ekvg = [sbuf(k, "ekvg0", [128, 513], F32), sbuf(k, "ekvg1", [128, 513], F32)]
        vg = [sbuf(k, "vg0", [128, 512], F32), sbuf(k, "vg1", [128, 512], F32)]
        srg = [sbuf(k, "srg0", [128, 512], BF16), sbuf(k, "srg1", [128, 512], BF16)]
        dent = [sbuf(k, "dent0", [128, 512], F32), sbuf(k, "dent1", [128, 512], F32)]
        eukg = [sbuf(k, "eukg0", [128, 512], F32), sbuf(k, "eukg1", [128, 512], F32)]
        euv = [sbuf(k, "euv0", [128, 512], F32), sbuf(k, "euv1", [128, 512], F32)]
        Pt = [sbuf(k, "Pt0", [128, 513], F32), sbuf(k, "Pt1", [128, 513], F32)]
        Qt = [sbuf(k, "Qt0", [128, 513], F32), sbuf(k, "Qt1", [128, 513], F32)]
        PC = sbuf(k, "PC", [128, 8], F32)
        QC = sbuf(k, "QC", [128, 8], F32)
        prest = sbuf(k, "prest", [128, 2 * 512], F32)
        x1g = sbuf(k, "x1g", [128, 2 * 512], F32)
        psK = [psum(k, "psK0", (128, 512)), psum(k, "psK1", (128, 512))]
        psV = [psum(k, "psV0", (128, 512)), psum(k, "psV1", (128, 512))]
        psR = [psum(k, "psR0", (128, 512)), psum(k, "psR1", (128, 512))]

        for kt in range(DKT):
            for j, src_ap in enumerate([wcol, ucol, mixk, mixv, mixr]):
                pa.dma(cols[:, kt * 8 + j:kt * 8 + j + 1],
                       src_ap[kt * 128:(kt + 1) * 128, :])
        M_COLS = pa.all()
        m_z = None
        for kt in range(DKT):
            m_z = k.op("vector", "memset", xn1p[:, kt * 2049:kt * 2049 + 1], 0.0)
        for kt in range(DKT):
            pa.dma(xn1p[:, kt * 2049 + 1:(kt + 1) * 2049], xn1[kt * 128:(kt + 1) * 128, :])
        M_XN = pa.all()
        m_a_kt = []
        for kt in range(DKT):
            k.op("scalar", "activation", albc8[:, kt * 513:kt * 513 + 1],
                 cols[:, kt * 8:kt * 8 + 1], AF.Exp, wait=M_COLS if kt == 0 else None)
            m_a_kt.append(k.op("scalar", "activation", albc8[:, kt * 513:kt * 513 + 1],
                               albc8[:, kt * 513:kt * 513 + 1], AF.Exp, scale=-1.0))
        m_alb = None
        for kt in range(DKT):
            k.op("vector", "tensor_scalar", albc8[:, kt * 513 + 1:(kt + 1) * 513],
                 id4[:, 0:512], 0.0, albc8[:, kt * 513:kt * 513 + 1], ALU.mult,
                 op1=ALU.add, wait=[m_a_kt[kt]] + ([M_ID] if kt == 0 else []))
            m_alb = k.op("vector", "memset", albc8[:, kt * 513:kt * 513 + 1], 1.0)
        m_pcz = k.op("vector", "memset", PC[:], 0.0)
        m_qcz = k.op("vector", "memset", QC[:], 0.0)

        casts = []
        for wsrc, wdst in [(WkT, w16a), (WvT, w16b), (WrT, w16c)]:
            for kt in range(DKT):
                i = len(casts)
                sl = wstg[:, (i % 2) * D:(i % 2) * D + D]
                mdl = pw.dma(sl, wsrc[kt * 128:(kt + 1) * 128, :],
                             wait=[casts[-2]] if i >= 2 else None)
                casts.append(k.op("scalar", "activation", wdst[:, kt * D:(kt + 1) * D],
                                  sl, AF.Copy, wait=[mdl]))
        m_wkvr = casts[-1]

        m_ek = [None, None]
        m_v = [None, None]
        m_sr = [None, None]
        m_ekv = [None, None]
        m_scanP = [None, None]
        m_scanQ = [None, None]
        m_num = [None, None]
        m_den = [None, None]
        m_recip = [None, None]
        m_t = [None, None]
        m_rwkv = [None, None]
        m_cbP = [None, None]
        m_cbQ = [None, None]
        m_lastmm = None
        for g in range(NTG):
            m_mixg = None
            for kt in range(DKT):
                xx = xn1p[:, kt * 2049 + 1 + g * 512:kt * 2049 + 1 + g * 512 + 512]
                xp = xn1p[:, kt * 2049 + g * 512:kt * 2049 + g * 512 + 512]
                for j, dst in [(2, xkg), (3, xvg), (4, xrg)]:
                    ds = dst[:, kt * 512:(kt + 1) * 512]
                    w = []
                    if kt == 0 and j == 2:
                        w += (M_XN + ([m_lastmm] if m_lastmm is not None else []))
                        if g == 0:
                            w += M_COLS + [m_z]
                    k.op("vector", "tensor_tensor", mixd[:], xx, xp, ALU.subtract,
                         wait=w or None)
                    m_mixg = k.op("vector", "scalar_tensor_tensor", ds, mixd[:],
                                  cols[:, kt * 8 + j:kt * 8 + j + 1], xp,
                                  ALU.mult, ALU.add)
            for mt in range(DKT):
                p = mt % 2
                def grp(ps_, w16_, rhs_, extra):
                    m = None
                    for kt in range(DKT):
                        w = list(extra) if kt == 0 else []
                        m = k.op("tensor", "matmul", ps_[0:128, 0:512],
                                 w16_[:, kt * D + mt * 128:kt * D + (mt + 1) * 128],
                                 rhs_[:, kt * 512:(kt + 1) * 512],
                                 start=(kt == 0), stop=(kt == DKT - 1), wait=w or None,
                                 inc=(st, 1) if kt == DKT - 1 else None)
                    return m

                ex = [m_mixg]
                if g == 0 and mt == 0:
                    ex.append(m_wkvr)
                if m_ek[p] is not None:
                    ex.append(m_ek[p])
                mmK = grp(psK[p], w16a, xkg, ex)
                mmV = grp(psV[p], w16b, xvg,
                          [m_v[p]] if m_v[p] is not None else [])
                mmR = grp(psR[p], w16c, xrg,
                          [m_sr[p]] if m_sr[p] is not None else [])
                m_lastmm = mmR
                w = [mmK]
                if m_den[p] is not None:
                    w.append(m_den[p])
                if g == 0 and mt < 2:
                    w.append(m_alb)
                m_ek[p] = k.op("scalar", "activation", ekg[p][:, 1:513],
                               psK[p][0:128, 0:512], AF.Exp, wait=w)
                m_euek = k.op("scalar", "activation", eukg[p][:],
                              psK[p][0:128, 0:512], AF.Exp,
                              bias=cols[:, mt * 8 + 1:mt * 8 + 2])
                m_ek[p] = m_euek
                w = [m_pcz] if (g == 0 and mt < 2) else [m_scanP[p]]
                m_cinP = k.op("scalar", "activation", ekvg[p][:, 0:1], PC[:, mt:mt + 1],
                              AF.Copy, wait=w)
                w = [m_qcz] if (g == 0 and mt < 2) else [m_scanQ[p]]
                m_cinQ = k.op("scalar", "activation", ekg[p][:, 0:1], QC[:, mt:mt + 1],
                              AF.Copy, wait=w)
                m_v[p] = k.op("scalar", "activation", vg[p][:], psV[p][0:128, 0:512],
                              AF.Copy, wait=[mmV] + ([m_t[p]] if m_t[p] is not None else []))
                m_sr[p] = k.op("scalar", "activation", srg[p][:], psR[p][0:128, 0:512],
                               AF.Sigmoid, wait=[mmR] + ([m_rwkv[p]] if m_rwkv[p] is not None else []))
                m_ekv[p] = k.op("gpsimd", "tensor_tensor", ekvg[p][:, 1:513],
                                ekg[p][:, 1:513], vg[p][:], ALU.mult,
                                wait=[m_ek[p], m_v[p]])
                m_euv = k.op("gpsimd", "tensor_tensor", euv[p][:], eukg[p][:],
                             vg[p][:], ALU.mult)
                w = [m_ekv[p], m_cinP]
                if m_num[p] is not None:
                    w.append(m_num[p])
                if m_cbP[p] is not None:
                    w.append(m_cbP[p])
                m_scanP[p] = k.op("vector", "tensor_tensor_scan", Pt[p][:, 0:513],
                                  albc8[:, mt * 513:(mt + 1) * 513], ekvg[p][:, 0:513],
                                  0.0, ALU.mult, ALU.add, wait=w)
                w = [m_cinQ]
                if m_den[p] is not None:
                    w.append(m_den[p])
                if m_cbQ[p] is not None:
                    w.append(m_cbQ[p])
                m_scanQ[p] = k.op("vector", "tensor_tensor_scan", Qt[p][:, 0:513],
                                  albc8[:, mt * 513:(mt + 1) * 513], ekg[p][:, 0:513],
                                  0.0, ALU.mult, ALU.add, wait=w)
                m_cbP[p] = k.op("scalar", "activation", PC[:, mt:mt + 1],
                                Pt[p][:, 512:513], AF.Copy, wait=[m_scanP[p]])
                m_cbQ[p] = k.op("scalar", "activation", QC[:, mt:mt + 1],
                                Qt[p][:, 512:513], AF.Copy, wait=[m_scanQ[p]])
                m_num[p] = k.op("gpsimd", "tensor_tensor", vg[p][:], euv[p][:],
                                Pt[p][:, 0:512], ALU.add, wait=[m_scanP[p]])
                m_den[p] = k.op("gpsimd", "tensor_tensor", dent[p][:], eukg[p][:],
                                Qt[p][:, 0:512], ALU.add, wait=[m_scanQ[p]])
                m_recip[p] = k.op("vector", "reciprocal", dent[p][:], dent[p][:],
                                  wait=[m_den[p]])
                m_t[p] = k.op("gpsimd", "tensor_tensor", ekvg[p][:, 1:513], vg[p][:],
                              dent[p][:], ALU.mult, wait=[m_recip[p]])
                m_rwkv[p] = k.op("gpsimd", "tensor_tensor",
                                 rwkvb[:, mt * L + g * 512:mt * L + g * 512 + 512],
                                 ekvg[p][:, 1:513], srg[p][:], ALU.mult, wait=[m_sr[p]])

        m_c2 = m_wkvr
        for kt in range(DKT):
            i = len(casts)
            sl = wstg[:, (i % 2) * D:(i % 2) * D + D]
            w = [casts[-2]]
            if kt == 0:
                w.append(m_ek[0])
                w.append(m_ek[1])
            mdl = pw.dma(sl, WoT[kt * 128:(kt + 1) * 128, :], wait=w)
            casts.append(k.op("scalar", "activation", w16a[:, kt * D:(kt + 1) * D],
                              sl, AF.Copy, wait=[mdl]))
        m_wo = casts[-1]
        pso = [psK[0], psK[1]]
        m_pr = [None, None]
        m_bank2 = [m_ek[0], m_ek[1]]
        first3 = True
        for g in range(NTG):
            gs = slice(g * 512, (g + 1) * 512)
            for mt in range(DKT):
                par = mt % 2
                wld = [m_pr[par]] if m_pr[par] is not None else None
                M_X1 = pa.dma(x1g[:, par * 512:par * 512 + 512],
                              x1T[mt * 128:(mt + 1) * 128, gs], wait=wld)
                m = None
                for kt in range(DKT):
                    w = []
                    if kt == 0:
                        if first3:
                            w += [m_wo, m_rwkv[0], m_rwkv[1]]
                            first3 = False
                        if m_bank2[par] is not None:
                            w.append(m_bank2[par])
                    m = k.op("tensor", "matmul", pso[par][0:128, 0:512],
                             w16a[:, kt * D + mt * 128:kt * D + (mt + 1) * 128],
                             rwkvb[:, kt * L + g * 512:kt * L + g * 512 + 512],
                             start=(kt == 0), stop=(kt == DKT - 1), wait=w or None,
                             inc=(st, 1) if kt == DKT - 1 else None)
                m_p = k.op("vector", "tensor_tensor", prest[:, par * 512:par * 512 + 512],
                           pso[par][0:128, 0:512], x1g[:, par * 512:par * 512 + 512],
                           ALU.add, wait=[m, M_X1] + ([m_pr[par]] if m_pr[par] is not None else []))
                m_bank2[par] = m_p
                m_pr[par] = po.dma(o_pres[mt * 128:(mt + 1) * 128, gs],
                                   prest[:, par * 512:par * 512 + 512], wait=[m_p])
        for s in po.sems:
            k.wait("gpsimd", s, k.val(s))
        k.finalize()
    return nc


def build_logits():
    nc = bass.Bass("TRN2", target_bir_lowering=False)
    xn1 = nc.dram_tensor("xn1", [D, L], BF16, kind="ExternalInput").ap()
    phi2 = nc.dram_tensor("phi2", [D, L], F32, kind="ExternalInput").ap()
    caps = declare_consts(nc)
    o_E = nc.dram_tensor("E16", [L, L], BF16, kind="ExternalOutput").ap()
    o_er = nc.dram_tensor("erinv", [L, 1], F32, kind="ExternalOutput").ap()

    es = ExitStack()
    with es:
        k = K(nc, es)
        ld = k.sem("ld")
        ldw = k.sem("ldw")
        st = k.sem("stn")
        so = k.sem("so")
        register_fconsts(k, caps, ld)

        xn = sbuf(k, "xn", [128, DKT * L], BF16)
        phi16 = sbuf(k, "phi16", [128, DKT * L], BF16)
        wstg = sbuf(k, "wstg", [128, 2 * L], F32)
        estage = sbuf(k, "estage", [128, 2 * 512], F32)
        e16s = sbuf(k, "e16s", [128, 2 * 512], BF16)
        erp = sbuf(k, "erp", [128, 4], F32)
        ercols = sbuf(k, "ercols", [128, 16], F32)
        ps2 = [psum(k, "psA", (128, 512)), psum(k, "psB", (128, 512))]

        pin = DPool(k, "pin", "sync", 4)
        for kt in range(DKT):
            pin.dma(xn[:, kt * L:(kt + 1) * L], xn1[kt * 128:(kt + 1) * 128, :])
        M_XNL = pin.all()
        m_ph = None
        for kt in range(DKT):
            parw = kt % 2
            w = [m_ph] if kt >= 2 else None
            mdl = pin.dma(wstg[:, parw * L:(parw + 1) * L],
                          phi2[kt * 128:(kt + 1) * 128, :], wait=w)
            m_ph = k.op("scalar", "activation", phi16[:, kt * L:(kt + 1) * L],
                        wstg[:, parw * L:(parw + 1) * L], AF.Copy, wait=[mdl])

        m_e = [None, None]
        m_c16 = [None, None]
        m_red = [None, None]
        m_st = [None, None]
        idx = 0
        for mt in range(16):
            for ng in range(NTG):
                par = idx % 2
                idx += 1
                m = None
                for kt in range(DKT):
                    w = []
                    if kt == 0:
                        if mt == 0 and ng == 0:
                            w += [m_ph] + M_XNL
                        if m_e[par] is not None:
                            w.append(m_e[par])
                    m = k.op("tensor", "matmul", ps2[par][0:128, 0:512],
                             xn[:, kt * L + mt * 128:kt * L + (mt + 1) * 128],
                             phi16[:, kt * L + ng * 512:kt * L + ng * 512 + 512],
                             start=(kt == 0), stop=(kt == DKT - 1), wait=w or None,
                             inc=(st, 1) if kt == DKT - 1 else None)
                w = [m]
                for mm_ in (m_c16[par], m_red[par]):
                    if mm_ is not None:
                        w.append(mm_)
                m_e[par] = k.op("scalar", "activation", estage[:, par * 512:par * 512 + 512],
                                ps2[par][0:128, 0:512], AF.Exp, wait=w)
                m_red[par] = k.op("vector", "tensor_reduce", erp[:, ng:ng + 1],
                                  estage[:, par * 512:par * 512 + 512], AX.X, ALU.add,
                                  wait=[m_e[par]])
                m_c16[par] = k.op("scalar", "activation", e16s[:, par * 512:par * 512 + 512],
                                  estage[:, par * 512:par * 512 + 512], AF.Copy,
                                  wait=[m_e[par]] + ([m_st[par]] if m_st[par] is not None else []))
                m_st[par] = pin.dma(o_E[mt * 128:(mt + 1) * 128, ng * 512:ng * 512 + 512],
                                    e16s[:, par * 512:par * 512 + 512], wait=[m_c16[par]],
                                    queue="gpsimd")
            k.op("vector", "tensor_reduce", ercols[:, mt:mt + 1], erp[:, 0:4], AX.X, ALU.add)
            k.op("vector", "reciprocal", ercols[:, mt:mt + 1], ercols[:, mt:mt + 1])
        m_fin = k.mark("vector")
        for mt in range(16):
            k.op("sync", "dma_start", out=o_er[mt * 128:(mt + 1) * 128, 0:1],
                 in_=ercols[:, mt:mt + 1], wait=[m_fin] if mt == 0 else None, inc=(so, 16))
        k.wait("gpsimd", so, k.val(so))
        for s in pin.sems:
            k.wait("gpsimd", s, k.val(s))
        k.finalize()
    return nc


def build_lb():
    nc = bass.Bass("TRN2", target_bir_lowering=False)
    Eg = nc.dram_tensor("Eg", [8 * 256, L], BF16, kind="ExternalInput").ap()
    caps = declare_consts(nc)
    o_D = nc.dram_tensor("Dinv16", [256, L], BF16, kind="ExternalOutput").ap()

    es = ExitStack()
    with es:
        k = K(nc, es)
        ld = k.sem("ld")
        so = k.sem("so")
        register_fconsts(k, caps, ld)
        etl = sbuf(k, "etl", [128, 8 * L], BF16)
        acc = sbuf(k, "acc", [128, L], F32)
        d16 = sbuf(k, "d16", [128, L], BF16)
        m_rd = None
        m_cp = None
        m_16 = None
        for t in range(2):
            pin = DPool(k, "pinb%d" % t, "sync", 4)
            for b in range(8):
                w = [m_rd, m_cp] if (t == 1 and b == 0) else None
                pin.dma(etl[:, b * L:(b + 1) * L],
                        Eg[b * 256 + t * 128:b * 256 + (t + 1) * 128, :], wait=w)
            M = pin.all()
            m_cp = k.op("scalar", "activation", acc[:], etl[:, 0:L], AF.Copy,
                        wait=M + ([m_16] if m_16 is not None else []))
            m_a = None
            for b in range(1, 8):
                m_a = k.op("vector", "tensor_tensor", acc[:], acc[:], etl[:, b * L:(b + 1) * L],
                           ALU.add, wait=[m_cp] if b == 1 else None)
            m_rd = k.op("vector", "reciprocal", acc[:], acc[:])
            m_16 = k.op("scalar", "activation", d16[:], acc[:], AF.Copy,
                        wait=[m_rd] + ([m_dst] if t > 0 else []))
            k.op("sync", "dma_start", out=o_D[t * 128:(t + 1) * 128, :], in_=d16[:],
                 wait=[m_16], inc=(so, 16))
            m_dst = (so, k.val(so))
        k.wait("gpsimd", so, k.val(so))
        k.finalize()
    return nc


def build_lc():
    nc = bass.Bass("TRN2", target_bir_lowering=False)
    E16b = nc.dram_tensor("E16b", [L, L], BF16, kind="ExternalInput").ap()
    Dinv = nc.dram_tensor("Dinv", [L, L], BF16, kind="ExternalInput").ap()
    xn1TM = nc.dram_tensor("xn1TM", [L, D], BF16, kind="ExternalInput").ap()
    caps = declare_consts(nc)
    o_s = nc.dram_tensor("slotsb", [D, L], BF16, kind="ExternalOutput").ap()

    es = ExitStack()
    with es:
        k = K(nc, es)
        ld = k.sem("ld")
        lde = k.sem("lde")
        st = k.sem("stn")
        so = k.sem("so")
        register_fconsts(k, caps, ld)
        xtm = sbuf(k, "xtm", [128, 16 * D], BF16)
        edsp = sbuf(k, "edsp", [128, 16 * L], BF16)
        estg = sbuf(k, "estg", [128, 2 * L], BF16)
        dstg = sbuf(k, "dstg", [128, 2 * L], BF16)
        sstg = sbuf(k, "sstg", [128, 2 * 512], BF16)
        ps2 = [psum(k, "psA", (128, 512)), psum(k, "psB", (128, 512))]

        pin = DPool(k, "pin", "sync", 4)
        for mt in range(16):
            pin.dma(xtm[:, mt * D:(mt + 1) * D], xn1TM[mt * 128:(mt + 1) * 128, :])
        M_XTL = pin.all()
        M_XT = M_XTL[0]
        m_ed = [None, None]
        for mt in range(16):
            par = mt % 2
            w = [m_ed[par]] if m_ed[par] is not None else None
            m1 = pin.dma(estg[:, par * L:(par + 1) * L], E16b[mt * 128:(mt + 1) * 128, :],
                         wait=w)
            m2 = pin.dma(dstg[:, par * L:(par + 1) * L], Dinv[mt * 128:(mt + 1) * 128, :],
                         wait=w)
            m_ed[par] = k.op("vector", "tensor_tensor", edsp[:, mt * L:(mt + 1) * L],
                             estg[:, par * L:(par + 1) * L], dstg[:, par * L:(par + 1) * L],
                             ALU.mult, wait=[m1, m2])
        m_edall = m_ed[0] if m_ed[1] is None else m_ed[1]
        m_edall = k.mark("vector")

        m_ev = [None, None]
        m_sst = [None, None]
        idx = 0
        for dt in range(DKT):
            for ng in range(NTG):
                par = idx % 2
                idx += 1
                m = None
                for mt in range(16):
                    w = []
                    if mt == 0:
                        if idx == 1:
                            w += [m_edall] + M_XTL
                        if m_ev[par] is not None:
                            w.append(m_ev[par])
                    m = k.op("tensor", "matmul", ps2[par][0:128, 0:512],
                             xtm[:, mt * D + dt * 128:mt * D + (dt + 1) * 128],
                             edsp[:, mt * L + ng * 512:mt * L + ng * 512 + 512],
                             start=(mt == 0), stop=(mt == 15), wait=w or None,
                             inc=(st, 1) if mt == 15 else None)
                m_ev[par] = k.op("vector", "tensor_copy", sstg[:, par * 512:par * 512 + 512],
                                 ps2[par][0:128, 0:512],
                                 wait=[m] + ([m_sst[par]] if m_sst[par] is not None else []))
                m_sst[par] = pin.dma(o_s[dt * 128:(dt + 1) * 128, ng * 512:ng * 512 + 512],
                                     sstg[:, par * 512:par * 512 + 512], wait=[m_ev[par]],
                                     queue="gpsimd")
        for s in pin.sems:
            k.wait("gpsimd", s, k.val(s))
        k.finalize()
    return nc


def build_ld():
    nc = bass.Bass("TRN2", target_bir_lowering=False)
    slots = nc.dram_tensor("slots", [D, L], BF16, kind="ExternalInput").ap()
    Wc = nc.dram_tensor("Wc", [8 * D, D], F32, kind="ExternalInput").ap()
    bc = nc.dram_tensor("bc", [8, D], F32, kind="ExternalInput").ap()
    caps = declare_consts(nc)
    o_y = nc.dram_tensor("yc", [8 * 256, D], BF16, kind="ExternalOutput").ap()

    es = ExitStack()
    with es:
        k = K(nc, es)
        ld = k.sem("ld")
        ldw = k.sem("ldw")
        st = k.sem("stn")
        so = k.sem("so")
        register_fconsts(k, caps, ld)
        onesrow = load_const(k, caps, "ONESROW", ld)
        slt = sbuf(k, "slt", [128, DKT * L], BF16)
        wstg = sbuf(k, "wstg", [128, 4 * D], F32)
        w16 = sbuf(k, "w16", [128, 2 * DKT * D], BF16)
        bstage = sbuf(k, "bstage", [1, 8 * D], F32)
        ystg = sbuf(k, "ystg", [128, 2 * 512], BF16)
        ps2 = [psum(k, "psA", (128, 512)), psum(k, "psB", (128, 512))]

        pin = DPool(k, "pin", "sync", 4)
        for kt in range(DKT):
            pin.dma(slt[:, kt * L:(kt + 1) * L], slots[kt * 128:(kt + 1) * 128, :])
        for nl in range(8):
            pin.dma(bstage[0:1, nl * D:(nl + 1) * D], bc[nl:nl + 1, :])
        M_INL = pin.all()
        M_IN = M_INL[0]

        m_cast = [None, None]
        m_nlmm = [None, None]
        m_ev = [None, None]
        m_yst = [None, None]
        idx = 0
        wcasts = []
        pw = DPool(k, "pw", "sync", 4)
        for nl in range(8):
            parn = nl % 2
            m_c = None
            for kt in range(DKT):
                i = len(wcasts)
                parw = i % 4
                w = [wcasts[-4]] if i >= 4 else None
                mdl = pw.dma(wstg[:, parw * D:(parw + 1) * D],
                             Wc[nl * D + kt * 128:nl * D + (kt + 1) * 128, :], wait=w)
                wl = [mdl]
                if kt == 0 and m_nlmm[parn] is not None:
                    wl.append(m_nlmm[parn])
                m_c = k.op("scalar", "activation",
                           w16[:, parn * DKT * D + kt * D:parn * DKT * D + (kt + 1) * D],
                           wstg[:, parw * D:(parw + 1) * D], AF.Copy, wait=wl)
                wcasts.append(m_c)
            m_cast[parn] = m_c
            for half in range(2):
                for eg in range(2):
                    par = idx % 2
                    idx += 1
                    m = None
                    for kt in range(DKT):
                        w = []
                        if kt == 0:
                            w.append(m_cast[parn])
                            if idx == 1:
                                w += M_INL
                            if m_ev[par] is not None:
                                w.append(m_ev[par])
                        m = k.op("tensor", "matmul", ps2[par][0:128, 0:512],
                                 slt[:, kt * L + nl * 256 + half * 128:kt * L + nl * 256 + half * 128 + 128],
                                 w16[:, parn * DKT * D + kt * D + eg * 512:parn * DKT * D + kt * D + eg * 512 + 512],
                                 start=(kt == 0), stop=False, wait=w or None)
                    m = k.op("tensor", "matmul", ps2[par][0:128, 0:512],
                             onesrow[0:1, 0:128],
                             bstage[0:1, nl * D + eg * 512:nl * D + eg * 512 + 512],
                             start=False, stop=True, inc=(st, 1))
                    m_nlmm[parn] = m
                    m_ev[par] = k.op("vector", "tensor_copy", ystg[:, par * 512:par * 512 + 512],
                                     ps2[par][0:128, 0:512],
                                     wait=[m] + ([m_yst[par]] if m_yst[par] is not None else []))
                    k.op("sync", "dma_start",
                         out=o_y[nl * 256 + half * 128:nl * 256 + half * 128 + 128,
                                 eg * 512:eg * 512 + 512],
                         in_=ystg[:, par * 512:par * 512 + 512], wait=[m_ev[par]], inc=(so, 16))
                    m_yst[par] = (so, k.val(so))
        k.wait("gpsimd", so, k.val(so))
        k.finalize()
    return nc


def build_le():
    nc = bass.Bass("TRN2", target_bir_lowering=False)
    yb = nc.dram_tensor("yb", [L, D], BF16, kind="ExternalInput").ap()
    EFM = nc.dram_tensor("EFM", [L, L], BF16, kind="ExternalInput").ap()
    erow = nc.dram_tensor("erow", [1, L], F32, kind="ExternalInput").ap()
    pres = nc.dram_tensor("pres", [D, L], F32, kind="ExternalInput").ap()
    caps = declare_consts(nc)
    o_out = nc.dram_tensor("outT", [D, L], F32, kind="ExternalOutput").ap()

    es = ExitStack()
    with es:
        k = K(nc, es)
        ld = k.sem("ld")
        ldp = k.sem("ldp")
        st = k.sem("stn")
        so = k.sem("so")
        register_fconsts(k, caps, ld)
        onesrow = load_const(k, caps, "ONESROW", ld)
        ytm = sbuf(k, "ytm", [128, 16 * D], BF16)
        efm = sbuf(k, "efm", [128, 16 * L], BF16)
        errow = sbuf(k, "errow", [1, L], F32)
        erbc = sbuf(k, "erbc", [128, 4 * 512], F32)
        prestg = sbuf(k, "prestg", [128, 2 * 512], F32)
        ostg = sbuf(k, "ostg", [128, 2 * 512], F32)
        o2 = sbuf(k, "o2", [128, 2 * 512], F32)
        ps2 = [psum(k, "psA", (128, 512)), psum(k, "psB", (128, 512))]
        psbc = psum(k, "psbc", (128, 512))

        pin = DPool(k, "pin", "sync", 4)
        for npt in range(16):
            pin.dma(ytm[:, npt * D:(npt + 1) * D], yb[npt * 128:(npt + 1) * 128, :])
        for npt in range(16):
            pin.dma(efm[:, npt * L:(npt + 1) * L], EFM[npt * 128:(npt + 1) * 128, :])
        pin.dma(errow[0:1, :], erow)
        M_ALLL = pin.all()
        M_ALL = M_ALLL[0]

        m_bc = None
        for mg in range(NTG):
            m = k.op("tensor", "matmul", psbc[0:128, 0:512], onesrow[0:1, 0:128],
                     errow[0:1, mg * 512:mg * 512 + 512], start=True, stop=True,
                     wait=(M_ALLL if mg == 0 else []) + ([m_bc] if m_bc is not None else []),
                     inc=(st, 1))
            m_bc = k.op("vector", "tensor_copy", erbc[:, mg * 512:mg * 512 + 512],
                        psbc[0:128, 0:512], wait=[m])
        m_bcall = m_bc

        m_mul = [None, None]
        m_add = [None, None]
        m_ost = [None, None]
        idx = 0
        for dt in range(DKT):
            for mg in range(NTG):
                par = idx % 2
                idx += 1
                wld = [m_mul[par]] if m_mul[par] is not None else None
                k.op("sync", "dma_start", out=prestg[:, par * 512:par * 512 + 512],
                     in_=pres[dt * 128:(dt + 1) * 128, mg * 512:mg * 512 + 512],
                     wait=wld, inc=(ldp, 16))
                M_P = (ldp, k.val(ldp))
                m = None
                for npt in range(16):
                    w = []
                    if npt == 0:
                        if idx == 1:
                            w.append(m_bcall)
                        if m_mul[par] is not None:
                            w.append(m_mul[par])
                    m = k.op("tensor", "matmul", ps2[par][0:128, 0:512],
                             ytm[:, npt * D + dt * 128:npt * D + (dt + 1) * 128],
                             efm[:, npt * L + mg * 512:npt * L + mg * 512 + 512],
                             start=(npt == 0), stop=(npt == 15), wait=w or None,
                             inc=(st, 1) if npt == 15 else None)
                m_mul[par] = k.op("vector", "tensor_tensor", ostg[:, par * 512:par * 512 + 512],
                                  ps2[par][0:128, 0:512], erbc[:, mg * 512:mg * 512 + 512],
                                  ALU.mult,
                                  wait=[m] + ([m_add[par]] if m_add[par] is not None else []))
                m_add[par] = k.op("vector", "tensor_tensor", o2[:, par * 512:par * 512 + 512],
                                  ostg[:, par * 512:par * 512 + 512],
                                  prestg[:, par * 512:par * 512 + 512], ALU.add,
                                  wait=[M_P] + ([m_ost[par]] if m_ost[par] is not None else []))
                k.op("sync", "dma_start",
                     out=o_out[dt * 128:(dt + 1) * 128, mg * 512:mg * 512 + 512],
                     in_=o2[:, par * 512:par * 512 + 512], wait=[m_add[par]], inc=(so, 16))
                m_ost[par] = (so, k.val(so))
        k.wait("gpsimd", so, k.val(so))
        k.finalize()
    return nc


def run_ln0(xT_shards, inputs):
    nc = build_ln0()
    cin = const_inputs()
    cT = np.ascontiguousarray(inputs["L0_ffn_centroids"].T)
    wrT = np.ascontiguousarray(inputs["L0_ffn_w_route"].T)
    in_maps = [dict(xT=xT_shards[i], cT=cT, wrT=wrT, **cin) for i in range(NCORES)]
    return run_bass_kernel_spmd(nc, in_maps, core_ids=list(range(NCORES)))


def make_hsel():
    HSELK = np.zeros((6 * 128, 4), np.float32)
    for d in range(768):
        HSELK[d, d // 192] = 1.0
    H2 = np.zeros((4, 6 * 128), np.float32)
    for d in range(768):
        H2[d // 192, d] = 1.0
    return HSELK, H2


def kernel(**inputs):
    import ml_dtypes
    from concourse.bass_utils import run_bass_kernel_spmd as spmd
    NB = NCORES
    cin = const_inputs()
    cc = lambda a: np.ascontiguousarray(a)
    ins = inputs

    def launch(nc, maps):
        return spmd(nc, maps, core_ids=list(range(NB))).results

    x0 = ins["L0_attn_x"]
    xT = [cc(x0[b].T) for b in range(NB)]
    cT = cc(ins["L0_ffn_centroids"].T)
    wrT = cc(ins["L0_ffn_w_route"].T)
    r = launch(build_ln0(), [dict(xT=xT[b], cT=cT, wrT=wrT, **cin) for b in range(NB)])
    xnhi = [r[b]["xnhi"] for b in range(NB)]
    gatesT = [cc(r[b]["gatesT"]) for b in range(NB)]

    convw = dict(W_in=ins["L0_attn_W_in"],
                 cwx=ins["L0_attn_conv_x_w"].reshape(D, 4),
                 cbx=ins["L0_attn_conv_x_b"].reshape(D, 1),
                 cwz=ins["L0_attn_conv_z_w"].reshape(D, 4),
                 cbz=ins["L0_attn_conv_z_b"].reshape(D, 1))
    r = launch(build_conv(), [dict(xn=xnhi[b], **convw, **cin) for b in range(NB)])
    xs = [r[b]["xs"] for b in range(NB)]
    zs = [r[b]["zs"] for b in range(NB)]

    HSELK, HSELKT = make_hsel()
    projw = dict(W_q=ins["L0_attn_W_q"], W_k=ins["L0_attn_W_k"], W_v=ins["L0_attn_W_v"],
                 W_g=ins["L0_attn_W_g"],
                 W_gkb=cc(np.concatenate([ins["L0_attn_W_gk"], ins["L0_attn_W_b"]], 1)),
                 dtb_bb=cc(np.concatenate([ins["L0_attn_dt_bias"], ins["L0_attn_b_b"]]).reshape(1, 8)),
                 A_log=cc(ins["L0_attn_A_log"].reshape(4, 1)), HSELK=HSELK, HSELKT=HSELKT)
    r = launch(build_proj(), [dict(xs=xs[b], **projw, **cin) for b in range(NB)])
    pj = r

    gnwbc = cc(np.tile(ins["L0_attn_g_norm_w"][None, :], (128, 1)).astype(np.float32))
    r = launch(build_delta(), [dict(qdm=pj[b]["qdm"], kdm=pj[b]["kdm"], ktm=pj[b]["ktm"],
                                    vtm=pj[b]["vtm"], gtm=pj[b]["gtm"], G=cc(pj[b]["G"]),
                                    beta=cc(pj[b]["beta"]), gnwbc=gnwbc, **cin)
                               for b in range(NB)])
    ogdm = [r[b]["ogdm"] for b in range(NB)]

    b1T = cc(ins["L0_ffn_b1"].reshape(8, 4, 128).transpose(0, 2, 1)).reshape(8 * 128, 4)
    ffnw = dict(w1=ins["L0_ffn_w1"].reshape(8 * D, 512), b1T=b1T, W_so=ins["L0_attn_W_so"])
    r = launch(build_ffnh(), [dict(xn=xnhi[b], og=ogdm[b], gatesT=gatesT[b], **ffnw, **cin)
                              for b in range(NB)])
    ht = [r[b]["ht"] for b in range(NB)]
    yst = [r[b]["yst"] for b in range(NB)]

    residw = dict(W_out=ins["L0_attn_W_out"], w2=ins["L0_ffn_w2"].reshape(8 * 512, D),
                  b2=ins["L0_ffn_b2"])
    r = launch(build_resid(), [dict(yst=yst[b], zs=zs[b], ht=ht[b], xT=xT[b],
                                    gatesT=gatesT[b], **residw, **cin) for b in range(NB)])
    x1T = [r[b]["x1T"] for b in range(NB)]
    xn1 = [r[b]["xn1"] for b in range(NB)]

    attw = dict(WkT=cc(ins["L1_attn_Wk"].T), WvT=cc(ins["L1_attn_Wv"].T),
                WrT=cc(ins["L1_attn_Wr"].T), WoT=cc(ins["L1_attn_Wo"].T),
                wcol=ins["L1_attn_w"].reshape(D, 1), ucol=ins["L1_attn_u"].reshape(D, 1),
                mixk=ins["L1_attn_mix_k"].reshape(D, 1),
                mixv=ins["L1_attn_mix_v"].reshape(D, 1),
                mixr=ins["L1_attn_mix_r"].reshape(D, 1))
    r = launch(build_l1attn(), [dict(xn1=xn1[b], x1T=x1T[b], **attw, **cin)
                                for b in range(NB)])
    pres = [r[b]["pres"] for b in range(NB)]

    phi2 = cc(ins["L1_ffn_phi"].reshape(D, L))
    r = launch(build_logits(), [dict(xn1=xn1[b], phi2=phi2, **cin) for b in range(NB)])
    E16 = [r[b]["E16"] for b in range(NB)]
    erinv = [r[b]["erinv"] for b in range(NB)]

    r = launch(build_lb(), [dict(Eg=cc(np.concatenate(
        [E16[b][i * 256:(i + 1) * 256] for b in range(NB)], 0)), **cin)
        for i in range(NB)])
    Dinv16 = np.concatenate([r[i]["Dinv16"] for i in range(NB)], 0)

    r = launch(build_lc(), [dict(E16b=E16[b], Dinv=Dinv16, xn1TM=cc(xn1[b].T), **cin)
                            for b in range(NB)])
    slotsb = [r[b]["slotsb"] for b in range(NB)]

    Wfull = ins["L1_ffn_W"]
    bfull = ins["L1_ffn_b"]
    maps = []
    for i in range(NB):
        sc = np.zeros((D, 2048), dtype=ml_dtypes.bfloat16)
        for nl in range(8):
            n = i * 8 + nl
            for b in range(NB):
                sc[:, (nl * 8 + b) * 32:(nl * 8 + b) * 32 + 32] = \
                    slotsb[b][:, n * 32:(n + 1) * 32]
        maps.append(dict(slots=sc, Wc=Wfull[i * 8:(i + 1) * 8].reshape(8 * D, D),
                         bc=bfull[i * 8:(i + 1) * 8], **cin))
    r = launch(build_ld(), maps)
    yc = [r[i]["yc"] for i in range(NB)]

    maps = []
    for b in range(NB):
        yb = np.zeros((2048, D), dtype=ml_dtypes.bfloat16)
        for n in range(64):
            yb[n * 32:(n + 1) * 32] = yc[n // 8][(n % 8) * 256 + b * 32:(n % 8) * 256 + b * 32 + 32]
        maps.append(dict(yb=yb, EFM=cc(E16[b].T), erow=cc(erinv[b].reshape(1, L)),
                         pres=pres[b], **cin))
    r = launch(build_le(), maps)
    out = np.stack([cc(r[b]["outT"].T) for b in range(NB)], 0)
    return np.asarray(out, np.float32)


# version 10
# speedup vs baseline: 1.0264x; 1.0264x over previous
import numpy as np
import ml_dtypes
from contextlib import ExitStack
import concourse.bass as bass
import concourse.mybir as mybir
from concourse.bass_utils import run_bass_kernel_spmd

F32 = mybir.dt.float32
BF16 = mybir.dt.bfloat16
I32 = mybir.dt.int32
AF = mybir.ActivationFunctionType
ALU = mybir.AluOpType
AX = mybir.AxisListType
BF16NP = ml_dtypes.bfloat16

NCORES = 8
D = 1024
L = 2048
DKT = D // 128
NTG = L // 512
EPS_LN = 1e-5


class K:
    ENGINES = ("sync", "tensor", "vector", "scalar", "gpsimd")
    SERIAL = ("vector", "scalar", "gpsimd")

    def __init__(self, nc, es):
        self.nc = nc
        self.es = es
        self.q = {e: [] for e in self.ENGINES}
        self._semcnt = {}
        self.serial = {e: self.sem("ser_" + e) for e in self.SERIAL}

    def sem(self, name):
        h = self.es.enter_context(self.nc.semaphore(name))
        self._semcnt[id(h)] = 0
        return h

    def bump(self, sem, amt):
        self._semcnt[id(sem)] += amt
        return self._semcnt[id(sem)]

    def val(self, sem):
        return self._semcnt[id(sem)]

    def emit(self, eng, fn):
        self.q[eng].append(fn)

    def op(self, eng, meth, *args, wait=None, inc=None, **kw):
        waits = [(s, v) for (s, v) in (wait or [])]
        incs = []
        marker = None
        if inc is not None and meth.startswith("dma_start"):
            prev = self.val(inc[0])
            if prev > 0 and not any(s is inc[0] and v >= prev for (s, v) in waits):
                waits.append((inc[0], prev))
        if inc is not None:
            self.bump(inc[0], inc[1])
            incs.append(inc)
            marker = (inc[0], self.val(inc[0]))
        if (eng in self.SERIAL and meth not in ("nop", "dma_start", "dma_start_transpose")
                and inc is None):
            ser = self.serial[eng]
            waits.append((ser, self.val(ser)))
            self.bump(ser, 1)
            incs.append((ser, 1))
            marker = (ser, self.val(ser))
        elif eng in self.SERIAL and meth not in ("nop", "dma_start", "dma_start_transpose"):
            raise AssertionError("one sem update per instruction: no user inc on serial engines")
        def f(e, meth=meth, args=args, kw=kw, waits=waits, incs=incs):
            for s, v in waits:
                e.wait_ge(s, v)
            ins = getattr(e, meth)(*args, **kw)
            for s, v in incs:
                ins = ins.then_inc(s, v)
        self.emit(eng, f)
        return marker

    def tr(self, out_ap, in_ap, ident_ap, wait=None, inc=None):
        return self.op("tensor", "transpose", out_ap, in_ap, ident_ap, wait=wait, inc=inc)

    def wait(self, eng, sem, val=None):
        v = self.val(sem) if val is None else val
        self.emit(eng, lambda e, s=sem, v=v: e.wait_ge(s, v))

    def waitm(self, eng, marker):
        self.emit(eng, lambda e, s=marker[0], v=marker[1]: e.wait_ge(s, v))

    def mark(self, eng):
        ser = self.serial[eng]
        return (ser, self.val(ser))

    def finalize(self):
        blk = self.es.enter_context(self.nc.Block())
        for name in self.ENGINES:
            ops = self.q[name]
            def runner(e, ops=ops):
                for f in ops:
                    f(e)
            getattr(blk, name)(runner)


class DPool:

    def __init__(self, k, name, queue="sync", n=4):
        self.k = k
        self.queue = queue
        self.sems = [k.sem(f"{name}{i}") for i in range(n)]
        self.i = 0

    def dma(self, out, in_, wait=None, queue=None):
        s = self.sems[self.i % len(self.sems)]
        self.i += 1
        return self.k.op(queue or self.queue, "dma_start", out=out, in_=in_,
                         wait=wait, inc=(s, 16))

    def all(self):
        return [(s, self.k.val(s)) for s in self.sems if self.k.val(s) > 0]


def sbuf(k, name, shape, dt=F32):
    return k.es.enter_context(k.nc.sbuf_tensor("sb_" + name, list(shape), dt))


def psum(k, name, shape=(128, 512), dt=F32):
    return k.es.enter_context(k.nc.psum_tensor("pb_" + name, list(shape), dt))


FVALS = [EPS_LN, 1.0 / D, -1.0, float(D), -1e30, -0.5, 0.5, 2.0, 1e-6, 1e-12, 384.0, 1.0 / 384.0, 1.0 / 192.0]


def make_consts():
    c = {}
    c["FCONSTS"] = np.tile(np.asarray(FVALS, np.float32)[None, :], (128, 1))
    c["IDENT"] = np.eye(128, dtype=np.float32)
    c["ONESCOL"] = np.ones((128, 1), np.float32)
    c["ONESROW"] = np.ones((1, 128), np.float32)
    c["ONESROW512"] = np.ones((1, 512), np.float32)
    c["CUMUL"] = np.triu(np.ones((128, 128), np.float32))
    c["MASK_STRICT"] = np.where(np.arange(128)[:, None] < np.arange(128)[None, :], 0.0, -1e9).astype(np.float32)
    c["MASK_INCL"] = np.where(np.arange(128)[:, None] <= np.arange(128)[None, :], 0.0, -1e9).astype(np.float32)
    c["MASK_STRICT_T"] = np.ascontiguousarray(c["MASK_STRICT"].T)
    c["IDENT4"] = np.tile(np.eye(128, dtype=np.float32), (1, 4))
    sel = np.zeros((4, 4 * 128), np.float32)
    for h in range(4):
        sel[h, h * 128:(h + 1) * 128] = 1.0
    c["SEL4"] = sel
    sel8 = np.zeros((8, 8 * 128), np.float32)
    for e in range(8):
        sel8[e, e * 128:(e + 1) * 128] = 1.0
    c["SEL8"] = sel8
    return c


_CONSTS = make_consts()
CONST_SPECS = {k: v.shape for k, v in _CONSTS.items()}


def declare_consts(nc):
    return {name: nc.dram_tensor("C_" + name, list(shape), F32, kind="ExternalInput").ap()
            for name, shape in CONST_SPECS.items()}


def const_inputs():
    return {"C_" + name: v for name, v in _CONSTS.items()}


def load_const(k, aps, name, ld):
    t = sbuf(k, "c_" + name, CONST_SPECS[name], F32)
    k.op("sync", "dma_start", out=t[:], in_=aps[name], inc=(ld, 16))
    return t


def register_fconsts(k, caps, ld):
    t = load_const(k, caps, "FCONSTS", ld)
    for j, v in enumerate(FVALS):
        k.nc.const_aps.register(F32, float(v), t[:, j:j + 1])
    return t


def build_ln0(stop=9):
    nc = bass.Bass("TRN2", target_bir_lowering=False)
    xT = nc.dram_tensor("xT", [D, L], F32, kind="ExternalInput").ap()
    cT = nc.dram_tensor("cT", [D, 8], F32, kind="ExternalInput").ap()
    wrT = nc.dram_tensor("wrT", [D, 8], F32, kind="ExternalInput").ap()
    caps = declare_consts(nc)
    o_xnhi = nc.dram_tensor("xnhi", [D, L], BF16, kind="ExternalOutput").ap()
    o_xnlo = nc.dram_tensor("xnlo", [D, L], BF16, kind="ExternalOutput").ap()
    o_gatesT = nc.dram_tensor("gatesT", [8, L], F32, kind="ExternalOutput").ap()

    es = ExitStack()
    with es:
        k = K(nc, es)
        ld = k.sem("ld")
        st = k.sem("stn")
        so = k.sem("so")

        register_fconsts(k, caps, ld)
        ident = load_const(k, caps, "IDENT", ld)
        onescol = load_const(k, caps, "ONESCOL", ld)
        onesrow = load_const(k, caps, "ONESROW", ld)
        onesrow512 = load_const(k, caps, "ONESROW512", ld)

        xg = sbuf(k, "xg", [128, DKT * 512], F32)
        xsq = sbuf(k, "xsq", [128, 2 * 512], F32)
        scw_all = sbuf(k, "scw_all", [128, DKT * 16], F32)
        scw_hi = sbuf(k, "scw_hi", [128, DKT * 16], BF16)
        scw_lo = sbuf(k, "scw_lo", [128, DKT * 16], BF16)
        csq = sbuf(k, "csq", [128, 8], F32)
        rowsg = sbuf(k, "rowsg", [1, 8 * 512], F32)
        rn = sbuf(k, "rn", [1, L], F32)
        cnrow = sbuf(k, "cnrow", [1, 16], F32)
        bcA = sbuf(k, "bcA", [128, 512], F32)
        bcB = sbuf(k, "bcB", [128, 512], F32)
        xn32 = sbuf(k, "xn32", [128, 512], F32)
        tsc = sbuf(k, "tsc", [128, 512], F32)
        xnhi = sbuf(k, "xnhi", [128, DKT * L], BF16)
        xnlo = sbuf(k, "xnlo", [128, DKT * L], BF16)
        lg = sbuf(k, "lg", [16, L], F32)
        bccn = sbuf(k, "bccn", [8, 512], F32)
        bcrn = sbuf(k, "bcrn", [8, 512], F32)
        ltm = sbuf(k, "ltm", [128, 128], F32)
        gtm = sbuf(k, "gtm", [128, 128], F32)
        m1 = sbuf(k, "m1", [128, 16], F32)
        m2 = sbuf(k, "m2", [128, 16], F32)
        ssum = sbuf(k, "ssum", [128, 16], F32)
        eq = sbuf(k, "eq", [128, 128], F32)
        keep = sbuf(k, "keep", [128, 128], F32)
        etm = sbuf(k, "etm", [128, 128], F32)
        gatesT = sbuf(k, "gatesT", [8, L], F32)

        ps_s = psum(k, "ps_s", (128, 1024))
        ps_bc = psum(k, "ps_bc", (128, 1024))
        ps_route = psum(k, "ps_route", (128, 1024))
        ps_tr = psum(k, "ps_tr")

        for kt in range(DKT):
            k.op("sync", "dma_start", out=scw_all[:, kt * 16:kt * 16 + 8],
                 in_=cT[kt * 128:(kt + 1) * 128, :], inc=(ld, 16))
            k.op("sync", "dma_start", out=scw_all[:, kt * 16 + 8:kt * 16 + 16],
                 in_=wrT[kt * 128:(kt + 1) * 128, :], inc=(ld, 16))
        M_LDC = (ld, k.val(ld))

        if stop <= 1:
            k.wait("gpsimd", ld, k.val(ld))
            k.op("gpsimd", "dma_start", out=o_xnhi[0:128, :], in_=xnhi[:, 0:L], inc=(so, 16))
            k.op("gpsimd", "dma_start", out=o_gatesT, in_=gatesT[:], inc=(so, 16))
            k.wait("gpsimd", so, k.val(so))
            k.finalize()
            return nc

        prev_mm = None
        for kt in range(DKT):
            w = [M_LDC] if kt == 0 else [prev_mm]
            msq = k.op("scalar", "activation", csq[:, 0:8], scw_all[:, kt * 16:kt * 16 + 8],
                       AF.Square, wait=w)
            prev_mm = k.op("tensor", "matmul", ps_bc[0:1, 0:8], onescol[:], csq[:, 0:8],
                           start=(kt == 0), stop=(kt == DKT - 1),
                           wait=[msq, M_LDC] if kt == 0 else [msq], inc=(st, 1))
        k.op("scalar", "activation", cnrow[0:1, 0:8], ps_bc[0:1, 0:8], AF.Ln, wait=[prev_mm])
        m_cn = k.op("scalar", "activation", cnrow[0:1, 0:8], cnrow[0:1, 0:8], AF.Exp, scale=-0.5)
        k.op("vector", "tensor_copy", scw_hi[:], scw_all[:], wait=[M_LDC])
        m_scw = k.op("vector", "tensor_tensor", scw_lo[:], scw_all[:], scw_hi[:], ALU.subtract)

        if stop <= 2:
            k.op("gpsimd", "dma_start", out=o_xnhi[0:128, :], in_=xnhi[:, 0:L], wait=[m_scw, m_cn], inc=(so, 16))
            k.op("gpsimd", "dma_start", out=o_gatesT, in_=gatesT[:], inc=(so, 16))
            k.wait("gpsimd", so, k.val(so))
            k.finalize()
            return nc

        Rg = lambda i: rowsg[0:1, i * 512:(i + 1) * 512]
        m_xn_done = None
        m_lg_done = None
        m_ev = None
        m_sq_prev = [None, None]
        nsq = 0
        for g in range(NTG):
            gs = slice(g * 512, (g + 1) * 512)
            ldw = []
            if m_xn_done is not None:
                ldw = [m_xn_done]
            for kt in range(DKT):
                k.op("sync", "dma_start", out=xg[:, kt * 512:(kt + 1) * 512],
                     in_=xT[kt * 128:(kt + 1) * 128, gs],
                     wait=ldw if kt == 0 else None, inc=(ld, 16))
            M_LDG = (ld, k.val(ld))

            for kt in range(DKT):
                buf = xsq[:, (nsq % 2) * 512:(nsq % 2) * 512 + 512]
                w = [M_LDG] if kt == 0 else []
                if m_sq_prev[nsq % 2] is not None:
                    w = w + [m_sq_prev[nsq % 2]]
                msq = k.op("scalar", "activation", buf, xg[:, kt * 512:(kt + 1) * 512],
                           AF.Square, wait=w)
                wmm = ([M_LDG] + ([m_ev] if m_ev is not None else [])) if kt == 0 else None
                k.op("tensor", "matmul", ps_s[0:1, 0:512], onescol[:], xg[:, kt * 512:(kt + 1) * 512],
                     start=(kt == 0), stop=(kt == DKT - 1),
                     wait=wmm)
                m_sq_prev[nsq % 2] = k.op("tensor", "matmul", ps_s[0:1, 512:1024], onescol[:], buf,
                                          start=(kt == 0), stop=(kt == DKT - 1),
                                          wait=[msq], inc=(st, 1))
                nsq += 1
            m_stats = m_sq_prev[(nsq - 1) % 2]
            k.op("vector", "tensor_copy", Rg(0), ps_s[0:1, 0:512], wait=[m_stats])
            m_ev = k.op("vector", "tensor_copy", Rg(1), ps_s[0:1, 512:1024])
            k.op("vector", "tensor_scalar", Rg(2), Rg(0), 1.0 / D, None, ALU.mult)
            k.op("vector", "tensor_scalar", Rg(3), Rg(1), 1.0 / D, None, ALU.mult)
            k.op("vector", "tensor_tensor", Rg(6), Rg(2), Rg(2), ALU.mult)
            m_var = k.op("vector", "tensor_tensor", Rg(3), Rg(3), Rg(6), ALU.subtract)
            k.op("scalar", "activation", Rg(6), Rg(3), AF.Ln, bias=EPS_LN, wait=[m_var])
            m_rstd = k.op("scalar", "activation", Rg(4), Rg(6), AF.Exp, scale=-0.5)
            k.op("vector", "tensor_tensor", Rg(5), Rg(2), Rg(4), ALU.mult, wait=[m_rstd])
            k.op("vector", "tensor_scalar", Rg(5), Rg(5), -1.0, None, ALU.mult)
            m_nm = k.op("vector", "tensor_scalar", Rg(7), Rg(3), float(D), None, ALU.mult)
            m_ln2 = k.op("scalar", "activation", Rg(7), Rg(7), AF.Ln, wait=[m_nm])
            m_sub = k.op("vector", "tensor_tensor", Rg(7), Rg(7), Rg(6), ALU.subtract, wait=[m_ln2])
            m_rn = k.op("scalar", "activation", rn[0:1, gs], Rg(7), AF.Exp, scale=-0.5, wait=[m_sub])
            if stop <= 3:
                m_xn_done = m_ev
                m_lg_done = m_ev
                continue
            wbc1 = [m_nm] + ([m_cn] if g == 0 else [])
            m_bc1 = k.op("tensor", "matmul", ps_bc[0:128, 0:512], onesrow[:], Rg(4),
                         start=True, stop=True, wait=wbc1, inc=(st, 1))
            m_cA = k.op("vector", "tensor_copy", bcA[:], ps_bc[0:128, 0:512], wait=[m_bc1])
            m_bc2 = k.op("tensor", "matmul", ps_bc[0:128, 0:512], onesrow[:], Rg(5),
                         start=True, stop=True, wait=[m_cA], inc=(st, 1))
            m_cB = k.op("vector", "tensor_copy", bcB[:], ps_bc[0:128, 0:512], wait=[m_bc2])
            for kt in range(DKT):
                xsl = slice(kt * 512, (kt + 1) * 512)
                osl = slice(kt * L + g * 512, kt * L + g * 512 + 512)
                k.op("vector", "tensor_tensor", tsc[:], xg[:, xsl], bcA[:], ALU.mult)
                k.op("vector", "tensor_tensor", xn32[:], tsc[:], bcB[:], ALU.add)
                k.op("vector", "tensor_copy", xnhi[:, osl], xn32[:])
                m_xn_done = k.op("vector", "tensor_tensor", xnlo[:, osl], xn32[:], xnhi[:, osl], ALU.subtract)
            if stop <= 4:
                m_lg_done = m_xn_done
                continue
            rw = [m_xn_done, m_scw]
            if m_lg_done is not None:
                rw.append(m_lg_done)
            for kt in range(DKT):
                osl = slice(kt * L + g * 512, kt * L + g * 512 + 512)
                chi = scw_hi[:, kt * 16:kt * 16 + 8]
                clo = scw_lo[:, kt * 16:kt * 16 + 8]
                whi = scw_hi[:, kt * 16 + 8:kt * 16 + 16]
                wlo = scw_lo[:, kt * 16 + 8:kt * 16 + 16]
                first, last = kt == 0, kt == DKT - 1
                k.op("tensor", "matmul", ps_route[0:8, 0:512], chi, xnhi[:, osl],
                     start=first, stop=False, wait=rw if first else None)
                k.op("tensor", "matmul", ps_route[0:8, 0:512], chi, xnlo[:, osl], start=False, stop=False)
                k.op("tensor", "matmul", ps_route[0:8, 0:512], clo, xnhi[:, osl], start=False, stop=last)
                k.op("tensor", "matmul", ps_route[0:8, 512:1024], whi, xnhi[:, osl], start=first, stop=False)
                k.op("tensor", "matmul", ps_route[0:8, 512:1024], whi, xnlo[:, osl], start=False, stop=False)
                mr = k.op("tensor", "matmul", ps_route[0:8, 512:1024], wlo, xnhi[:, osl],
                          start=False, stop=last, inc=(st, 1) if last else None)
            m_bcn = k.op("tensor", "matmul", ps_bc[0:8, 0:512], cnrow[0:1, 0:8], onesrow512[:],
                         start=True, stop=True, wait=[m_cn, m_cB], inc=(st, 1))
            m_brn = k.op("tensor", "matmul", ps_bc[0:8, 512:1024], onesrow[0:1, 0:8], rn[0:1, gs],
                         start=True, stop=True, wait=[m_rn], inc=(st, 1))
            k.op("vector", "tensor_copy", bccn[:], ps_bc[0:8, 0:512], wait=[m_bcn, m_brn, mr])
            k.op("vector", "tensor_copy", bcrn[:], ps_bc[0:8, 512:1024])
            k.op("vector", "tensor_tensor", lg[0:8, gs], ps_route[0:8, 0:512], bccn[:], ALU.mult)
            k.op("vector", "tensor_tensor", lg[0:8, gs], lg[0:8, gs], bcrn[:], ALU.mult)
            m_lg_done = k.op("vector", "tensor_tensor", lg[0:8, gs], lg[0:8, gs],
                             ps_route[0:8, 512:1024], ALU.add)
        if stop <= 5:
            k.op("gpsimd", "dma_start", out=o_xnhi[0:128, :], in_=xnhi[:, 0:L], wait=[m_lg_done], inc=(so, 16))
            k.op("gpsimd", "dma_start", out=o_gatesT, in_=gatesT[:], inc=(so, 16))
            k.wait("gpsimd", so, k.val(so))
            k.finalize()
            return nc
        m_trf = None
        for c in range(16):
            m_trf = k.tr(ps_tr[0:128, c * 8:(c + 1) * 8],
                 lg[0:8, c * 128:(c + 1) * 128], ident[0:8, 0:8],
                 wait=[m_lg_done, M_LDC] if c == 0 else None,
                 inc=(st, 1) if c == 15 else None)
        m_ltm = k.op("vector", "tensor_copy", ltm[:], ps_tr[0:128, 0:128], wait=[m_trf])
        if stop <= 6:
            k.op("gpsimd", "dma_start", out=o_xnhi[0:128, :], in_=xnhi[:, 0:L], wait=[m_ltm], inc=(so, 16))
            k.op("gpsimd", "dma_start", out=o_gatesT, in_=gatesT[:], inc=(so, 16))
            k.wait("gpsimd", so, k.val(so))
            k.finalize()
            return nc
        ltm3 = ltm[:].rearrange("p (c e) -> p c e", e=8)
        bview = lambda t: t[:].unsqueeze(2).broadcast_to([128, 16, 8])
        k.op("vector", "tensor_reduce", m1[:], ltm3, AX.X, ALU.max)
        k.op("vector", "tensor_tensor", eq[:].rearrange("p (c e) -> p c e", e=8), ltm3, bview(m1), ALU.is_ge)
        k.op("vector", "scalar_tensor_tensor", etm[:], eq[:], -1e30, ltm[:], ALU.mult, ALU.add)
        k.op("vector", "tensor_reduce", m2[:], etm[:].rearrange("p (c e) -> p c e", e=8), AX.X, ALU.max)
        k.op("vector", "tensor_tensor", keep[:].rearrange("p (c e) -> p c e", e=8), ltm3, bview(m2), ALU.is_ge)
        m_e = k.op("vector", "tensor_tensor", etm[:].rearrange("p (c e) -> p c e", e=8), ltm3, bview(m1), ALU.subtract)
        m_ex = k.op("scalar", "activation", etm[:], etm[:], AF.Exp, wait=[m_e])
        k.op("vector", "tensor_tensor", etm[:], etm[:], keep[:], ALU.mult, wait=[m_ex])
        k.op("vector", "tensor_reduce", ssum[:], etm[:].rearrange("p (c e) -> p c e", e=8), AX.X, ALU.add)
        k.op("vector", "reciprocal", ssum[:], ssum[:])
        m_gtm = k.op("vector", "tensor_tensor", gtm[:].rearrange("p (c e) -> p c e", e=8),
                     etm[:].rearrange("p (c e) -> p c e", e=8), bview(ssum), ALU.mult)
        if stop <= 7:
            k.op("gpsimd", "dma_start", out=o_xnhi[0:128, :], in_=xnhi[:, 0:L], wait=[m_gtm], inc=(so, 16))
            k.op("gpsimd", "dma_start", out=o_gatesT, in_=gatesT[:], inc=(so, 16))
            k.wait("gpsimd", so, k.val(so))
            k.finalize()
            return nc
        m_gt_ev = None
        for g in range(NTG):
            m_trb = None
            for j in range(4):
                c = g * 4 + j
                w = [m_gtm] if (g == 0 and j == 0) else ([m_gt_ev] if j == 0 else None)
                m_trb = k.tr(ps_tr[0:8, j * 128:(j + 1) * 128],
                     gtm[:, c * 8:(c + 1) * 8], ident[:], wait=w,
                     inc=(st, 1) if j == 3 else None)
            m_gt_ev = k.op("vector", "tensor_copy", gatesT[:, g * 512:(g + 1) * 512],
                           ps_tr[0:8, 0:512], wait=[m_trb])

        if stop <= 4:
            k.op("gpsimd", "dma_start", out=o_xnhi[0:128, :], in_=xnhi[:, 0:L], wait=[m_xn_done], inc=(so, 16))
            k.op("gpsimd", "dma_start", out=o_gatesT, in_=gatesT[:], inc=(so, 16))
            k.wait("gpsimd", so, k.val(so))
            k.finalize()
            return nc
        for kt in range(DKT):
            k.op("gpsimd", "dma_start", out=o_xnhi[kt * 128:(kt + 1) * 128, :],
                 in_=xnhi[:, kt * L:(kt + 1) * L],
                 wait=[m_xn_done] if kt == 0 else None, inc=(so, 16))
            k.op("gpsimd", "dma_start", out=o_xnlo[kt * 128:(kt + 1) * 128, :],
                 in_=xnlo[:, kt * L:(kt + 1) * L], inc=(so, 16))
        k.op("gpsimd", "dma_start", out=o_gatesT, in_=gatesT[:], wait=[m_gt_ev], inc=(so, 16))
        k.wait("gpsimd", so, k.val(so))
        k.finalize()
    return nc


def build_conv(nmt=16, notaps=False, nomm=False, noevac=False, serial_evac=False, ngroups=4, nomemset=False, evac_eng='scalar', evac_dst='xzrow1', rawxz=False, rawacc=False):
    nc = bass.Bass("TRN2", target_bir_lowering=False)
    xn = nc.dram_tensor("xn", [D, L], BF16, kind="ExternalInput").ap()
    W_in = nc.dram_tensor("W_in", [D, 2 * D], F32, kind="ExternalInput").ap()
    cwx = nc.dram_tensor("cwx", [D, 4], F32, kind="ExternalInput").ap()
    cbx = nc.dram_tensor("cbx", [D, 1], F32, kind="ExternalInput").ap()
    cwz = nc.dram_tensor("cwz", [D, 4], F32, kind="ExternalInput").ap()
    cbz = nc.dram_tensor("cbz", [D, 1], F32, kind="ExternalInput").ap()
    caps = declare_consts(nc)
    o_xs = nc.dram_tensor("xs", [D, L], BF16, kind="ExternalOutput").ap()
    o_zs = nc.dram_tensor("zs", [D, L], BF16, kind="ExternalOutput").ap()

    es = ExitStack()
    with es:
        k = K(nc, es)
        ld = k.sem("ld")
        ldw = k.sem("ldw")
        st = k.sem("stn")
        so = k.sem("so")
        pin = DPool(k, "pin", "sync", 4)
        register_fconsts(k, caps, ld)

        xnb = sbuf(k, "xnb", [128, DKT * L], BF16)
        w32 = sbuf(k, "w32", [128, 2 * D], F32)
        w16 = sbuf(k, "w16", [128, 2 * D], BF16)
        cw = sbuf(k, "cw", [128, 2 * 4 * DKT], F32)
        cb = sbuf(k, "cb", [128, 2 * DKT], F32)
        xzrow = sbuf(k, "xzrow", [128, 2 * 2051], F32)
        fresh = sbuf(k, "fresh", [128, 512], F32)
        acc1 = sbuf(k, "acc1", [128, L], F32)
        acc2 = sbuf(k, "acc2", [128, L], F32)
        outt = sbuf(k, "outt", [128, L], BF16)
        ps = [psum(k, "ps0"), psum(k, "ps1")]

        for kt in range(DKT):
            pin.dma(xnb[:, kt * L:(kt + 1) * L], xn[kt * 128:(kt + 1) * 128, :])
        for kt in range(DKT):
            k.op("sync", "dma_start", out=cw[:, kt * 4:kt * 4 + 4],
                 in_=cwx[kt * 128:(kt + 1) * 128, :], inc=(ld, 16))
            k.op("sync", "dma_start", out=cw[:, 32 + kt * 4:32 + kt * 4 + 4],
                 in_=cwz[kt * 128:(kt + 1) * 128, :], inc=(ld, 16))
        for kt in range(DKT):
            k.op("sync", "dma_start", out=cb[:, kt:kt + 1], in_=cbx[kt * 128:(kt + 1) * 128, :], inc=(ld, 16))
            k.op("sync", "dma_start", out=cb[:, DKT + kt:DKT + kt + 1], in_=cbz[kt * 128:(kt + 1) * 128, :], inc=(ld, 16))
        M_LD = (ld, k.val(ld))

        m_zero = None
        if not nomemset:
            for bi in range(2):
                base = bi * 2051
                k.op("vector", "memset", xzrow[:, base:base + 1], 0.0)
                m_zero = k.op("vector", "memset", xzrow[:, base + 2049:base + 2051], 0.0)

        m_cast = [None, None]
        m_conv = [None, None]
        m_evac = [None, None]
        m_evac_last = None
        m_mmdone = [None, None]
        m_silu_prev = None
        for mt in range(nmt):
            bi = mt % 2
            wlw = [m_cast[bi]] if m_cast[bi] is not None else []
            if m_mmdone[bi] is not None:
                pass
            mws = []
            for kt in range(DKT):
                mws.append(pin.dma(w32[:, bi * D + kt * 128:bi * D + (kt + 1) * 128],
                                   W_in[kt * 128:(kt + 1) * 128, mt * 128:(mt + 1) * 128],
                                   wait=wlw if kt == 0 else None))
            M_W = mws
            mc = None
            for kt in range(DKT):
                w = None
                if kt == 0:
                    w = list(M_W)
                    if m_mmdone[bi] is not None:
                        w.append(m_mmdone[bi])
                mc = k.op("gpsimd", "tensor_copy", w16[:, bi * D + kt * 128:bi * D + (kt + 1) * 128],
                          w32[:, bi * D + kt * 128:bi * D + (kt + 1) * 128],
                          wait=w)
            m_cast[bi] = mc
            base = bi * 2051
            if nomm:
                continue
            for g in range(ngroups):
                pbank = (mt * NTG + g) % 2
                for kt in range(DKT):
                    w = []
                    if kt == 0:
                        if g == 0:
                            w = [mc, M_LD]
                            if m_conv[bi] is not None:
                                w.append(m_conv[bi])
                        if m_evac[pbank] is not None:
                            w.append(m_evac[pbank])
                        if serial_evac and m_evac_last is not None:
                            w.append(m_evac_last)
                    m_mm = k.op("tensor", "matmul", ps[pbank][0:128, 0:512],
                         w16[:, bi * D + kt * 128:bi * D + (kt + 1) * 128],
                         xnb[:, kt * L + g * 512:kt * L + g * 512 + 512],
                         start=(kt == 0), stop=(kt == DKT - 1),
                         wait=w if w else None,
                         inc=(st, 1) if kt == DKT - 1 else None)
                m_mmdone[bi] = m_mm
                if noevac:
                    m_evac[pbank] = m_mm
                    m_evac_last = m_mm
                    continue
                if evac_dst == "fresh":
                    dstap = fresh[:, 0:512]
                elif evac_dst == "xzrow0":
                    dstap = xzrow[:, base + g * 512:base + g * 512 + 512]
                else:
                    dstap = xzrow[:, base + 1 + g * 512:base + 1 + g * 512 + 512]
                m_evac[pbank] = k.op(evac_eng, "copy" if evac_eng == "scalar" else "tensor_copy",
                                  dstap,
                                  ps[pbank][0:128, 0:512],
                                  wait=[m_mm] + ([m_zero] if (mt == 0 and g == 0 and m_zero is not None) else []))
                m_evac_last = m_evac[pbank]
            if rawxz:
                mr = k.op("vector", "tensor_copy", outt[:], xzrow[:, base + 1:base + 2049], wait=[m_evac_last])
                dst = o_xs if mt < 8 else o_zs
                row = (mt % 8) * 128
                k.op("gpsimd", "dma_start", out=dst[row:row + 128, :], in_=outt[:], wait=[mr], inc=(so, 16))
                k.wait("vector", so, k.val(so))
                continue
            if notaps:
                continue
            cwcol = (0 if mt < 8 else 32) + (mt % 8) * 4
            cbcol = (0 if mt < 8 else DKT) + (mt % 8)
            wfirst = [m_evac_last] + ([m_silu_prev] if m_silu_prev is not None else [])
            k.op("vector", "tensor_scalar", acc1[:], xzrow[:, base + 0:base + 2048],
                 cw[:, cwcol:cwcol + 1], None, ALU.mult, wait=wfirst)
            k.op("vector", "scalar_tensor_tensor", acc2[:], xzrow[:, base + 1:base + 1 + 2048],
                 cw[:, cwcol + 1:cwcol + 2], acc1[:], ALU.mult, ALU.add)
            k.op("vector", "scalar_tensor_tensor", acc1[:], xzrow[:, base + 2:base + 2 + 2048],
                 cw[:, cwcol + 2:cwcol + 3], acc2[:], ALU.mult, ALU.add)
            k.op("vector", "scalar_tensor_tensor", acc2[:], xzrow[:, base + 3:base + 3 + 2048],
                 cw[:, cwcol + 3:cwcol + 4], acc1[:], ALU.mult, ALU.add)
            m_conv[bi] = k.mark("vector")
            if rawacc:
                mr = k.op("vector", "tensor_copy", outt[:], acc2[:])
                dst = o_xs if mt < 8 else o_zs
                row = (mt % 8) * 128
                k.op("gpsimd", "dma_start", out=dst[row:row + 128, :], in_=outt[:], wait=[mr], inc=(so, 16))
                k.wait("vector", so, k.val(so))
                m_silu_prev = mr
                continue
            m_act = k.op("scalar", "activation", outt[:], acc2[:], AF.Silu,
                         bias=cb[:, cbcol:cbcol + 1], wait=[m_conv[bi]])
            m_silu_prev = m_act
            dst = o_xs if mt < 8 else o_zs
            row = (mt % 8) * 128
            k.op("scalar", "dma_start", out=dst[row:row + 128, :], in_=outt[:], wait=[m_act], inc=(so, 16))
            k.wait("scalar", so, k.val(so))
        k.wait("gpsimd", so, k.val(so))
        k.finalize()
    return nc


def build_proj(stop=9, qstop=9):
    nc = bass.Bass("TRN2", target_bir_lowering=False)
    xs = nc.dram_tensor("xs", [D, L], BF16, kind="ExternalInput").ap()
    W_q = nc.dram_tensor("W_q", [D, 768], F32, kind="ExternalInput").ap()
    W_k = nc.dram_tensor("W_k", [D, 768], F32, kind="ExternalInput").ap()
    W_v = nc.dram_tensor("W_v", [D, 1536], F32, kind="ExternalInput").ap()
    W_g = nc.dram_tensor("W_g", [D, 1536], F32, kind="ExternalInput").ap()
    W_gkb = nc.dram_tensor("W_gkb", [D, 8], F32, kind="ExternalInput").ap()
    dtb_bb = nc.dram_tensor("dtb_bb", [1, 8], F32, kind="ExternalInput").ap()
    A_log = nc.dram_tensor("A_log", [4, 1], F32, kind="ExternalInput").ap()
    HSELK = nc.dram_tensor("HSELK", [6 * 128, 4], F32, kind="ExternalInput").ap()
    HSELKT = nc.dram_tensor("HSELKT", [4, 6 * 128], F32, kind="ExternalInput").ap()
    caps = declare_consts(nc)
    o_qdm = nc.dram_tensor("qdm", [768, L], BF16, kind="ExternalOutput").ap()
    o_kdm = nc.dram_tensor("kdm", [768, L], BF16, kind="ExternalOutput").ap()
    o_ktm = nc.dram_tensor("ktm", [L, 768], BF16, kind="ExternalOutput").ap()
    o_vtm = nc.dram_tensor("vtm", [L, 1536], BF16, kind="ExternalOutput").ap()
    o_gtm = nc.dram_tensor("gtm", [L, 1536], BF16, kind="ExternalOutput").ap()
    o_G = nc.dram_tensor("G", [4, L], F32, kind="ExternalOutput").ap()
    o_beta = nc.dram_tensor("beta", [L, 4], F32, kind="ExternalOutput").ap()

    es = ExitStack()
    with es:
        k = K(nc, es)
        ld = k.sem("ld")
        st = k.sem("stn")
        pa = DPool(k, "pa", "sync", 4)
        pw = DPool(k, "pw", "sync", 4)
        po = DPool(k, "po", "gpsimd", 4)
        register_fconsts(k, caps, ld)
        ident = load_const(k, caps, "IDENT", ld)
        onesrow = load_const(k, caps, "ONESROW", ld)
        cumul = load_const(k, caps, "CUMUL", ld)
        M_CONST = (ld, k.val(ld))

        xsb = sbuf(k, "xsb", [128, DKT * L], BF16)
        wq16 = sbuf(k, "wq16", [128, DKT * 768], BF16)
        wk16 = sbuf(k, "wk16", [128, DKT * 768], BF16)
        wv16 = sbuf(k, "wv16", [128, DKT * 1536], BF16)
        wg16 = sbuf(k, "wg16", [128, DKT * 1536], BF16)
        wgkb16 = sbuf(k, "wgkb16", [128, DKT * 8], BF16)
        wld4 = sbuf(k, "wld4", [128, 4 * 1536], F32)
        hselk = sbuf(k, "hselk", [128, 6 * 4], F32)
        hselkt = sbuf(k, "hselkt", [4, 6 * 128], F32)
        dtbb = sbuf(k, "dtbb", [1, 8], F32)
        negA = sbuf(k, "negA", [4, 1], F32)
        qf = sbuf(k, "qf", [128, 6 * 512], F32)
        qsq = sbuf(k, "qsq", [128, 512], F32)
        rstdq = sbuf(k, "rstdq", [4, L], F32)
        rstdk = sbuf(k, "rstdk", [4, L], F32)
        rowsc = sbuf(k, "rowsc", [4, 512], F32)
        outb = sbuf(k, "outb", [128, 2 * 512], BF16)
        rstdTall = sbuf(k, "rstdTall", [128, 64], F32)
        gkball = sbuf(k, "gkball", [128, 256], F32)
        Grow = sbuf(k, "Grow", [4, L], F32)

        ps_a = psum(k, "ps_a", (128, 1024))
        ps_s = psum(k, "ps_s", (128, 512))
        ps_m = psum(k, "ps_m", (128, 512))

        for kt in range(DKT):
            pa.dma(xsb[:, kt * L:(kt + 1) * L], xs[kt * 128:(kt + 1) * 128, :])
        for t in range(6):
            pa.dma(hselk[:, t * 4:(t + 1) * 4], HSELK[t * 128:(t + 1) * 128, :])
        pa.dma(hselkt[0:4, :], HSELKT)
        pa.dma(dtbb[:], dtb_bb)
        pa.dma(negA[:], A_log)
        M_LDL = pa.all()
        M_LD = M_LDL[0]
        m_nA = k.op("scalar", "activation", negA[:], negA[:], AF.Exp, wait=M_LDL)
        m_nA = k.op("vector", "tensor_scalar", negA[:], negA[:], -1.0, None, ALU.mult, wait=[m_nA])

        wcasts = []

        def load_w16(dst, src_ap, width):
            for kt in range(DKT):
                i = len(wcasts)
                buf = wld4[:, (i % 4) * 1536:(i % 4) * 1536 + width]
                mdl = pw.dma(buf, src_ap[kt * 128:(kt + 1) * 128, :],
                             wait=[wcasts[-4]] if i >= 4 else None)
                wcasts.append(k.op("scalar", "activation",
                                   dst[:, kt * width:(kt + 1) * width], buf, AF.Copy,
                                   wait=[mdl]))
            return wcasts[-1]

        m_wq = load_w16(wq16, W_q, 768)
        m_wk = load_w16(wk16, W_k, 768)
        m_wgkb = load_w16(wgkb16, W_gkb, 8)
        m_wv = load_w16(wv16, W_v, 1536)
        m_wg = load_w16(wg16, W_g, 1536)

        m_bank = [None, None]
        m_pss = None
        m_psm = None
        m_outb = [None, None]

        def qk_dm(w16t, m_w, o_dm, rstd_rows):
            nonlocal m_pss
            m_bcmul = None
            for g in range(NTG):
                gs = slice(g * 512, (g + 1) * 512)
                for mt in range(6):
                    bank = mt % 2
                    m = None
                    for kt in range(DKT):
                        w = []
                        if kt == 0:
                            if g == 0 and mt == 0:
                                w += [m_w] + M_LDL
                            if m_bank[bank] is not None:
                                w += (m_bank[bank] if isinstance(m_bank[bank], list) else [m_bank[bank]])
                        m = k.op("tensor", "matmul", ps_a[0:128, bank * 512:bank * 512 + 512],
                                 w16t[:, kt * 768 + mt * 128:kt * 768 + (mt + 1) * 128],
                                 xsb[:, kt * L + g * 512:kt * L + g * 512 + 512],
                                 start=(kt == 0), stop=(kt == DKT - 1),
                                 wait=w or None, inc=(st, 1) if kt == DKT - 1 else None)
                    m_cp = k.op("vector", "tensor_copy", qf[:, mt * 512:(mt + 1) * 512],
                                ps_a[0:128, bank * 512:bank * 512 + 512], wait=[m])
                    m_bank[bank] = [m_cp]
                    if qstop <= 2:
                        continue
                    m_sq = k.op("scalar", "activation", qsq[:],
                                qf[:, mt * 512:(mt + 1) * 512], AF.Square, wait=[m_cp])
                    wsum = [m_sq]
                    if mt == 0 and m_pss is not None:
                        wsum.append(m_pss)
                    k.op("tensor", "matmul", ps_s[0:4, 0:512], hselk[:, mt * 4:(mt + 1) * 4], qsq[:],
                         start=(mt == 0), stop=(mt == 5), wait=wsum,
                         inc=(st, 1) if mt == 5 else None)
                if qstop <= 3:
                    continue
                m_srow = (st, k.val(st))
                k.op("scalar", "activation", rowsc[0:4, :], ps_s[0:4, 0:512], AF.Ln,
                     bias=1e-6, wait=[m_srow])
                m_r = k.op("scalar", "activation", rstd_rows[0:4, gs], rowsc[0:4, :],
                           AF.Exp, scale=-0.5)
                m_pss = m_r
                if qstop <= 4:
                    continue
                for mt in range(6):
                    wbc = [m_r]
                    if m_bcmul is not None:
                        wbc.append(m_bcmul)
                    m_bc = k.op("tensor", "matmul", ps_s[0:128, 0:512],
                                hselkt[0:4, mt * 128:(mt + 1) * 128], rstd_rows[0:4, gs],
                                start=True, stop=True, wait=wbc, inc=(st, 1))
                    par = mt % 2
                    ob = outb[:, par * 512:par * 512 + 512]
                    wmul = [m_bc]
                    if m_outb[par] is not None:
                        wmul.append(m_outb[par])
                    m_bcmul = k.op("vector", "tensor_tensor", ob, qf[:, mt * 512:(mt + 1) * 512],
                                   ps_s[0:128, 0:512], ALU.mult, wait=wmul)
                    m_outb[par] = po.dma(o_dm[mt * 128:(mt + 1) * 128, gs], ob,
                                         wait=[m_bcmul])
                m_pss = m_bcmul

        if stop >= 2:
            qk_dm(wq16, m_wq, o_qdm, rstdq)
        if stop >= 3:
            qk_dm(wk16, m_wk, o_kdm, rstdk)

        m = None
        for c in range(16):
            cs = slice(c * 128, (c + 1) * 128)
            for kt in range(DKT):
                stat = xsb[:, kt * L + c * 128:kt * L + (c + 1) * 128]
                w = []
                if c == 0 and kt == 0:
                    w = [m_wgkb, M_CONST] + ([m_psm] if m_psm is not None else [])
                k.op("tensor", "matmul", ps_m[0:128, c * 8:c * 8 + 8], stat,
                     wgkb16[:, kt * 8:(kt + 1) * 8],
                     start=(kt == 0), stop=False, wait=w or None)
            m = k.op("tensor", "matmul", ps_m[0:128, c * 8:c * 8 + 8], onesrow[0:1, :],
                     dtbb[:], start=False, stop=True,
                     wait=M_LDL if c == 0 else None, inc=(st, 1) if c == 15 else None)
        m_gkev = k.op("vector", "tensor_copy", gkball[:, 0:128], ps_m[0:128, 0:128], wait=[m])
        m_psm = m_gkev
        gkv = gkball[:, 0:128].rearrange("p (c x) -> p c x", x=8)
        gkc = gkball[:, 128:192]
        btc_ = gkball[:, 192:256]
        k.op("vector", "tensor_copy", gkc.rearrange("p (c x) -> p c x", x=4),
             gkv[:, :, 0:4])
        m_cpb = k.op("vector", "tensor_copy", btc_.rearrange("p (c x) -> p c x", x=4),
                     gkv[:, :, 4:8])
        m_ab = k.op("scalar", "activation", gkball[:, 0:64], gkc, AF.Abs, wait=[m_cpb])
        m_ab = k.op("scalar", "activation", gkball[:, 0:64], gkball[:, 0:64], AF.Exp,
                    scale=-1.0)
        m_ab = k.op("scalar", "activation", gkball[:, 0:64], gkball[:, 0:64], AF.Ln,
                    bias=1.0)
        m_mx = k.op("vector", "tensor_scalar", gkc, gkc, 0.0, None, ALU.max, wait=[m_ab])
        m_sp = k.op("vector", "tensor_tensor", gkc, gkc, gkball[:, 0:64], ALU.add)
        m_bt = k.op("scalar", "activation", btc_, btc_, AF.Sigmoid, wait=[m_sp])
        for c in range(16):
            po.dma(o_beta[c * 128:(c + 1) * 128, :], btc_[:, c * 4:c * 4 + 4],
                   wait=[m_bt] if c == 0 else None)
        m_Gs = None
        for cb in range(4):
            m_G = None
            for cc in range(4):
                c = cb * 4 + cc
                w = []
                if cc == 0:
                    w = [m_sp] + ([m_Gs] if m_Gs is not None else
                                  ([m_pss] if m_pss is not None else []))
                m_G = k.op("tensor", "matmul", ps_s[0:4, cc * 128:(cc + 1) * 128],
                           gkc[:, c * 4:c * 4 + 4], cumul[:],
                           start=True, stop=True, wait=w or None,
                           inc=(st, 1) if cc == 3 else None)
            m_Gs = k.op("vector", "tensor_scalar", Grow[0:4, cb * 512:(cb + 1) * 512],
                        ps_s[0:4, 0:512], negA[:, 0:1], None, ALU.mult,
                        wait=[m_G, m_nA] if cb == 0 else [m_G])
        m_pss = m_Gs

        m_trB = None
        for c in range(16):
            wtr = []
            if c == 0:
                wtr = [m_psm]
            m_trB = k.tr(ps_m[0:128, c * 4:c * 4 + 4], rstdk[0:4, c * 128:(c + 1) * 128],
                         ident[0:4, 0:4], wait=wtr or None, inc=(st, 1))
        m_rT = k.op("vector", "tensor_copy", rstdTall[:], ps_m[0:128, 0:64], wait=[m_trB])
        m_psm = m_rT

        for c in range(16 if stop >= 5 else 0):
            cs = slice(c * 128, (c + 1) * 128)
            for wt16, o_t in [(wv16, o_vtm), (wg16, o_gtm)]:
                for j in range(3):
                    bank = j % 2
                    m = None
                    for kt in range(DKT):
                        stat = xsb[:, kt * L + c * 128:kt * L + (c + 1) * 128]
                        w = []
                        if kt == 0:
                            if c == 0 and j == 0 and wt16 is wv16:
                                w += [m_wv, m_wg]
                            if m_bank[bank] is not None:
                                w += (m_bank[bank] if isinstance(m_bank[bank], list) else [m_bank[bank]])
                        m = k.op("tensor", "matmul", ps_a[0:128, bank * 512:bank * 512 + 512],
                                 stat, wt16[:, kt * 1536 + j * 512:kt * 1536 + (j + 1) * 512],
                                 start=(kt == 0), stop=(kt == DKT - 1),
                                 wait=w or None, inc=(st, 1) if kt == DKT - 1 else None)
                    par = j % 2
                    ob = outb[:, par * 512:par * 512 + 512]
                    w = [m]
                    if m_outb[par] is not None:
                        w.append(m_outb[par])
                    m_ev = k.op("vector", "tensor_copy", ob,
                                ps_a[0:128, bank * 512:bank * 512 + 512], wait=w)
                    m_bank[bank] = m_ev
                    m_outb[par] = po.dma(o_t[cs, j * 512:(j + 1) * 512], ob, wait=[m_ev])
            for j in range(2):
                bank = j % 2
                m = None
                for kt in range(DKT):
                    stat = xsb[:, kt * L + c * 128:kt * L + (c + 1) * 128]
                    w = []
                    if kt == 0 and m_bank[bank] is not None:
                        w += (m_bank[bank] if isinstance(m_bank[bank], list) else [m_bank[bank]])
                    if kt == 0 and c == 0 and j == 0:
                        w.append(m_rT)
                    m = k.op("tensor", "matmul", ps_a[0:128, bank * 512:bank * 512 + 384],
                             stat, wk16[:, kt * 768 + j * 384:kt * 768 + (j + 1) * 384],
                             start=(kt == 0), stop=(kt == DKT - 1),
                             wait=w or None, inc=(st, 1) if kt == DKT - 1 else None)
                par = j % 2
                ob = outb[:, par * 512:par * 512 + 384]
                w = [m]
                if m_outb[par] is not None:
                    w.append(m_outb[par])
                bview = rstdTall[:, c * 4 + 2 * j:c * 4 + 2 * j + 2].unsqueeze(2).broadcast_to([128, 2, 192])
                m_ev = k.op("vector", "tensor_tensor",
                            ob.rearrange("p (h d) -> p h d", d=192),
                            ps_a[0:128, bank * 512:bank * 512 + 384].rearrange("p (h d) -> p h d", d=192),
                            bview, ALU.mult, wait=w)
                m_bank[bank] = m_ev
                m_outb[par] = po.dma(o_ktm[cs, j * 384:(j + 1) * 384], ob, wait=[m_ev])
        if stop >= 7:
            po.dma(o_G, Grow[0:4, :], wait=[m_Gs])
        for pp in (po, pa, pw):
            for s in pp.sems:
                k.wait("gpsimd", s, k.val(s))
        k.wait("gpsimd", ld, k.val(ld))
        k.finalize()
    return nc


_HSLC = {0: [(0, 0, 128), (1, 0, 64)], 1: [(1, 64, 64), (2, 0, 128)],
         2: [(3, 0, 128), (4, 0, 64)], 3: [(4, 64, 64), (5, 0, 128)]}


def build_delta():
    nc = bass.Bass("TRN2", target_bir_lowering=False)
    qdm = nc.dram_tensor("qdm", [768, L], BF16, kind="ExternalInput").ap()
    kdm = nc.dram_tensor("kdm", [768, L], BF16, kind="ExternalInput").ap()
    ktm = nc.dram_tensor("ktm", [L, 768], BF16, kind="ExternalInput").ap()
    vtm = nc.dram_tensor("vtm", [L, 1536], BF16, kind="ExternalInput").ap()
    gtm = nc.dram_tensor("gtm", [L, 1536], BF16, kind="ExternalInput").ap()
    Gin = nc.dram_tensor("G", [4, L], F32, kind="ExternalInput").ap()
    betain = nc.dram_tensor("beta", [L, 4], F32, kind="ExternalInput").ap()
    gnwbc = nc.dram_tensor("gnwbc", [128, 384], F32, kind="ExternalInput").ap()
    caps = declare_consts(nc)
    o_og = nc.dram_tensor("ogdm", [1536, L], BF16, kind="ExternalOutput").ap()

    es = ExitStack()
    with es:
        k = K(nc, es)
        ldc = k.sem("ldc")
        st = k.sem("stn")
        pl = DPool(k, "pl", "sync", 4)
        po = DPool(k, "po", "gpsimd", 4)
        register_fconsts(k, caps, ldc)
        ident = load_const(k, caps, "IDENT", ldc)
        ident4 = load_const(k, caps, "IDENT4", ldc)
        onesrow = load_const(k, caps, "ONESROW", ldc)
        mstrict = load_const(k, caps, "MASK_STRICT", ldc)
        mstrictT = load_const(k, caps, "MASK_STRICT_T", ldc)
        mincl = load_const(k, caps, "MASK_INCL", ldc)
        sel4 = load_const(k, caps, "SEL4", ldc)

        qc = sbuf(k, "qc", [128, 2 * 768], BF16)
        kc = sbuf(k, "kc", [128, 2 * 768], BF16)
        ktc = sbuf(k, "ktc", [128, 2 * 768], BF16)
        vc = sbuf(k, "vc", [128, 2 * 1536], BF16)
        gc = sbuf(k, "gc", [128, 2 * 1536], BF16)
        btc = sbuf(k, "btc", [128, 2 * 4], F32)
        Gr = sbuf(k, "Gr", [4, L], F32)
        GBr = sbuf(k, "GBr", [4, L], F32)
        brow = sbuf(k, "brow", [4, L], F32)
        gnw = sbuf(k, "gnw", [128, 384], F32)
        id16 = sbuf(k, "id16", [128, 128], BF16)
        Gtm = sbuf(k, "Gtm", [128, 2 * 4], F32)
        GBtm = sbuf(k, "GBtm", [128, 2 * 4], F32)
        gamtm = sbuf(k, "gamtm", [128, 2 * 4], F32)
        gTrow = sbuf(k, "gTrow", [1, 2 * 4], F32)
        gTbc = sbuf(k, "gTbc", [128, 2 * 4], F32)
        egti = sbuf(k, "egti", [128, 2 * 4], F32)
        bgam = sbuf(k, "bgam", [128, 2 * 4], F32)
        nbgam = sbuf(k, "nbgam", [128, 2 * 4], F32)
        Mts16 = sbuf(k, "Mts16", [128, 2 * 512], BF16)
        Ms16 = sbuf(k, "Ms16", [128, 2 * 512], BF16)
        At16 = sbuf(k, "At16", [128, 2 * 512], BF16)
        Xb = sbuf(k, "Xb", [128, 2 * 512], BF16)
        Yb = sbuf(k, "Yb", [128, 2 * 512], BF16)
        Rb = sbuf(k, "Rb", [128, 512], BF16)
        esc = sbuf(k, "esc", [128, 512], F32)
        gq16 = sbuf(k, "gq16", [128, 2 * 768], BF16)
        bv16 = sbuf(k, "bv16", [128, 2 * 4 * 384], BF16)
        nbgk16 = sbuf(k, "nbgk16", [128, 2 * 4 * 192], BF16)
        ktg16 = sbuf(k, "ktg16", [128, 2 * 4 * 192], BF16)
        sg32 = sbuf(k, "sg32", [128, 2 * 4 * 384], BF16)
        nw16 = sbuf(k, "nw16", [128, 4 * 256], BF16)
        u16 = sbuf(k, "u16", [128, 4 * 384], BF16)
        S0f = sbuf(k, "S0f", [128, 4 * 768], F32)
        S016 = sbuf(k, "S016", [128, 4 * 768], BF16)
        of32 = sbuf(k, "of32", [128, 4 * 384], F32)
        osq = sbuf(k, "osq", [128, 4 * 384], F32)
        orst = sbuf(k, "orst", [128, 8], F32)
        og16 = sbuf(k, "og16", [128, 4 * 384], BF16)
        ogst = sbuf(k, "ogst", [128, 4 * 384], BF16)

        psKK = psum(k, "psKK", (128, 512))
        psN1 = psum(k, "psN1", (128, 512))
        psN2 = psum(k, "psN2", (128, 512))
        psN3 = psum(k, "psN3", (128, 512))
        psU = psum(k, "psU", (128, 512))
        psO = psum(k, "psO", (128, 512))
        psM = psum(k, "psM", (128, 512))
        psT16 = psum(k, "psT16", (128, 512), BF16)

        k.op("sync", "dma_start", out=Gr[0:4, :], in_=Gin, inc=(ldc, 16))
        k.op("sync", "dma_start", out=gnw[:], in_=gnwbc, inc=(ldc, 16))
        M_LDC = (ldc, k.val(ldc))
        m_id16 = k.op("vector", "tensor_copy", id16[:], ident[:], wait=[M_LDC])
        m_tr = None
        for c in range(16):
            mb = pl.dma(btc[:, 0:4], betain[c * 128:(c + 1) * 128, :],
                        wait=[m_tr] if m_tr is not None else None)
            m_t = k.tr(psM[0:4, 0:128], btc[:, 0:4], ident[:], wait=[mb], inc=(st, 1))
            m_tr = k.op("vector", "tensor_copy", brow[0:4, c * 128:(c + 1) * 128],
                        psM[0:4, 0:128], wait=[m_t])
        m_ln = k.op("scalar", "activation", GBr[0:4, :], brow[0:4, :], AF.Ln, wait=[m_tr])
        m_gb = k.op("vector", "tensor_tensor", GBr[0:4, :], GBr[0:4, :], Gr[0:4, :], ALU.add,
                    wait=[m_ln])
        m_s0a = k.op("vector", "memset", S0f[:], 0.0)
        m_s0 = k.op("vector", "memset", S016[:], 0.0)

        def load_chunk(c, waitfree):
            p = c % 2
            cs = slice(c * 128, (c + 1) * 128)
            first = True
            for t in range(6):
                pl.dma(qc[:, p * 768 + t * 128:p * 768 + (t + 1) * 128],
                       qdm[t * 128:(t + 1) * 128, cs], wait=waitfree if first else None)
                first = False
                pl.dma(kc[:, p * 768 + t * 128:p * 768 + (t + 1) * 128],
                       kdm[t * 128:(t + 1) * 128, cs])
            pl.dma(ktc[:, p * 768:(p + 1) * 768], ktm[cs, :])
            pl.dma(vc[:, p * 1536:(p + 1) * 1536], vtm[cs, :])
            pl.dma(gc[:, p * 1536:(p + 1) * 1536], gtm[cs, :])
            pl.dma(btc[:, p * 4:(p + 1) * 4], betain[cs, :])
            return pl.all()

        def hsl(buf, p, h):
            return [(buf[r0:r0 + rows, p * 768 + t * 128:p * 768 + (t + 1) * 128], rows)
                    for (t, r0, rows) in _HSLC[h]]

        HR = {h: (_HSLC[h][0][2], _HSLC[h][1][2]) for h in range(4)}
        HB = {h: (_HSLC[h][0][1], _HSLC[h][1][1]) for h in range(4)}
        SA_f = lambda h: S0f[HB[h][0]:HB[h][0] + HR[h][0], h * 768:h * 768 + 384]
        SB_f = lambda h: S0f[HB[h][1]:HB[h][1] + HR[h][1], h * 768 + 384:h * 768 + 768]
        SA = lambda h: S016[HB[h][0]:HB[h][0] + HR[h][0], h * 768:h * 768 + 384]
        SB = lambda h: S016[HB[h][1]:HB[h][1] + HR[h][1], h * 768 + 384:h * 768 + 768]

        e3 = esc[:].rearrange("p (h t) -> p h t", t=128)
        ps3 = lambda ps: ps[0:128, 0:512].rearrange("p (h t) -> p h t", t=128)
        mrep = lambda t: t[:].unsqueeze(1).broadcast_to([128, 4, 128])

        m_KK_free = None
        m_N1_free = None
        m_N2_free = None
        m_N3_free = None
        m_psU_free = None
        m_psO_free = None
        m_psM_free = None
        m_psT_free = None
        m_free = None
        m_chain_mm = [None, None]
        m_chain_S = [None, None]
        m_seqdone = {h: None for h in range(4)}
        m_of_h = {h: None for h in range(4)}
        m_ogst = [None, None, None]
        m_sq_h = {h: None for h in range(4)}
        m_trog_h = {h: None for h in range(4)}
        m_S16 = {h: m_s0 for h in range(4)}
        m_u16_h = {h: None for h in range(4)}

        M_CH = [None, None]
        PM = [None, None]

        def prep(c, M_L):
            p = c % 2
            cs = slice(c * 128, (c + 1) * 128)
            P4 = slice(p * 4, (p + 1) * 4)
            wch = [m_chain_S[p]] if m_chain_S[p] is not None else []
            wmm = [m_chain_mm[p]] if m_chain_mm[p] is not None else []
            nonlocal m_N3_free, m_KK_free, m_N1_free, m_N2_free
            wps = [m_N3_free] if m_N3_free is not None else []
            m_t1 = k.tr(psN3[0:128, 0:4], Gr[0:4, cs], ident[0:4, 0:4],
                        wait=[m_gb] + wps + wch, inc=(st, 1))
            m_c1 = k.op("vector", "tensor_copy", Gtm[:, P4], psN3[0:128, 0:4], wait=[m_t1])
            m_t2 = k.tr(psN3[0:128, 4:8], GBr[0:4, cs], ident[0:4, 0:4], inc=(st, 1))
            m_c2 = k.op("vector", "tensor_copy", GBtm[:, P4], psN3[0:128, 4:8], wait=[m_t2])
            m_t3 = k.tr(psN3[0:1, 8:12], Gr[0:4, c * 128 + 127:c * 128 + 128],
                        ident[0:4, 0:4], inc=(st, 1))
            m_c3 = k.op("vector", "tensor_copy", gTrow[0:1, P4], psN3[0:1, 8:12], wait=[m_t3])
            m_t4 = k.op("tensor", "matmul", psN3[0:128, 12:16], onesrow[:], gTrow[0:1, P4],
                        start=True, stop=True, wait=[m_c3], inc=(st, 1))
            m_c4 = k.op("vector", "tensor_copy", gTbc[:, P4], psN3[0:128, 12:16], wait=[m_t4])
            m_gam = k.op("scalar", "activation", gamtm[:, P4], Gtm[:, P4], AF.Exp, wait=[m_c1])
            m_eg = k.op("vector", "tensor_tensor", egti[:, P4], gTbc[:, P4], Gtm[:, P4],
                        ALU.subtract, wait=[m_c4])
            m_eg = k.op("scalar", "activation", egti[:, P4], egti[:, P4], AF.Exp, wait=[m_eg])
            m_gT = k.op("scalar", "activation", gTbc[:, P4], gTbc[:, P4], AF.Exp)
            m_bg = k.op("vector", "tensor_tensor", bgam[:, P4], btc[:, P4], gamtm[:, P4],
                        ALU.mult, wait=M_L + [m_gam])
            m_nbg = k.op("vector", "tensor_scalar", nbgam[:, P4], bgam[:, P4], -1.0, None,
                         ALU.mult)
            m_kk = None
            m_kq = None
            for h in range(4):
                ks_ = hsl(kc, p, h)
                qs_ = hsl(qc, p, h)
                hb = slice(h * 128, (h + 1) * 128)
                for i, (ap, rows) in enumerate(ks_):
                    w = []
                    if h == 0 and i == 0:
                        w += M_L + wmm
                        if m_KK_free is not None:
                            w.append(m_KK_free)
                    m_kk = k.op("tensor", "matmul", psKK[0:128, hb], ap, ap,
                                start=(i == 0), stop=(i == 1), wait=w or None,
                                inc=(st, 1) if (h == 3 and i == 1) else None)
                for i, ((kap, kr), (qap, qr)) in enumerate(zip(ks_, qs_)):
                    w = [m_c4] if (h == 0 and i == 0) else []
                    m_kq = k.op("tensor", "matmul", psN3[0:128, hb], kap, qap,
                                start=(i == 0), stop=(i == 1), wait=w or None,
                                inc=(st, 1) if (h == 3 and i == 1) else None)
            bGB = GBtm[:, P4].unsqueeze(2).broadcast_to([128, 4, 128])
            bG = Gtm[:, P4].unsqueeze(2).broadcast_to([128, 4, 128])
            S5 = slice(p * 512, (p + 1) * 512)
            m_b2 = None
            for h in range(4):
                w = [m_N2_free] if (h == 0 and m_N2_free is not None) else None
                m_b2 = k.op("tensor", "matmul", psN2[0:128, h * 128:(h + 1) * 128],
                            sel4[0:4, h * 128:(h + 1) * 128], GBr[0:4, cs],
                            start=True, stop=True, wait=w,
                            inc=(st, 1) if h == 3 else None)
            m_e = k.op("vector", "scalar_tensor_tensor", e3, ps3(psN2), 1.0, mrep(mstrict),
                       ALU.mult, ALU.add, wait=[m_b2, m_c1, m_c2])
            m_e = k.op("vector", "tensor_tensor", e3, e3, bG, ALU.subtract)
            m_e = k.op("scalar", "activation", esc[:], esc[:], AF.Exp, wait=[m_e])
            wv = [m_e, m_kk] + wmm
            m_mts = k.op("vector", "tensor_tensor", Mts16[:, S5], esc[:], psKK[0:128, 0:512],
                         ALU.mult, wait=wv)
            m_y0 = k.op("gpsimd", "tensor_tensor",
                        Yb[:, S5].rearrange("p (h t) -> p h t", t=128),
                        mrep(ident), Mts16[:, S5].rearrange("p (h t) -> p h t", t=128),
                        ALU.subtract, wait=[M_LDC, m_mts] + wmm)
            m_N2_free = m_e
            m_b1 = None
            for h in range(4):
                w = [m_N1_free] if (h == 0 and m_N1_free is not None) else None
                m_b1 = k.op("tensor", "matmul", psN1[0:128, h * 128:(h + 1) * 128],
                            sel4[0:4, h * 128:(h + 1) * 128], Gr[0:4, cs],
                            start=True, stop=True, wait=w,
                            inc=(st, 1) if h == 3 else None)
            m_e = k.op("vector", "scalar_tensor_tensor", e3, ps3(psN1), -1.0, bGB,
                       ALU.mult, ALU.add, wait=[m_b1])
            m_e = k.op("vector", "tensor_tensor", e3, e3, mrep(mstrictT), ALU.add)
            m_e = k.op("scalar", "activation", esc[:], esc[:], AF.Exp, wait=[m_e])
            m_ms = k.op("vector", "tensor_tensor", Ms16[:, S5], esc[:], psKK[0:128, 0:512],
                        ALU.mult, wait=[m_e])
            m_KK_free = m_ms
            m_x0 = k.op("gpsimd", "tensor_tensor",
                        Xb[:, S5].rearrange("p (h t) -> p h t", t=128),
                        mrep(ident), Ms16[:, S5].rearrange("p (h t) -> p h t", t=128),
                        ALU.subtract, wait=[m_ms, m_y0])
            m_N1_free = m_e
            m_b3 = None
            for h in range(4):
                w = [m_N2_free] if h == 0 else None
                m_b3 = k.op("tensor", "matmul", psN2[0:128, h * 128:(h + 1) * 128],
                            sel4[0:4, h * 128:(h + 1) * 128], Gr[0:4, cs],
                            start=True, stop=True, wait=w,
                            inc=(st, 1) if h == 3 else None)
            m_e = k.op("vector", "scalar_tensor_tensor", e3, ps3(psN2), 1.0, mrep(mincl),
                       ALU.mult, ALU.add, wait=[m_b3])
            m_e = k.op("vector", "tensor_tensor", e3, e3, bG, ALU.subtract)
            m_e = k.op("scalar", "activation", esc[:], esc[:], AF.Exp, wait=[m_e])
            m_at = k.op("vector", "tensor_tensor", At16[:, S5], esc[:], psN3[0:128, 0:512],
                        ALU.mult, wait=[m_e, m_kq] + wmm)
            m_N2_free = m_e
            m_N3_free = m_at
            m_b4 = None
            for h in range(4):
                w = [m_N1_free] if h == 0 else None
                m_b4 = k.op("tensor", "matmul", psN1[0:128, h * 128:(h + 1) * 128],
                            sel4[0:4, h * 128:(h + 1) * 128], Gr[0:4, cs],
                            start=True, stop=True, wait=w,
                            inc=(st, 1) if h == 3 else None)
            m_ge = k.op("vector", "tensor_copy", esc[:], psN1[0:128, 0:512], wait=[m_b4, m_at])
            m_N1_free = m_ge
            m_ge = k.op("scalar", "activation", esc[:], esc[:], AF.Exp, wait=[m_ge])
            m_gq = None
            PB = p * 768
            for t, hh in [(0, 0), (2, 1), (3, 2), (5, 3)]:
                m_gq = k.op("gpsimd", "tensor_tensor", gq16[:, PB + t * 128:PB + (t + 1) * 128],
                            qc[:, PB + t * 128:PB + (t + 1) * 128],
                            esc[:, hh * 128:(hh + 1) * 128], ALU.mult,
                            wait=[m_ge] + wmm if t == 0 else None)
            m_gq = k.op("gpsimd", "tensor_tensor", gq16[0:64, PB + 128:PB + 256],
                        qc[0:64, PB + 128:PB + 256], esc[0:64, 0:128], ALU.mult)
            m_gq = k.op("gpsimd", "tensor_tensor", gq16[64:128, PB + 128:PB + 256],
                        qc[64:128, PB + 128:PB + 256], esc[64:128, 128:256], ALU.mult)
            m_gq = k.op("gpsimd", "tensor_tensor", gq16[0:64, PB + 512:PB + 640],
                        qc[0:64, PB + 512:PB + 640], esc[0:64, 256:384], ALU.mult)
            m_gq = k.op("gpsimd", "tensor_tensor", gq16[64:128, PB + 512:PB + 640],
                        qc[64:128, PB + 512:PB + 640], esc[64:128, 384:512], ALU.mult)
            S5r = lambda t: t[:, S5]
            m_xa, m_ya = m_x0, m_x0
            for itn in range(6):
                m_ax = None
                for h in range(4):
                    hb = slice(p * 512 + h * 128, p * 512 + (h + 1) * 128)
                    hbr = slice(h * 128, (h + 1) * 128)
                    w = []
                    if h == 0:
                        w = [m_x0, m_id16] if itn == 0 else [m_xa]
                        if itn == 0 and m_KK_free is not None:
                            w.append(m_KK_free)
                    k.op("tensor", "matmul", psKK[0:128, hbr], Mts16[:, hb], Xb[:, hb],
                         start=True, stop=False, wait=w or None)
                    m_ax = k.op("tensor", "matmul", psKK[0:128, hbr], id16[:], Xb[:, hb],
                                start=False, stop=True,
                                inc=(st, 1) if h == 3 else None)
                m_r = k.op("vector", "scalar_tensor_tensor", Rb[:], psKK[0:128, 0:512], -1.0,
                           ident4[:], ALU.mult, ALU.add, wait=[m_ax])
                m_xr = None
                for h in range(4):
                    hb = slice(p * 512 + h * 128, p * 512 + (h + 1) * 128)
                    hbr = slice(h * 128, (h + 1) * 128)
                    k.op("tensor", "matmul", psN1[0:128, hbr], Yb[:, hb], Rb[:, hbr],
                         start=True, stop=True,
                         wait=[m_r, m_ya, m_N1_free] if h == 0 else None)
                    m_xr = k.op("tensor", "matmul", psN2[0:128, hbr], Rb[:, hbr], Yb[:, hb],
                                start=True, stop=True,
                                wait=[m_N2_free] if h == 0 else None,
                                inc=(st, 1) if h == 3 else None)
                m_xa = k.op("vector", "tensor_tensor", Xb[:, S5], Xb[:, S5],
                            psN1[0:128, 0:512], ALU.add, wait=[m_xr])
                m_ya = k.op("vector", "tensor_tensor", Yb[:, S5], Yb[:, S5],
                            psN2[0:128, 0:512], ALU.add)
                m_N1_free = m_xa
                m_N2_free = m_ya
            m_KK_free = m_r
            m_bv = m_nk = m_kg = m_sg = None
            for h in range(4):
                vslc = vc[:, p * 1536 + h * 384:p * 1536 + (h + 1) * 384]
                gslc = gc[:, p * 1536 + h * 384:p * 1536 + (h + 1) * 384]
                ktslc = ktc[:, p * 768 + h * 192:p * 768 + (h + 1) * 192]
                w0 = M_L + wmm if h == 0 else []
                m_bv = k.op("scalar", "activation", bv16[:, p * 1536 + h * 384:p * 1536 + (h + 1) * 384],
                            vslc, AF.Copy, scale=btc[:, p * 4 + h:p * 4 + h + 1],
                            wait=w0 + [m_bg])
                m_nk = k.op("scalar", "activation", nbgk16[:, p * 768 + h * 192:p * 768 + (h + 1) * 192],
                            ktslc, AF.Copy, scale=nbgam[:, p * 4 + h:p * 4 + h + 1],
                            wait=[m_nbg] if h == 0 else None)
                m_kg = k.op("scalar", "activation", ktg16[:, p * 768 + h * 192:p * 768 + (h + 1) * 192],
                            ktslc, AF.Copy, scale=egti[:, p * 4 + h:p * 4 + h + 1],
                            wait=[m_eg] if h == 0 else None)
                m_sg = k.op("scalar", "activation", sg32[:, p * 1536 + h * 384:p * 1536 + (h + 1) * 384],
                            gslc, AF.Silu, wait=None)
            return dict(T=m_ya, at=m_at, gq=m_gq, bv=m_bv, nk=m_nk, kg=m_kg, sg=m_sg,
                        gT=m_gT, kk=m_kk, kq=m_kq)

        def chain(c, pm):
            p = c % 2
            cs = slice(c * 128, (c + 1) * 128)
            S5 = slice(p * 512, (p + 1) * 512)
            nonlocal m_psU_free, m_psO_free, m_psM_free, m_psT_free
            m_lastmm = None
            m_lastS = None
            for h in range(4):
                hb = slice(p * 512 + h * 128, p * 512 + (h + 1) * 128)
                R1, R2 = HR[h]
                B1, B2 = HB[h]
                BV = bv16[:, p * 1536 + h * 384:p * 1536 + (h + 1) * 384]
                NK = nbgk16[:, p * 768 + h * 192:p * 768 + (h + 1) * 192]
                KG = ktg16[:, p * 768 + h * 192:p * 768 + (h + 1) * 192]
                SG = sg32[:, p * 1536 + h * 384:p * 1536 + (h + 1) * 384]
                NW = nw16[:, h * 256:(h + 1) * 256]
                U = u16[:, h * 384:(h + 1) * 384]
                OF = of32[:, h * 384:(h + 1) * 384]
                OG = og16[:, h * 384:(h + 1) * 384]
                OGS = ogst[:, h * 384:(h + 1) * 384]
                wW = [pm["T"], pm["nk"]]
                if m_psM_free is not None:
                    wW.append(m_psM_free)
                if m_seqdone[h] is not None:
                    wW.append(m_seqdone[h])
                m_w1 = k.op("tensor", "matmul", psM[B1:B1 + R1, 0:128], NK[:, 0:R1],
                            Yb[:, hb], start=True, stop=True, wait=wW, inc=(st, 1))
                m_w2 = k.op("tensor", "matmul", psM[B2:B2 + R2, 128:256], NK[:, R1:192],
                            Yb[:, hb], start=True, stop=True, inc=(st, 1))
                m_nw = k.op("vector", "tensor_copy", NW[B1:B1 + R1, 0:128],
                            psM[B1:B1 + R1, 0:128], wait=[m_w1, m_w2])
                m_nw = k.op("vector", "tensor_copy", NW[B2:B2 + R2, 128:256],
                            psM[B2:B2 + R2, 128:256])
                wu = [pm["bv"], m_nw]
                if m_psU_free is not None:
                    wu.append(m_psU_free)
                k.op("tensor", "matmul", psU[0:128, 0:384], Yb[:, hb], BV,
                     start=True, stop=False, wait=wu)
                k.op("tensor", "matmul", psU[0:128, 0:384], NW[B1:B1 + R1, 0:128], SA(h),
                     start=False, stop=False, wait=[m_S16[h]])
                m_u = k.op("tensor", "matmul", psU[0:128, 0:384], NW[B2:B2 + R2, 128:256],
                           SB(h), start=False, stop=True, inc=(st, 1))
                m_u16 = k.op("scalar", "copy", U, psU[0:128, 0:384], wait=[m_u])
                m_u16_h[h] = m_u16
                (q1, r1), (q2, r2) = hsl(gq16, p, h)
                wo = [pm["gq"]]
                if m_psO_free is not None:
                    wo.append(m_psO_free)
                k.op("tensor", "matmul", psO[0:128, 0:384], q1, SA(h),
                     start=True, stop=False, wait=wo)
                k.op("tensor", "matmul", psO[0:128, 0:384], q2, SB(h),
                     start=False, stop=False)
                m_o = k.op("tensor", "matmul", psO[0:128, 0:384], At16[:, hb], U,
                           start=False, stop=True, wait=[m_u16], inc=(st, 1))
                k.op("tensor", "matmul", psU[B1:B1 + R1, 0:384], KG[:, 0:R1], U,
                     start=True, stop=True, wait=[pm["kg"], m_u16])
                m_s2 = k.op("tensor", "matmul", psM[B2:B2 + R2, 0:384], KG[:, R1:192], U,
                            start=True, stop=True, inc=(st, 1))
                m_sh = k.op("vector", "scalar_tensor_tensor", SA_f(h), SA_f(h),
                            gTbc[B1:B1 + R1, p * 4 + h:p * 4 + h + 1], psU[B1:B1 + R1, 0:384],
                            ALU.mult, ALU.add, wait=[m_s2, pm["gT"], m_s0a])
                m_sl = k.op("vector", "scalar_tensor_tensor", SB_f(h), SB_f(h),
                            gTbc[B2:B2 + R2, p * 4 + h:p * 4 + h + 1], psM[B2:B2 + R2, 0:384],
                            ALU.mult, ALU.add)
                m_psU_free = m_sh
                m_psM_free = m_sl
                m_sc = k.op("gpsimd", "tensor_copy", SA(h), SA_f(h), wait=[m_sh])
                m_sc = k.op("gpsimd", "tensor_copy", SB(h), SB_f(h), wait=[m_sl])
                m_S16[h] = m_sc
                wof = [m_o]
                if m_sq_h[0] is not None:
                    wof.append(m_sq_h[0])
                m_of = k.op("scalar", "activation", OF, psO[0:128, 0:384], AF.Copy,
                            wait=wof)
                m_psO_free = m_of
                m_of_h[h] = m_of
                m_seqdone[h] = m_o
                m_lastmm = m_s2
                m_lastS = m_sl
            p15 = slice(p * 1536, (p + 1) * 1536)
            m_sqB = k.op("scalar", "activation", osq[:], of32[:], AF.Square,
                         wait=[m_of_h[3]])
            m_rd = k.op("vector", "tensor_reduce", orst[:, 0:4],
                        osq[:].rearrange("p (h d) -> p h d", d=384), AX.X, ALU.add,
                        wait=[m_sqB])
            m_r1 = k.op("scalar", "activation", orst[:, 4:8], orst[:, 0:4],
                        AF.Ln, scale=1.0 / 384.0, bias=EPS_LN, wait=[m_rd])
            m_r2 = k.op("scalar", "activation", orst[:, 4:8], orst[:, 4:8],
                        AF.Exp, scale=-0.5)
            m_t1_ = None
            for h in range(4):
                m_t1_ = k.op("vector", "tensor_scalar", of32[:, h * 384:(h + 1) * 384],
                             of32[:, h * 384:(h + 1) * 384], orst[:, 4 + h:5 + h], None,
                             ALU.mult, wait=[m_r2] if h == 0 else None)
            m_t2_ = k.op("gpsimd", "tensor_tensor", of32[:], of32[:],
                         sg32[:, p15], ALU.mult, wait=[m_t1_, pm["sg"]])
            wt3 = [M_LDC, m_t2_]
            if m_trog_h[0] is not None:
                wt3.append(m_trog_h[0])
            m_t3_ = k.op("gpsimd", "tensor_tensor",
                         og16[:].rearrange("p (h d) -> p h d", d=384),
                         of32[:].rearrange("p (h d) -> p h d", d=384),
                         gnw[:].unsqueeze(1).broadcast_to([128, 4, 384]), ALU.mult,
                         wait=wt3)
            m_sq_h[0] = m_t3_
            for blk in range(3):
                m_trog = None
                for j in range(4):
                    col = blk * 512 + j * 128
                    w = [m_t3_]
                    if j == 0 and m_psT_free is not None:
                        w.append(m_psT_free)
                    m_trog = k.tr(psT16[0:128, j * 128:(j + 1) * 128],
                                  og16[:, col:col + 128], id16[:],
                                  wait=w if j == 0 else None,
                                  inc=(st, 1) if j == 3 else None)
                if blk == 2:
                    m_trog_h[0] = m_trog
                wog = [m_trog]
                if m_ogst[blk] is not None:
                    wog.append(m_ogst[blk])
                m_ogev = k.op("vector", "tensor_copy", ogst[:, blk * 512:(blk + 1) * 512],
                              psT16[0:128, 0:512], wait=wog)
                m_psT_free = m_ogev
                for j in range(4):
                    col = blk * 512 + j * 128
                    m_ogst[blk] = po.dma(o_og[col:col + 128, cs],
                                         ogst[:, col:col + 128],
                                         wait=[m_ogev] if j == 0 else None)
            return m_lastmm, m_lastS

        M_CH[0] = load_chunk(0, [m_gb])
        PM[0] = prep(0, M_CH[0])
        for c in range(16):
            p = c % 2
            if c + 1 < 16:
                M_CH[1 - p] = load_chunk(c + 1, m_free)
                PM[1 - p] = prep(c + 1, M_CH[1 - p])
            mm, mS = chain(c, PM[p])
            m_chain_mm[p] = mm
            m_chain_S[p] = mS
            m_free = [PM[p]["kk"], PM[p]["kq"], PM[p]["gq"], PM[p]["kg"], PM[p]["sg"],
                      PM[p]["bv"]]
        for s in po.sems:
            k.wait("gpsimd", s, k.val(s))
        k.finalize()
    return nc


def build_ffnh():
    nc = bass.Bass("TRN2", target_bir_lowering=False)
    xn = nc.dram_tensor("xn", [D, L], BF16, kind="ExternalInput").ap()
    og = nc.dram_tensor("og", [1536, L], BF16, kind="ExternalInput").ap()
    gatesT = nc.dram_tensor("gatesT", [8, L], F32, kind="ExternalInput").ap()
    w1 = nc.dram_tensor("w1", [8 * D, 512], F32, kind="ExternalInput").ap()
    b1T = nc.dram_tensor("b1T", [8 * 128, 4], F32, kind="ExternalInput").ap()
    W_so = nc.dram_tensor("W_so", [1536, D], F32, kind="ExternalInput").ap()
    caps = declare_consts(nc)
    o_ht = nc.dram_tensor("ht", [4096, L], BF16, kind="ExternalOutput").ap()
    o_yst = nc.dram_tensor("yst", [D, L], BF16, kind="ExternalOutput").ap()

    es = ExitStack()
    with es:
        k = K(nc, es)
        ld = k.sem("ld")
        st = k.sem("stn")
        pa = DPool(k, "pa", "sync", 4)
        pw = DPool(k, "pw", "sync", 4)
        po = DPool(k, "po", "gpsimd", 4)
        register_fconsts(k, caps, ld)
        sel8 = load_const(k, caps, "SEL8", ld)

        xnb = sbuf(k, "xnb", [128, DKT * L], BF16)
        ogb = sbuf(k, "ogb", [128, 12 * L], BF16)
        gts = sbuf(k, "gts", [8, L], F32)
        wso16 = sbuf(k, "wso16", [128, 12 * D], BF16)
        wstg = sbuf(k, "wstg", [128, 4 * D], F32)
        w116 = sbuf(k, "w116", [128, 2 * DKT * 512], BF16)
        b1s = sbuf(k, "b1s", [128, 8 * 4], F32)
        bcg = sbuf(k, "bcg", [128, 2 * 512], F32)
        hstage = sbuf(k, "hstage", [128, 2 * 512], F32)
        outb = sbuf(k, "outb", [128, 2 * 512], BF16)
        ps = [psum(k, "psA", (128, 512)), psum(k, "psB", (128, 512))]
        psg = psum(k, "psg", (128, 512))

        casts = []
        for kt in range(12):
            sl = len(casts) % 4
            wl = [casts[-4]] if len(casts) >= 4 else None
            mdl = pw.dma(wstg[:, sl * D:(sl + 1) * D], W_so[kt * 128:(kt + 1) * 128, :],
                         wait=wl)
            casts.append(k.op("scalar", "activation", wso16[:, kt * D:(kt + 1) * D],
                              wstg[:, sl * D:(sl + 1) * D], AF.Copy, wait=[mdl]))
        m_wso = casts[-1]

        for kt in range(12):
            pa.dma(ogb[:, kt * L:(kt + 1) * L], og[kt * 128:(kt + 1) * 128, :])
        M_OG = pa.all()
        for kt in range(DKT):
            pa.dma(xnb[:, kt * L:(kt + 1) * L], xn[kt * 128:(kt + 1) * 128, :])
        pa.dma(gts[0:8, :], gatesT)
        for e in range(8):
            pa.dma(b1s[:, e * 4:(e + 1) * 4], b1T[e * 128:(e + 1) * 128, :])
        M_LD = pa.all()

        m_bank = [None, None]
        m_outb = [None, None]
        for g in range(NTG):
            gs = slice(g * 512, (g + 1) * 512)
            for mt in range(DKT):
                bank = mt % 2
                m = None
                for kt in range(12):
                    w = []
                    if kt == 0:
                        if g == 0 and mt == 0:
                            w += [m_wso] + M_OG
                        if m_bank[bank] is not None:
                            w.append(m_bank[bank])
                    m = k.op("tensor", "matmul", ps[bank][0:128, 0:512],
                             wso16[:, kt * D + mt * 128:kt * D + (mt + 1) * 128],
                             ogb[:, kt * L + g * 512:kt * L + g * 512 + 512],
                             start=(kt == 0), stop=(kt == 11),
                             wait=w or None, inc=(st, 1) if kt == 11 else None)
                par = mt % 2
                ob = outb[:, par * 512:par * 512 + 512]
                w = [m]
                if m_outb[par] is not None:
                    w.append(m_outb[par])
                m_ev = k.op("vector", "tensor_copy", ob, ps[bank][0:128, 0:512], wait=w)
                m_bank[bank] = m_ev
                m_outb[par] = po.dma(o_yst[mt * 128:(mt + 1) * 128, gs], ob, wait=[m_ev])

        w1casts = []
        m_bcg = [None, None]
        m_hfree = [None, None]
        mm_used = {}

        def w1slot(i):
            return wstg[:, (i % 8) * 512:(i % 8) * 512 + 512]

        for e in range(8):
            pwp = e % 2
            for kt in range(DKT):
                i = len(w1casts)
                wld = []
                if i >= 8:
                    wld.append(w1casts[-8])
                if i == 0:
                    wld.append(m_wso)
                if e >= 2 and kt == 0:
                    wld.append(mm_used[e - 2])
                mdl = pw.dma(w1slot(i), w1[e * D + kt * 128:e * D + (kt + 1) * 128, :],
                             wait=wld or None)
                w1casts.append(k.op("scalar", "activation",
                                    w116[:, pwp * DKT * 512 + kt * 512:pwp * DKT * 512 + (kt + 1) * 512],
                                    w1slot(i), AF.Copy, wait=[mdl]))
            m_w1c = w1casts[-1]
            for g in range(NTG):
                gs = slice(g * 512, (g + 1) * 512)
                gp = g % 2
                m_gb = k.op("tensor", "matmul", psg[0:128, 0:512],
                            sel8[0:8, e * 128:(e + 1) * 128], gts[0:8, gs],
                            start=True, stop=True,
                            wait=([m_bcg[gp]] if m_bcg[gp] is not None else M_LD),
                            inc=(st, 1))
                m_bc = k.op("vector", "tensor_copy", bcg[:, gp * 512:gp * 512 + 512],
                            psg[0:128, 0:512], wait=[m_gb])
                m_bcg[gp] = m_bc
                for mt in range(4):
                    bank = mt % 2
                    m = None
                    for kt in range(DKT):
                        w = []
                        if kt == 0:
                            w.append(m_w1c)
                            if m_bank[bank] is not None:
                                w.append(m_bank[bank])
                        m = k.op("tensor", "matmul", ps[bank][0:128, 0:512],
                                 w116[:, pwp * DKT * 512 + kt * 512 + mt * 128:
                                       pwp * DKT * 512 + kt * 512 + (mt + 1) * 128],
                                 xnb[:, kt * L + g * 512:kt * L + g * 512 + 512],
                                 start=(kt == 0), stop=(kt == DKT - 1),
                                 wait=w or None, inc=(st, 1) if kt == DKT - 1 else None)
                    mm_used[e] = m
                    hp = mt % 2
                    hst = hstage[:, hp * 512:hp * 512 + 512]
                    m_h = k.op("scalar", "activation", hst, ps[bank][0:128, 0:512],
                               AF.Gelu, bias=b1s[:, e * 4 + mt:e * 4 + mt + 1],
                               wait=[m] + ([m_hfree[hp]] if m_hfree[hp] is not None else []))
                    m_bank[bank] = m_h
                    par = mt % 2
                    ob = outb[:, par * 512:par * 512 + 512]
                    w = [m_h, m_bc]
                    if m_outb[par] is not None:
                        w.append(m_outb[par])
                    m_ht = k.op("vector", "tensor_tensor", ob, hst,
                                bcg[:, gp * 512:gp * 512 + 512], ALU.mult, wait=w)
                    m_hfree[hp] = m_ht
                    row = e * 512 + mt * 128
                    m_outb[par] = po.dma(o_ht[row:row + 128, gs], ob, wait=[m_ht])
        for s in po.sems:
            k.wait("gpsimd", s, k.val(s))
        k.finalize()
    return nc


def build_resid():
    nc = bass.Bass("TRN2", target_bir_lowering=False)
    yst = nc.dram_tensor("yst", [D, L], BF16, kind="ExternalInput").ap()
    zs = nc.dram_tensor("zs", [D, L], BF16, kind="ExternalInput").ap()
    ht = nc.dram_tensor("ht", [4096, L], BF16, kind="ExternalInput").ap()
    xT = nc.dram_tensor("xT", [D, L], F32, kind="ExternalInput").ap()
    W_out = nc.dram_tensor("W_out", [2 * D, D], F32, kind="ExternalInput").ap()
    w2 = nc.dram_tensor("w2", [8 * 512, D], F32, kind="ExternalInput").ap()
    b2 = nc.dram_tensor("b2", [8, D], F32, kind="ExternalInput").ap()
    gatesT = nc.dram_tensor("gatesT", [8, L], F32, kind="ExternalInput").ap()
    caps = declare_consts(nc)
    o_x1 = nc.dram_tensor("x1T", [D, L], F32, kind="ExternalOutput").ap()
    o_xn1 = nc.dram_tensor("xn1", [D, L], BF16, kind="ExternalOutput").ap()

    es = ExitStack()
    with es:
        k = K(nc, es)
        ld = k.sem("ld")
        st = k.sem("stn")
        so = k.sem("so")
        pw = DPool(k, "pw", "sync", 4)
        pg = DPool(k, "pg", "sync", 4)
        po = DPool(k, "po", "gpsimd", 4)
        register_fconsts(k, caps, ld)
        onescol = load_const(k, caps, "ONESCOL", ld)
        onesrow = load_const(k, caps, "ONESROW", ld)

        wout16 = sbuf(k, "wout16", [128, 16 * D], BF16)
        w216 = sbuf(k, "w216", [128, 32 * D], BF16)
        wstg = sbuf(k, "wstg", [128, 2 * D], F32)
        b2s = sbuf(k, "b2s", [8, D], F32)
        gts = sbuf(k, "gts", [8, L], F32)
        ystg = sbuf(k, "ystg", [128, DKT * 512], BF16)
        zsg = sbuf(k, "zsg", [128, DKT * 512], BF16)
        htg = sbuf(k, "htg", [128, 32 * 512], BF16)
        xg = sbuf(k, "xg", [128, DKT * 512], F32)
        x1s = xg
        xsq = sbuf(k, "xsq", [128, 2 * 512], F32)
        rowsg = sbuf(k, "rowsg", [1, 4 * 512], F32)
        bcA = sbuf(k, "bcA", [128, 512], F32)
        bcB = sbuf(k, "bcB", [128, 512], F32)
        xnt = sbuf(k, "xnt", [128, 2 * 512], BF16)
        ps = [psum(k, "psA", (128, 512)), psum(k, "psB", (128, 512))]
        ps_s = psum(k, "ps_s", (128, 1024))
        ps_bc = psum(k, "ps_bc", (128, 512))

        k.op("sync", "dma_start", out=gts[0:8, :], in_=gatesT, inc=(ld, 16))
        k.op("sync", "dma_start", out=b2s[0:8, :], in_=b2, inc=(ld, 16))
        M_LD = (ld, k.val(ld))
        m_c = None
        casts = []
        for j, (wsrc, n, wdst) in enumerate([(W_out, 16, wout16), (w2, 32, w216)]):
            for kt in range(n):
                parw = len(casts) % 2
                wl = [casts[-2]] if len(casts) >= 2 else None
                mdl = pw.dma(wstg[:, parw * D:(parw + 1) * D],
                             wsrc[kt * 128:(kt + 1) * 128, :], wait=wl)
                m_c = k.op("scalar", "activation", wdst[:, kt * D:(kt + 1) * D],
                           wstg[:, parw * D:(parw + 1) * D], AF.Copy, wait=[mdl])
                casts.append(m_c)
        m_wc = m_c

        m_bank = [None, None]
        m_xn = {}
        m_x1done = {}
        m_x1store = {}
        m_mmlast = {}
        m_xnst = [None, None]
        m_ev = None
        nsq = 0
        for g in range(NTG):
            gs = slice(g * 512, (g + 1) * 512)
            if g == 0:
                for kt in range(DKT):
                    pg.dma(ystg[:, kt * 512:(kt + 1) * 512], yst[kt * 128:(kt + 1) * 128, gs])
                    pg.dma(zsg[:, kt * 512:(kt + 1) * 512], zs[kt * 128:(kt + 1) * 128, gs])
                for kt in range(32):
                    pg.dma(htg[:, kt * 512:(kt + 1) * 512], ht[kt * 128:(kt + 1) * 128, gs])
            for kt in range(DKT):
                wx = None
                if g > 0:
                    wx = [m_xn[(g - 1, kt)], m_x1store[(g - 1, kt)]]
                pg.dma(xg[:, kt * 512:(kt + 1) * 512], xT[kt * 128:(kt + 1) * 128, gs],
                       wait=wx)
            M_G = pg.all()
            for mt in range(DKT):
                bank = mt % 2
                wmm = list(M_G) if mt < 2 else []
                if g == 0 and mt == 0:
                    wmm += [m_wc, M_LD]
                if m_bank[bank] is not None:
                    wmm.append(m_bank[bank])
                first = True
                for kt in range(DKT):
                    k.op("tensor", "matmul", ps[bank][0:128, 0:512],
                         wout16[:, kt * D + mt * 128:kt * D + (mt + 1) * 128],
                         ystg[:, kt * 512:(kt + 1) * 512],
                         start=first, stop=False, wait=wmm if first else None)
                    first = False
                for kt in range(DKT):
                    k.op("tensor", "matmul", ps[bank][0:128, 0:512],
                         wout16[:, (8 + kt) * D + mt * 128:(8 + kt) * D + (mt + 1) * 128],
                         zsg[:, kt * 512:(kt + 1) * 512], start=False, stop=False)
                for kt in range(32):
                    k.op("tensor", "matmul", ps[bank][0:128, 0:512],
                         w216[:, kt * D + mt * 128:kt * D + (mt + 1) * 128],
                         htg[:, kt * 512:(kt + 1) * 512], start=False, stop=False)
                m = k.op("tensor", "matmul", ps[bank][0:128, 0:512],
                         b2s[0:8, mt * 128:(mt + 1) * 128], gts[0:8, gs],
                         start=False, stop=True, inc=(st, 1))
                m_mmlast[(g, mt)] = m
                m_x1t = k.op("vector", "tensor_tensor", x1s[:, mt * 512:(mt + 1) * 512],
                             ps[bank][0:128, 0:512], xg[:, mt * 512:(mt + 1) * 512], ALU.add,
                             wait=[m])
                m_bank[bank] = m_x1t
                m_x1done[(g, mt)] = m_x1t
                m_x1store[(g, mt)] = po.dma(o_x1[mt * 128:(mt + 1) * 128, gs],
                                            x1s[:, mt * 512:(mt + 1) * 512], wait=[m_x1t])
                buf = xsq[:, (nsq % 2) * 512:(nsq % 2) * 512 + 512]
                wsq = [m_x1t]
                if nsq >= 2:
                    wsq.append((st, k.val(st)))
                m_sq = k.op("scalar", "activation", buf, x1s[:, mt * 512:(mt + 1) * 512],
                            AF.Square, wait=wsq)
                wst1 = [m_x1t]
                if mt == 0 and m_ev is not None:
                    wst1.append(m_ev)
                k.op("tensor", "matmul", ps_s[0:1, 0:512], onescol[:],
                     x1s[:, mt * 512:(mt + 1) * 512],
                     start=(mt == 0), stop=(mt == DKT - 1), wait=wst1)
                k.op("tensor", "matmul", ps_s[0:1, 512:1024], onescol[:], buf,
                     start=(mt == 0), stop=(mt == DKT - 1), wait=[m_sq],
                     inc=(st, 1) if mt == DKT - 1 else None)
                nsq += 1
            m_stats = (st, k.val(st))
            if g + 1 < NTG:
                gs2 = slice((g + 1) * 512, (g + 2) * 512)
                mlast = m_mmlast[(g, DKT - 1)]
                for kt in range(DKT):
                    pg.dma(ystg[:, kt * 512:(kt + 1) * 512],
                           yst[kt * 128:(kt + 1) * 128, gs2], wait=[mlast] if kt == 0 else None)
                    pg.dma(zsg[:, kt * 512:(kt + 1) * 512], zs[kt * 128:(kt + 1) * 128, gs2])
                for kt in range(32):
                    pg.dma(htg[:, kt * 512:(kt + 1) * 512], ht[kt * 128:(kt + 1) * 128, gs2])
            Rg = lambda i: rowsg[0:1, i * 512:(i + 1) * 512]
            k.op("vector", "tensor_copy", Rg(0), ps_s[0:1, 0:512], wait=[m_stats])
            m_ev = k.op("vector", "tensor_copy", Rg(1), ps_s[0:1, 512:1024])
            k.op("vector", "tensor_scalar", Rg(0), Rg(0), 1.0 / D, None, ALU.mult)
            k.op("vector", "tensor_scalar", Rg(1), Rg(1), 1.0 / D, None, ALU.mult)
            k.op("vector", "tensor_tensor", Rg(2), Rg(0), Rg(0), ALU.mult)
            m_var = k.op("vector", "tensor_tensor", Rg(1), Rg(1), Rg(2), ALU.subtract)
            k.op("scalar", "activation", Rg(2), Rg(1), AF.Ln, bias=EPS_LN, wait=[m_var])
            m_rstd = k.op("scalar", "activation", Rg(3), Rg(2), AF.Exp, scale=-0.5)
            k.op("vector", "tensor_tensor", Rg(2), Rg(0), Rg(3), ALU.mult, wait=[m_rstd])
            m_nm = k.op("vector", "tensor_scalar", Rg(2), Rg(2), -1.0, None, ALU.mult)
            m_bc1 = k.op("tensor", "matmul", ps_bc[0:128, 0:512], onesrow[:], Rg(3),
                         start=True, stop=True, wait=[m_nm], inc=(st, 1))
            m_cA = k.op("vector", "tensor_copy", bcA[:], ps_bc[0:128, 0:512], wait=[m_bc1])
            m_bc2 = k.op("tensor", "matmul", ps_bc[0:128, 0:512], onesrow[:], Rg(2),
                         start=True, stop=True, wait=[m_cA], inc=(st, 1))
            m_cB = k.op("vector", "tensor_copy", bcB[:], ps_bc[0:128, 0:512], wait=[m_bc2])
            for mt in range(DKT):
                par = mt % 2
                ob = xnt[:, par * 512:par * 512 + 512]
                w = [m_cB]
                if m_xnst[par] is not None:
                    w.append(m_xnst[par])
                k.op("vector", "tensor_tensor", xsq[:, 0:512], x1s[:, mt * 512:(mt + 1) * 512],
                     bcA[:], ALU.mult, wait=w)
                m_x = k.op("vector", "tensor_tensor", ob, xsq[:, 0:512], bcB[:], ALU.add)
                m_xn[(g, mt)] = m_x
                po.dma(o_xn1[mt * 128:(mt + 1) * 128, gs], ob, wait=[m_x])
                m_xnst[par] = (po.sems[(po.i - 1) % 4], k.val(po.sems[(po.i - 1) % 4]))
        for s in po.sems:
            k.wait("gpsimd", s, k.val(s))
        k.finalize()
    return nc


def build_l1attn():
    nc = bass.Bass("TRN2", target_bir_lowering=False)
    xn1 = nc.dram_tensor("xn1", [D, L], BF16, kind="ExternalInput").ap()
    x1T = nc.dram_tensor("x1T", [D, L], F32, kind="ExternalInput").ap()
    WkT = nc.dram_tensor("WkT", [D, D], F32, kind="ExternalInput").ap()
    WvT = nc.dram_tensor("WvT", [D, D], F32, kind="ExternalInput").ap()
    WrT = nc.dram_tensor("WrT", [D, D], F32, kind="ExternalInput").ap()
    WoT = nc.dram_tensor("WoT", [D, D], F32, kind="ExternalInput").ap()
    wcol = nc.dram_tensor("wcol", [D, 1], F32, kind="ExternalInput").ap()
    ucol = nc.dram_tensor("ucol", [D, 1], F32, kind="ExternalInput").ap()
    mixk = nc.dram_tensor("mixk", [D, 1], F32, kind="ExternalInput").ap()
    mixv = nc.dram_tensor("mixv", [D, 1], F32, kind="ExternalInput").ap()
    mixr = nc.dram_tensor("mixr", [D, 1], F32, kind="ExternalInput").ap()
    phi2 = nc.dram_tensor("phi2", [D, L], F32, kind="ExternalInput").ap()
    caps = declare_consts(nc)
    o_pres = nc.dram_tensor("pres", [D, L], F32, kind="ExternalOutput").ap()
    o_E = nc.dram_tensor("E16", [L, L], BF16, kind="ExternalOutput").ap()
    o_er = nc.dram_tensor("erinv", [L, 1], F32, kind="ExternalOutput").ap()

    es = ExitStack()
    with es:
        k = K(nc, es)
        ld = k.sem("ld")
        st = k.sem("stn")
        pa = DPool(k, "pa", "sync", 4)
        pw = DPool(k, "pw", "sync", 4)
        po = DPool(k, "po", "gpsimd", 4)
        register_fconsts(k, caps, ld)
        id4 = load_const(k, caps, "IDENT4", ld)
        M_ID = (ld, k.val(ld))

        xn1p = sbuf(k, "xn1p", [128, DKT * 2049], BF16)
        w16a = sbuf(k, "w16a", [128, DKT * D], BF16)
        w16b = sbuf(k, "w16b", [128, DKT * D], BF16)
        w16c = sbuf(k, "w16c", [128, DKT * D], BF16)
        wstg = sbuf(k, "wstg", [128, 2 * D], F32)
        rwkvb = sbuf(k, "rwkvb", [128, DKT * L], BF16)
        cols = sbuf(k, "cols", [128, 8 * 8], F32)
        es2 = ExitStack()
        s2 = lambda name, shape, dt=F32: es2.enter_context(
            nc.sbuf_tensor("sb_" + name, list(shape), dt))
        xkg = s2("xkg", [128, DKT * 512], BF16)
        xvg = s2("xvg", [128, DKT * 512], BF16)
        xrg = s2("xrg", [128, DKT * 512], BF16)
        mixd = s2("mixd", [128, 512], BF16)
        albc8 = s2("albc8", [128, DKT * 513], F32)
        ekg = [s2("ekg0", [128, 513], F32), s2("ekg1", [128, 513], F32)]
        ekvg = [s2("ekvg0", [128, 513], F32), s2("ekvg1", [128, 513], F32)]
        vg = [s2("vg0", [128, 512], F32), s2("vg1", [128, 512], F32)]
        srg = [s2("srg0", [128, 512], BF16), s2("srg1", [128, 512], BF16)]
        dent = [s2("dent0", [128, 512], F32), s2("dent1", [128, 512], F32)]
        eukg = [s2("eukg0", [128, 512], F32), s2("eukg1", [128, 512], F32)]
        euv = [s2("euv0", [128, 512], F32), s2("euv1", [128, 512], F32)]
        Pt = [s2("Pt0", [128, 513], F32), s2("Pt1", [128, 513], F32)]
        Qt = [s2("Qt0", [128, 513], F32), s2("Qt1", [128, 513], F32)]
        PC = s2("PC", [128, 8], F32)
        QC = s2("QC", [128, 8], F32)
        psK = [psum(k, "psK0", (128, 512)), psum(k, "psK1", (128, 512))]
        psV = [psum(k, "psV0", (128, 512)), psum(k, "psV1", (128, 512))]
        psR = [psum(k, "psR0", (128, 512)), psum(k, "psR1", (128, 512))]

        for kt in range(DKT):
            for j, src_ap in enumerate([wcol, ucol, mixk, mixv, mixr]):
                pa.dma(cols[:, kt * 8 + j:kt * 8 + j + 1],
                       src_ap[kt * 128:(kt + 1) * 128, :])
        M_COLS = pa.all()
        m_z = None
        for kt in range(DKT):
            m_z = k.op("vector", "memset", xn1p[:, kt * 2049:kt * 2049 + 1], 0.0)
        for kt in range(DKT):
            pa.dma(xn1p[:, kt * 2049 + 1:(kt + 1) * 2049], xn1[kt * 128:(kt + 1) * 128, :])
        M_XN = pa.all()
        m_a_kt = []
        for kt in range(DKT):
            k.op("scalar", "activation", albc8[:, kt * 513:kt * 513 + 1],
                 cols[:, kt * 8:kt * 8 + 1], AF.Exp, wait=M_COLS if kt == 0 else None)
            m_a_kt.append(k.op("scalar", "activation", albc8[:, kt * 513:kt * 513 + 1],
                               albc8[:, kt * 513:kt * 513 + 1], AF.Exp, scale=-1.0))
        m_alb = None
        for kt in range(DKT):
            k.op("vector", "tensor_scalar", albc8[:, kt * 513 + 1:(kt + 1) * 513],
                 id4[:, 0:512], 0.0, albc8[:, kt * 513:kt * 513 + 1], ALU.mult,
                 op1=ALU.add, wait=[m_a_kt[kt]] + ([M_ID] if kt == 0 else []))
            m_alb = k.op("vector", "memset", albc8[:, kt * 513:kt * 513 + 1], 1.0)
        m_pcz = k.op("vector", "memset", PC[:], 0.0)
        m_qcz = k.op("vector", "memset", QC[:], 0.0)

        casts = []
        for wsrc, wdst in [(WkT, w16a), (WvT, w16b), (WrT, w16c)]:
            for kt in range(DKT):
                i = len(casts)
                sl = wstg[:, (i % 2) * D:(i % 2) * D + D]
                mdl = pw.dma(sl, wsrc[kt * 128:(kt + 1) * 128, :],
                             wait=[casts[-2]] if i >= 2 else None)
                casts.append(k.op("scalar", "activation", wdst[:, kt * D:(kt + 1) * D],
                                  sl, AF.Copy, wait=[mdl]))
        m_wkvr = casts[-1]

        m_ek = [None, None]
        m_v = [None, None]
        m_sr = [None, None]
        m_ekv = [None, None]
        m_scanP = [None, None]
        m_scanQ = [None, None]
        m_num = [None, None]
        m_den = [None, None]
        m_recip = [None, None]
        m_t = [None, None]
        m_rwkv = [None, None]
        m_cbP = [None, None]
        m_cbQ = [None, None]
        m_lastmm = None
        for g in range(NTG):
            m_mixg = None
            for kt in range(DKT):
                xx = xn1p[:, kt * 2049 + 1 + g * 512:kt * 2049 + 1 + g * 512 + 512]
                xp = xn1p[:, kt * 2049 + g * 512:kt * 2049 + g * 512 + 512]
                for j, dst in [(2, xkg), (3, xvg), (4, xrg)]:
                    ds = dst[:, kt * 512:(kt + 1) * 512]
                    w = []
                    if kt == 0 and j == 2:
                        w += (M_XN + ([m_lastmm] if m_lastmm is not None else []))
                        if g == 0:
                            w += M_COLS + [m_z]
                    k.op("vector", "tensor_tensor", mixd[:], xx, xp, ALU.subtract,
                         wait=w or None)
                    m_mixg = k.op("vector", "scalar_tensor_tensor", ds, mixd[:],
                                  cols[:, kt * 8 + j:kt * 8 + j + 1], xp,
                                  ALU.mult, ALU.add)
            for mt in range(DKT):
                p = mt % 2
                def grp(ps_, w16_, rhs_, extra):
                    m = None
                    for kt in range(DKT):
                        w = list(extra) if kt == 0 else []
                        m = k.op("tensor", "matmul", ps_[0:128, 0:512],
                                 w16_[:, kt * D + mt * 128:kt * D + (mt + 1) * 128],
                                 rhs_[:, kt * 512:(kt + 1) * 512],
                                 start=(kt == 0), stop=(kt == DKT - 1), wait=w or None,
                                 inc=(st, 1) if kt == DKT - 1 else None)
                    return m

                ex = [m_mixg]
                if g == 0 and mt == 0:
                    ex.append(m_wkvr)
                if m_ek[p] is not None:
                    ex.append(m_ek[p])
                mmK = grp(psK[p], w16a, xkg, ex)
                mmV = grp(psV[p], w16b, xvg,
                          [m_v[p]] if m_v[p] is not None else [])
                mmR = grp(psR[p], w16c, xrg,
                          [m_sr[p]] if m_sr[p] is not None else [])
                m_lastmm = mmR
                w = [mmK]
                if m_den[p] is not None:
                    w.append(m_den[p])
                if g == 0 and mt < 2:
                    w.append(m_alb)
                m_ek[p] = k.op("scalar", "activation", ekg[p][:, 1:513],
                               psK[p][0:128, 0:512], AF.Exp, wait=w)
                m_euek = k.op("scalar", "activation", eukg[p][:],
                              psK[p][0:128, 0:512], AF.Exp,
                              bias=cols[:, mt * 8 + 1:mt * 8 + 2])
                m_ek[p] = m_euek
                w = [m_pcz] if (g == 0 and mt < 2) else [m_scanP[p]]
                m_cinP = k.op("scalar", "activation", ekvg[p][:, 0:1], PC[:, mt:mt + 1],
                              AF.Copy, wait=w)
                w = [m_qcz] if (g == 0 and mt < 2) else [m_scanQ[p]]
                m_cinQ = k.op("scalar", "activation", ekg[p][:, 0:1], QC[:, mt:mt + 1],
                              AF.Copy, wait=w)
                m_v[p] = k.op("scalar", "activation", vg[p][:], psV[p][0:128, 0:512],
                              AF.Copy, wait=[mmV] + ([m_t[p]] if m_t[p] is not None else []))
                m_sr[p] = k.op("scalar", "activation", srg[p][:], psR[p][0:128, 0:512],
                               AF.Sigmoid, wait=[mmR] + ([m_rwkv[p]] if m_rwkv[p] is not None else []))
                m_ekv[p] = k.op("gpsimd", "tensor_tensor", ekvg[p][:, 1:513],
                                ekg[p][:, 1:513], vg[p][:], ALU.mult,
                                wait=[m_ek[p], m_v[p]])
                m_euv = k.op("gpsimd", "tensor_tensor", euv[p][:], eukg[p][:],
                             vg[p][:], ALU.mult)
                w = [m_ekv[p], m_cinP]
                if m_num[p] is not None:
                    w.append(m_num[p])
                if m_cbP[p] is not None:
                    w.append(m_cbP[p])
                m_scanP[p] = k.op("vector", "tensor_tensor_scan", Pt[p][:, 0:513],
                                  albc8[:, mt * 513:(mt + 1) * 513], ekvg[p][:, 0:513],
                                  0.0, ALU.mult, ALU.add, wait=w)
                w = [m_cinQ]
                if m_den[p] is not None:
                    w.append(m_den[p])
                if m_cbQ[p] is not None:
                    w.append(m_cbQ[p])
                m_scanQ[p] = k.op("vector", "tensor_tensor_scan", Qt[p][:, 0:513],
                                  albc8[:, mt * 513:(mt + 1) * 513], ekg[p][:, 0:513],
                                  0.0, ALU.mult, ALU.add, wait=w)
                m_cbP[p] = k.op("scalar", "activation", PC[:, mt:mt + 1],
                                Pt[p][:, 512:513], AF.Copy, wait=[m_scanP[p]])
                m_cbQ[p] = k.op("scalar", "activation", QC[:, mt:mt + 1],
                                Qt[p][:, 512:513], AF.Copy, wait=[m_scanQ[p]])
                m_num[p] = k.op("gpsimd", "tensor_tensor", vg[p][:], euv[p][:],
                                Pt[p][:, 0:512], ALU.add, wait=[m_scanP[p]])
                m_den[p] = k.op("gpsimd", "tensor_tensor", dent[p][:], eukg[p][:],
                                Qt[p][:, 0:512], ALU.add, wait=[m_scanQ[p]])
                m_recip[p] = k.op("vector", "reciprocal", dent[p][:], dent[p][:],
                                  wait=[m_den[p]])
                m_t[p] = k.op("gpsimd", "tensor_tensor", ekvg[p][:, 1:513], vg[p][:],
                              dent[p][:], ALU.mult, wait=[m_recip[p]])
                m_rwkv[p] = k.op("gpsimd", "tensor_tensor",
                                 rwkvb[:, mt * L + g * 512:mt * L + g * 512 + 512],
                                 ekvg[p][:, 1:513], srg[p][:], ALU.mult, wait=[m_sr[p]])

        es2.close()
        MV2 = k.mark("vector")
        M_P2 = [m_lastmm, m_rwkv[0], m_rwkv[1], MV2]
        phi16 = sbuf(k, "phi16", [128, DKT * L], BF16)
        phstg = sbuf(k, "phstg", [128, 2 * L], F32)
        estage = sbuf(k, "estage", [128, 2 * 512], F32)
        e16s = sbuf(k, "e16s", [128, 2 * 512], BF16)
        erp = sbuf(k, "erp", [128, 4], F32)
        ercols = sbuf(k, "ercols", [128, 16], F32)
        prest = sbuf(k, "prest", [128, 2 * 512], F32)
        x1g = sbuf(k, "x1g", [128, 2 * 512], F32)

        m_c2 = m_wkvr
        for kt in range(DKT):
            i = len(casts)
            sl = wstg[:, (i % 2) * D:(i % 2) * D + D]
            w = [casts[-2]]
            if kt == 0:
                w.append(m_ek[0])
                w.append(m_ek[1])
            mdl = pw.dma(sl, WoT[kt * 128:(kt + 1) * 128, :], wait=w)
            casts.append(k.op("scalar", "activation", w16a[:, kt * D:(kt + 1) * D],
                              sl, AF.Copy, wait=[mdl]))
        m_wo = casts[-1]
        m_phc = None
        for kt in range(DKT):
            sl = phstg[:, (kt % 2) * L:(kt % 2) * L + L]
            w = [m_phc] if kt >= 2 else list(M_P2)
            mdl = pw.dma(sl, phi2[kt * 128:(kt + 1) * 128, :], wait=w)
            m_phc = k.op("scalar", "activation", phi16[:, kt * L:(kt + 1) * L],
                         sl, AF.Copy, wait=[mdl] + (M_P2 if kt == 0 else []))
        m_ph = m_phc

        psE = [psV[0], psV[1]]
        m_eE = [m_v[0], m_v[1]]
        m_c16 = [None, None]
        m_red = [None, None]
        m_stE = [None, None]
        idxE = 0
        for mtE in range(16):
            for ng in range(NTG):
                parE = idxE % 2
                idxE += 1
                m = None
                for kt in range(DKT):
                    w = []
                    if kt == 0:
                        if mtE == 0 and ng == 0:
                            w.append(m_ph)
                        if m_eE[parE] is not None:
                            w.append(m_eE[parE])
                    m = k.op("tensor", "matmul", psE[parE][0:128, 0:512],
                             xn1p[:, kt * 2049 + 1 + mtE * 128:kt * 2049 + 1 + (mtE + 1) * 128],
                             phi16[:, kt * L + ng * 512:kt * L + ng * 512 + 512],
                             start=(kt == 0), stop=(kt == DKT - 1), wait=w or None,
                             inc=(st, 1) if kt == DKT - 1 else None)
                w = [m]
                for mm_ in (m_c16[parE], m_red[parE]):
                    if mm_ is not None:
                        w.append(mm_)
                m_eE[parE] = k.op("scalar", "activation", estage[:, parE * 512:parE * 512 + 512],
                                  psE[parE][0:128, 0:512], AF.Exp, wait=w)
                m_red[parE] = k.op("vector", "tensor_reduce", erp[:, ng:ng + 1],
                                   estage[:, parE * 512:parE * 512 + 512], AX.X, ALU.add,
                                   wait=[m_eE[parE]])
                m_c16[parE] = k.op("scalar", "activation", e16s[:, parE * 512:parE * 512 + 512],
                                   estage[:, parE * 512:parE * 512 + 512], AF.Copy,
                                   wait=[m_eE[parE]] + ([m_stE[parE]] if m_stE[parE] is not None else []))
                m_stE[parE] = po.dma(o_E[mtE * 128:(mtE + 1) * 128, ng * 512:ng * 512 + 512],
                                     e16s[:, parE * 512:parE * 512 + 512], wait=[m_c16[parE]])
            k.op("vector", "tensor_reduce", ercols[:, mtE:mtE + 1], erp[:, 0:4], AX.X, ALU.add)
            k.op("vector", "reciprocal", ercols[:, mtE:mtE + 1], ercols[:, mtE:mtE + 1])
        m_finE = k.mark("vector")
        for mtE in range(16):
            po.dma(o_er[mtE * 128:(mtE + 1) * 128, 0:1], ercols[:, mtE:mtE + 1],
                   wait=[m_finE] if mtE == 0 else None)

        pso = [psK[0], psK[1]]
        m_pr = [None, None]
        m_bank2 = [m_ek[0], m_ek[1]]
        first3 = True
        for g in range(NTG):
            gs = slice(g * 512, (g + 1) * 512)
            for mt in range(DKT):
                par = mt % 2
                wld = [m_pr[par]] if m_pr[par] is not None else (
                    M_P2 if (g == 0 and mt < 2) else None)
                M_X1 = pa.dma(x1g[:, par * 512:par * 512 + 512],
                              x1T[mt * 128:(mt + 1) * 128, gs], wait=wld)
                m = None
                for kt in range(DKT):
                    w = []
                    if kt == 0:
                        if first3:
                            w += [m_wo, m_rwkv[0], m_rwkv[1]]
                            first3 = False
                        if m_bank2[par] is not None:
                            w.append(m_bank2[par])
                    m = k.op("tensor", "matmul", pso[par][0:128, 0:512],
                             w16a[:, kt * D + mt * 128:kt * D + (mt + 1) * 128],
                             rwkvb[:, kt * L + g * 512:kt * L + g * 512 + 512],
                             start=(kt == 0), stop=(kt == DKT - 1), wait=w or None,
                             inc=(st, 1) if kt == DKT - 1 else None)
                m_p = k.op("vector", "tensor_tensor", prest[:, par * 512:par * 512 + 512],
                           pso[par][0:128, 0:512], x1g[:, par * 512:par * 512 + 512],
                           ALU.add, wait=[m, M_X1] + ([m_pr[par]] if m_pr[par] is not None else []))
                m_bank2[par] = m_p
                m_pr[par] = po.dma(o_pres[mt * 128:(mt + 1) * 128, gs],
                                   prest[:, par * 512:par * 512 + 512], wait=[m_p])
        for s in po.sems:
            k.wait("gpsimd", s, k.val(s))
        k.finalize()
    return nc


def build_logits():
    nc = bass.Bass("TRN2", target_bir_lowering=False)
    xn1 = nc.dram_tensor("xn1", [D, L], BF16, kind="ExternalInput").ap()
    phi2 = nc.dram_tensor("phi2", [D, L], F32, kind="ExternalInput").ap()
    caps = declare_consts(nc)
    o_E = nc.dram_tensor("E16", [L, L], BF16, kind="ExternalOutput").ap()
    o_er = nc.dram_tensor("erinv", [L, 1], F32, kind="ExternalOutput").ap()

    es = ExitStack()
    with es:
        k = K(nc, es)
        ld = k.sem("ld")
        ldw = k.sem("ldw")
        st = k.sem("stn")
        so = k.sem("so")
        register_fconsts(k, caps, ld)

        xn = sbuf(k, "xn", [128, DKT * L], BF16)
        phi16 = sbuf(k, "phi16", [128, DKT * L], BF16)
        wstg = sbuf(k, "wstg", [128, 2 * L], F32)
        estage = sbuf(k, "estage", [128, 2 * 512], F32)
        e16s = sbuf(k, "e16s", [128, 2 * 512], BF16)
        erp = sbuf(k, "erp", [128, 4], F32)
        ercols = sbuf(k, "ercols", [128, 16], F32)
        ps2 = [psum(k, "psA", (128, 512)), psum(k, "psB", (128, 512))]

        pin = DPool(k, "pin", "sync", 4)
        for kt in range(DKT):
            pin.dma(xn[:, kt * L:(kt + 1) * L], xn1[kt * 128:(kt + 1) * 128, :])
        M_XNL = pin.all()
        m_ph = None
        for kt in range(DKT):
            parw = kt % 2
            w = [m_ph] if kt >= 2 else None
            mdl = pin.dma(wstg[:, parw * L:(parw + 1) * L],
                          phi2[kt * 128:(kt + 1) * 128, :], wait=w)
            m_ph = k.op("scalar", "activation", phi16[:, kt * L:(kt + 1) * L],
                        wstg[:, parw * L:(parw + 1) * L], AF.Copy, wait=[mdl])

        m_e = [None, None]
        m_c16 = [None, None]
        m_red = [None, None]
        m_st = [None, None]
        idx = 0
        for mt in range(16):
            for ng in range(NTG):
                par = idx % 2
                idx += 1
                m = None
                for kt in range(DKT):
                    w = []
                    if kt == 0:
                        if mt == 0 and ng == 0:
                            w += [m_ph] + M_XNL
                        if m_e[par] is not None:
                            w.append(m_e[par])
                    m = k.op("tensor", "matmul", ps2[par][0:128, 0:512],
                             xn[:, kt * L + mt * 128:kt * L + (mt + 1) * 128],
                             phi16[:, kt * L + ng * 512:kt * L + ng * 512 + 512],
                             start=(kt == 0), stop=(kt == DKT - 1), wait=w or None,
                             inc=(st, 1) if kt == DKT - 1 else None)
                w = [m]
                for mm_ in (m_c16[par], m_red[par]):
                    if mm_ is not None:
                        w.append(mm_)
                m_e[par] = k.op("scalar", "activation", estage[:, par * 512:par * 512 + 512],
                                ps2[par][0:128, 0:512], AF.Exp, wait=w)
                m_red[par] = k.op("vector", "tensor_reduce", erp[:, ng:ng + 1],
                                  estage[:, par * 512:par * 512 + 512], AX.X, ALU.add,
                                  wait=[m_e[par]])
                m_c16[par] = k.op("scalar", "activation", e16s[:, par * 512:par * 512 + 512],
                                  estage[:, par * 512:par * 512 + 512], AF.Copy,
                                  wait=[m_e[par]] + ([m_st[par]] if m_st[par] is not None else []))
                m_st[par] = pin.dma(o_E[mt * 128:(mt + 1) * 128, ng * 512:ng * 512 + 512],
                                    e16s[:, par * 512:par * 512 + 512], wait=[m_c16[par]],
                                    queue="gpsimd")
            k.op("vector", "tensor_reduce", ercols[:, mt:mt + 1], erp[:, 0:4], AX.X, ALU.add)
            k.op("vector", "reciprocal", ercols[:, mt:mt + 1], ercols[:, mt:mt + 1])
        m_fin = k.mark("vector")
        for mt in range(16):
            k.op("sync", "dma_start", out=o_er[mt * 128:(mt + 1) * 128, 0:1],
                 in_=ercols[:, mt:mt + 1], wait=[m_fin] if mt == 0 else None, inc=(so, 16))
        k.wait("gpsimd", so, k.val(so))
        for s in pin.sems:
            k.wait("gpsimd", s, k.val(s))
        k.finalize()
    return nc


def build_lb():
    nc = bass.Bass("TRN2", target_bir_lowering=False)
    Eg = nc.dram_tensor("Eg", [8 * 256, L], BF16, kind="ExternalInput").ap()
    caps = declare_consts(nc)
    o_D = nc.dram_tensor("Dinv16", [256, L], BF16, kind="ExternalOutput").ap()

    es = ExitStack()
    with es:
        k = K(nc, es)
        ld = k.sem("ld")
        so = k.sem("so")
        register_fconsts(k, caps, ld)
        etl = sbuf(k, "etl", [128, 8 * L], BF16)
        acc = sbuf(k, "acc", [128, L], F32)
        d16 = sbuf(k, "d16", [128, L], BF16)
        m_rd = None
        m_cp = None
        m_16 = None
        for t in range(2):
            pin = DPool(k, "pinb%d" % t, "sync", 4)
            for b in range(8):
                w = [m_rd, m_cp] if (t == 1 and b == 0) else None
                pin.dma(etl[:, b * L:(b + 1) * L],
                        Eg[b * 256 + t * 128:b * 256 + (t + 1) * 128, :], wait=w)
            M = pin.all()
            m_cp = k.op("scalar", "activation", acc[:], etl[:, 0:L], AF.Copy,
                        wait=M + ([m_16] if m_16 is not None else []))
            m_a = None
            for b in range(1, 8):
                m_a = k.op("vector", "tensor_tensor", acc[:], acc[:], etl[:, b * L:(b + 1) * L],
                           ALU.add, wait=[m_cp] if b == 1 else None)
            m_rd = k.op("vector", "reciprocal", acc[:], acc[:])
            m_16 = k.op("scalar", "activation", d16[:], acc[:], AF.Copy,
                        wait=[m_rd] + ([m_dst] if t > 0 else []))
            k.op("sync", "dma_start", out=o_D[t * 128:(t + 1) * 128, :], in_=d16[:],
                 wait=[m_16], inc=(so, 16))
            m_dst = (so, k.val(so))
        k.wait("gpsimd", so, k.val(so))
        k.finalize()
    return nc


def build_lc():
    nc = bass.Bass("TRN2", target_bir_lowering=False)
    E16b = nc.dram_tensor("E16b", [L, L], BF16, kind="ExternalInput").ap()
    Dinv = nc.dram_tensor("Dinv", [L, L], BF16, kind="ExternalInput").ap()
    xn1TM = nc.dram_tensor("xn1TM", [L, D], BF16, kind="ExternalInput").ap()
    caps = declare_consts(nc)
    o_s = nc.dram_tensor("slotsb", [D, L], BF16, kind="ExternalOutput").ap()

    es = ExitStack()
    with es:
        k = K(nc, es)
        ld = k.sem("ld")
        lde = k.sem("lde")
        st = k.sem("stn")
        so = k.sem("so")
        register_fconsts(k, caps, ld)
        xtm = sbuf(k, "xtm", [128, 16 * D], BF16)
        edsp = sbuf(k, "edsp", [128, 16 * L], BF16)
        estg = sbuf(k, "estg", [128, 2 * L], BF16)
        dstg = sbuf(k, "dstg", [128, 2 * L], BF16)
        sstg = sbuf(k, "sstg", [128, 2 * 512], BF16)
        ps2 = [psum(k, "psA", (128, 512)), psum(k, "psB", (128, 512))]

        pin = DPool(k, "pin", "sync", 4)
        for mt in range(16):
            pin.dma(xtm[:, mt * D:(mt + 1) * D], xn1TM[mt * 128:(mt + 1) * 128, :])
        M_XTL = pin.all()
        M_XT = M_XTL[0]
        m_ed = [None, None]
        for mt in range(16):
            par = mt % 2
            w = [m_ed[par]] if m_ed[par] is not None else None
            m1 = pin.dma(estg[:, par * L:(par + 1) * L], E16b[mt * 128:(mt + 1) * 128, :],
                         wait=w)
            m2 = pin.dma(dstg[:, par * L:(par + 1) * L], Dinv[mt * 128:(mt + 1) * 128, :],
                         wait=w)
            m_ed[par] = k.op("vector", "tensor_tensor", edsp[:, mt * L:(mt + 1) * L],
                             estg[:, par * L:(par + 1) * L], dstg[:, par * L:(par + 1) * L],
                             ALU.mult, wait=[m1, m2])
        m_edall = m_ed[0] if m_ed[1] is None else m_ed[1]
        m_edall = k.mark("vector")

        m_ev = [None, None]
        m_sst = [None, None]
        idx = 0
        for dt in range(DKT):
            for ng in range(NTG):
                par = idx % 2
                idx += 1
                m = None
                for mt in range(16):
                    w = []
                    if mt == 0:
                        if idx == 1:
                            w += [m_edall] + M_XTL
                        if m_ev[par] is not None:
                            w.append(m_ev[par])
                    m = k.op("tensor", "matmul", ps2[par][0:128, 0:512],
                             xtm[:, mt * D + dt * 128:mt * D + (dt + 1) * 128],
                             edsp[:, mt * L + ng * 512:mt * L + ng * 512 + 512],
                             start=(mt == 0), stop=(mt == 15), wait=w or None,
                             inc=(st, 1) if mt == 15 else None)
                m_ev[par] = k.op("vector", "tensor_copy", sstg[:, par * 512:par * 512 + 512],
                                 ps2[par][0:128, 0:512],
                                 wait=[m] + ([m_sst[par]] if m_sst[par] is not None else []))
                m_sst[par] = pin.dma(o_s[dt * 128:(dt + 1) * 128, ng * 512:ng * 512 + 512],
                                     sstg[:, par * 512:par * 512 + 512], wait=[m_ev[par]],
                                     queue="gpsimd")
        for s in pin.sems:
            k.wait("gpsimd", s, k.val(s))
        k.finalize()
    return nc


def build_ld():
    nc = bass.Bass("TRN2", target_bir_lowering=False)
    slots = nc.dram_tensor("slots", [D, L], BF16, kind="ExternalInput").ap()
    Wc = nc.dram_tensor("Wc", [8 * D, D], F32, kind="ExternalInput").ap()
    bc = nc.dram_tensor("bc", [8, D], F32, kind="ExternalInput").ap()
    caps = declare_consts(nc)
    o_y = nc.dram_tensor("yc", [8 * 256, D], BF16, kind="ExternalOutput").ap()

    es = ExitStack()
    with es:
        k = K(nc, es)
        ld = k.sem("ld")
        ldw = k.sem("ldw")
        st = k.sem("stn")
        so = k.sem("so")
        register_fconsts(k, caps, ld)
        onesrow = load_const(k, caps, "ONESROW", ld)
        slt = sbuf(k, "slt", [128, DKT * L], BF16)
        wstg = sbuf(k, "wstg", [128, 4 * D], F32)
        w16 = sbuf(k, "w16", [128, 2 * DKT * D], BF16)
        bstage = sbuf(k, "bstage", [1, 8 * D], F32)
        ystg = sbuf(k, "ystg", [128, 2 * 512], BF16)
        ps2 = [psum(k, "psA", (128, 512)), psum(k, "psB", (128, 512))]

        pin = DPool(k, "pin", "sync", 4)
        for kt in range(DKT):
            pin.dma(slt[:, kt * L:(kt + 1) * L], slots[kt * 128:(kt + 1) * 128, :])
        for nl in range(8):
            pin.dma(bstage[0:1, nl * D:(nl + 1) * D], bc[nl:nl + 1, :])
        M_INL = pin.all()
        M_IN = M_INL[0]

        m_cast = [None, None]
        m_nlmm = [None, None]
        m_ev = [None, None]
        m_yst = [None, None]
        idx = 0
        wcasts = []
        pw = DPool(k, "pw", "sync", 4)
        for nl in range(8):
            parn = nl % 2
            m_c = None
            for kt in range(DKT):
                i = len(wcasts)
                parw = i % 4
                w = [wcasts[-4]] if i >= 4 else None
                mdl = pw.dma(wstg[:, parw * D:(parw + 1) * D],
                             Wc[nl * D + kt * 128:nl * D + (kt + 1) * 128, :], wait=w)
                wl = [mdl]
                if kt == 0 and m_nlmm[parn] is not None:
                    wl.append(m_nlmm[parn])
                m_c = k.op("scalar", "activation",
                           w16[:, parn * DKT * D + kt * D:parn * DKT * D + (kt + 1) * D],
                           wstg[:, parw * D:(parw + 1) * D], AF.Copy, wait=wl)
                wcasts.append(m_c)
            m_cast[parn] = m_c
            for half in range(2):
                for eg in range(2):
                    par = idx % 2
                    idx += 1
                    m = None
                    for kt in range(DKT):
                        w = []
                        if kt == 0:
                            w.append(m_cast[parn])
                            if idx == 1:
                                w += M_INL
                            if m_ev[par] is not None:
                                w.append(m_ev[par])
                        m = k.op("tensor", "matmul", ps2[par][0:128, 0:512],
                                 slt[:, kt * L + nl * 256 + half * 128:kt * L + nl * 256 + half * 128 + 128],
                                 w16[:, parn * DKT * D + kt * D + eg * 512:parn * DKT * D + kt * D + eg * 512 + 512],
                                 start=(kt == 0), stop=False, wait=w or None)
                    m = k.op("tensor", "matmul", ps2[par][0:128, 0:512],
                             onesrow[0:1, 0:128],
                             bstage[0:1, nl * D + eg * 512:nl * D + eg * 512 + 512],
                             start=False, stop=True, inc=(st, 1))
                    m_nlmm[parn] = m
                    m_ev[par] = k.op("vector", "tensor_copy", ystg[:, par * 512:par * 512 + 512],
                                     ps2[par][0:128, 0:512],
                                     wait=[m] + ([m_yst[par]] if m_yst[par] is not None else []))
                    k.op("sync", "dma_start",
                         out=o_y[nl * 256 + half * 128:nl * 256 + half * 128 + 128,
                                 eg * 512:eg * 512 + 512],
                         in_=ystg[:, par * 512:par * 512 + 512], wait=[m_ev[par]], inc=(so, 16))
                    m_yst[par] = (so, k.val(so))
        k.wait("gpsimd", so, k.val(so))
        k.finalize()
    return nc


def build_le():
    nc = bass.Bass("TRN2", target_bir_lowering=False)
    yb = nc.dram_tensor("yb", [L, D], BF16, kind="ExternalInput").ap()
    EFM = nc.dram_tensor("EFM", [L, L], BF16, kind="ExternalInput").ap()
    erow = nc.dram_tensor("erow", [1, L], F32, kind="ExternalInput").ap()
    pres = nc.dram_tensor("pres", [D, L], F32, kind="ExternalInput").ap()
    caps = declare_consts(nc)
    o_out = nc.dram_tensor("outT", [D, L], F32, kind="ExternalOutput").ap()

    es = ExitStack()
    with es:
        k = K(nc, es)
        ld = k.sem("ld")
        ldp = k.sem("ldp")
        st = k.sem("stn")
        so = k.sem("so")
        register_fconsts(k, caps, ld)
        onesrow = load_const(k, caps, "ONESROW", ld)
        ytm = sbuf(k, "ytm", [128, 16 * D], BF16)
        efm = sbuf(k, "efm", [128, 16 * L], BF16)
        errow = sbuf(k, "errow", [1, L], F32)
        erbc = sbuf(k, "erbc", [128, 4 * 512], F32)
        prestg = sbuf(k, "prestg", [128, 2 * 512], F32)
        ostg = sbuf(k, "ostg", [128, 2 * 512], F32)
        o2 = sbuf(k, "o2", [128, 2 * 512], F32)
        ps2 = [psum(k, "psA", (128, 512)), psum(k, "psB", (128, 512))]
        psbc = psum(k, "psbc", (128, 512))

        pin = DPool(k, "pin", "sync", 4)
        for npt in range(16):
            pin.dma(ytm[:, npt * D:(npt + 1) * D], yb[npt * 128:(npt + 1) * 128, :])
        for npt in range(16):
            pin.dma(efm[:, npt * L:(npt + 1) * L], EFM[npt * 128:(npt + 1) * 128, :])
        pin.dma(errow[0:1, :], erow)
        M_ALLL = pin.all()
        M_ALL = M_ALLL[0]

        m_bc = None
        for mg in range(NTG):
            m = k.op("tensor", "matmul", psbc[0:128, 0:512], onesrow[0:1, 0:128],
                     errow[0:1, mg * 512:mg * 512 + 512], start=True, stop=True,
                     wait=(M_ALLL if mg == 0 else []) + ([m_bc] if m_bc is not None else []),
                     inc=(st, 1))
            m_bc = k.op("vector", "tensor_copy", erbc[:, mg * 512:mg * 512 + 512],
                        psbc[0:128, 0:512], wait=[m])
        m_bcall = m_bc

        m_mul = [None, None]
        m_add = [None, None]
        m_ost = [None, None]
        idx = 0
        for dt in range(DKT):
            for mg in range(NTG):
                par = idx % 2
                idx += 1
                wld = [m_mul[par]] if m_mul[par] is not None else None
                k.op("sync", "dma_start", out=prestg[:, par * 512:par * 512 + 512],
                     in_=pres[dt * 128:(dt + 1) * 128, mg * 512:mg * 512 + 512],
                     wait=wld, inc=(ldp, 16))
                M_P = (ldp, k.val(ldp))
                m = None
                for npt in range(16):
                    w = []
                    if npt == 0:
                        if idx == 1:
                            w.append(m_bcall)
                        if m_mul[par] is not None:
                            w.append(m_mul[par])
                    m = k.op("tensor", "matmul", ps2[par][0:128, 0:512],
                             ytm[:, npt * D + dt * 128:npt * D + (dt + 1) * 128],
                             efm[:, npt * L + mg * 512:npt * L + mg * 512 + 512],
                             start=(npt == 0), stop=(npt == 15), wait=w or None,
                             inc=(st, 1) if npt == 15 else None)
                m_mul[par] = k.op("vector", "tensor_tensor", ostg[:, par * 512:par * 512 + 512],
                                  ps2[par][0:128, 0:512], erbc[:, mg * 512:mg * 512 + 512],
                                  ALU.mult,
                                  wait=[m] + ([m_add[par]] if m_add[par] is not None else []))
                m_add[par] = k.op("vector", "tensor_tensor", o2[:, par * 512:par * 512 + 512],
                                  ostg[:, par * 512:par * 512 + 512],
                                  prestg[:, par * 512:par * 512 + 512], ALU.add,
                                  wait=[M_P] + ([m_ost[par]] if m_ost[par] is not None else []))
                k.op("sync", "dma_start",
                     out=o_out[dt * 128:(dt + 1) * 128, mg * 512:mg * 512 + 512],
                     in_=o2[:, par * 512:par * 512 + 512], wait=[m_add[par]], inc=(so, 16))
                m_ost[par] = (so, k.val(so))
        k.wait("gpsimd", so, k.val(so))
        k.finalize()
    return nc


def run_ln0(xT_shards, inputs):
    nc = build_ln0()
    cin = const_inputs()
    cT = np.ascontiguousarray(inputs["L0_ffn_centroids"].T)
    wrT = np.ascontiguousarray(inputs["L0_ffn_w_route"].T)
    in_maps = [dict(xT=xT_shards[i], cT=cT, wrT=wrT, **cin) for i in range(NCORES)]
    return run_bass_kernel_spmd(nc, in_maps, core_ids=list(range(NCORES)))


def make_hsel():
    HSELK = np.zeros((6 * 128, 4), np.float32)
    for d in range(768):
        HSELK[d, d // 192] = 1.0
    H2 = np.zeros((4, 6 * 128), np.float32)
    for d in range(768):
        H2[d // 192, d] = 1.0
    return HSELK, H2


def kernel(**inputs):
    import ml_dtypes
    from concourse.bass_utils import run_bass_kernel_spmd as spmd
    NB = NCORES
    cin = const_inputs()
    cc = lambda a: np.ascontiguousarray(a)
    ins = inputs

    def launch(nc, maps):
        return spmd(nc, maps, core_ids=list(range(NB))).results

    x0 = ins["L0_attn_x"]
    xT = [cc(x0[b].T) for b in range(NB)]
    cT = cc(ins["L0_ffn_centroids"].T)
    wrT = cc(ins["L0_ffn_w_route"].T)
    r = launch(build_ln0(), [dict(xT=xT[b], cT=cT, wrT=wrT, **cin) for b in range(NB)])
    xnhi = [r[b]["xnhi"] for b in range(NB)]
    gatesT = [cc(r[b]["gatesT"]) for b in range(NB)]

    convw = dict(W_in=ins["L0_attn_W_in"],
                 cwx=ins["L0_attn_conv_x_w"].reshape(D, 4),
                 cbx=ins["L0_attn_conv_x_b"].reshape(D, 1),
                 cwz=ins["L0_attn_conv_z_w"].reshape(D, 4),
                 cbz=ins["L0_attn_conv_z_b"].reshape(D, 1))
    r = launch(build_conv(), [dict(xn=xnhi[b], **convw, **cin) for b in range(NB)])
    xs = [r[b]["xs"] for b in range(NB)]
    zs = [r[b]["zs"] for b in range(NB)]

    HSELK, HSELKT = make_hsel()
    projw = dict(W_q=ins["L0_attn_W_q"], W_k=ins["L0_attn_W_k"], W_v=ins["L0_attn_W_v"],
                 W_g=ins["L0_attn_W_g"],
                 W_gkb=cc(np.concatenate([ins["L0_attn_W_gk"], ins["L0_attn_W_b"]], 1)),
                 dtb_bb=cc(np.concatenate([ins["L0_attn_dt_bias"], ins["L0_attn_b_b"]]).reshape(1, 8)),
                 A_log=cc(ins["L0_attn_A_log"].reshape(4, 1)), HSELK=HSELK, HSELKT=HSELKT)
    r = launch(build_proj(), [dict(xs=xs[b], **projw, **cin) for b in range(NB)])
    pj = r

    gnwbc = cc(np.tile(ins["L0_attn_g_norm_w"][None, :], (128, 1)).astype(np.float32))
    r = launch(build_delta(), [dict(qdm=pj[b]["qdm"], kdm=pj[b]["kdm"], ktm=pj[b]["ktm"],
                                    vtm=pj[b]["vtm"], gtm=pj[b]["gtm"], G=cc(pj[b]["G"]),
                                    beta=cc(pj[b]["beta"]), gnwbc=gnwbc, **cin)
                               for b in range(NB)])
    ogdm = [r[b]["ogdm"] for b in range(NB)]

    b1T = cc(ins["L0_ffn_b1"].reshape(8, 4, 128).transpose(0, 2, 1)).reshape(8 * 128, 4)
    ffnw = dict(w1=ins["L0_ffn_w1"].reshape(8 * D, 512), b1T=b1T, W_so=ins["L0_attn_W_so"])
    r = launch(build_ffnh(), [dict(xn=xnhi[b], og=ogdm[b], gatesT=gatesT[b], **ffnw, **cin)
                              for b in range(NB)])
    ht = [r[b]["ht"] for b in range(NB)]
    yst = [r[b]["yst"] for b in range(NB)]

    residw = dict(W_out=ins["L0_attn_W_out"], w2=ins["L0_ffn_w2"].reshape(8 * 512, D),
                  b2=ins["L0_ffn_b2"])
    r = launch(build_resid(), [dict(yst=yst[b], zs=zs[b], ht=ht[b], xT=xT[b],
                                    gatesT=gatesT[b], **residw, **cin) for b in range(NB)])
    x1T = [r[b]["x1T"] for b in range(NB)]
    xn1 = [r[b]["xn1"] for b in range(NB)]

    phi2 = cc(ins["L1_ffn_phi"].reshape(D, L))
    attw = dict(WkT=cc(ins["L1_attn_Wk"].T), WvT=cc(ins["L1_attn_Wv"].T),
                WrT=cc(ins["L1_attn_Wr"].T), WoT=cc(ins["L1_attn_Wo"].T),
                wcol=ins["L1_attn_w"].reshape(D, 1), ucol=ins["L1_attn_u"].reshape(D, 1),
                mixk=ins["L1_attn_mix_k"].reshape(D, 1),
                mixv=ins["L1_attn_mix_v"].reshape(D, 1),
                mixr=ins["L1_attn_mix_r"].reshape(D, 1), phi2=phi2)
    r = launch(build_l1attn(), [dict(xn1=xn1[b], x1T=x1T[b], **attw, **cin)
                                for b in range(NB)])
    pres = [r[b]["pres"] for b in range(NB)]
    E16 = [r[b]["E16"] for b in range(NB)]
    erinv = [r[b]["erinv"] for b in range(NB)]

    r = launch(build_lb(), [dict(Eg=cc(np.concatenate(
        [E16[b][i * 256:(i + 1) * 256] for b in range(NB)], 0)), **cin)
        for i in range(NB)])
    Dinv16 = np.concatenate([r[i]["Dinv16"] for i in range(NB)], 0)

    r = launch(build_lc(), [dict(E16b=E16[b], Dinv=Dinv16, xn1TM=cc(xn1[b].T), **cin)
                            for b in range(NB)])
    slotsb = [r[b]["slotsb"] for b in range(NB)]

    Wfull = ins["L1_ffn_W"]
    bfull = ins["L1_ffn_b"]
    maps = []
    for i in range(NB):
        sc = np.zeros((D, 2048), dtype=ml_dtypes.bfloat16)
        for nl in range(8):
            n = i * 8 + nl
            for b in range(NB):
                sc[:, (nl * 8 + b) * 32:(nl * 8 + b) * 32 + 32] = \
                    slotsb[b][:, n * 32:(n + 1) * 32]
        maps.append(dict(slots=sc, Wc=Wfull[i * 8:(i + 1) * 8].reshape(8 * D, D),
                         bc=bfull[i * 8:(i + 1) * 8], **cin))
    r = launch(build_ld(), maps)
    yc = [r[i]["yc"] for i in range(NB)]

    maps = []
    for b in range(NB):
        yb = np.zeros((2048, D), dtype=ml_dtypes.bfloat16)
        for n in range(64):
            yb[n * 32:(n + 1) * 32] = yc[n // 8][(n % 8) * 256 + b * 32:(n % 8) * 256 + b * 32 + 32]
        maps.append(dict(yb=yb, EFM=cc(E16[b].T), erow=cc(erinv[b].reshape(1, L)),
                         pres=pres[b], **cin))
    r = launch(build_le(), maps)
    out = np.stack([cc(r[b]["outT"].T) for b in range(NB)], 0)
    return np.asarray(out, np.float32)


# version 11
# speedup vs baseline: 1.0359x; 1.0092x over previous
import numpy as np
import ml_dtypes
from contextlib import ExitStack
import concourse.bass as bass
import concourse.mybir as mybir
from concourse.bass_utils import run_bass_kernel_spmd

F32 = mybir.dt.float32
BF16 = mybir.dt.bfloat16
I32 = mybir.dt.int32
AF = mybir.ActivationFunctionType
ALU = mybir.AluOpType
AX = mybir.AxisListType
BF16NP = ml_dtypes.bfloat16

NCORES = 8
D = 1024
L = 2048
DKT = D // 128
NTG = L // 512
EPS_LN = 1e-5


class K:
    ENGINES = ("sync", "tensor", "vector", "scalar", "gpsimd")
    SERIAL = ("vector", "scalar", "gpsimd")

    def __init__(self, nc, es):
        self.nc = nc
        self.es = es
        self.q = {e: [] for e in self.ENGINES}
        self._semcnt = {}
        self.serial = {e: self.sem("ser_" + e) for e in self.SERIAL}

    def sem(self, name):
        h = self.es.enter_context(self.nc.semaphore(name))
        self._semcnt[id(h)] = 0
        return h

    def bump(self, sem, amt):
        self._semcnt[id(sem)] += amt
        return self._semcnt[id(sem)]

    def val(self, sem):
        return self._semcnt[id(sem)]

    def emit(self, eng, fn):
        self.q[eng].append(fn)

    def op(self, eng, meth, *args, wait=None, inc=None, **kw):
        waits = [(s, v) for (s, v) in (wait or [])]
        incs = []
        marker = None
        if inc is not None and meth.startswith("dma_start"):
            prev = self.val(inc[0])
            if prev > 0 and not any(s is inc[0] and v >= prev for (s, v) in waits):
                waits.append((inc[0], prev))
        if inc is not None:
            self.bump(inc[0], inc[1])
            incs.append(inc)
            marker = (inc[0], self.val(inc[0]))
        if (eng in self.SERIAL and meth not in ("nop", "dma_start", "dma_start_transpose")
                and inc is None):
            ser = self.serial[eng]
            waits.append((ser, self.val(ser)))
            self.bump(ser, 1)
            incs.append((ser, 1))
            marker = (ser, self.val(ser))
        elif eng in self.SERIAL and meth not in ("nop", "dma_start", "dma_start_transpose"):
            raise AssertionError("one sem update per instruction: no user inc on serial engines")
        def f(e, meth=meth, args=args, kw=kw, waits=waits, incs=incs):
            for s, v in waits:
                e.wait_ge(s, v)
            ins = getattr(e, meth)(*args, **kw)
            for s, v in incs:
                ins = ins.then_inc(s, v)
        self.emit(eng, f)
        return marker

    def tr(self, out_ap, in_ap, ident_ap, wait=None, inc=None):
        return self.op("tensor", "transpose", out_ap, in_ap, ident_ap, wait=wait, inc=inc)

    def wait(self, eng, sem, val=None):
        v = self.val(sem) if val is None else val
        self.emit(eng, lambda e, s=sem, v=v: e.wait_ge(s, v))

    def waitm(self, eng, marker):
        self.emit(eng, lambda e, s=marker[0], v=marker[1]: e.wait_ge(s, v))

    def mark(self, eng):
        ser = self.serial[eng]
        return (ser, self.val(ser))

    def finalize(self):
        blk = self.es.enter_context(self.nc.Block())
        for name in self.ENGINES:
            ops = self.q[name]
            def runner(e, ops=ops):
                for f in ops:
                    f(e)
            getattr(blk, name)(runner)


class DPool:

    def __init__(self, k, name, queue="sync", n=4):
        self.k = k
        self.queue = queue
        self.sems = [k.sem(f"{name}{i}") for i in range(n)]
        self.i = 0

    def dma(self, out, in_, wait=None, queue=None):
        s = self.sems[self.i % len(self.sems)]
        self.i += 1
        return self.k.op(queue or self.queue, "dma_start", out=out, in_=in_,
                         wait=wait, inc=(s, 16))

    def all(self):
        return [(s, self.k.val(s)) for s in self.sems if self.k.val(s) > 0]


def sbuf(k, name, shape, dt=F32):
    return k.es.enter_context(k.nc.sbuf_tensor("sb_" + name, list(shape), dt))


def psum(k, name, shape=(128, 512), dt=F32):
    return k.es.enter_context(k.nc.psum_tensor("pb_" + name, list(shape), dt))


FVALS = [EPS_LN, 1.0 / D, -1.0, float(D), -1e30, -0.5, 0.5, 2.0, 1e-6, 1e-12, 384.0, 1.0 / 384.0, 1.0 / 192.0]


def make_consts():
    c = {}
    c["FCONSTS"] = np.tile(np.asarray(FVALS, np.float32)[None, :], (128, 1))
    c["IDENT"] = np.eye(128, dtype=np.float32)
    c["ONESCOL"] = np.ones((128, 1), np.float32)
    c["ONESROW"] = np.ones((1, 128), np.float32)
    c["ONESROW512"] = np.ones((1, 512), np.float32)
    c["CUMUL"] = np.triu(np.ones((128, 128), np.float32))
    c["MASK_STRICT"] = np.where(np.arange(128)[:, None] < np.arange(128)[None, :], 0.0, -1e9).astype(np.float32)
    c["MASK_INCL"] = np.where(np.arange(128)[:, None] <= np.arange(128)[None, :], 0.0, -1e9).astype(np.float32)
    c["MASK_STRICT_T"] = np.ascontiguousarray(c["MASK_STRICT"].T)
    c["IDENT4"] = np.tile(np.eye(128, dtype=np.float32), (1, 4))
    sel = np.zeros((4, 4 * 128), np.float32)
    for h in range(4):
        sel[h, h * 128:(h + 1) * 128] = 1.0
    c["SEL4"] = sel
    sel8 = np.zeros((8, 8 * 128), np.float32)
    for e in range(8):
        sel8[e, e * 128:(e + 1) * 128] = 1.0
    c["SEL8"] = sel8
    return c


_CONSTS = make_consts()
CONST_SPECS = {k: v.shape for k, v in _CONSTS.items()}


def declare_consts(nc):
    return {name: nc.dram_tensor("C_" + name, list(shape), F32, kind="ExternalInput").ap()
            for name, shape in CONST_SPECS.items()}


def const_inputs():
    return {"C_" + name: v for name, v in _CONSTS.items()}


def load_const(k, aps, name, ld):
    t = sbuf(k, "c_" + name, CONST_SPECS[name], F32)
    k.op("sync", "dma_start", out=t[:], in_=aps[name], inc=(ld, 16))
    return t


def register_fconsts(k, caps, ld):
    t = load_const(k, caps, "FCONSTS", ld)
    for j, v in enumerate(FVALS):
        k.nc.const_aps.register(F32, float(v), t[:, j:j + 1])
    return t


def build_ln0(stop=9):
    nc = bass.Bass("TRN2", target_bir_lowering=False)
    xT = nc.dram_tensor("xT", [D, L], F32, kind="ExternalInput").ap()
    cT = nc.dram_tensor("cT", [D, 8], F32, kind="ExternalInput").ap()
    wrT = nc.dram_tensor("wrT", [D, 8], F32, kind="ExternalInput").ap()
    caps = declare_consts(nc)
    o_xnhi = nc.dram_tensor("xnhi", [D, L], BF16, kind="ExternalOutput").ap()
    o_xnlo = nc.dram_tensor("xnlo", [D, L], BF16, kind="ExternalOutput").ap()
    o_gatesT = nc.dram_tensor("gatesT", [8, L], F32, kind="ExternalOutput").ap()

    es = ExitStack()
    with es:
        k = K(nc, es)
        ld = k.sem("ld")
        st = k.sem("stn")
        so = k.sem("so")

        register_fconsts(k, caps, ld)
        ident = load_const(k, caps, "IDENT", ld)
        onescol = load_const(k, caps, "ONESCOL", ld)
        onesrow = load_const(k, caps, "ONESROW", ld)
        onesrow512 = load_const(k, caps, "ONESROW512", ld)

        xg = sbuf(k, "xg", [128, DKT * 512], F32)
        xsq = sbuf(k, "xsq", [128, 2 * 512], F32)
        scw_all = sbuf(k, "scw_all", [128, DKT * 16], F32)
        scw_hi = sbuf(k, "scw_hi", [128, DKT * 16], BF16)
        scw_lo = sbuf(k, "scw_lo", [128, DKT * 16], BF16)
        csq = sbuf(k, "csq", [128, 8], F32)
        rowsg = sbuf(k, "rowsg", [1, 8 * 512], F32)
        rn = sbuf(k, "rn", [1, L], F32)
        cnrow = sbuf(k, "cnrow", [1, 16], F32)
        bcA = sbuf(k, "bcA", [128, 512], F32)
        bcB = sbuf(k, "bcB", [128, 512], F32)
        xn32 = sbuf(k, "xn32", [128, 512], F32)
        tsc = sbuf(k, "tsc", [128, 512], F32)
        xnhi = sbuf(k, "xnhi", [128, DKT * L], BF16)
        xnlo = sbuf(k, "xnlo", [128, DKT * L], BF16)
        lg = sbuf(k, "lg", [16, L], F32)
        bccn = sbuf(k, "bccn", [8, 512], F32)
        bcrn = sbuf(k, "bcrn", [8, 512], F32)
        ltm = sbuf(k, "ltm", [128, 128], F32)
        gtm = sbuf(k, "gtm", [128, 128], F32)
        m1 = sbuf(k, "m1", [128, 16], F32)
        m2 = sbuf(k, "m2", [128, 16], F32)
        ssum = sbuf(k, "ssum", [128, 16], F32)
        eq = sbuf(k, "eq", [128, 128], F32)
        keep = sbuf(k, "keep", [128, 128], F32)
        etm = sbuf(k, "etm", [128, 128], F32)
        gatesT = sbuf(k, "gatesT", [8, L], F32)

        ps_s = psum(k, "ps_s", (128, 1024))
        ps_bc = psum(k, "ps_bc", (128, 1024))
        ps_route = psum(k, "ps_route", (128, 1024))
        ps_tr = psum(k, "ps_tr")

        for kt in range(DKT):
            k.op("sync", "dma_start", out=scw_all[:, kt * 16:kt * 16 + 8],
                 in_=cT[kt * 128:(kt + 1) * 128, :], inc=(ld, 16))
            k.op("sync", "dma_start", out=scw_all[:, kt * 16 + 8:kt * 16 + 16],
                 in_=wrT[kt * 128:(kt + 1) * 128, :], inc=(ld, 16))
        M_LDC = (ld, k.val(ld))

        if stop <= 1:
            k.wait("gpsimd", ld, k.val(ld))
            k.op("gpsimd", "dma_start", out=o_xnhi[0:128, :], in_=xnhi[:, 0:L], inc=(so, 16))
            k.op("gpsimd", "dma_start", out=o_gatesT, in_=gatesT[:], inc=(so, 16))
            k.wait("gpsimd", so, k.val(so))
            k.finalize()
            return nc

        prev_mm = None
        for kt in range(DKT):
            w = [M_LDC] if kt == 0 else [prev_mm]
            msq = k.op("scalar", "activation", csq[:, 0:8], scw_all[:, kt * 16:kt * 16 + 8],
                       AF.Square, wait=w)
            prev_mm = k.op("tensor", "matmul", ps_bc[0:1, 0:8], onescol[:], csq[:, 0:8],
                           start=(kt == 0), stop=(kt == DKT - 1),
                           wait=[msq, M_LDC] if kt == 0 else [msq], inc=(st, 1))
        k.op("scalar", "activation", cnrow[0:1, 0:8], ps_bc[0:1, 0:8], AF.Ln, wait=[prev_mm])
        m_cn = k.op("scalar", "activation", cnrow[0:1, 0:8], cnrow[0:1, 0:8], AF.Exp, scale=-0.5)
        k.op("vector", "tensor_copy", scw_hi[:], scw_all[:], wait=[M_LDC])
        m_scw = k.op("vector", "tensor_tensor", scw_lo[:], scw_all[:], scw_hi[:], ALU.subtract)

        if stop <= 2:
            k.op("gpsimd", "dma_start", out=o_xnhi[0:128, :], in_=xnhi[:, 0:L], wait=[m_scw, m_cn], inc=(so, 16))
            k.op("gpsimd", "dma_start", out=o_gatesT, in_=gatesT[:], inc=(so, 16))
            k.wait("gpsimd", so, k.val(so))
            k.finalize()
            return nc

        Rg = lambda i: rowsg[0:1, i * 512:(i + 1) * 512]
        m_xn_done = None
        m_lg_done = None
        m_ev = None
        m_sq_prev = [None, None]
        nsq = 0
        for g in range(NTG):
            gs = slice(g * 512, (g + 1) * 512)
            ldw = []
            if m_xn_done is not None:
                ldw = [m_xn_done]
            for kt in range(DKT):
                k.op("sync", "dma_start", out=xg[:, kt * 512:(kt + 1) * 512],
                     in_=xT[kt * 128:(kt + 1) * 128, gs],
                     wait=ldw if kt == 0 else None, inc=(ld, 16))
            M_LDG = (ld, k.val(ld))

            for kt in range(DKT):
                buf = xsq[:, (nsq % 2) * 512:(nsq % 2) * 512 + 512]
                w = [M_LDG] if kt == 0 else []
                if m_sq_prev[nsq % 2] is not None:
                    w = w + [m_sq_prev[nsq % 2]]
                msq = k.op("scalar", "activation", buf, xg[:, kt * 512:(kt + 1) * 512],
                           AF.Square, wait=w)
                wmm = ([M_LDG] + ([m_ev] if m_ev is not None else [])) if kt == 0 else None
                k.op("tensor", "matmul", ps_s[0:1, 0:512], onescol[:], xg[:, kt * 512:(kt + 1) * 512],
                     start=(kt == 0), stop=(kt == DKT - 1),
                     wait=wmm)
                m_sq_prev[nsq % 2] = k.op("tensor", "matmul", ps_s[0:1, 512:1024], onescol[:], buf,
                                          start=(kt == 0), stop=(kt == DKT - 1),
                                          wait=[msq], inc=(st, 1))
                nsq += 1
            m_stats = m_sq_prev[(nsq - 1) % 2]
            k.op("vector", "tensor_copy", Rg(0), ps_s[0:1, 0:512], wait=[m_stats])
            m_ev = k.op("vector", "tensor_copy", Rg(1), ps_s[0:1, 512:1024])
            k.op("vector", "tensor_scalar", Rg(2), Rg(0), 1.0 / D, None, ALU.mult)
            k.op("vector", "tensor_scalar", Rg(3), Rg(1), 1.0 / D, None, ALU.mult)
            k.op("vector", "tensor_tensor", Rg(6), Rg(2), Rg(2), ALU.mult)
            m_var = k.op("vector", "tensor_tensor", Rg(3), Rg(3), Rg(6), ALU.subtract)
            k.op("scalar", "activation", Rg(6), Rg(3), AF.Ln, bias=EPS_LN, wait=[m_var])
            m_rstd = k.op("scalar", "activation", Rg(4), Rg(6), AF.Exp, scale=-0.5)
            k.op("vector", "tensor_tensor", Rg(5), Rg(2), Rg(4), ALU.mult, wait=[m_rstd])
            k.op("vector", "tensor_scalar", Rg(5), Rg(5), -1.0, None, ALU.mult)
            m_nm = k.op("vector", "tensor_scalar", Rg(7), Rg(3), float(D), None, ALU.mult)
            m_ln2 = k.op("scalar", "activation", Rg(7), Rg(7), AF.Ln, wait=[m_nm])
            m_sub = k.op("vector", "tensor_tensor", Rg(7), Rg(7), Rg(6), ALU.subtract, wait=[m_ln2])
            m_rn = k.op("scalar", "activation", rn[0:1, gs], Rg(7), AF.Exp, scale=-0.5, wait=[m_sub])
            if stop <= 3:
                m_xn_done = m_ev
                m_lg_done = m_ev
                continue
            wbc1 = [m_nm] + ([m_cn] if g == 0 else [])
            m_bc1 = k.op("tensor", "matmul", ps_bc[0:128, 0:512], onesrow[:], Rg(4),
                         start=True, stop=True, wait=wbc1, inc=(st, 1))
            m_cA = k.op("vector", "tensor_copy", bcA[:], ps_bc[0:128, 0:512], wait=[m_bc1])
            m_bc2 = k.op("tensor", "matmul", ps_bc[0:128, 0:512], onesrow[:], Rg(5),
                         start=True, stop=True, wait=[m_cA], inc=(st, 1))
            m_cB = k.op("vector", "tensor_copy", bcB[:], ps_bc[0:128, 0:512], wait=[m_bc2])
            for kt in range(DKT):
                xsl = slice(kt * 512, (kt + 1) * 512)
                osl = slice(kt * L + g * 512, kt * L + g * 512 + 512)
                k.op("vector", "tensor_tensor", tsc[:], xg[:, xsl], bcA[:], ALU.mult)
                k.op("vector", "tensor_tensor", xn32[:], tsc[:], bcB[:], ALU.add)
                k.op("vector", "tensor_copy", xnhi[:, osl], xn32[:])
                m_xn_done = k.op("vector", "tensor_tensor", xnlo[:, osl], xn32[:], xnhi[:, osl], ALU.subtract)
            if stop <= 4:
                m_lg_done = m_xn_done
                continue
            rw = [m_xn_done, m_scw]
            if m_lg_done is not None:
                rw.append(m_lg_done)
            for kt in range(DKT):
                osl = slice(kt * L + g * 512, kt * L + g * 512 + 512)
                chi = scw_hi[:, kt * 16:kt * 16 + 8]
                clo = scw_lo[:, kt * 16:kt * 16 + 8]
                whi = scw_hi[:, kt * 16 + 8:kt * 16 + 16]
                wlo = scw_lo[:, kt * 16 + 8:kt * 16 + 16]
                first, last = kt == 0, kt == DKT - 1
                k.op("tensor", "matmul", ps_route[0:8, 0:512], chi, xnhi[:, osl],
                     start=first, stop=False, wait=rw if first else None)
                k.op("tensor", "matmul", ps_route[0:8, 0:512], chi, xnlo[:, osl], start=False, stop=False)
                k.op("tensor", "matmul", ps_route[0:8, 0:512], clo, xnhi[:, osl], start=False, stop=last)
                k.op("tensor", "matmul", ps_route[0:8, 512:1024], whi, xnhi[:, osl], start=first, stop=False)
                k.op("tensor", "matmul", ps_route[0:8, 512:1024], whi, xnlo[:, osl], start=False, stop=False)
                mr = k.op("tensor", "matmul", ps_route[0:8, 512:1024], wlo, xnhi[:, osl],
                          start=False, stop=last, inc=(st, 1) if last else None)
            m_bcn = k.op("tensor", "matmul", ps_bc[0:8, 0:512], cnrow[0:1, 0:8], onesrow512[:],
                         start=True, stop=True, wait=[m_cn, m_cB], inc=(st, 1))
            m_brn = k.op("tensor", "matmul", ps_bc[0:8, 512:1024], onesrow[0:1, 0:8], rn[0:1, gs],
                         start=True, stop=True, wait=[m_rn], inc=(st, 1))
            k.op("vector", "tensor_copy", bccn[:], ps_bc[0:8, 0:512], wait=[m_bcn, m_brn, mr])
            k.op("vector", "tensor_copy", bcrn[:], ps_bc[0:8, 512:1024])
            k.op("vector", "tensor_tensor", lg[0:8, gs], ps_route[0:8, 0:512], bccn[:], ALU.mult)
            k.op("vector", "tensor_tensor", lg[0:8, gs], lg[0:8, gs], bcrn[:], ALU.mult)
            m_lg_done = k.op("vector", "tensor_tensor", lg[0:8, gs], lg[0:8, gs],
                             ps_route[0:8, 512:1024], ALU.add)
        if stop <= 5:
            k.op("gpsimd", "dma_start", out=o_xnhi[0:128, :], in_=xnhi[:, 0:L], wait=[m_lg_done], inc=(so, 16))
            k.op("gpsimd", "dma_start", out=o_gatesT, in_=gatesT[:], inc=(so, 16))
            k.wait("gpsimd", so, k.val(so))
            k.finalize()
            return nc
        m_trf = None
        for c in range(16):
            m_trf = k.tr(ps_tr[0:128, c * 8:(c + 1) * 8],
                 lg[0:8, c * 128:(c + 1) * 128], ident[0:8, 0:8],
                 wait=[m_lg_done, M_LDC] if c == 0 else None,
                 inc=(st, 1) if c == 15 else None)
        m_ltm = k.op("vector", "tensor_copy", ltm[:], ps_tr[0:128, 0:128], wait=[m_trf])
        if stop <= 6:
            k.op("gpsimd", "dma_start", out=o_xnhi[0:128, :], in_=xnhi[:, 0:L], wait=[m_ltm], inc=(so, 16))
            k.op("gpsimd", "dma_start", out=o_gatesT, in_=gatesT[:], inc=(so, 16))
            k.wait("gpsimd", so, k.val(so))
            k.finalize()
            return nc
        ltm3 = ltm[:].rearrange("p (c e) -> p c e", e=8)
        bview = lambda t: t[:].unsqueeze(2).broadcast_to([128, 16, 8])
        k.op("vector", "tensor_reduce", m1[:], ltm3, AX.X, ALU.max)
        k.op("vector", "tensor_tensor", eq[:].rearrange("p (c e) -> p c e", e=8), ltm3, bview(m1), ALU.is_ge)
        k.op("vector", "scalar_tensor_tensor", etm[:], eq[:], -1e30, ltm[:], ALU.mult, ALU.add)
        k.op("vector", "tensor_reduce", m2[:], etm[:].rearrange("p (c e) -> p c e", e=8), AX.X, ALU.max)
        k.op("vector", "tensor_tensor", keep[:].rearrange("p (c e) -> p c e", e=8), ltm3, bview(m2), ALU.is_ge)
        m_e = k.op("vector", "tensor_tensor", etm[:].rearrange("p (c e) -> p c e", e=8), ltm3, bview(m1), ALU.subtract)
        m_ex = k.op("scalar", "activation", etm[:], etm[:], AF.Exp, wait=[m_e])
        k.op("vector", "tensor_tensor", etm[:], etm[:], keep[:], ALU.mult, wait=[m_ex])
        k.op("vector", "tensor_reduce", ssum[:], etm[:].rearrange("p (c e) -> p c e", e=8), AX.X, ALU.add)
        k.op("vector", "reciprocal", ssum[:], ssum[:])
        m_gtm = k.op("vector", "tensor_tensor", gtm[:].rearrange("p (c e) -> p c e", e=8),
                     etm[:].rearrange("p (c e) -> p c e", e=8), bview(ssum), ALU.mult)
        if stop <= 7:
            k.op("gpsimd", "dma_start", out=o_xnhi[0:128, :], in_=xnhi[:, 0:L], wait=[m_gtm], inc=(so, 16))
            k.op("gpsimd", "dma_start", out=o_gatesT, in_=gatesT[:], inc=(so, 16))
            k.wait("gpsimd", so, k.val(so))
            k.finalize()
            return nc
        m_gt_ev = None
        for g in range(NTG):
            m_trb = None
            for j in range(4):
                c = g * 4 + j
                w = [m_gtm] if (g == 0 and j == 0) else ([m_gt_ev] if j == 0 else None)
                m_trb = k.tr(ps_tr[0:8, j * 128:(j + 1) * 128],
                     gtm[:, c * 8:(c + 1) * 8], ident[:], wait=w,
                     inc=(st, 1) if j == 3 else None)
            m_gt_ev = k.op("vector", "tensor_copy", gatesT[:, g * 512:(g + 1) * 512],
                           ps_tr[0:8, 0:512], wait=[m_trb])

        if stop <= 4:
            k.op("gpsimd", "dma_start", out=o_xnhi[0:128, :], in_=xnhi[:, 0:L], wait=[m_xn_done], inc=(so, 16))
            k.op("gpsimd", "dma_start", out=o_gatesT, in_=gatesT[:], inc=(so, 16))
            k.wait("gpsimd", so, k.val(so))
            k.finalize()
            return nc
        for kt in range(DKT):
            k.op("gpsimd", "dma_start", out=o_xnhi[kt * 128:(kt + 1) * 128, :],
                 in_=xnhi[:, kt * L:(kt + 1) * L],
                 wait=[m_xn_done] if kt == 0 else None, inc=(so, 16))
            k.op("gpsimd", "dma_start", out=o_xnlo[kt * 128:(kt + 1) * 128, :],
                 in_=xnlo[:, kt * L:(kt + 1) * L], inc=(so, 16))
        k.op("gpsimd", "dma_start", out=o_gatesT, in_=gatesT[:], wait=[m_gt_ev], inc=(so, 16))
        k.wait("gpsimd", so, k.val(so))
        k.finalize()
    return nc


def build_conv(nmt=16, notaps=False, nomm=False, noevac=False, serial_evac=False, ngroups=4, nomemset=False, evac_eng='scalar', evac_dst='xzrow1', rawxz=False, rawacc=False):
    nc = bass.Bass("TRN2", target_bir_lowering=False)
    xn = nc.dram_tensor("xn", [D, L], BF16, kind="ExternalInput").ap()
    W_in = nc.dram_tensor("W_in", [D, 2 * D], F32, kind="ExternalInput").ap()
    cwx = nc.dram_tensor("cwx", [D, 4], F32, kind="ExternalInput").ap()
    cbx = nc.dram_tensor("cbx", [D, 1], F32, kind="ExternalInput").ap()
    cwz = nc.dram_tensor("cwz", [D, 4], F32, kind="ExternalInput").ap()
    cbz = nc.dram_tensor("cbz", [D, 1], F32, kind="ExternalInput").ap()
    caps = declare_consts(nc)
    o_xs = nc.dram_tensor("xs", [D, L], BF16, kind="ExternalOutput").ap()
    o_zs = nc.dram_tensor("zs", [D, L], BF16, kind="ExternalOutput").ap()

    es = ExitStack()
    with es:
        k = K(nc, es)
        ld = k.sem("ld")
        ldw = k.sem("ldw")
        st = k.sem("stn")
        so = k.sem("so")
        pin = DPool(k, "pin", "sync", 4)
        register_fconsts(k, caps, ld)

        xnb = sbuf(k, "xnb", [128, DKT * L], BF16)
        w32 = sbuf(k, "w32", [128, 2 * D], F32)
        w16 = sbuf(k, "w16", [128, 2 * D], BF16)
        cw = sbuf(k, "cw", [128, 2 * 4 * DKT], F32)
        cb = sbuf(k, "cb", [128, 2 * DKT], F32)
        xzrow = sbuf(k, "xzrow", [128, 2 * 2051], F32)
        fresh = sbuf(k, "fresh", [128, 512], F32)
        acc1 = sbuf(k, "acc1", [128, L], F32)
        acc2 = sbuf(k, "acc2", [128, L], F32)
        outt = sbuf(k, "outt", [128, L], BF16)
        ps = [psum(k, "ps0"), psum(k, "ps1")]

        for kt in range(DKT):
            pin.dma(xnb[:, kt * L:(kt + 1) * L], xn[kt * 128:(kt + 1) * 128, :])
        for kt in range(DKT):
            k.op("sync", "dma_start", out=cw[:, kt * 4:kt * 4 + 4],
                 in_=cwx[kt * 128:(kt + 1) * 128, :], inc=(ld, 16))
            k.op("sync", "dma_start", out=cw[:, 32 + kt * 4:32 + kt * 4 + 4],
                 in_=cwz[kt * 128:(kt + 1) * 128, :], inc=(ld, 16))
        for kt in range(DKT):
            k.op("sync", "dma_start", out=cb[:, kt:kt + 1], in_=cbx[kt * 128:(kt + 1) * 128, :], inc=(ld, 16))
            k.op("sync", "dma_start", out=cb[:, DKT + kt:DKT + kt + 1], in_=cbz[kt * 128:(kt + 1) * 128, :], inc=(ld, 16))
        M_LD = (ld, k.val(ld))

        m_zero = None
        if not nomemset:
            for bi in range(2):
                base = bi * 2051
                k.op("vector", "memset", xzrow[:, base:base + 1], 0.0)
                m_zero = k.op("vector", "memset", xzrow[:, base + 2049:base + 2051], 0.0)

        m_cast = [None, None]
        m_conv = [None, None]
        m_evac = [None, None]
        m_evac_last = None
        m_mmdone = [None, None]
        m_silu_prev = None
        for mt in range(nmt):
            bi = mt % 2
            wlw = [m_cast[bi]] if m_cast[bi] is not None else []
            if m_mmdone[bi] is not None:
                pass
            mws = []
            for kt in range(DKT):
                mws.append(pin.dma(w32[:, bi * D + kt * 128:bi * D + (kt + 1) * 128],
                                   W_in[kt * 128:(kt + 1) * 128, mt * 128:(mt + 1) * 128],
                                   wait=wlw if kt == 0 else None))
            M_W = mws
            mc = None
            for kt in range(DKT):
                w = None
                if kt == 0:
                    w = list(M_W)
                    if m_mmdone[bi] is not None:
                        w.append(m_mmdone[bi])
                mc = k.op("gpsimd", "tensor_copy", w16[:, bi * D + kt * 128:bi * D + (kt + 1) * 128],
                          w32[:, bi * D + kt * 128:bi * D + (kt + 1) * 128],
                          wait=w)
            m_cast[bi] = mc
            base = bi * 2051
            if nomm:
                continue
            for g in range(ngroups):
                pbank = (mt * NTG + g) % 2
                for kt in range(DKT):
                    w = []
                    if kt == 0:
                        if g == 0:
                            w = [mc, M_LD]
                            if m_conv[bi] is not None:
                                w.append(m_conv[bi])
                        if m_evac[pbank] is not None:
                            w.append(m_evac[pbank])
                        if serial_evac and m_evac_last is not None:
                            w.append(m_evac_last)
                    m_mm = k.op("tensor", "matmul", ps[pbank][0:128, 0:512],
                         w16[:, bi * D + kt * 128:bi * D + (kt + 1) * 128],
                         xnb[:, kt * L + g * 512:kt * L + g * 512 + 512],
                         start=(kt == 0), stop=(kt == DKT - 1),
                         wait=w if w else None,
                         inc=(st, 1) if kt == DKT - 1 else None)
                m_mmdone[bi] = m_mm
                if noevac:
                    m_evac[pbank] = m_mm
                    m_evac_last = m_mm
                    continue
                if evac_dst == "fresh":
                    dstap = fresh[:, 0:512]
                elif evac_dst == "xzrow0":
                    dstap = xzrow[:, base + g * 512:base + g * 512 + 512]
                else:
                    dstap = xzrow[:, base + 1 + g * 512:base + 1 + g * 512 + 512]
                m_evac[pbank] = k.op(evac_eng, "copy" if evac_eng == "scalar" else "tensor_copy",
                                  dstap,
                                  ps[pbank][0:128, 0:512],
                                  wait=[m_mm] + ([m_zero] if (mt == 0 and g == 0 and m_zero is not None) else []))
                m_evac_last = m_evac[pbank]
            if rawxz:
                mr = k.op("vector", "tensor_copy", outt[:], xzrow[:, base + 1:base + 2049], wait=[m_evac_last])
                dst = o_xs if mt < 8 else o_zs
                row = (mt % 8) * 128
                k.op("gpsimd", "dma_start", out=dst[row:row + 128, :], in_=outt[:], wait=[mr], inc=(so, 16))
                k.wait("vector", so, k.val(so))
                continue
            if notaps:
                continue
            cwcol = (0 if mt < 8 else 32) + (mt % 8) * 4
            cbcol = (0 if mt < 8 else DKT) + (mt % 8)
            wfirst = [m_evac_last] + ([m_silu_prev] if m_silu_prev is not None else [])
            k.op("vector", "tensor_scalar", acc1[:], xzrow[:, base + 0:base + 2048],
                 cw[:, cwcol:cwcol + 1], None, ALU.mult, wait=wfirst)
            k.op("vector", "scalar_tensor_tensor", acc2[:], xzrow[:, base + 1:base + 1 + 2048],
                 cw[:, cwcol + 1:cwcol + 2], acc1[:], ALU.mult, ALU.add)
            k.op("vector", "scalar_tensor_tensor", acc1[:], xzrow[:, base + 2:base + 2 + 2048],
                 cw[:, cwcol + 2:cwcol + 3], acc2[:], ALU.mult, ALU.add)
            k.op("vector", "scalar_tensor_tensor", acc2[:], xzrow[:, base + 3:base + 3 + 2048],
                 cw[:, cwcol + 3:cwcol + 4], acc1[:], ALU.mult, ALU.add)
            m_conv[bi] = k.mark("vector")
            if rawacc:
                mr = k.op("vector", "tensor_copy", outt[:], acc2[:])
                dst = o_xs if mt < 8 else o_zs
                row = (mt % 8) * 128
                k.op("gpsimd", "dma_start", out=dst[row:row + 128, :], in_=outt[:], wait=[mr], inc=(so, 16))
                k.wait("vector", so, k.val(so))
                m_silu_prev = mr
                continue
            m_act = k.op("scalar", "activation", outt[:], acc2[:], AF.Silu,
                         bias=cb[:, cbcol:cbcol + 1], wait=[m_conv[bi]])
            m_silu_prev = m_act
            dst = o_xs if mt < 8 else o_zs
            row = (mt % 8) * 128
            k.op("scalar", "dma_start", out=dst[row:row + 128, :], in_=outt[:], wait=[m_act], inc=(so, 16))
            k.wait("scalar", so, k.val(so))
        k.wait("gpsimd", so, k.val(so))
        k.finalize()
    return nc


def build_ln0conv():
    stop = 9
    nc = bass.Bass("TRN2", target_bir_lowering=False)
    xT = nc.dram_tensor("xT", [D, L], F32, kind="ExternalInput").ap()
    cT = nc.dram_tensor("cT", [D, 8], F32, kind="ExternalInput").ap()
    wrT = nc.dram_tensor("wrT", [D, 8], F32, kind="ExternalInput").ap()
    W_in = nc.dram_tensor("W_in", [D, 2 * D], F32, kind="ExternalInput").ap()
    cwx = nc.dram_tensor("cwx", [D, 4], F32, kind="ExternalInput").ap()
    cbx = nc.dram_tensor("cbx", [D, 1], F32, kind="ExternalInput").ap()
    cwz = nc.dram_tensor("cwz", [D, 4], F32, kind="ExternalInput").ap()
    cbz = nc.dram_tensor("cbz", [D, 1], F32, kind="ExternalInput").ap()
    caps = declare_consts(nc)
    o_xnhi = nc.dram_tensor("xnhi", [D, L], BF16, kind="ExternalOutput").ap()
    o_xnlo = nc.dram_tensor("xnlo", [D, L], BF16, kind="ExternalOutput").ap()
    o_gatesT = nc.dram_tensor("gatesT", [8, L], F32, kind="ExternalOutput").ap()
    o_xs = nc.dram_tensor("xs", [D, L], BF16, kind="ExternalOutput").ap()
    o_zs = nc.dram_tensor("zs", [D, L], BF16, kind="ExternalOutput").ap()

    es = ExitStack()
    with es:
        k = K(nc, es)
        ld = k.sem("ld")
        st = k.sem("stn")
        so = k.sem("so")

        register_fconsts(k, caps, ld)
        ident = load_const(k, caps, "IDENT", ld)
        onescol = load_const(k, caps, "ONESCOL", ld)
        onesrow = load_const(k, caps, "ONESROW", ld)
        onesrow512 = load_const(k, caps, "ONESROW512", ld)

        xg = sbuf(k, "xg", [128, DKT * 512], F32)
        xsq = sbuf(k, "xsq", [128, 2 * 512], F32)
        scw_all = sbuf(k, "scw_all", [128, DKT * 16], F32)
        scw_hi = sbuf(k, "scw_hi", [128, DKT * 16], BF16)
        scw_lo = sbuf(k, "scw_lo", [128, DKT * 16], BF16)
        csq = sbuf(k, "csq", [128, 8], F32)
        rowsg = sbuf(k, "rowsg", [1, 8 * 512], F32)
        rn = sbuf(k, "rn", [1, L], F32)
        cnrow = sbuf(k, "cnrow", [1, 16], F32)
        bcA = sbuf(k, "bcA", [128, 512], F32)
        bcB = sbuf(k, "bcB", [128, 512], F32)
        xn32 = sbuf(k, "xn32", [128, 512], F32)
        tsc = sbuf(k, "tsc", [128, 512], F32)
        xnhi = sbuf(k, "xnhi", [128, DKT * L], BF16)
        xnlo = sbuf(k, "xnlo", [128, DKT * L], BF16)
        lg = sbuf(k, "lg", [16, L], F32)
        bccn = sbuf(k, "bccn", [8, 512], F32)
        bcrn = sbuf(k, "bcrn", [8, 512], F32)
        ltm = sbuf(k, "ltm", [128, 128], F32)
        gtm = sbuf(k, "gtm", [128, 128], F32)
        m1 = sbuf(k, "m1", [128, 16], F32)
        m2 = sbuf(k, "m2", [128, 16], F32)
        ssum = sbuf(k, "ssum", [128, 16], F32)
        eq = sbuf(k, "eq", [128, 128], F32)
        keep = sbuf(k, "keep", [128, 128], F32)
        etm = sbuf(k, "etm", [128, 128], F32)
        gatesT = sbuf(k, "gatesT", [8, L], F32)

        ps_s = psum(k, "ps_s", (128, 1024))
        ps_bc = psum(k, "ps_bc", (128, 1024))
        ps_route = psum(k, "ps_route", (128, 1024))
        ps_tr = psum(k, "ps_tr")

        for kt in range(DKT):
            k.op("sync", "dma_start", out=scw_all[:, kt * 16:kt * 16 + 8],
                 in_=cT[kt * 128:(kt + 1) * 128, :], inc=(ld, 16))
            k.op("sync", "dma_start", out=scw_all[:, kt * 16 + 8:kt * 16 + 16],
                 in_=wrT[kt * 128:(kt + 1) * 128, :], inc=(ld, 16))
        M_LDC = (ld, k.val(ld))

        if stop <= 1:
            k.wait("gpsimd", ld, k.val(ld))
            k.op("gpsimd", "dma_start", out=o_xnhi[0:128, :], in_=xnhi[:, 0:L], inc=(so, 16))
            k.op("gpsimd", "dma_start", out=o_gatesT, in_=gatesT[:], inc=(so, 16))
            k.wait("gpsimd", so, k.val(so))
            k.finalize()
            return nc

        prev_mm = None
        for kt in range(DKT):
            w = [M_LDC] if kt == 0 else [prev_mm]
            msq = k.op("scalar", "activation", csq[:, 0:8], scw_all[:, kt * 16:kt * 16 + 8],
                       AF.Square, wait=w)
            prev_mm = k.op("tensor", "matmul", ps_bc[0:1, 0:8], onescol[:], csq[:, 0:8],
                           start=(kt == 0), stop=(kt == DKT - 1),
                           wait=[msq, M_LDC] if kt == 0 else [msq], inc=(st, 1))
        k.op("scalar", "activation", cnrow[0:1, 0:8], ps_bc[0:1, 0:8], AF.Ln, wait=[prev_mm])
        m_cn = k.op("scalar", "activation", cnrow[0:1, 0:8], cnrow[0:1, 0:8], AF.Exp, scale=-0.5)
        k.op("vector", "tensor_copy", scw_hi[:], scw_all[:], wait=[M_LDC])
        m_scw = k.op("vector", "tensor_tensor", scw_lo[:], scw_all[:], scw_hi[:], ALU.subtract)

        if stop <= 2:
            k.op("gpsimd", "dma_start", out=o_xnhi[0:128, :], in_=xnhi[:, 0:L], wait=[m_scw, m_cn], inc=(so, 16))
            k.op("gpsimd", "dma_start", out=o_gatesT, in_=gatesT[:], inc=(so, 16))
            k.wait("gpsimd", so, k.val(so))
            k.finalize()
            return nc

        Rg = lambda i: rowsg[0:1, i * 512:(i + 1) * 512]
        m_xn_done = None
        m_lg_done = None
        m_ev = None
        m_sq_prev = [None, None]
        nsq = 0
        for g in range(NTG):
            gs = slice(g * 512, (g + 1) * 512)
            ldw = []
            if m_xn_done is not None:
                ldw = [m_xn_done]
            for kt in range(DKT):
                k.op("sync", "dma_start", out=xg[:, kt * 512:(kt + 1) * 512],
                     in_=xT[kt * 128:(kt + 1) * 128, gs],
                     wait=ldw if kt == 0 else None, inc=(ld, 16))
            M_LDG = (ld, k.val(ld))

            for kt in range(DKT):
                buf = xsq[:, (nsq % 2) * 512:(nsq % 2) * 512 + 512]
                w = [M_LDG] if kt == 0 else []
                if m_sq_prev[nsq % 2] is not None:
                    w = w + [m_sq_prev[nsq % 2]]
                msq = k.op("scalar", "activation", buf, xg[:, kt * 512:(kt + 1) * 512],
                           AF.Square, wait=w)
                wmm = ([M_LDG] + ([m_ev] if m_ev is not None else [])) if kt == 0 else None
                k.op("tensor", "matmul", ps_s[0:1, 0:512], onescol[:], xg[:, kt * 512:(kt + 1) * 512],
                     start=(kt == 0), stop=(kt == DKT - 1),
                     wait=wmm)
                m_sq_prev[nsq % 2] = k.op("tensor", "matmul", ps_s[0:1, 512:1024], onescol[:], buf,
                                          start=(kt == 0), stop=(kt == DKT - 1),
                                          wait=[msq], inc=(st, 1))
                nsq += 1
            m_stats = m_sq_prev[(nsq - 1) % 2]
            k.op("vector", "tensor_copy", Rg(0), ps_s[0:1, 0:512], wait=[m_stats])
            m_ev = k.op("vector", "tensor_copy", Rg(1), ps_s[0:1, 512:1024])
            k.op("vector", "tensor_scalar", Rg(2), Rg(0), 1.0 / D, None, ALU.mult)
            k.op("vector", "tensor_scalar", Rg(3), Rg(1), 1.0 / D, None, ALU.mult)
            k.op("vector", "tensor_tensor", Rg(6), Rg(2), Rg(2), ALU.mult)
            m_var = k.op("vector", "tensor_tensor", Rg(3), Rg(3), Rg(6), ALU.subtract)
            k.op("scalar", "activation", Rg(6), Rg(3), AF.Ln, bias=EPS_LN, wait=[m_var])
            m_rstd = k.op("scalar", "activation", Rg(4), Rg(6), AF.Exp, scale=-0.5)
            k.op("vector", "tensor_tensor", Rg(5), Rg(2), Rg(4), ALU.mult, wait=[m_rstd])
            k.op("vector", "tensor_scalar", Rg(5), Rg(5), -1.0, None, ALU.mult)
            m_nm = k.op("vector", "tensor_scalar", Rg(7), Rg(3), float(D), None, ALU.mult)
            m_ln2 = k.op("scalar", "activation", Rg(7), Rg(7), AF.Ln, wait=[m_nm])
            m_sub = k.op("vector", "tensor_tensor", Rg(7), Rg(7), Rg(6), ALU.subtract, wait=[m_ln2])
            m_rn = k.op("scalar", "activation", rn[0:1, gs], Rg(7), AF.Exp, scale=-0.5, wait=[m_sub])
            if stop <= 3:
                m_xn_done = m_ev
                m_lg_done = m_ev
                continue
            wbc1 = [m_nm] + ([m_cn] if g == 0 else [])
            m_bc1 = k.op("tensor", "matmul", ps_bc[0:128, 0:512], onesrow[:], Rg(4),
                         start=True, stop=True, wait=wbc1, inc=(st, 1))
            m_cA = k.op("vector", "tensor_copy", bcA[:], ps_bc[0:128, 0:512], wait=[m_bc1])
            m_bc2 = k.op("tensor", "matmul", ps_bc[0:128, 0:512], onesrow[:], Rg(5),
                         start=True, stop=True, wait=[m_cA], inc=(st, 1))
            m_cB = k.op("vector", "tensor_copy", bcB[:], ps_bc[0:128, 0:512], wait=[m_bc2])
            for kt in range(DKT):
                xsl = slice(kt * 512, (kt + 1) * 512)
                osl = slice(kt * L + g * 512, kt * L + g * 512 + 512)
                k.op("vector", "tensor_tensor", tsc[:], xg[:, xsl], bcA[:], ALU.mult)
                k.op("vector", "tensor_tensor", xn32[:], tsc[:], bcB[:], ALU.add)
                k.op("vector", "tensor_copy", xnhi[:, osl], xn32[:])
                m_xn_done = k.op("vector", "tensor_tensor", xnlo[:, osl], xn32[:], xnhi[:, osl], ALU.subtract)
            if stop <= 4:
                m_lg_done = m_xn_done
                continue
            rw = [m_xn_done, m_scw]
            if m_lg_done is not None:
                rw.append(m_lg_done)
            for kt in range(DKT):
                osl = slice(kt * L + g * 512, kt * L + g * 512 + 512)
                chi = scw_hi[:, kt * 16:kt * 16 + 8]
                clo = scw_lo[:, kt * 16:kt * 16 + 8]
                whi = scw_hi[:, kt * 16 + 8:kt * 16 + 16]
                wlo = scw_lo[:, kt * 16 + 8:kt * 16 + 16]
                first, last = kt == 0, kt == DKT - 1
                k.op("tensor", "matmul", ps_route[0:8, 0:512], chi, xnhi[:, osl],
                     start=first, stop=False, wait=rw if first else None)
                k.op("tensor", "matmul", ps_route[0:8, 0:512], chi, xnlo[:, osl], start=False, stop=False)
                k.op("tensor", "matmul", ps_route[0:8, 0:512], clo, xnhi[:, osl], start=False, stop=last)
                k.op("tensor", "matmul", ps_route[0:8, 512:1024], whi, xnhi[:, osl], start=first, stop=False)
                k.op("tensor", "matmul", ps_route[0:8, 512:1024], whi, xnlo[:, osl], start=False, stop=False)
                mr = k.op("tensor", "matmul", ps_route[0:8, 512:1024], wlo, xnhi[:, osl],
                          start=False, stop=last, inc=(st, 1) if last else None)
            m_bcn = k.op("tensor", "matmul", ps_bc[0:8, 0:512], cnrow[0:1, 0:8], onesrow512[:],
                         start=True, stop=True, wait=[m_cn, m_cB], inc=(st, 1))
            m_brn = k.op("tensor", "matmul", ps_bc[0:8, 512:1024], onesrow[0:1, 0:8], rn[0:1, gs],
                         start=True, stop=True, wait=[m_rn], inc=(st, 1))
            k.op("vector", "tensor_copy", bccn[:], ps_bc[0:8, 0:512], wait=[m_bcn, m_brn, mr])
            k.op("vector", "tensor_copy", bcrn[:], ps_bc[0:8, 512:1024])
            k.op("vector", "tensor_tensor", lg[0:8, gs], ps_route[0:8, 0:512], bccn[:], ALU.mult)
            k.op("vector", "tensor_tensor", lg[0:8, gs], lg[0:8, gs], bcrn[:], ALU.mult)
            m_lg_done = k.op("vector", "tensor_tensor", lg[0:8, gs], lg[0:8, gs],
                             ps_route[0:8, 512:1024], ALU.add)
        if stop <= 5:
            k.op("gpsimd", "dma_start", out=o_xnhi[0:128, :], in_=xnhi[:, 0:L], wait=[m_lg_done], inc=(so, 16))
            k.op("gpsimd", "dma_start", out=o_gatesT, in_=gatesT[:], inc=(so, 16))
            k.wait("gpsimd", so, k.val(so))
            k.finalize()
            return nc
        m_trf = None
        for c in range(16):
            m_trf = k.tr(ps_tr[0:128, c * 8:(c + 1) * 8],
                 lg[0:8, c * 128:(c + 1) * 128], ident[0:8, 0:8],
                 wait=[m_lg_done, M_LDC] if c == 0 else None,
                 inc=(st, 1) if c == 15 else None)
        m_ltm = k.op("vector", "tensor_copy", ltm[:], ps_tr[0:128, 0:128], wait=[m_trf])
        if stop <= 6:
            k.op("gpsimd", "dma_start", out=o_xnhi[0:128, :], in_=xnhi[:, 0:L], wait=[m_ltm], inc=(so, 16))
            k.op("gpsimd", "dma_start", out=o_gatesT, in_=gatesT[:], inc=(so, 16))
            k.wait("gpsimd", so, k.val(so))
            k.finalize()
            return nc
        ltm3 = ltm[:].rearrange("p (c e) -> p c e", e=8)
        bview = lambda t: t[:].unsqueeze(2).broadcast_to([128, 16, 8])
        k.op("vector", "tensor_reduce", m1[:], ltm3, AX.X, ALU.max)
        k.op("vector", "tensor_tensor", eq[:].rearrange("p (c e) -> p c e", e=8), ltm3, bview(m1), ALU.is_ge)
        k.op("vector", "scalar_tensor_tensor", etm[:], eq[:], -1e30, ltm[:], ALU.mult, ALU.add)
        k.op("vector", "tensor_reduce", m2[:], etm[:].rearrange("p (c e) -> p c e", e=8), AX.X, ALU.max)
        k.op("vector", "tensor_tensor", keep[:].rearrange("p (c e) -> p c e", e=8), ltm3, bview(m2), ALU.is_ge)
        m_e = k.op("vector", "tensor_tensor", etm[:].rearrange("p (c e) -> p c e", e=8), ltm3, bview(m1), ALU.subtract)
        m_ex = k.op("scalar", "activation", etm[:], etm[:], AF.Exp, wait=[m_e])
        k.op("vector", "tensor_tensor", etm[:], etm[:], keep[:], ALU.mult, wait=[m_ex])
        k.op("vector", "tensor_reduce", ssum[:], etm[:].rearrange("p (c e) -> p c e", e=8), AX.X, ALU.add)
        k.op("vector", "reciprocal", ssum[:], ssum[:])
        m_gtm = k.op("vector", "tensor_tensor", gtm[:].rearrange("p (c e) -> p c e", e=8),
                     etm[:].rearrange("p (c e) -> p c e", e=8), bview(ssum), ALU.mult)
        if stop <= 7:
            k.op("gpsimd", "dma_start", out=o_xnhi[0:128, :], in_=xnhi[:, 0:L], wait=[m_gtm], inc=(so, 16))
            k.op("gpsimd", "dma_start", out=o_gatesT, in_=gatesT[:], inc=(so, 16))
            k.wait("gpsimd", so, k.val(so))
            k.finalize()
            return nc
        m_gt_ev = None
        for g in range(NTG):
            m_trb = None
            for j in range(4):
                c = g * 4 + j
                w = [m_gtm] if (g == 0 and j == 0) else ([m_gt_ev] if j == 0 else None)
                m_trb = k.tr(ps_tr[0:8, j * 128:(j + 1) * 128],
                     gtm[:, c * 8:(c + 1) * 8], ident[:], wait=w,
                     inc=(st, 1) if j == 3 else None)
            m_gt_ev = k.op("vector", "tensor_copy", gatesT[:, g * 512:(g + 1) * 512],
                           ps_tr[0:8, 0:512], wait=[m_trb])

        if stop <= 4:
            k.op("gpsimd", "dma_start", out=o_xnhi[0:128, :], in_=xnhi[:, 0:L], wait=[m_xn_done], inc=(so, 16))
            k.op("gpsimd", "dma_start", out=o_gatesT, in_=gatesT[:], inc=(so, 16))
            k.wait("gpsimd", so, k.val(so))
            k.finalize()
            return nc
        for kt in range(DKT):
            k.op("gpsimd", "dma_start", out=o_xnhi[kt * 128:(kt + 1) * 128, :],
                 in_=xnhi[:, kt * L:(kt + 1) * L],
                 wait=[m_xn_done] if kt == 0 else None, inc=(so, 16))
            k.op("gpsimd", "dma_start", out=o_xnlo[kt * 128:(kt + 1) * 128, :],
                 in_=xnlo[:, kt * L:(kt + 1) * L], inc=(so, 16))
        k.op("gpsimd", "dma_start", out=o_gatesT, in_=gatesT[:], wait=[m_gt_ev], inc=(so, 16))

        pin = DPool(k, "pin", "sync", 4)
        w32 = sbuf(k, "w32", [128, 2 * D], F32)
        w16 = sbuf(k, "w16", [128, 2 * D], BF16)
        cw = sbuf(k, "cw", [128, 2 * 4 * DKT], F32)
        cb = sbuf(k, "cb", [128, 2 * DKT], F32)
        xzrow = sbuf(k, "xzrow", [128, 2 * 2051], F32)
        acc1 = sbuf(k, "acc1", [128, L], F32)
        acc2 = sbuf(k, "acc2", [128, L], F32)
        outt = sbuf(k, "outt", [128, L], BF16)
        psC = [ps_route[0:128, 0:512], ps_route[0:128, 512:1024]]
        M_LN0V = k.mark("vector")
        for kt in range(DKT):
            k.op("sync", "dma_start", out=cw[:, kt * 4:kt * 4 + 4],
                 in_=cwx[kt * 128:(kt + 1) * 128, :], inc=(ld, 16))
            k.op("sync", "dma_start", out=cw[:, 32 + kt * 4:32 + kt * 4 + 4],
                 in_=cwz[kt * 128:(kt + 1) * 128, :], inc=(ld, 16))
        for kt in range(DKT):
            k.op("sync", "dma_start", out=cb[:, kt:kt + 1],
                 in_=cbx[kt * 128:(kt + 1) * 128, :], inc=(ld, 16))
            k.op("sync", "dma_start", out=cb[:, DKT + kt:DKT + kt + 1],
                 in_=cbz[kt * 128:(kt + 1) * 128, :], inc=(ld, 16))
        M_LD2 = (ld, k.val(ld))
        for bi in range(2):
            base = bi * 2051
            k.op("vector", "memset", xzrow[:, base:base + 1], 0.0)
            m_zero = k.op("vector", "memset", xzrow[:, base + 2049:base + 2051], 0.0)
        m_cast = [None, None]
        m_conv = [None, None]
        m_evac = [None, None]
        m_mmdone = [None, None]
        m_silu_prev = None
        for mt in range(16):
            bi = mt % 2
            wlw = [m_cast[bi]] if m_cast[bi] is not None else []
            mws = []
            for kt in range(DKT):
                mws.append(pin.dma(w32[:, bi * D + kt * 128:bi * D + (kt + 1) * 128],
                                   W_in[kt * 128:(kt + 1) * 128, mt * 128:(mt + 1) * 128],
                                   wait=wlw if kt == 0 else None))
            mc = None
            for kt in range(DKT):
                w = None
                if kt == 0:
                    w = list(mws)
                    if m_mmdone[bi] is not None:
                        w.append(m_mmdone[bi])
                mc = k.op("gpsimd", "tensor_copy",
                          w16[:, bi * D + kt * 128:bi * D + (kt + 1) * 128],
                          w32[:, bi * D + kt * 128:bi * D + (kt + 1) * 128], wait=w)
            m_cast[bi] = mc
            base = bi * 2051
            for g in range(NTG):
                pbank = (mt * NTG + g) % 2
                for kt in range(DKT):
                    w = []
                    if kt == 0:
                        if g == 0:
                            w = [mc, M_LD2]
                            if mt < 2:
                                w.append(M_LN0V)
                            if m_conv[bi] is not None:
                                w.append(m_conv[bi])
                        if m_evac[pbank] is not None:
                            w.append(m_evac[pbank])
                    m_mm = k.op("tensor", "matmul", psC[pbank],
                                w16[:, bi * D + kt * 128:bi * D + (kt + 1) * 128],
                                xnhi[:, kt * L + g * 512:kt * L + g * 512 + 512],
                                start=(kt == 0), stop=(kt == DKT - 1),
                                wait=w if w else None,
                                inc=(st, 1) if kt == DKT - 1 else None)
                m_mmdone[bi] = m_mm
                m_evac[pbank] = k.op("scalar", "copy",
                                     xzrow[:, base + 1 + g * 512:base + 1 + g * 512 + 512],
                                     psC[pbank],
                                     wait=[m_mm] + ([m_zero] if (mt == 0 and g == 0) else []))
                m_evac_last = m_evac[pbank]
            cwcol = (0 if mt < 8 else 32) + (mt % 8) * 4
            cbcol = (0 if mt < 8 else DKT) + (mt % 8)
            wfirst = [m_evac_last] + ([m_silu_prev] if m_silu_prev is not None else [])
            k.op("vector", "tensor_scalar", acc1[:], xzrow[:, base + 0:base + 2048],
                 cw[:, cwcol:cwcol + 1], None, ALU.mult, wait=wfirst)
            k.op("vector", "scalar_tensor_tensor", acc2[:], xzrow[:, base + 1:base + 1 + 2048],
                 cw[:, cwcol + 1:cwcol + 2], acc1[:], ALU.mult, ALU.add)
            k.op("vector", "scalar_tensor_tensor", acc1[:], xzrow[:, base + 2:base + 2 + 2048],
                 cw[:, cwcol + 2:cwcol + 3], acc2[:], ALU.mult, ALU.add)
            k.op("vector", "scalar_tensor_tensor", acc2[:], xzrow[:, base + 3:base + 3 + 2048],
                 cw[:, cwcol + 3:cwcol + 4], acc1[:], ALU.mult, ALU.add)
            m_conv[bi] = k.mark("vector")
            m_act = k.op("scalar", "activation", outt[:], acc2[:], AF.Silu,
                         bias=cb[:, cbcol:cbcol + 1], wait=[m_conv[bi]])
            m_silu_prev = m_act
            dst = o_xs if mt < 8 else o_zs
            row = (mt % 8) * 128
            k.op("scalar", "dma_start", out=dst[row:row + 128, :], in_=outt[:],
                 wait=[m_act], inc=(so, 16))
            k.wait("scalar", so, k.val(so))
        for s in pin.sems:
            k.wait("gpsimd", s, k.val(s))
        k.wait("gpsimd", so, k.val(so))
        k.wait("gpsimd", ld, k.val(ld))
        k.finalize()
    return nc




def build_proj(stop=9, qstop=9):
    nc = bass.Bass("TRN2", target_bir_lowering=False)
    xs = nc.dram_tensor("xs", [D, L], BF16, kind="ExternalInput").ap()
    W_q = nc.dram_tensor("W_q", [D, 768], F32, kind="ExternalInput").ap()
    W_k = nc.dram_tensor("W_k", [D, 768], F32, kind="ExternalInput").ap()
    W_v = nc.dram_tensor("W_v", [D, 1536], F32, kind="ExternalInput").ap()
    W_g = nc.dram_tensor("W_g", [D, 1536], F32, kind="ExternalInput").ap()
    W_gkb = nc.dram_tensor("W_gkb", [D, 8], F32, kind="ExternalInput").ap()
    dtb_bb = nc.dram_tensor("dtb_bb", [1, 8], F32, kind="ExternalInput").ap()
    A_log = nc.dram_tensor("A_log", [4, 1], F32, kind="ExternalInput").ap()
    HSELK = nc.dram_tensor("HSELK", [6 * 128, 4], F32, kind="ExternalInput").ap()
    HSELKT = nc.dram_tensor("HSELKT", [4, 6 * 128], F32, kind="ExternalInput").ap()
    caps = declare_consts(nc)
    o_qdm = nc.dram_tensor("qdm", [768, L], BF16, kind="ExternalOutput").ap()
    o_kdm = nc.dram_tensor("kdm", [768, L], BF16, kind="ExternalOutput").ap()
    o_ktm = nc.dram_tensor("ktm", [L, 768], BF16, kind="ExternalOutput").ap()
    o_vtm = nc.dram_tensor("vtm", [L, 1536], BF16, kind="ExternalOutput").ap()
    o_gtm = nc.dram_tensor("gtm", [L, 1536], BF16, kind="ExternalOutput").ap()
    o_G = nc.dram_tensor("G", [4, L], F32, kind="ExternalOutput").ap()
    o_beta = nc.dram_tensor("beta", [L, 4], F32, kind="ExternalOutput").ap()

    es = ExitStack()
    with es:
        k = K(nc, es)
        ld = k.sem("ld")
        st = k.sem("stn")
        pa = DPool(k, "pa", "sync", 4)
        pw = DPool(k, "pw", "sync", 4)
        po = DPool(k, "po", "gpsimd", 4)
        register_fconsts(k, caps, ld)
        ident = load_const(k, caps, "IDENT", ld)
        onesrow = load_const(k, caps, "ONESROW", ld)
        cumul = load_const(k, caps, "CUMUL", ld)
        M_CONST = (ld, k.val(ld))

        xsb = sbuf(k, "xsb", [128, DKT * L], BF16)
        wq16 = sbuf(k, "wq16", [128, DKT * 768], BF16)
        wk16 = sbuf(k, "wk16", [128, DKT * 768], BF16)
        wv16 = sbuf(k, "wv16", [128, DKT * 1536], BF16)
        wg16 = sbuf(k, "wg16", [128, DKT * 1536], BF16)
        wgkb16 = sbuf(k, "wgkb16", [128, DKT * 8], BF16)
        wld4 = sbuf(k, "wld4", [128, 4 * 1536], F32)
        hselk = sbuf(k, "hselk", [128, 6 * 4], F32)
        hselkt = sbuf(k, "hselkt", [4, 6 * 128], F32)
        dtbb = sbuf(k, "dtbb", [1, 8], F32)
        negA = sbuf(k, "negA", [4, 1], F32)
        qf = sbuf(k, "qf", [128, 6 * 512], F32)
        qsq = sbuf(k, "qsq", [128, 512], F32)
        rstdq = sbuf(k, "rstdq", [4, L], F32)
        rstdk = sbuf(k, "rstdk", [4, L], F32)
        rowsc = sbuf(k, "rowsc", [4, 512], F32)
        outb = sbuf(k, "outb", [128, 2 * 512], BF16)
        rstdTall = sbuf(k, "rstdTall", [128, 64], F32)
        gkball = sbuf(k, "gkball", [128, 256], F32)
        Grow = sbuf(k, "Grow", [4, L], F32)

        ps_a = psum(k, "ps_a", (128, 1024))
        ps_s = psum(k, "ps_s", (128, 512))
        ps_m = psum(k, "ps_m", (128, 512))

        for kt in range(DKT):
            pa.dma(xsb[:, kt * L:(kt + 1) * L], xs[kt * 128:(kt + 1) * 128, :])
        for t in range(6):
            pa.dma(hselk[:, t * 4:(t + 1) * 4], HSELK[t * 128:(t + 1) * 128, :])
        pa.dma(hselkt[0:4, :], HSELKT)
        pa.dma(dtbb[:], dtb_bb)
        pa.dma(negA[:], A_log)
        M_LDL = pa.all()
        M_LD = M_LDL[0]
        m_nA = k.op("scalar", "activation", negA[:], negA[:], AF.Exp, wait=M_LDL)
        m_nA = k.op("vector", "tensor_scalar", negA[:], negA[:], -1.0, None, ALU.mult, wait=[m_nA])

        wcasts = []

        def load_w16(dst, src_ap, width):
            for kt in range(DKT):
                i = len(wcasts)
                buf = wld4[:, (i % 4) * 1536:(i % 4) * 1536 + width]
                mdl = pw.dma(buf, src_ap[kt * 128:(kt + 1) * 128, :],
                             wait=[wcasts[-4]] if i >= 4 else None)
                wcasts.append(k.op("scalar", "activation",
                                   dst[:, kt * width:(kt + 1) * width], buf, AF.Copy,
                                   wait=[mdl]))
            return wcasts[-1]

        m_wq = load_w16(wq16, W_q, 768)
        m_wk = load_w16(wk16, W_k, 768)
        m_wgkb = load_w16(wgkb16, W_gkb, 8)
        m_wv = load_w16(wv16, W_v, 1536)
        m_wg = load_w16(wg16, W_g, 1536)

        m_bank = [None, None]
        m_pss = None
        m_psm = None
        m_outb = [None, None]

        def qk_dm(w16t, m_w, o_dm, rstd_rows):
            nonlocal m_pss
            m_bcmul = None
            for g in range(NTG):
                gs = slice(g * 512, (g + 1) * 512)
                for mt in range(6):
                    bank = mt % 2
                    m = None
                    for kt in range(DKT):
                        w = []
                        if kt == 0:
                            if g == 0 and mt == 0:
                                w += [m_w] + M_LDL
                            if m_bank[bank] is not None:
                                w += (m_bank[bank] if isinstance(m_bank[bank], list) else [m_bank[bank]])
                        m = k.op("tensor", "matmul", ps_a[0:128, bank * 512:bank * 512 + 512],
                                 w16t[:, kt * 768 + mt * 128:kt * 768 + (mt + 1) * 128],
                                 xsb[:, kt * L + g * 512:kt * L + g * 512 + 512],
                                 start=(kt == 0), stop=(kt == DKT - 1),
                                 wait=w or None, inc=(st, 1) if kt == DKT - 1 else None)
                    m_cp = k.op("vector", "tensor_copy", qf[:, mt * 512:(mt + 1) * 512],
                                ps_a[0:128, bank * 512:bank * 512 + 512], wait=[m])
                    m_bank[bank] = [m_cp]
                    if qstop <= 2:
                        continue
                    m_sq = k.op("scalar", "activation", qsq[:],
                                qf[:, mt * 512:(mt + 1) * 512], AF.Square, wait=[m_cp])
                    wsum = [m_sq]
                    if mt == 0 and m_pss is not None:
                        wsum.append(m_pss)
                    k.op("tensor", "matmul", ps_s[0:4, 0:512], hselk[:, mt * 4:(mt + 1) * 4], qsq[:],
                         start=(mt == 0), stop=(mt == 5), wait=wsum,
                         inc=(st, 1) if mt == 5 else None)
                if qstop <= 3:
                    continue
                m_srow = (st, k.val(st))
                k.op("scalar", "activation", rowsc[0:4, :], ps_s[0:4, 0:512], AF.Ln,
                     bias=1e-6, wait=[m_srow])
                m_r = k.op("scalar", "activation", rstd_rows[0:4, gs], rowsc[0:4, :],
                           AF.Exp, scale=-0.5)
                m_pss = m_r
                if qstop <= 4:
                    continue
                for mt in range(6):
                    wbc = [m_r]
                    if m_bcmul is not None:
                        wbc.append(m_bcmul)
                    m_bc = k.op("tensor", "matmul", ps_s[0:128, 0:512],
                                hselkt[0:4, mt * 128:(mt + 1) * 128], rstd_rows[0:4, gs],
                                start=True, stop=True, wait=wbc, inc=(st, 1))
                    par = mt % 2
                    ob = outb[:, par * 512:par * 512 + 512]
                    wmul = [m_bc]
                    if m_outb[par] is not None:
                        wmul.append(m_outb[par])
                    m_bcmul = k.op("vector", "tensor_tensor", ob, qf[:, mt * 512:(mt + 1) * 512],
                                   ps_s[0:128, 0:512], ALU.mult, wait=wmul)
                    m_outb[par] = po.dma(o_dm[mt * 128:(mt + 1) * 128, gs], ob,
                                         wait=[m_bcmul])
                m_pss = m_bcmul

        if stop >= 2:
            qk_dm(wq16, m_wq, o_qdm, rstdq)
        if stop >= 3:
            qk_dm(wk16, m_wk, o_kdm, rstdk)

        m = None
        for c in range(16):
            cs = slice(c * 128, (c + 1) * 128)
            for kt in range(DKT):
                stat = xsb[:, kt * L + c * 128:kt * L + (c + 1) * 128]
                w = []
                if c == 0 and kt == 0:
                    w = [m_wgkb, M_CONST] + ([m_psm] if m_psm is not None else [])
                k.op("tensor", "matmul", ps_m[0:128, c * 8:c * 8 + 8], stat,
                     wgkb16[:, kt * 8:(kt + 1) * 8],
                     start=(kt == 0), stop=False, wait=w or None)
            m = k.op("tensor", "matmul", ps_m[0:128, c * 8:c * 8 + 8], onesrow[0:1, :],
                     dtbb[:], start=False, stop=True,
                     wait=M_LDL if c == 0 else None, inc=(st, 1) if c == 15 else None)
        m_gkev = k.op("vector", "tensor_copy", gkball[:, 0:128], ps_m[0:128, 0:128], wait=[m])
        m_psm = m_gkev
        gkv = gkball[:, 0:128].rearrange("p (c x) -> p c x", x=8)
        gkc = gkball[:, 128:192]
        btc_ = gkball[:, 192:256]
        k.op("vector", "tensor_copy", gkc.rearrange("p (c x) -> p c x", x=4),
             gkv[:, :, 0:4])
        m_cpb = k.op("vector", "tensor_copy", btc_.rearrange("p (c x) -> p c x", x=4),
                     gkv[:, :, 4:8])
        m_ab = k.op("scalar", "activation", gkball[:, 0:64], gkc, AF.Abs, wait=[m_cpb])
        m_ab = k.op("scalar", "activation", gkball[:, 0:64], gkball[:, 0:64], AF.Exp,
                    scale=-1.0)
        m_ab = k.op("scalar", "activation", gkball[:, 0:64], gkball[:, 0:64], AF.Ln,
                    bias=1.0)
        m_mx = k.op("vector", "tensor_scalar", gkc, gkc, 0.0, None, ALU.max, wait=[m_ab])
        m_sp = k.op("vector", "tensor_tensor", gkc, gkc, gkball[:, 0:64], ALU.add)
        m_bt = k.op("scalar", "activation", btc_, btc_, AF.Sigmoid, wait=[m_sp])
        for c in range(16):
            po.dma(o_beta[c * 128:(c + 1) * 128, :], btc_[:, c * 4:c * 4 + 4],
                   wait=[m_bt] if c == 0 else None)
        m_Gs = None
        for cb in range(4):
            m_G = None
            for cc in range(4):
                c = cb * 4 + cc
                w = []
                if cc == 0:
                    w = [m_sp] + ([m_Gs] if m_Gs is not None else
                                  ([m_pss] if m_pss is not None else []))
                m_G = k.op("tensor", "matmul", ps_s[0:4, cc * 128:(cc + 1) * 128],
                           gkc[:, c * 4:c * 4 + 4], cumul[:],
                           start=True, stop=True, wait=w or None,
                           inc=(st, 1) if cc == 3 else None)
            m_Gs = k.op("vector", "tensor_scalar", Grow[0:4, cb * 512:(cb + 1) * 512],
                        ps_s[0:4, 0:512], negA[:, 0:1], None, ALU.mult,
                        wait=[m_G, m_nA] if cb == 0 else [m_G])
        m_pss = m_Gs

        m_trB = None
        for c in range(16):
            wtr = []
            if c == 0:
                wtr = [m_psm]
            m_trB = k.tr(ps_m[0:128, c * 4:c * 4 + 4], rstdk[0:4, c * 128:(c + 1) * 128],
                         ident[0:4, 0:4], wait=wtr or None, inc=(st, 1))
        m_rT = k.op("vector", "tensor_copy", rstdTall[:], ps_m[0:128, 0:64], wait=[m_trB])
        m_psm = m_rT

        for c in range(16 if stop >= 5 else 0):
            cs = slice(c * 128, (c + 1) * 128)
            for wt16, o_t in [(wv16, o_vtm), (wg16, o_gtm)]:
                for j in range(3):
                    bank = j % 2
                    m = None
                    for kt in range(DKT):
                        stat = xsb[:, kt * L + c * 128:kt * L + (c + 1) * 128]
                        w = []
                        if kt == 0:
                            if c == 0 and j == 0 and wt16 is wv16:
                                w += [m_wv, m_wg]
                            if m_bank[bank] is not None:
                                w += (m_bank[bank] if isinstance(m_bank[bank], list) else [m_bank[bank]])
                        m = k.op("tensor", "matmul", ps_a[0:128, bank * 512:bank * 512 + 512],
                                 stat, wt16[:, kt * 1536 + j * 512:kt * 1536 + (j + 1) * 512],
                                 start=(kt == 0), stop=(kt == DKT - 1),
                                 wait=w or None, inc=(st, 1) if kt == DKT - 1 else None)
                    par = j % 2
                    ob = outb[:, par * 512:par * 512 + 512]
                    w = [m]
                    if m_outb[par] is not None:
                        w.append(m_outb[par])
                    m_ev = k.op("vector", "tensor_copy", ob,
                                ps_a[0:128, bank * 512:bank * 512 + 512], wait=w)
                    m_bank[bank] = m_ev
                    m_outb[par] = po.dma(o_t[cs, j * 512:(j + 1) * 512], ob, wait=[m_ev])
            for j in range(2):
                bank = j % 2
                m = None
                for kt in range(DKT):
                    stat = xsb[:, kt * L + c * 128:kt * L + (c + 1) * 128]
                    w = []
                    if kt == 0 and m_bank[bank] is not None:
                        w += (m_bank[bank] if isinstance(m_bank[bank], list) else [m_bank[bank]])
                    if kt == 0 and c == 0 and j == 0:
                        w.append(m_rT)
                    m = k.op("tensor", "matmul", ps_a[0:128, bank * 512:bank * 512 + 384],
                             stat, wk16[:, kt * 768 + j * 384:kt * 768 + (j + 1) * 384],
                             start=(kt == 0), stop=(kt == DKT - 1),
                             wait=w or None, inc=(st, 1) if kt == DKT - 1 else None)
                par = j % 2
                ob = outb[:, par * 512:par * 512 + 384]
                w = [m]
                if m_outb[par] is not None:
                    w.append(m_outb[par])
                bview = rstdTall[:, c * 4 + 2 * j:c * 4 + 2 * j + 2].unsqueeze(2).broadcast_to([128, 2, 192])
                m_ev = k.op("vector", "tensor_tensor",
                            ob.rearrange("p (h d) -> p h d", d=192),
                            ps_a[0:128, bank * 512:bank * 512 + 384].rearrange("p (h d) -> p h d", d=192),
                            bview, ALU.mult, wait=w)
                m_bank[bank] = m_ev
                m_outb[par] = po.dma(o_ktm[cs, j * 384:(j + 1) * 384], ob, wait=[m_ev])
        if stop >= 7:
            po.dma(o_G, Grow[0:4, :], wait=[m_Gs])
        for pp in (po, pa, pw):
            for s in pp.sems:
                k.wait("gpsimd", s, k.val(s))
        k.wait("gpsimd", ld, k.val(ld))
        k.finalize()
    return nc


_HSLC = {0: [(0, 0, 128), (1, 0, 64)], 1: [(1, 64, 64), (2, 0, 128)],
         2: [(3, 0, 128), (4, 0, 64)], 3: [(4, 64, 64), (5, 0, 128)]}


def build_delta():
    nc = bass.Bass("TRN2", target_bir_lowering=False)
    qdm = nc.dram_tensor("qdm", [768, L], BF16, kind="ExternalInput").ap()
    kdm = nc.dram_tensor("kdm", [768, L], BF16, kind="ExternalInput").ap()
    ktm = nc.dram_tensor("ktm", [L, 768], BF16, kind="ExternalInput").ap()
    vtm = nc.dram_tensor("vtm", [L, 1536], BF16, kind="ExternalInput").ap()
    gtm = nc.dram_tensor("gtm", [L, 1536], BF16, kind="ExternalInput").ap()
    Gin = nc.dram_tensor("G", [4, L], F32, kind="ExternalInput").ap()
    betain = nc.dram_tensor("beta", [L, 4], F32, kind="ExternalInput").ap()
    gnwbc = nc.dram_tensor("gnwbc", [128, 384], F32, kind="ExternalInput").ap()
    caps = declare_consts(nc)
    o_og = nc.dram_tensor("ogdm", [1536, L], BF16, kind="ExternalOutput").ap()

    es = ExitStack()
    with es:
        k = K(nc, es)
        ldc = k.sem("ldc")
        st = k.sem("stn")
        pl = DPool(k, "pl", "sync", 4)
        po = DPool(k, "po", "gpsimd", 4)
        register_fconsts(k, caps, ldc)
        ident = load_const(k, caps, "IDENT", ldc)
        ident4 = load_const(k, caps, "IDENT4", ldc)
        onesrow = load_const(k, caps, "ONESROW", ldc)
        mstrict = load_const(k, caps, "MASK_STRICT", ldc)
        mstrictT = load_const(k, caps, "MASK_STRICT_T", ldc)
        mincl = load_const(k, caps, "MASK_INCL", ldc)
        sel4 = load_const(k, caps, "SEL4", ldc)

        qc = sbuf(k, "qc", [128, 2 * 768], BF16)
        kc = sbuf(k, "kc", [128, 2 * 768], BF16)
        ktc = sbuf(k, "ktc", [128, 2 * 768], BF16)
        vc = sbuf(k, "vc", [128, 2 * 1536], BF16)
        gc = sbuf(k, "gc", [128, 2 * 1536], BF16)
        btc = sbuf(k, "btc", [128, 2 * 4], F32)
        Gr = sbuf(k, "Gr", [4, L], F32)
        GBr = sbuf(k, "GBr", [4, L], F32)
        brow = sbuf(k, "brow", [4, L], F32)
        gnw = sbuf(k, "gnw", [128, 384], F32)
        id16 = sbuf(k, "id16", [128, 128], BF16)
        Gtm = sbuf(k, "Gtm", [128, 2 * 4], F32)
        GBtm = sbuf(k, "GBtm", [128, 2 * 4], F32)
        gamtm = sbuf(k, "gamtm", [128, 2 * 4], F32)
        gTrow = sbuf(k, "gTrow", [1, 2 * 4], F32)
        gTbc = sbuf(k, "gTbc", [128, 2 * 4], F32)
        egti = sbuf(k, "egti", [128, 2 * 4], F32)
        bgam = sbuf(k, "bgam", [128, 2 * 4], F32)
        nbgam = sbuf(k, "nbgam", [128, 2 * 4], F32)
        Mts16 = sbuf(k, "Mts16", [128, 2 * 512], BF16)
        Ms16 = sbuf(k, "Ms16", [128, 2 * 512], BF16)
        At16 = sbuf(k, "At16", [128, 2 * 512], BF16)
        Xb = sbuf(k, "Xb", [128, 2 * 512], BF16)
        Yb = sbuf(k, "Yb", [128, 2 * 512], BF16)
        Rb = sbuf(k, "Rb", [128, 512], BF16)
        esc = sbuf(k, "esc", [128, 512], F32)
        gq16 = sbuf(k, "gq16", [128, 2 * 768], BF16)
        bv16 = sbuf(k, "bv16", [128, 2 * 4 * 384], BF16)
        nbgk16 = sbuf(k, "nbgk16", [128, 2 * 4 * 192], BF16)
        ktg16 = sbuf(k, "ktg16", [128, 2 * 4 * 192], BF16)
        sg32 = sbuf(k, "sg32", [128, 2 * 4 * 384], BF16)
        nw16 = sbuf(k, "nw16", [128, 4 * 256], BF16)
        u16 = sbuf(k, "u16", [128, 4 * 384], BF16)
        S0f = sbuf(k, "S0f", [128, 4 * 768], F32)
        S016 = sbuf(k, "S016", [128, 4 * 768], BF16)
        of32 = sbuf(k, "of32", [128, 4 * 384], F32)
        osq = sbuf(k, "osq", [128, 4 * 384], F32)
        orst = sbuf(k, "orst", [128, 8], F32)
        og16 = sbuf(k, "og16", [128, 4 * 384], BF16)
        ogst = sbuf(k, "ogst", [128, 4 * 384], BF16)

        psKK = psum(k, "psKK", (128, 512))
        psN1 = psum(k, "psN1", (128, 512))
        psN2 = psum(k, "psN2", (128, 512))
        psN3 = psum(k, "psN3", (128, 512))
        psU = psum(k, "psU", (128, 512))
        psO = psum(k, "psO", (128, 512))
        psM = psum(k, "psM", (128, 512))
        psT16 = psum(k, "psT16", (128, 512), BF16)

        k.op("sync", "dma_start", out=Gr[0:4, :], in_=Gin, inc=(ldc, 16))
        k.op("sync", "dma_start", out=gnw[:], in_=gnwbc, inc=(ldc, 16))
        M_LDC = (ldc, k.val(ldc))
        m_id16 = k.op("vector", "tensor_copy", id16[:], ident[:], wait=[M_LDC])
        m_tr = None
        for c in range(16):
            mb = pl.dma(btc[:, 0:4], betain[c * 128:(c + 1) * 128, :],
                        wait=[m_tr] if m_tr is not None else None)
            m_t = k.tr(psM[0:4, 0:128], btc[:, 0:4], ident[:], wait=[mb], inc=(st, 1))
            m_tr = k.op("vector", "tensor_copy", brow[0:4, c * 128:(c + 1) * 128],
                        psM[0:4, 0:128], wait=[m_t])
        m_ln = k.op("scalar", "activation", GBr[0:4, :], brow[0:4, :], AF.Ln, wait=[m_tr])
        m_gb = k.op("vector", "tensor_tensor", GBr[0:4, :], GBr[0:4, :], Gr[0:4, :], ALU.add,
                    wait=[m_ln])
        m_s0a = k.op("vector", "memset", S0f[:], 0.0)
        m_s0 = k.op("vector", "memset", S016[:], 0.0)

        def load_chunk(c, waitfree):
            p = c % 2
            cs = slice(c * 128, (c + 1) * 128)
            first = True
            for t in range(6):
                pl.dma(qc[:, p * 768 + t * 128:p * 768 + (t + 1) * 128],
                       qdm[t * 128:(t + 1) * 128, cs], wait=waitfree if first else None)
                first = False
                pl.dma(kc[:, p * 768 + t * 128:p * 768 + (t + 1) * 128],
                       kdm[t * 128:(t + 1) * 128, cs])
            pl.dma(ktc[:, p * 768:(p + 1) * 768], ktm[cs, :])
            pl.dma(vc[:, p * 1536:(p + 1) * 1536], vtm[cs, :])
            pl.dma(gc[:, p * 1536:(p + 1) * 1536], gtm[cs, :])
            pl.dma(btc[:, p * 4:(p + 1) * 4], betain[cs, :])
            return pl.all()

        def hsl(buf, p, h):
            return [(buf[r0:r0 + rows, p * 768 + t * 128:p * 768 + (t + 1) * 128], rows)
                    for (t, r0, rows) in _HSLC[h]]

        HR = {h: (_HSLC[h][0][2], _HSLC[h][1][2]) for h in range(4)}
        HB = {h: (_HSLC[h][0][1], _HSLC[h][1][1]) for h in range(4)}
        SA_f = lambda h: S0f[HB[h][0]:HB[h][0] + HR[h][0], h * 768:h * 768 + 384]
        SB_f = lambda h: S0f[HB[h][1]:HB[h][1] + HR[h][1], h * 768 + 384:h * 768 + 768]
        SA = lambda h: S016[HB[h][0]:HB[h][0] + HR[h][0], h * 768:h * 768 + 384]
        SB = lambda h: S016[HB[h][1]:HB[h][1] + HR[h][1], h * 768 + 384:h * 768 + 768]

        e3 = esc[:].rearrange("p (h t) -> p h t", t=128)
        ps3 = lambda ps: ps[0:128, 0:512].rearrange("p (h t) -> p h t", t=128)
        mrep = lambda t: t[:].unsqueeze(1).broadcast_to([128, 4, 128])

        m_KK_free = None
        m_N1_free = None
        m_N2_free = None
        m_N3_free = None
        m_psU_free = None
        m_psO_free = None
        m_psM_free = None
        m_psT_free = None
        m_free = None
        m_chain_mm = [None, None]
        m_chain_S = [None, None]
        m_seqdone = {h: None for h in range(4)}
        m_of_h = {h: None for h in range(4)}
        m_ogst = [None, None, None]
        m_sq_h = {h: None for h in range(4)}
        m_trog_h = {h: None for h in range(4)}
        m_S16 = {h: m_s0 for h in range(4)}
        m_u16_h = {h: None for h in range(4)}

        M_CH = [None, None]
        PM = [None, None]

        def prep(c, M_L):
            p = c % 2
            cs = slice(c * 128, (c + 1) * 128)
            P4 = slice(p * 4, (p + 1) * 4)
            wch = [m_chain_S[p]] if m_chain_S[p] is not None else []
            wmm = [m_chain_mm[p]] if m_chain_mm[p] is not None else []
            nonlocal m_N3_free, m_KK_free, m_N1_free, m_N2_free
            wps = [m_N3_free] if m_N3_free is not None else []
            m_t1 = k.tr(psN3[0:128, 0:4], Gr[0:4, cs], ident[0:4, 0:4],
                        wait=[m_gb] + wps + wch, inc=(st, 1))
            m_c1 = k.op("vector", "tensor_copy", Gtm[:, P4], psN3[0:128, 0:4], wait=[m_t1])
            m_t2 = k.tr(psN3[0:128, 4:8], GBr[0:4, cs], ident[0:4, 0:4], inc=(st, 1))
            m_c2 = k.op("vector", "tensor_copy", GBtm[:, P4], psN3[0:128, 4:8], wait=[m_t2])
            m_t3 = k.tr(psN3[0:1, 8:12], Gr[0:4, c * 128 + 127:c * 128 + 128],
                        ident[0:4, 0:4], inc=(st, 1))
            m_c3 = k.op("vector", "tensor_copy", gTrow[0:1, P4], psN3[0:1, 8:12], wait=[m_t3])
            m_t4 = k.op("tensor", "matmul", psN3[0:128, 12:16], onesrow[:], gTrow[0:1, P4],
                        start=True, stop=True, wait=[m_c3], inc=(st, 1))
            m_c4 = k.op("vector", "tensor_copy", gTbc[:, P4], psN3[0:128, 12:16], wait=[m_t4])
            m_gam = k.op("scalar", "activation", gamtm[:, P4], Gtm[:, P4], AF.Exp, wait=[m_c1])
            m_eg = k.op("vector", "tensor_tensor", egti[:, P4], gTbc[:, P4], Gtm[:, P4],
                        ALU.subtract, wait=[m_c4])
            m_eg = k.op("scalar", "activation", egti[:, P4], egti[:, P4], AF.Exp, wait=[m_eg])
            m_gT = k.op("scalar", "activation", gTbc[:, P4], gTbc[:, P4], AF.Exp)
            m_bg = k.op("vector", "tensor_tensor", bgam[:, P4], btc[:, P4], gamtm[:, P4],
                        ALU.mult, wait=M_L + [m_gam])
            m_nbg = k.op("vector", "tensor_scalar", nbgam[:, P4], bgam[:, P4], -1.0, None,
                         ALU.mult)
            m_kk = None
            m_kq = None
            for h in range(4):
                ks_ = hsl(kc, p, h)
                qs_ = hsl(qc, p, h)
                hb = slice(h * 128, (h + 1) * 128)
                for i, (ap, rows) in enumerate(ks_):
                    w = []
                    if h == 0 and i == 0:
                        w += M_L + wmm
                        if m_KK_free is not None:
                            w.append(m_KK_free)
                    m_kk = k.op("tensor", "matmul", psKK[0:128, hb], ap, ap,
                                start=(i == 0), stop=(i == 1), wait=w or None,
                                inc=(st, 1) if (h == 3 and i == 1) else None)
                for i, ((kap, kr), (qap, qr)) in enumerate(zip(ks_, qs_)):
                    w = [m_c4] if (h == 0 and i == 0) else []
                    m_kq = k.op("tensor", "matmul", psN3[0:128, hb], kap, qap,
                                start=(i == 0), stop=(i == 1), wait=w or None,
                                inc=(st, 1) if (h == 3 and i == 1) else None)
            bGB = GBtm[:, P4].unsqueeze(2).broadcast_to([128, 4, 128])
            bG = Gtm[:, P4].unsqueeze(2).broadcast_to([128, 4, 128])
            S5 = slice(p * 512, (p + 1) * 512)
            m_b2 = None
            for h in range(4):
                w = [m_N2_free] if (h == 0 and m_N2_free is not None) else None
                m_b2 = k.op("tensor", "matmul", psN2[0:128, h * 128:(h + 1) * 128],
                            sel4[0:4, h * 128:(h + 1) * 128], GBr[0:4, cs],
                            start=True, stop=True, wait=w,
                            inc=(st, 1) if h == 3 else None)
            m_e = k.op("vector", "scalar_tensor_tensor", e3, ps3(psN2), 1.0, mrep(mstrict),
                       ALU.mult, ALU.add, wait=[m_b2, m_c1, m_c2])
            m_e = k.op("vector", "tensor_tensor", e3, e3, bG, ALU.subtract)
            m_e = k.op("scalar", "activation", esc[:], esc[:], AF.Exp, wait=[m_e])
            wv = [m_e, m_kk] + wmm
            m_mts = k.op("vector", "tensor_tensor", Mts16[:, S5], esc[:], psKK[0:128, 0:512],
                         ALU.mult, wait=wv)
            m_y0 = k.op("gpsimd", "tensor_tensor",
                        Yb[:, S5].rearrange("p (h t) -> p h t", t=128),
                        mrep(ident), Mts16[:, S5].rearrange("p (h t) -> p h t", t=128),
                        ALU.subtract, wait=[M_LDC, m_mts] + wmm)
            m_N2_free = m_e
            m_b1 = None
            for h in range(4):
                w = [m_N1_free] if (h == 0 and m_N1_free is not None) else None
                m_b1 = k.op("tensor", "matmul", psN1[0:128, h * 128:(h + 1) * 128],
                            sel4[0:4, h * 128:(h + 1) * 128], Gr[0:4, cs],
                            start=True, stop=True, wait=w,
                            inc=(st, 1) if h == 3 else None)
            m_e = k.op("vector", "scalar_tensor_tensor", e3, ps3(psN1), -1.0, bGB,
                       ALU.mult, ALU.add, wait=[m_b1])
            m_e = k.op("vector", "tensor_tensor", e3, e3, mrep(mstrictT), ALU.add)
            m_e = k.op("scalar", "activation", esc[:], esc[:], AF.Exp, wait=[m_e])
            m_ms = k.op("vector", "tensor_tensor", Ms16[:, S5], esc[:], psKK[0:128, 0:512],
                        ALU.mult, wait=[m_e])
            m_KK_free = m_ms
            m_x0 = k.op("gpsimd", "tensor_tensor",
                        Xb[:, S5].rearrange("p (h t) -> p h t", t=128),
                        mrep(ident), Ms16[:, S5].rearrange("p (h t) -> p h t", t=128),
                        ALU.subtract, wait=[m_ms, m_y0])
            m_N1_free = m_e
            m_b3 = None
            for h in range(4):
                w = [m_N2_free] if h == 0 else None
                m_b3 = k.op("tensor", "matmul", psN2[0:128, h * 128:(h + 1) * 128],
                            sel4[0:4, h * 128:(h + 1) * 128], Gr[0:4, cs],
                            start=True, stop=True, wait=w,
                            inc=(st, 1) if h == 3 else None)
            m_e = k.op("vector", "scalar_tensor_tensor", e3, ps3(psN2), 1.0, mrep(mincl),
                       ALU.mult, ALU.add, wait=[m_b3])
            m_e = k.op("vector", "tensor_tensor", e3, e3, bG, ALU.subtract)
            m_e = k.op("scalar", "activation", esc[:], esc[:], AF.Exp, wait=[m_e])
            m_at = k.op("vector", "tensor_tensor", At16[:, S5], esc[:], psN3[0:128, 0:512],
                        ALU.mult, wait=[m_e, m_kq] + wmm)
            m_N2_free = m_e
            m_N3_free = m_at
            m_b4 = None
            for h in range(4):
                w = [m_N1_free] if h == 0 else None
                m_b4 = k.op("tensor", "matmul", psN1[0:128, h * 128:(h + 1) * 128],
                            sel4[0:4, h * 128:(h + 1) * 128], Gr[0:4, cs],
                            start=True, stop=True, wait=w,
                            inc=(st, 1) if h == 3 else None)
            m_ge = k.op("vector", "tensor_copy", esc[:], psN1[0:128, 0:512], wait=[m_b4, m_at])
            m_N1_free = m_ge
            m_ge = k.op("scalar", "activation", esc[:], esc[:], AF.Exp, wait=[m_ge])
            m_gq = None
            PB = p * 768
            for t, hh in [(0, 0), (2, 1), (3, 2), (5, 3)]:
                m_gq = k.op("gpsimd", "tensor_tensor", gq16[:, PB + t * 128:PB + (t + 1) * 128],
                            qc[:, PB + t * 128:PB + (t + 1) * 128],
                            esc[:, hh * 128:(hh + 1) * 128], ALU.mult,
                            wait=[m_ge] + wmm if t == 0 else None)
            m_gq = k.op("gpsimd", "tensor_tensor", gq16[0:64, PB + 128:PB + 256],
                        qc[0:64, PB + 128:PB + 256], esc[0:64, 0:128], ALU.mult)
            m_gq = k.op("gpsimd", "tensor_tensor", gq16[64:128, PB + 128:PB + 256],
                        qc[64:128, PB + 128:PB + 256], esc[64:128, 128:256], ALU.mult)
            m_gq = k.op("gpsimd", "tensor_tensor", gq16[0:64, PB + 512:PB + 640],
                        qc[0:64, PB + 512:PB + 640], esc[0:64, 256:384], ALU.mult)
            m_gq = k.op("gpsimd", "tensor_tensor", gq16[64:128, PB + 512:PB + 640],
                        qc[64:128, PB + 512:PB + 640], esc[64:128, 384:512], ALU.mult)
            S5r = lambda t: t[:, S5]
            m_xa, m_ya = m_x0, m_x0
            for itn in range(6):
                m_ax = None
                for h in range(4):
                    hb = slice(p * 512 + h * 128, p * 512 + (h + 1) * 128)
                    hbr = slice(h * 128, (h + 1) * 128)
                    w = []
                    if h == 0:
                        w = [m_x0, m_id16] if itn == 0 else [m_xa]
                        if itn == 0 and m_KK_free is not None:
                            w.append(m_KK_free)
                    k.op("tensor", "matmul", psKK[0:128, hbr], Mts16[:, hb], Xb[:, hb],
                         start=True, stop=False, wait=w or None)
                    m_ax = k.op("tensor", "matmul", psKK[0:128, hbr], id16[:], Xb[:, hb],
                                start=False, stop=True,
                                inc=(st, 1) if h == 3 else None)
                m_r = k.op("vector", "scalar_tensor_tensor", Rb[:], psKK[0:128, 0:512], -1.0,
                           ident4[:], ALU.mult, ALU.add, wait=[m_ax])
                m_xr = None
                for h in range(4):
                    hb = slice(p * 512 + h * 128, p * 512 + (h + 1) * 128)
                    hbr = slice(h * 128, (h + 1) * 128)
                    k.op("tensor", "matmul", psN1[0:128, hbr], Yb[:, hb], Rb[:, hbr],
                         start=True, stop=True,
                         wait=[m_r, m_ya, m_N1_free] if h == 0 else None)
                    m_xr = k.op("tensor", "matmul", psN2[0:128, hbr], Rb[:, hbr], Yb[:, hb],
                                start=True, stop=True,
                                wait=[m_N2_free] if h == 0 else None,
                                inc=(st, 1) if h == 3 else None)
                m_xa = k.op("vector", "tensor_tensor", Xb[:, S5], Xb[:, S5],
                            psN1[0:128, 0:512], ALU.add, wait=[m_xr])
                m_ya = k.op("vector", "tensor_tensor", Yb[:, S5], Yb[:, S5],
                            psN2[0:128, 0:512], ALU.add)
                m_N1_free = m_xa
                m_N2_free = m_ya
            m_KK_free = m_r
            m_bv = m_nk = m_kg = m_sg = None
            for h in range(4):
                vslc = vc[:, p * 1536 + h * 384:p * 1536 + (h + 1) * 384]
                gslc = gc[:, p * 1536 + h * 384:p * 1536 + (h + 1) * 384]
                ktslc = ktc[:, p * 768 + h * 192:p * 768 + (h + 1) * 192]
                w0 = M_L + wmm if h == 0 else []
                m_bv = k.op("scalar", "activation", bv16[:, p * 1536 + h * 384:p * 1536 + (h + 1) * 384],
                            vslc, AF.Copy, scale=btc[:, p * 4 + h:p * 4 + h + 1],
                            wait=w0 + [m_bg])
                m_nk = k.op("scalar", "activation", nbgk16[:, p * 768 + h * 192:p * 768 + (h + 1) * 192],
                            ktslc, AF.Copy, scale=nbgam[:, p * 4 + h:p * 4 + h + 1],
                            wait=[m_nbg] if h == 0 else None)
                m_kg = k.op("scalar", "activation", ktg16[:, p * 768 + h * 192:p * 768 + (h + 1) * 192],
                            ktslc, AF.Copy, scale=egti[:, p * 4 + h:p * 4 + h + 1],
                            wait=[m_eg] if h == 0 else None)
                m_sg = k.op("scalar", "activation", sg32[:, p * 1536 + h * 384:p * 1536 + (h + 1) * 384],
                            gslc, AF.Silu, wait=None)
            return dict(T=m_ya, at=m_at, gq=m_gq, bv=m_bv, nk=m_nk, kg=m_kg, sg=m_sg,
                        gT=m_gT, kk=m_kk, kq=m_kq)

        def chain(c, pm):
            p = c % 2
            cs = slice(c * 128, (c + 1) * 128)
            S5 = slice(p * 512, (p + 1) * 512)
            nonlocal m_psU_free, m_psO_free, m_psM_free, m_psT_free
            m_lastmm = None
            m_lastS = None
            for h in range(4):
                hb = slice(p * 512 + h * 128, p * 512 + (h + 1) * 128)
                R1, R2 = HR[h]
                B1, B2 = HB[h]
                BV = bv16[:, p * 1536 + h * 384:p * 1536 + (h + 1) * 384]
                NK = nbgk16[:, p * 768 + h * 192:p * 768 + (h + 1) * 192]
                KG = ktg16[:, p * 768 + h * 192:p * 768 + (h + 1) * 192]
                SG = sg32[:, p * 1536 + h * 384:p * 1536 + (h + 1) * 384]
                NW = nw16[:, h * 256:(h + 1) * 256]
                U = u16[:, h * 384:(h + 1) * 384]
                OF = of32[:, h * 384:(h + 1) * 384]
                OG = og16[:, h * 384:(h + 1) * 384]
                OGS = ogst[:, h * 384:(h + 1) * 384]
                wW = [pm["T"], pm["nk"]]
                if m_psM_free is not None:
                    wW.append(m_psM_free)
                if m_seqdone[h] is not None:
                    wW.append(m_seqdone[h])
                m_w1 = k.op("tensor", "matmul", psM[B1:B1 + R1, 0:128], NK[:, 0:R1],
                            Yb[:, hb], start=True, stop=True, wait=wW, inc=(st, 1))
                m_w2 = k.op("tensor", "matmul", psM[B2:B2 + R2, 128:256], NK[:, R1:192],
                            Yb[:, hb], start=True, stop=True, inc=(st, 1))
                m_nw = k.op("vector", "tensor_copy", NW[B1:B1 + R1, 0:128],
                            psM[B1:B1 + R1, 0:128], wait=[m_w1, m_w2])
                m_nw = k.op("vector", "tensor_copy", NW[B2:B2 + R2, 128:256],
                            psM[B2:B2 + R2, 128:256])
                wu = [pm["bv"], m_nw]
                if m_psU_free is not None:
                    wu.append(m_psU_free)
                k.op("tensor", "matmul", psU[0:128, 0:384], Yb[:, hb], BV,
                     start=True, stop=False, wait=wu)
                k.op("tensor", "matmul", psU[0:128, 0:384], NW[B1:B1 + R1, 0:128], SA(h),
                     start=False, stop=False, wait=[m_S16[h]])
                m_u = k.op("tensor", "matmul", psU[0:128, 0:384], NW[B2:B2 + R2, 128:256],
                           SB(h), start=False, stop=True, inc=(st, 1))
                m_u16 = k.op("scalar", "copy", U, psU[0:128, 0:384], wait=[m_u])
                m_u16_h[h] = m_u16
                (q1, r1), (q2, r2) = hsl(gq16, p, h)
                wo = [pm["gq"]]
                if m_psO_free is not None:
                    wo.append(m_psO_free)
                k.op("tensor", "matmul", psO[0:128, 0:384], q1, SA(h),
                     start=True, stop=False, wait=wo)
                k.op("tensor", "matmul", psO[0:128, 0:384], q2, SB(h),
                     start=False, stop=False)
                m_o = k.op("tensor", "matmul", psO[0:128, 0:384], At16[:, hb], U,
                           start=False, stop=True, wait=[m_u16], inc=(st, 1))
                k.op("tensor", "matmul", psU[B1:B1 + R1, 0:384], KG[:, 0:R1], U,
                     start=True, stop=True, wait=[pm["kg"], m_u16])
                m_s2 = k.op("tensor", "matmul", psM[B2:B2 + R2, 0:384], KG[:, R1:192], U,
                            start=True, stop=True, inc=(st, 1))
                m_sh = k.op("vector", "scalar_tensor_tensor", SA_f(h), SA_f(h),
                            gTbc[B1:B1 + R1, p * 4 + h:p * 4 + h + 1], psU[B1:B1 + R1, 0:384],
                            ALU.mult, ALU.add, wait=[m_s2, pm["gT"], m_s0a])
                m_sl = k.op("vector", "scalar_tensor_tensor", SB_f(h), SB_f(h),
                            gTbc[B2:B2 + R2, p * 4 + h:p * 4 + h + 1], psM[B2:B2 + R2, 0:384],
                            ALU.mult, ALU.add)
                m_psU_free = m_sh
                m_psM_free = m_sl
                m_sc = k.op("gpsimd", "tensor_copy", SA(h), SA_f(h), wait=[m_sh])
                m_sc = k.op("gpsimd", "tensor_copy", SB(h), SB_f(h), wait=[m_sl])
                m_S16[h] = m_sc
                wof = [m_o]
                if m_sq_h[0] is not None:
                    wof.append(m_sq_h[0])
                m_of = k.op("scalar", "activation", OF, psO[0:128, 0:384], AF.Copy,
                            wait=wof)
                m_psO_free = m_of
                m_of_h[h] = m_of
                m_seqdone[h] = m_o
                m_lastmm = m_s2
                m_lastS = m_sl
            p15 = slice(p * 1536, (p + 1) * 1536)
            m_sqB = k.op("scalar", "activation", osq[:], of32[:], AF.Square,
                         wait=[m_of_h[3]])
            m_rd = k.op("vector", "tensor_reduce", orst[:, 0:4],
                        osq[:].rearrange("p (h d) -> p h d", d=384), AX.X, ALU.add,
                        wait=[m_sqB])
            m_r1 = k.op("scalar", "activation", orst[:, 4:8], orst[:, 0:4],
                        AF.Ln, scale=1.0 / 384.0, bias=EPS_LN, wait=[m_rd])
            m_r2 = k.op("scalar", "activation", orst[:, 4:8], orst[:, 4:8],
                        AF.Exp, scale=-0.5)
            m_t1_ = None
            for h in range(4):
                m_t1_ = k.op("vector", "tensor_scalar", of32[:, h * 384:(h + 1) * 384],
                             of32[:, h * 384:(h + 1) * 384], orst[:, 4 + h:5 + h], None,
                             ALU.mult, wait=[m_r2] if h == 0 else None)
            m_t2_ = k.op("gpsimd", "tensor_tensor", of32[:], of32[:],
                         sg32[:, p15], ALU.mult, wait=[m_t1_, pm["sg"]])
            wt3 = [M_LDC, m_t2_]
            if m_trog_h[0] is not None:
                wt3.append(m_trog_h[0])
            m_t3_ = k.op("gpsimd", "tensor_tensor",
                         og16[:].rearrange("p (h d) -> p h d", d=384),
                         of32[:].rearrange("p (h d) -> p h d", d=384),
                         gnw[:].unsqueeze(1).broadcast_to([128, 4, 384]), ALU.mult,
                         wait=wt3)
            m_sq_h[0] = m_t3_
            for blk in range(3):
                m_trog = None
                for j in range(4):
                    col = blk * 512 + j * 128
                    w = [m_t3_]
                    if j == 0 and m_psT_free is not None:
                        w.append(m_psT_free)
                    m_trog = k.tr(psT16[0:128, j * 128:(j + 1) * 128],
                                  og16[:, col:col + 128], id16[:],
                                  wait=w if j == 0 else None,
                                  inc=(st, 1) if j == 3 else None)
                if blk == 2:
                    m_trog_h[0] = m_trog
                wog = [m_trog]
                if m_ogst[blk] is not None:
                    wog.append(m_ogst[blk])
                m_ogev = k.op("vector", "tensor_copy", ogst[:, blk * 512:(blk + 1) * 512],
                              psT16[0:128, 0:512], wait=wog)
                m_psT_free = m_ogev
                for j in range(4):
                    col = blk * 512 + j * 128
                    m_ogst[blk] = po.dma(o_og[col:col + 128, cs],
                                         ogst[:, col:col + 128],
                                         wait=[m_ogev] if j == 0 else None)
            return m_lastmm, m_lastS

        M_CH[0] = load_chunk(0, [m_gb])
        PM[0] = prep(0, M_CH[0])
        for c in range(16):
            p = c % 2
            if c + 1 < 16:
                M_CH[1 - p] = load_chunk(c + 1, m_free)
                PM[1 - p] = prep(c + 1, M_CH[1 - p])
            mm, mS = chain(c, PM[p])
            m_chain_mm[p] = mm
            m_chain_S[p] = mS
            m_free = [PM[p]["kk"], PM[p]["kq"], PM[p]["gq"], PM[p]["kg"], PM[p]["sg"],
                      PM[p]["bv"]]
        for s in po.sems:
            k.wait("gpsimd", s, k.val(s))
        k.finalize()
    return nc


def build_ffnh():
    nc = bass.Bass("TRN2", target_bir_lowering=False)
    xn = nc.dram_tensor("xn", [D, L], BF16, kind="ExternalInput").ap()
    og = nc.dram_tensor("og", [1536, L], BF16, kind="ExternalInput").ap()
    gatesT = nc.dram_tensor("gatesT", [8, L], F32, kind="ExternalInput").ap()
    w1 = nc.dram_tensor("w1", [8 * D, 512], F32, kind="ExternalInput").ap()
    b1T = nc.dram_tensor("b1T", [8 * 128, 4], F32, kind="ExternalInput").ap()
    W_so = nc.dram_tensor("W_so", [1536, D], F32, kind="ExternalInput").ap()
    caps = declare_consts(nc)
    o_ht = nc.dram_tensor("ht", [4096, L], BF16, kind="ExternalOutput").ap()
    o_yst = nc.dram_tensor("yst", [D, L], BF16, kind="ExternalOutput").ap()

    es = ExitStack()
    with es:
        k = K(nc, es)
        ld = k.sem("ld")
        st = k.sem("stn")
        pa = DPool(k, "pa", "sync", 4)
        pw = DPool(k, "pw", "sync", 4)
        po = DPool(k, "po", "gpsimd", 4)
        register_fconsts(k, caps, ld)
        sel8 = load_const(k, caps, "SEL8", ld)

        xnb = sbuf(k, "xnb", [128, DKT * L], BF16)
        ogb = sbuf(k, "ogb", [128, 12 * L], BF16)
        gts = sbuf(k, "gts", [8, L], F32)
        wso16 = sbuf(k, "wso16", [128, 12 * D], BF16)
        wstg = sbuf(k, "wstg", [128, 4 * D], F32)
        w116 = sbuf(k, "w116", [128, 2 * DKT * 512], BF16)
        b1s = sbuf(k, "b1s", [128, 8 * 4], F32)
        bcg = sbuf(k, "bcg", [128, 2 * 512], F32)
        hstage = sbuf(k, "hstage", [128, 2 * 512], F32)
        outb = sbuf(k, "outb", [128, 2 * 512], BF16)
        ps = [psum(k, "psA", (128, 512)), psum(k, "psB", (128, 512))]
        psg = psum(k, "psg", (128, 512))

        casts = []
        for kt in range(12):
            sl = len(casts) % 4
            wl = [casts[-4]] if len(casts) >= 4 else None
            mdl = pw.dma(wstg[:, sl * D:(sl + 1) * D], W_so[kt * 128:(kt + 1) * 128, :],
                         wait=wl)
            casts.append(k.op("scalar", "activation", wso16[:, kt * D:(kt + 1) * D],
                              wstg[:, sl * D:(sl + 1) * D], AF.Copy, wait=[mdl]))
        m_wso = casts[-1]

        for kt in range(12):
            pa.dma(ogb[:, kt * L:(kt + 1) * L], og[kt * 128:(kt + 1) * 128, :])
        M_OG = pa.all()
        for kt in range(DKT):
            pa.dma(xnb[:, kt * L:(kt + 1) * L], xn[kt * 128:(kt + 1) * 128, :])
        pa.dma(gts[0:8, :], gatesT)
        for e in range(8):
            pa.dma(b1s[:, e * 4:(e + 1) * 4], b1T[e * 128:(e + 1) * 128, :])
        M_LD = pa.all()

        m_bank = [None, None]
        m_outb = [None, None]
        for g in range(NTG):
            gs = slice(g * 512, (g + 1) * 512)
            for mt in range(DKT):
                bank = mt % 2
                m = None
                for kt in range(12):
                    w = []
                    if kt == 0:
                        if g == 0 and mt == 0:
                            w += [m_wso] + M_OG
                        if m_bank[bank] is not None:
                            w.append(m_bank[bank])
                    m = k.op("tensor", "matmul", ps[bank][0:128, 0:512],
                             wso16[:, kt * D + mt * 128:kt * D + (mt + 1) * 128],
                             ogb[:, kt * L + g * 512:kt * L + g * 512 + 512],
                             start=(kt == 0), stop=(kt == 11),
                             wait=w or None, inc=(st, 1) if kt == 11 else None)
                par = mt % 2
                ob = outb[:, par * 512:par * 512 + 512]
                w = [m]
                if m_outb[par] is not None:
                    w.append(m_outb[par])
                m_ev = k.op("vector", "tensor_copy", ob, ps[bank][0:128, 0:512], wait=w)
                m_bank[bank] = m_ev
                m_outb[par] = po.dma(o_yst[mt * 128:(mt + 1) * 128, gs], ob, wait=[m_ev])

        w1casts = []
        m_bcg = [None, None]
        m_hfree = [None, None]
        mm_used = {}

        def w1slot(i):
            return wstg[:, (i % 8) * 512:(i % 8) * 512 + 512]

        for e in range(8):
            pwp = e % 2
            for kt in range(DKT):
                i = len(w1casts)
                wld = []
                if i >= 8:
                    wld.append(w1casts[-8])
                if i == 0:
                    wld.append(m_wso)
                if e >= 2 and kt == 0:
                    wld.append(mm_used[e - 2])
                mdl = pw.dma(w1slot(i), w1[e * D + kt * 128:e * D + (kt + 1) * 128, :],
                             wait=wld or None)
                w1casts.append(k.op("scalar", "activation",
                                    w116[:, pwp * DKT * 512 + kt * 512:pwp * DKT * 512 + (kt + 1) * 512],
                                    w1slot(i), AF.Copy, wait=[mdl]))
            m_w1c = w1casts[-1]
            for g in range(NTG):
                gs = slice(g * 512, (g + 1) * 512)
                gp = g % 2
                m_gb = k.op("tensor", "matmul", psg[0:128, 0:512],
                            sel8[0:8, e * 128:(e + 1) * 128], gts[0:8, gs],
                            start=True, stop=True,
                            wait=([m_bcg[gp]] if m_bcg[gp] is not None else M_LD),
                            inc=(st, 1))
                m_bc = k.op("vector", "tensor_copy", bcg[:, gp * 512:gp * 512 + 512],
                            psg[0:128, 0:512], wait=[m_gb])
                m_bcg[gp] = m_bc
                for mt in range(4):
                    bank = mt % 2
                    m = None
                    for kt in range(DKT):
                        w = []
                        if kt == 0:
                            w.append(m_w1c)
                            if m_bank[bank] is not None:
                                w.append(m_bank[bank])
                        m = k.op("tensor", "matmul", ps[bank][0:128, 0:512],
                                 w116[:, pwp * DKT * 512 + kt * 512 + mt * 128:
                                       pwp * DKT * 512 + kt * 512 + (mt + 1) * 128],
                                 xnb[:, kt * L + g * 512:kt * L + g * 512 + 512],
                                 start=(kt == 0), stop=(kt == DKT - 1),
                                 wait=w or None, inc=(st, 1) if kt == DKT - 1 else None)
                    mm_used[e] = m
                    hp = mt % 2
                    hst = hstage[:, hp * 512:hp * 512 + 512]
                    m_h = k.op("scalar", "activation", hst, ps[bank][0:128, 0:512],
                               AF.Gelu, bias=b1s[:, e * 4 + mt:e * 4 + mt + 1],
                               wait=[m] + ([m_hfree[hp]] if m_hfree[hp] is not None else []))
                    m_bank[bank] = m_h
                    par = mt % 2
                    ob = outb[:, par * 512:par * 512 + 512]
                    w = [m_h, m_bc]
                    if m_outb[par] is not None:
                        w.append(m_outb[par])
                    m_ht = k.op("vector", "tensor_tensor", ob, hst,
                                bcg[:, gp * 512:gp * 512 + 512], ALU.mult, wait=w)
                    m_hfree[hp] = m_ht
                    row = e * 512 + mt * 128
                    m_outb[par] = po.dma(o_ht[row:row + 128, gs], ob, wait=[m_ht])
        for s in po.sems:
            k.wait("gpsimd", s, k.val(s))
        k.finalize()
    return nc


def build_resid():
    nc = bass.Bass("TRN2", target_bir_lowering=False)
    yst = nc.dram_tensor("yst", [D, L], BF16, kind="ExternalInput").ap()
    zs = nc.dram_tensor("zs", [D, L], BF16, kind="ExternalInput").ap()
    ht = nc.dram_tensor("ht", [4096, L], BF16, kind="ExternalInput").ap()
    xT = nc.dram_tensor("xT", [D, L], F32, kind="ExternalInput").ap()
    W_out = nc.dram_tensor("W_out", [2 * D, D], F32, kind="ExternalInput").ap()
    w2 = nc.dram_tensor("w2", [8 * 512, D], F32, kind="ExternalInput").ap()
    b2 = nc.dram_tensor("b2", [8, D], F32, kind="ExternalInput").ap()
    gatesT = nc.dram_tensor("gatesT", [8, L], F32, kind="ExternalInput").ap()
    caps = declare_consts(nc)
    o_x1 = nc.dram_tensor("x1T", [D, L], F32, kind="ExternalOutput").ap()
    o_xn1 = nc.dram_tensor("xn1", [D, L], BF16, kind="ExternalOutput").ap()

    es = ExitStack()
    with es:
        k = K(nc, es)
        ld = k.sem("ld")
        st = k.sem("stn")
        so = k.sem("so")
        pw = DPool(k, "pw", "sync", 4)
        pg = DPool(k, "pg", "sync", 4)
        po = DPool(k, "po", "gpsimd", 4)
        register_fconsts(k, caps, ld)
        onescol = load_const(k, caps, "ONESCOL", ld)
        onesrow = load_const(k, caps, "ONESROW", ld)

        wout16 = sbuf(k, "wout16", [128, 16 * D], BF16)
        w216 = sbuf(k, "w216", [128, 32 * D], BF16)
        wstg = sbuf(k, "wstg", [128, 2 * D], F32)
        b2s = sbuf(k, "b2s", [8, D], F32)
        gts = sbuf(k, "gts", [8, L], F32)
        ystg = sbuf(k, "ystg", [128, DKT * 512], BF16)
        zsg = sbuf(k, "zsg", [128, DKT * 512], BF16)
        htg = sbuf(k, "htg", [128, 32 * 512], BF16)
        xg = sbuf(k, "xg", [128, DKT * 512], F32)
        x1s = xg
        xsq = sbuf(k, "xsq", [128, 2 * 512], F32)
        rowsg = sbuf(k, "rowsg", [1, 4 * 512], F32)
        bcA = sbuf(k, "bcA", [128, 512], F32)
        bcB = sbuf(k, "bcB", [128, 512], F32)
        xnt = sbuf(k, "xnt", [128, 2 * 512], BF16)
        ps = [psum(k, "psA", (128, 512)), psum(k, "psB", (128, 512))]
        ps_s = psum(k, "ps_s", (128, 1024))
        ps_bc = psum(k, "ps_bc", (128, 512))

        k.op("sync", "dma_start", out=gts[0:8, :], in_=gatesT, inc=(ld, 16))
        k.op("sync", "dma_start", out=b2s[0:8, :], in_=b2, inc=(ld, 16))
        M_LD = (ld, k.val(ld))
        m_c = None
        casts = []
        for j, (wsrc, n, wdst) in enumerate([(W_out, 16, wout16), (w2, 32, w216)]):
            for kt in range(n):
                parw = len(casts) % 2
                wl = [casts[-2]] if len(casts) >= 2 else None
                mdl = pw.dma(wstg[:, parw * D:(parw + 1) * D],
                             wsrc[kt * 128:(kt + 1) * 128, :], wait=wl)
                m_c = k.op("scalar", "activation", wdst[:, kt * D:(kt + 1) * D],
                           wstg[:, parw * D:(parw + 1) * D], AF.Copy, wait=[mdl])
                casts.append(m_c)
        m_wc = m_c

        m_bank = [None, None]
        m_xn = {}
        m_x1done = {}
        m_x1store = {}
        m_mmlast = {}
        m_xnst = [None, None]
        m_ev = None
        nsq = 0
        for g in range(NTG):
            gs = slice(g * 512, (g + 1) * 512)
            if g == 0:
                for kt in range(DKT):
                    pg.dma(ystg[:, kt * 512:(kt + 1) * 512], yst[kt * 128:(kt + 1) * 128, gs])
                    pg.dma(zsg[:, kt * 512:(kt + 1) * 512], zs[kt * 128:(kt + 1) * 128, gs])
                for kt in range(32):
                    pg.dma(htg[:, kt * 512:(kt + 1) * 512], ht[kt * 128:(kt + 1) * 128, gs])
            for kt in range(DKT):
                wx = None
                if g > 0:
                    wx = [m_xn[(g - 1, kt)], m_x1store[(g - 1, kt)]]
                pg.dma(xg[:, kt * 512:(kt + 1) * 512], xT[kt * 128:(kt + 1) * 128, gs],
                       wait=wx)
            M_G = pg.all()
            for mt in range(DKT):
                bank = mt % 2
                wmm = list(M_G) if mt < 2 else []
                if g == 0 and mt == 0:
                    wmm += [m_wc, M_LD]
                if m_bank[bank] is not None:
                    wmm.append(m_bank[bank])
                first = True
                for kt in range(DKT):
                    k.op("tensor", "matmul", ps[bank][0:128, 0:512],
                         wout16[:, kt * D + mt * 128:kt * D + (mt + 1) * 128],
                         ystg[:, kt * 512:(kt + 1) * 512],
                         start=first, stop=False, wait=wmm if first else None)
                    first = False
                for kt in range(DKT):
                    k.op("tensor", "matmul", ps[bank][0:128, 0:512],
                         wout16[:, (8 + kt) * D + mt * 128:(8 + kt) * D + (mt + 1) * 128],
                         zsg[:, kt * 512:(kt + 1) * 512], start=False, stop=False)
                for kt in range(32):
                    k.op("tensor", "matmul", ps[bank][0:128, 0:512],
                         w216[:, kt * D + mt * 128:kt * D + (mt + 1) * 128],
                         htg[:, kt * 512:(kt + 1) * 512], start=False, stop=False)
                m = k.op("tensor", "matmul", ps[bank][0:128, 0:512],
                         b2s[0:8, mt * 128:(mt + 1) * 128], gts[0:8, gs],
                         start=False, stop=True, inc=(st, 1))
                m_mmlast[(g, mt)] = m
                m_x1t = k.op("vector", "tensor_tensor", x1s[:, mt * 512:(mt + 1) * 512],
                             ps[bank][0:128, 0:512], xg[:, mt * 512:(mt + 1) * 512], ALU.add,
                             wait=[m])
                m_bank[bank] = m_x1t
                m_x1done[(g, mt)] = m_x1t
                m_x1store[(g, mt)] = po.dma(o_x1[mt * 128:(mt + 1) * 128, gs],
                                            x1s[:, mt * 512:(mt + 1) * 512], wait=[m_x1t])
                buf = xsq[:, (nsq % 2) * 512:(nsq % 2) * 512 + 512]
                wsq = [m_x1t]
                if nsq >= 2:
                    wsq.append((st, k.val(st)))
                m_sq = k.op("scalar", "activation", buf, x1s[:, mt * 512:(mt + 1) * 512],
                            AF.Square, wait=wsq)
                wst1 = [m_x1t]
                if mt == 0 and m_ev is not None:
                    wst1.append(m_ev)
                k.op("tensor", "matmul", ps_s[0:1, 0:512], onescol[:],
                     x1s[:, mt * 512:(mt + 1) * 512],
                     start=(mt == 0), stop=(mt == DKT - 1), wait=wst1)
                k.op("tensor", "matmul", ps_s[0:1, 512:1024], onescol[:], buf,
                     start=(mt == 0), stop=(mt == DKT - 1), wait=[m_sq],
                     inc=(st, 1) if mt == DKT - 1 else None)
                nsq += 1
            m_stats = (st, k.val(st))
            if g + 1 < NTG:
                gs2 = slice((g + 1) * 512, (g + 2) * 512)
                mlast = m_mmlast[(g, DKT - 1)]
                for kt in range(DKT):
                    pg.dma(ystg[:, kt * 512:(kt + 1) * 512],
                           yst[kt * 128:(kt + 1) * 128, gs2], wait=[mlast] if kt == 0 else None)
                    pg.dma(zsg[:, kt * 512:(kt + 1) * 512], zs[kt * 128:(kt + 1) * 128, gs2])
                for kt in range(32):
                    pg.dma(htg[:, kt * 512:(kt + 1) * 512], ht[kt * 128:(kt + 1) * 128, gs2])
            Rg = lambda i: rowsg[0:1, i * 512:(i + 1) * 512]
            k.op("vector", "tensor_copy", Rg(0), ps_s[0:1, 0:512], wait=[m_stats])
            m_ev = k.op("vector", "tensor_copy", Rg(1), ps_s[0:1, 512:1024])
            k.op("vector", "tensor_scalar", Rg(0), Rg(0), 1.0 / D, None, ALU.mult)
            k.op("vector", "tensor_scalar", Rg(1), Rg(1), 1.0 / D, None, ALU.mult)
            k.op("vector", "tensor_tensor", Rg(2), Rg(0), Rg(0), ALU.mult)
            m_var = k.op("vector", "tensor_tensor", Rg(1), Rg(1), Rg(2), ALU.subtract)
            k.op("scalar", "activation", Rg(2), Rg(1), AF.Ln, bias=EPS_LN, wait=[m_var])
            m_rstd = k.op("scalar", "activation", Rg(3), Rg(2), AF.Exp, scale=-0.5)
            k.op("vector", "tensor_tensor", Rg(2), Rg(0), Rg(3), ALU.mult, wait=[m_rstd])
            m_nm = k.op("vector", "tensor_scalar", Rg(2), Rg(2), -1.0, None, ALU.mult)
            m_bc1 = k.op("tensor", "matmul", ps_bc[0:128, 0:512], onesrow[:], Rg(3),
                         start=True, stop=True, wait=[m_nm], inc=(st, 1))
            m_cA = k.op("vector", "tensor_copy", bcA[:], ps_bc[0:128, 0:512], wait=[m_bc1])
            m_bc2 = k.op("tensor", "matmul", ps_bc[0:128, 0:512], onesrow[:], Rg(2),
                         start=True, stop=True, wait=[m_cA], inc=(st, 1))
            m_cB = k.op("vector", "tensor_copy", bcB[:], ps_bc[0:128, 0:512], wait=[m_bc2])
            for mt in range(DKT):
                par = mt % 2
                ob = xnt[:, par * 512:par * 512 + 512]
                w = [m_cB]
                if m_xnst[par] is not None:
                    w.append(m_xnst[par])
                k.op("vector", "tensor_tensor", xsq[:, 0:512], x1s[:, mt * 512:(mt + 1) * 512],
                     bcA[:], ALU.mult, wait=w)
                m_x = k.op("vector", "tensor_tensor", ob, xsq[:, 0:512], bcB[:], ALU.add)
                m_xn[(g, mt)] = m_x
                po.dma(o_xn1[mt * 128:(mt + 1) * 128, gs], ob, wait=[m_x])
                m_xnst[par] = (po.sems[(po.i - 1) % 4], k.val(po.sems[(po.i - 1) % 4]))
        for s in po.sems:
            k.wait("gpsimd", s, k.val(s))
        k.finalize()
    return nc


def build_l1attn():
    nc = bass.Bass("TRN2", target_bir_lowering=False)
    xn1 = nc.dram_tensor("xn1", [D, L], BF16, kind="ExternalInput").ap()
    x1T = nc.dram_tensor("x1T", [D, L], F32, kind="ExternalInput").ap()
    WkT = nc.dram_tensor("WkT", [D, D], F32, kind="ExternalInput").ap()
    WvT = nc.dram_tensor("WvT", [D, D], F32, kind="ExternalInput").ap()
    WrT = nc.dram_tensor("WrT", [D, D], F32, kind="ExternalInput").ap()
    WoT = nc.dram_tensor("WoT", [D, D], F32, kind="ExternalInput").ap()
    wcol = nc.dram_tensor("wcol", [D, 1], F32, kind="ExternalInput").ap()
    ucol = nc.dram_tensor("ucol", [D, 1], F32, kind="ExternalInput").ap()
    mixk = nc.dram_tensor("mixk", [D, 1], F32, kind="ExternalInput").ap()
    mixv = nc.dram_tensor("mixv", [D, 1], F32, kind="ExternalInput").ap()
    mixr = nc.dram_tensor("mixr", [D, 1], F32, kind="ExternalInput").ap()
    phi2 = nc.dram_tensor("phi2", [D, L], F32, kind="ExternalInput").ap()
    caps = declare_consts(nc)
    o_pres = nc.dram_tensor("pres", [D, L], F32, kind="ExternalOutput").ap()
    o_E = nc.dram_tensor("E16", [L, L], BF16, kind="ExternalOutput").ap()
    o_er = nc.dram_tensor("erinv", [L, 1], F32, kind="ExternalOutput").ap()

    es = ExitStack()
    with es:
        k = K(nc, es)
        ld = k.sem("ld")
        st = k.sem("stn")
        pa = DPool(k, "pa", "sync", 4)
        pw = DPool(k, "pw", "sync", 4)
        po = DPool(k, "po", "gpsimd", 4)
        register_fconsts(k, caps, ld)
        id4 = load_const(k, caps, "IDENT4", ld)
        M_ID = (ld, k.val(ld))

        xn1p = sbuf(k, "xn1p", [128, DKT * 2049], BF16)
        w16a = sbuf(k, "w16a", [128, DKT * D], BF16)
        w16b = sbuf(k, "w16b", [128, DKT * D], BF16)
        w16c = sbuf(k, "w16c", [128, DKT * D], BF16)
        wstg = sbuf(k, "wstg", [128, 2 * D], F32)
        rwkvb = sbuf(k, "rwkvb", [128, DKT * L], BF16)
        cols = sbuf(k, "cols", [128, 8 * 8], F32)
        es2 = ExitStack()
        s2 = lambda name, shape, dt=F32: es2.enter_context(
            nc.sbuf_tensor("sb_" + name, list(shape), dt))
        xkg = s2("xkg", [128, DKT * 512], BF16)
        xvg = s2("xvg", [128, DKT * 512], BF16)
        xrg = s2("xrg", [128, DKT * 512], BF16)
        mixd = s2("mixd", [128, 512], BF16)
        albc8 = s2("albc8", [128, DKT * 513], F32)
        ekg = [s2("ekg0", [128, 513], F32), s2("ekg1", [128, 513], F32)]
        ekvg = [s2("ekvg0", [128, 513], F32), s2("ekvg1", [128, 513], F32)]
        vg = [s2("vg0", [128, 512], F32), s2("vg1", [128, 512], F32)]
        srg = [s2("srg0", [128, 512], BF16), s2("srg1", [128, 512], BF16)]
        dent = [s2("dent0", [128, 512], F32), s2("dent1", [128, 512], F32)]
        eukg = [s2("eukg0", [128, 512], F32), s2("eukg1", [128, 512], F32)]
        euv = [s2("euv0", [128, 512], F32), s2("euv1", [128, 512], F32)]
        Pt = [s2("Pt0", [128, 513], F32), s2("Pt1", [128, 513], F32)]
        Qt = [s2("Qt0", [128, 513], F32), s2("Qt1", [128, 513], F32)]
        PC = s2("PC", [128, 8], F32)
        QC = s2("QC", [128, 8], F32)
        psK = [psum(k, "psK0", (128, 512)), psum(k, "psK1", (128, 512))]
        psV = [psum(k, "psV0", (128, 512)), psum(k, "psV1", (128, 512))]
        psR = [psum(k, "psR0", (128, 512)), psum(k, "psR1", (128, 512))]

        for kt in range(DKT):
            for j, src_ap in enumerate([wcol, ucol, mixk, mixv, mixr]):
                pa.dma(cols[:, kt * 8 + j:kt * 8 + j + 1],
                       src_ap[kt * 128:(kt + 1) * 128, :])
        M_COLS = pa.all()
        m_z = None
        for kt in range(DKT):
            m_z = k.op("vector", "memset", xn1p[:, kt * 2049:kt * 2049 + 1], 0.0)
        for kt in range(DKT):
            pa.dma(xn1p[:, kt * 2049 + 1:(kt + 1) * 2049], xn1[kt * 128:(kt + 1) * 128, :])
        M_XN = pa.all()
        m_a_kt = []
        for kt in range(DKT):
            k.op("scalar", "activation", albc8[:, kt * 513:kt * 513 + 1],
                 cols[:, kt * 8:kt * 8 + 1], AF.Exp, wait=M_COLS if kt == 0 else None)
            m_a_kt.append(k.op("scalar", "activation", albc8[:, kt * 513:kt * 513 + 1],
                               albc8[:, kt * 513:kt * 513 + 1], AF.Exp, scale=-1.0))
        m_alb = None
        for kt in range(DKT):
            k.op("vector", "tensor_scalar", albc8[:, kt * 513 + 1:(kt + 1) * 513],
                 id4[:, 0:512], 0.0, albc8[:, kt * 513:kt * 513 + 1], ALU.mult,
                 op1=ALU.add, wait=[m_a_kt[kt]] + ([M_ID] if kt == 0 else []))
            m_alb = k.op("vector", "memset", albc8[:, kt * 513:kt * 513 + 1], 1.0)
        m_pcz = k.op("vector", "memset", PC[:], 0.0)
        m_qcz = k.op("vector", "memset", QC[:], 0.0)

        casts = []
        for wsrc, wdst in [(WkT, w16a), (WvT, w16b), (WrT, w16c)]:
            for kt in range(DKT):
                i = len(casts)
                sl = wstg[:, (i % 2) * D:(i % 2) * D + D]
                mdl = pw.dma(sl, wsrc[kt * 128:(kt + 1) * 128, :],
                             wait=[casts[-2]] if i >= 2 else None)
                casts.append(k.op("scalar", "activation", wdst[:, kt * D:(kt + 1) * D],
                                  sl, AF.Copy, wait=[mdl]))
        m_wkvr = casts[-1]

        m_ek = [None, None]
        m_v = [None, None]
        m_sr = [None, None]
        m_ekv = [None, None]
        m_scanP = [None, None]
        m_scanQ = [None, None]
        m_num = [None, None]
        m_den = [None, None]
        m_recip = [None, None]
        m_t = [None, None]
        m_rwkv = [None, None]
        m_cbP = [None, None]
        m_cbQ = [None, None]
        m_lastmm = None
        for g in range(NTG):
            m_mixg = None
            for kt in range(DKT):
                xx = xn1p[:, kt * 2049 + 1 + g * 512:kt * 2049 + 1 + g * 512 + 512]
                xp = xn1p[:, kt * 2049 + g * 512:kt * 2049 + g * 512 + 512]
                for j, dst in [(2, xkg), (3, xvg), (4, xrg)]:
                    ds = dst[:, kt * 512:(kt + 1) * 512]
                    w = []
                    if kt == 0 and j == 2:
                        w += (M_XN + ([m_lastmm] if m_lastmm is not None else []))
                        if g == 0:
                            w += M_COLS + [m_z]
                    k.op("vector", "tensor_tensor", mixd[:], xx, xp, ALU.subtract,
                         wait=w or None)
                    m_mixg = k.op("vector", "scalar_tensor_tensor", ds, mixd[:],
                                  cols[:, kt * 8 + j:kt * 8 + j + 1], xp,
                                  ALU.mult, ALU.add)
            for mt in range(DKT):
                p = mt % 2
                def grp(ps_, w16_, rhs_, extra):
                    m = None
                    for kt in range(DKT):
                        w = list(extra) if kt == 0 else []
                        m = k.op("tensor", "matmul", ps_[0:128, 0:512],
                                 w16_[:, kt * D + mt * 128:kt * D + (mt + 1) * 128],
                                 rhs_[:, kt * 512:(kt + 1) * 512],
                                 start=(kt == 0), stop=(kt == DKT - 1), wait=w or None,
                                 inc=(st, 1) if kt == DKT - 1 else None)
                    return m

                ex = [m_mixg]
                if g == 0 and mt == 0:
                    ex.append(m_wkvr)
                if m_ek[p] is not None:
                    ex.append(m_ek[p])
                mmK = grp(psK[p], w16a, xkg, ex)
                mmV = grp(psV[p], w16b, xvg,
                          [m_v[p]] if m_v[p] is not None else [])
                mmR = grp(psR[p], w16c, xrg,
                          [m_sr[p]] if m_sr[p] is not None else [])
                m_lastmm = mmR
                w = [mmK]
                if m_den[p] is not None:
                    w.append(m_den[p])
                if g == 0 and mt < 2:
                    w.append(m_alb)
                m_ek[p] = k.op("scalar", "activation", ekg[p][:, 1:513],
                               psK[p][0:128, 0:512], AF.Exp, wait=w)
                m_euek = k.op("scalar", "activation", eukg[p][:],
                              psK[p][0:128, 0:512], AF.Exp,
                              bias=cols[:, mt * 8 + 1:mt * 8 + 2])
                m_ek[p] = m_euek
                w = [m_pcz] if (g == 0 and mt < 2) else [m_scanP[p]]
                m_cinP = k.op("scalar", "activation", ekvg[p][:, 0:1], PC[:, mt:mt + 1],
                              AF.Copy, wait=w)
                w = [m_qcz] if (g == 0 and mt < 2) else [m_scanQ[p]]
                m_cinQ = k.op("scalar", "activation", ekg[p][:, 0:1], QC[:, mt:mt + 1],
                              AF.Copy, wait=w)
                m_v[p] = k.op("scalar", "activation", vg[p][:], psV[p][0:128, 0:512],
                              AF.Copy, wait=[mmV] + ([m_t[p]] if m_t[p] is not None else []))
                m_sr[p] = k.op("scalar", "activation", srg[p][:], psR[p][0:128, 0:512],
                               AF.Sigmoid, wait=[mmR] + ([m_rwkv[p]] if m_rwkv[p] is not None else []))
                m_ekv[p] = k.op("gpsimd", "tensor_tensor", ekvg[p][:, 1:513],
                                ekg[p][:, 1:513], vg[p][:], ALU.mult,
                                wait=[m_ek[p], m_v[p]])
                m_euv = k.op("gpsimd", "tensor_tensor", euv[p][:], eukg[p][:],
                             vg[p][:], ALU.mult)
                w = [m_ekv[p], m_cinP]
                if m_num[p] is not None:
                    w.append(m_num[p])
                if m_cbP[p] is not None:
                    w.append(m_cbP[p])
                m_scanP[p] = k.op("vector", "tensor_tensor_scan", Pt[p][:, 0:513],
                                  albc8[:, mt * 513:(mt + 1) * 513], ekvg[p][:, 0:513],
                                  0.0, ALU.mult, ALU.add, wait=w)
                w = [m_cinQ]
                if m_den[p] is not None:
                    w.append(m_den[p])
                if m_cbQ[p] is not None:
                    w.append(m_cbQ[p])
                m_scanQ[p] = k.op("vector", "tensor_tensor_scan", Qt[p][:, 0:513],
                                  albc8[:, mt * 513:(mt + 1) * 513], ekg[p][:, 0:513],
                                  0.0, ALU.mult, ALU.add, wait=w)
                m_cbP[p] = k.op("scalar", "activation", PC[:, mt:mt + 1],
                                Pt[p][:, 512:513], AF.Copy, wait=[m_scanP[p]])
                m_cbQ[p] = k.op("scalar", "activation", QC[:, mt:mt + 1],
                                Qt[p][:, 512:513], AF.Copy, wait=[m_scanQ[p]])
                m_num[p] = k.op("gpsimd", "tensor_tensor", vg[p][:], euv[p][:],
                                Pt[p][:, 0:512], ALU.add, wait=[m_scanP[p]])
                m_den[p] = k.op("gpsimd", "tensor_tensor", dent[p][:], eukg[p][:],
                                Qt[p][:, 0:512], ALU.add, wait=[m_scanQ[p]])
                m_recip[p] = k.op("vector", "reciprocal", dent[p][:], dent[p][:],
                                  wait=[m_den[p]])
                m_t[p] = k.op("gpsimd", "tensor_tensor", ekvg[p][:, 1:513], vg[p][:],
                              dent[p][:], ALU.mult, wait=[m_recip[p]])
                m_rwkv[p] = k.op("gpsimd", "tensor_tensor",
                                 rwkvb[:, mt * L + g * 512:mt * L + g * 512 + 512],
                                 ekvg[p][:, 1:513], srg[p][:], ALU.mult, wait=[m_sr[p]])

        es2.close()
        MV2 = k.mark("vector")
        M_P2 = [m_lastmm, m_rwkv[0], m_rwkv[1], MV2]
        phi16 = sbuf(k, "phi16", [128, DKT * L], BF16)
        phstg = sbuf(k, "phstg", [128, 2 * L], F32)
        estage = sbuf(k, "estage", [128, 2 * 512], F32)
        e16s = sbuf(k, "e16s", [128, 2 * 512], BF16)
        erp = sbuf(k, "erp", [128, 4], F32)
        ercols = sbuf(k, "ercols", [128, 16], F32)
        prest = sbuf(k, "prest", [128, 2 * 512], F32)
        x1g = sbuf(k, "x1g", [128, 2 * 512], F32)

        m_c2 = m_wkvr
        for kt in range(DKT):
            i = len(casts)
            sl = wstg[:, (i % 2) * D:(i % 2) * D + D]
            w = [casts[-2]]
            if kt == 0:
                w.append(m_ek[0])
                w.append(m_ek[1])
            mdl = pw.dma(sl, WoT[kt * 128:(kt + 1) * 128, :], wait=w)
            casts.append(k.op("scalar", "activation", w16a[:, kt * D:(kt + 1) * D],
                              sl, AF.Copy, wait=[mdl]))
        m_wo = casts[-1]
        m_phc = None
        for kt in range(DKT):
            sl = phstg[:, (kt % 2) * L:(kt % 2) * L + L]
            w = [m_phc] if kt >= 2 else list(M_P2)
            mdl = pw.dma(sl, phi2[kt * 128:(kt + 1) * 128, :], wait=w)
            m_phc = k.op("scalar", "activation", phi16[:, kt * L:(kt + 1) * L],
                         sl, AF.Copy, wait=[mdl] + (M_P2 if kt == 0 else []))
        m_ph = m_phc

        psE = [psV[0], psV[1]]
        m_eE = [m_v[0], m_v[1]]
        m_c16 = [None, None]
        m_red = [None, None]
        m_stE = [None, None]
        idxE = 0
        for mtE in range(16):
            for ng in range(NTG):
                parE = idxE % 2
                idxE += 1
                m = None
                for kt in range(DKT):
                    w = []
                    if kt == 0:
                        if mtE == 0 and ng == 0:
                            w.append(m_ph)
                        if m_eE[parE] is not None:
                            w.append(m_eE[parE])
                    m = k.op("tensor", "matmul", psE[parE][0:128, 0:512],
                             xn1p[:, kt * 2049 + 1 + mtE * 128:kt * 2049 + 1 + (mtE + 1) * 128],
                             phi16[:, kt * L + ng * 512:kt * L + ng * 512 + 512],
                             start=(kt == 0), stop=(kt == DKT - 1), wait=w or None,
                             inc=(st, 1) if kt == DKT - 1 else None)
                w = [m]
                for mm_ in (m_c16[parE], m_red[parE]):
                    if mm_ is not None:
                        w.append(mm_)
                m_eE[parE] = k.op("scalar", "activation", estage[:, parE * 512:parE * 512 + 512],
                                  psE[parE][0:128, 0:512], AF.Exp, wait=w)
                m_red[parE] = k.op("vector", "tensor_reduce", erp[:, ng:ng + 1],
                                   estage[:, parE * 512:parE * 512 + 512], AX.X, ALU.add,
                                   wait=[m_eE[parE]])
                m_c16[parE] = k.op("scalar", "activation", e16s[:, parE * 512:parE * 512 + 512],
                                   estage[:, parE * 512:parE * 512 + 512], AF.Copy,
                                   wait=[m_eE[parE]] + ([m_stE[parE]] if m_stE[parE] is not None else []))
                m_stE[parE] = po.dma(o_E[mtE * 128:(mtE + 1) * 128, ng * 512:ng * 512 + 512],
                                     e16s[:, parE * 512:parE * 512 + 512], wait=[m_c16[parE]])
            k.op("vector", "tensor_reduce", ercols[:, mtE:mtE + 1], erp[:, 0:4], AX.X, ALU.add)
            k.op("vector", "reciprocal", ercols[:, mtE:mtE + 1], ercols[:, mtE:mtE + 1])
        m_finE = k.mark("vector")
        for mtE in range(16):
            po.dma(o_er[mtE * 128:(mtE + 1) * 128, 0:1], ercols[:, mtE:mtE + 1],
                   wait=[m_finE] if mtE == 0 else None)

        pso = [psK[0], psK[1]]
        m_pr = [None, None]
        m_bank2 = [m_ek[0], m_ek[1]]
        first3 = True
        for g in range(NTG):
            gs = slice(g * 512, (g + 1) * 512)
            for mt in range(DKT):
                par = mt % 2
                wld = [m_pr[par]] if m_pr[par] is not None else (
                    M_P2 if (g == 0 and mt < 2) else None)
                M_X1 = pa.dma(x1g[:, par * 512:par * 512 + 512],
                              x1T[mt * 128:(mt + 1) * 128, gs], wait=wld)
                m = None
                for kt in range(DKT):
                    w = []
                    if kt == 0:
                        if first3:
                            w += [m_wo, m_rwkv[0], m_rwkv[1]]
                            first3 = False
                        if m_bank2[par] is not None:
                            w.append(m_bank2[par])
                    m = k.op("tensor", "matmul", pso[par][0:128, 0:512],
                             w16a[:, kt * D + mt * 128:kt * D + (mt + 1) * 128],
                             rwkvb[:, kt * L + g * 512:kt * L + g * 512 + 512],
                             start=(kt == 0), stop=(kt == DKT - 1), wait=w or None,
                             inc=(st, 1) if kt == DKT - 1 else None)
                m_p = k.op("vector", "tensor_tensor", prest[:, par * 512:par * 512 + 512],
                           pso[par][0:128, 0:512], x1g[:, par * 512:par * 512 + 512],
                           ALU.add, wait=[m, M_X1] + ([m_pr[par]] if m_pr[par] is not None else []))
                m_bank2[par] = m_p
                m_pr[par] = po.dma(o_pres[mt * 128:(mt + 1) * 128, gs],
                                   prest[:, par * 512:par * 512 + 512], wait=[m_p])
        for s in po.sems:
            k.wait("gpsimd", s, k.val(s))
        k.finalize()
    return nc


def build_logits():
    nc = bass.Bass("TRN2", target_bir_lowering=False)
    xn1 = nc.dram_tensor("xn1", [D, L], BF16, kind="ExternalInput").ap()
    phi2 = nc.dram_tensor("phi2", [D, L], F32, kind="ExternalInput").ap()
    caps = declare_consts(nc)
    o_E = nc.dram_tensor("E16", [L, L], BF16, kind="ExternalOutput").ap()
    o_er = nc.dram_tensor("erinv", [L, 1], F32, kind="ExternalOutput").ap()

    es = ExitStack()
    with es:
        k = K(nc, es)
        ld = k.sem("ld")
        ldw = k.sem("ldw")
        st = k.sem("stn")
        so = k.sem("so")
        register_fconsts(k, caps, ld)

        xn = sbuf(k, "xn", [128, DKT * L], BF16)
        phi16 = sbuf(k, "phi16", [128, DKT * L], BF16)
        wstg = sbuf(k, "wstg", [128, 2 * L], F32)
        estage = sbuf(k, "estage", [128, 2 * 512], F32)
        e16s = sbuf(k, "e16s", [128, 2 * 512], BF16)
        erp = sbuf(k, "erp", [128, 4], F32)
        ercols = sbuf(k, "ercols", [128, 16], F32)
        ps2 = [psum(k, "psA", (128, 512)), psum(k, "psB", (128, 512))]

        pin = DPool(k, "pin", "sync", 4)
        for kt in range(DKT):
            pin.dma(xn[:, kt * L:(kt + 1) * L], xn1[kt * 128:(kt + 1) * 128, :])
        M_XNL = pin.all()
        m_ph = None
        for kt in range(DKT):
            parw = kt % 2
            w = [m_ph] if kt >= 2 else None
            mdl = pin.dma(wstg[:, parw * L:(parw + 1) * L],
                          phi2[kt * 128:(kt + 1) * 128, :], wait=w)
            m_ph = k.op("scalar", "activation", phi16[:, kt * L:(kt + 1) * L],
                        wstg[:, parw * L:(parw + 1) * L], AF.Copy, wait=[mdl])

        m_e = [None, None]
        m_c16 = [None, None]
        m_red = [None, None]
        m_st = [None, None]
        idx = 0
        for mt in range(16):
            for ng in range(NTG):
                par = idx % 2
                idx += 1
                m = None
                for kt in range(DKT):
                    w = []
                    if kt == 0:
                        if mt == 0 and ng == 0:
                            w += [m_ph] + M_XNL
                        if m_e[par] is not None:
                            w.append(m_e[par])
                    m = k.op("tensor", "matmul", ps2[par][0:128, 0:512],
                             xn[:, kt * L + mt * 128:kt * L + (mt + 1) * 128],
                             phi16[:, kt * L + ng * 512:kt * L + ng * 512 + 512],
                             start=(kt == 0), stop=(kt == DKT - 1), wait=w or None,
                             inc=(st, 1) if kt == DKT - 1 else None)
                w = [m]
                for mm_ in (m_c16[par], m_red[par]):
                    if mm_ is not None:
                        w.append(mm_)
                m_e[par] = k.op("scalar", "activation", estage[:, par * 512:par * 512 + 512],
                                ps2[par][0:128, 0:512], AF.Exp, wait=w)
                m_red[par] = k.op("vector", "tensor_reduce", erp[:, ng:ng + 1],
                                  estage[:, par * 512:par * 512 + 512], AX.X, ALU.add,
                                  wait=[m_e[par]])
                m_c16[par] = k.op("scalar", "activation", e16s[:, par * 512:par * 512 + 512],
                                  estage[:, par * 512:par * 512 + 512], AF.Copy,
                                  wait=[m_e[par]] + ([m_st[par]] if m_st[par] is not None else []))
                m_st[par] = pin.dma(o_E[mt * 128:(mt + 1) * 128, ng * 512:ng * 512 + 512],
                                    e16s[:, par * 512:par * 512 + 512], wait=[m_c16[par]],
                                    queue="gpsimd")
            k.op("vector", "tensor_reduce", ercols[:, mt:mt + 1], erp[:, 0:4], AX.X, ALU.add)
            k.op("vector", "reciprocal", ercols[:, mt:mt + 1], ercols[:, mt:mt + 1])
        m_fin = k.mark("vector")
        for mt in range(16):
            k.op("sync", "dma_start", out=o_er[mt * 128:(mt + 1) * 128, 0:1],
                 in_=ercols[:, mt:mt + 1], wait=[m_fin] if mt == 0 else None, inc=(so, 16))
        k.wait("gpsimd", so, k.val(so))
        for s in pin.sems:
            k.wait("gpsimd", s, k.val(s))
        k.finalize()
    return nc


def build_lb():
    nc = bass.Bass("TRN2", target_bir_lowering=False)
    Eg = nc.dram_tensor("Eg", [8 * 256, L], BF16, kind="ExternalInput").ap()
    caps = declare_consts(nc)
    o_D = nc.dram_tensor("Dinv16", [256, L], BF16, kind="ExternalOutput").ap()

    es = ExitStack()
    with es:
        k = K(nc, es)
        ld = k.sem("ld")
        so = k.sem("so")
        register_fconsts(k, caps, ld)
        etl = sbuf(k, "etl", [128, 8 * L], BF16)
        acc = sbuf(k, "acc", [128, L], F32)
        d16 = sbuf(k, "d16", [128, L], BF16)
        m_rd = None
        m_cp = None
        m_16 = None
        for t in range(2):
            pin = DPool(k, "pinb%d" % t, "sync", 4)
            for b in range(8):
                w = [m_rd, m_cp] if (t == 1 and b == 0) else None
                pin.dma(etl[:, b * L:(b + 1) * L],
                        Eg[b * 256 + t * 128:b * 256 + (t + 1) * 128, :], wait=w)
            M = pin.all()
            m_cp = k.op("scalar", "activation", acc[:], etl[:, 0:L], AF.Copy,
                        wait=M + ([m_16] if m_16 is not None else []))
            m_a = None
            for b in range(1, 8):
                m_a = k.op("vector", "tensor_tensor", acc[:], acc[:], etl[:, b * L:(b + 1) * L],
                           ALU.add, wait=[m_cp] if b == 1 else None)
            m_rd = k.op("vector", "reciprocal", acc[:], acc[:])
            m_16 = k.op("scalar", "activation", d16[:], acc[:], AF.Copy,
                        wait=[m_rd] + ([m_dst] if t > 0 else []))
            k.op("sync", "dma_start", out=o_D[t * 128:(t + 1) * 128, :], in_=d16[:],
                 wait=[m_16], inc=(so, 16))
            m_dst = (so, k.val(so))
        k.wait("gpsimd", so, k.val(so))
        k.finalize()
    return nc


def build_lc():
    nc = bass.Bass("TRN2", target_bir_lowering=False)
    E16b = nc.dram_tensor("E16b", [L, L], BF16, kind="ExternalInput").ap()
    Dinv = nc.dram_tensor("Dinv", [L, L], BF16, kind="ExternalInput").ap()
    xn1TM = nc.dram_tensor("xn1TM", [L, D], BF16, kind="ExternalInput").ap()
    caps = declare_consts(nc)
    o_s = nc.dram_tensor("slotsb", [D, L], BF16, kind="ExternalOutput").ap()

    es = ExitStack()
    with es:
        k = K(nc, es)
        ld = k.sem("ld")
        lde = k.sem("lde")
        st = k.sem("stn")
        so = k.sem("so")
        register_fconsts(k, caps, ld)
        xtm = sbuf(k, "xtm", [128, 16 * D], BF16)
        edsp = sbuf(k, "edsp", [128, 16 * L], BF16)
        estg = sbuf(k, "estg", [128, 2 * L], BF16)
        dstg = sbuf(k, "dstg", [128, 2 * L], BF16)
        sstg = sbuf(k, "sstg", [128, 2 * 512], BF16)
        ps2 = [psum(k, "psA", (128, 512)), psum(k, "psB", (128, 512))]

        pin = DPool(k, "pin", "sync", 4)
        for mt in range(16):
            pin.dma(xtm[:, mt * D:(mt + 1) * D], xn1TM[mt * 128:(mt + 1) * 128, :])
        M_XTL = pin.all()
        M_XT = M_XTL[0]
        m_ed = [None, None]
        for mt in range(16):
            par = mt % 2
            w = [m_ed[par]] if m_ed[par] is not None else None
            m1 = pin.dma(estg[:, par * L:(par + 1) * L], E16b[mt * 128:(mt + 1) * 128, :],
                         wait=w)
            m2 = pin.dma(dstg[:, par * L:(par + 1) * L], Dinv[mt * 128:(mt + 1) * 128, :],
                         wait=w)
            m_ed[par] = k.op("vector", "tensor_tensor", edsp[:, mt * L:(mt + 1) * L],
                             estg[:, par * L:(par + 1) * L], dstg[:, par * L:(par + 1) * L],
                             ALU.mult, wait=[m1, m2])
        m_edall = m_ed[0] if m_ed[1] is None else m_ed[1]
        m_edall = k.mark("vector")

        m_ev = [None, None]
        m_sst = [None, None]
        idx = 0
        for dt in range(DKT):
            for ng in range(NTG):
                par = idx % 2
                idx += 1
                m = None
                for mt in range(16):
                    w = []
                    if mt == 0:
                        if idx == 1:
                            w += [m_edall] + M_XTL
                        if m_ev[par] is not None:
                            w.append(m_ev[par])
                    m = k.op("tensor", "matmul", ps2[par][0:128, 0:512],
                             xtm[:, mt * D + dt * 128:mt * D + (dt + 1) * 128],
                             edsp[:, mt * L + ng * 512:mt * L + ng * 512 + 512],
                             start=(mt == 0), stop=(mt == 15), wait=w or None,
                             inc=(st, 1) if mt == 15 else None)
                m_ev[par] = k.op("vector", "tensor_copy", sstg[:, par * 512:par * 512 + 512],
                                 ps2[par][0:128, 0:512],
                                 wait=[m] + ([m_sst[par]] if m_sst[par] is not None else []))
                m_sst[par] = pin.dma(o_s[dt * 128:(dt + 1) * 128, ng * 512:ng * 512 + 512],
                                     sstg[:, par * 512:par * 512 + 512], wait=[m_ev[par]],
                                     queue="gpsimd")
        for s in pin.sems:
            k.wait("gpsimd", s, k.val(s))
        k.finalize()
    return nc


def build_ld():
    nc = bass.Bass("TRN2", target_bir_lowering=False)
    slots = nc.dram_tensor("slots", [D, L], BF16, kind="ExternalInput").ap()
    Wc = nc.dram_tensor("Wc", [8 * D, D], F32, kind="ExternalInput").ap()
    bc = nc.dram_tensor("bc", [8, D], F32, kind="ExternalInput").ap()
    caps = declare_consts(nc)
    o_y = nc.dram_tensor("yc", [8 * 256, D], BF16, kind="ExternalOutput").ap()

    es = ExitStack()
    with es:
        k = K(nc, es)
        ld = k.sem("ld")
        ldw = k.sem("ldw")
        st = k.sem("stn")
        so = k.sem("so")
        register_fconsts(k, caps, ld)
        onesrow = load_const(k, caps, "ONESROW", ld)
        slt = sbuf(k, "slt", [128, DKT * L], BF16)
        wstg = sbuf(k, "wstg", [128, 4 * D], F32)
        w16 = sbuf(k, "w16", [128, 2 * DKT * D], BF16)
        bstage = sbuf(k, "bstage", [1, 8 * D], F32)
        ystg = sbuf(k, "ystg", [128, 2 * 512], BF16)
        ps2 = [psum(k, "psA", (128, 512)), psum(k, "psB", (128, 512))]

        pin = DPool(k, "pin", "sync", 4)
        for kt in range(DKT):
            pin.dma(slt[:, kt * L:(kt + 1) * L], slots[kt * 128:(kt + 1) * 128, :])
        for nl in range(8):
            pin.dma(bstage[0:1, nl * D:(nl + 1) * D], bc[nl:nl + 1, :])
        M_INL = pin.all()
        M_IN = M_INL[0]

        m_cast = [None, None]
        m_nlmm = [None, None]
        m_ev = [None, None]
        m_yst = [None, None]
        idx = 0
        wcasts = []
        pw = DPool(k, "pw", "sync", 4)
        for nl in range(8):
            parn = nl % 2
            m_c = None
            for kt in range(DKT):
                i = len(wcasts)
                parw = i % 4
                w = [wcasts[-4]] if i >= 4 else None
                mdl = pw.dma(wstg[:, parw * D:(parw + 1) * D],
                             Wc[nl * D + kt * 128:nl * D + (kt + 1) * 128, :], wait=w)
                wl = [mdl]
                if kt == 0 and m_nlmm[parn] is not None:
                    wl.append(m_nlmm[parn])
                m_c = k.op("scalar", "activation",
                           w16[:, parn * DKT * D + kt * D:parn * DKT * D + (kt + 1) * D],
                           wstg[:, parw * D:(parw + 1) * D], AF.Copy, wait=wl)
                wcasts.append(m_c)
            m_cast[parn] = m_c
            for half in range(2):
                for eg in range(2):
                    par = idx % 2
                    idx += 1
                    m = None
                    for kt in range(DKT):
                        w = []
                        if kt == 0:
                            w.append(m_cast[parn])
                            if idx == 1:
                                w += M_INL
                            if m_ev[par] is not None:
                                w.append(m_ev[par])
                        m = k.op("tensor", "matmul", ps2[par][0:128, 0:512],
                                 slt[:, kt * L + nl * 256 + half * 128:kt * L + nl * 256 + half * 128 + 128],
                                 w16[:, parn * DKT * D + kt * D + eg * 512:parn * DKT * D + kt * D + eg * 512 + 512],
                                 start=(kt == 0), stop=False, wait=w or None)
                    m = k.op("tensor", "matmul", ps2[par][0:128, 0:512],
                             onesrow[0:1, 0:128],
                             bstage[0:1, nl * D + eg * 512:nl * D + eg * 512 + 512],
                             start=False, stop=True, inc=(st, 1))
                    m_nlmm[parn] = m
                    m_ev[par] = k.op("vector", "tensor_copy", ystg[:, par * 512:par * 512 + 512],
                                     ps2[par][0:128, 0:512],
                                     wait=[m] + ([m_yst[par]] if m_yst[par] is not None else []))
                    k.op("sync", "dma_start",
                         out=o_y[nl * 256 + half * 128:nl * 256 + half * 128 + 128,
                                 eg * 512:eg * 512 + 512],
                         in_=ystg[:, par * 512:par * 512 + 512], wait=[m_ev[par]], inc=(so, 16))
                    m_yst[par] = (so, k.val(so))
        k.wait("gpsimd", so, k.val(so))
        k.finalize()
    return nc


def build_le():
    nc = bass.Bass("TRN2", target_bir_lowering=False)
    yb = nc.dram_tensor("yb", [L, D], BF16, kind="ExternalInput").ap()
    EFM = nc.dram_tensor("EFM", [L, L], BF16, kind="ExternalInput").ap()
    erow = nc.dram_tensor("erow", [1, L], F32, kind="ExternalInput").ap()
    pres = nc.dram_tensor("pres", [D, L], F32, kind="ExternalInput").ap()
    caps = declare_consts(nc)
    o_out = nc.dram_tensor("outT", [D, L], F32, kind="ExternalOutput").ap()

    es = ExitStack()
    with es:
        k = K(nc, es)
        ld = k.sem("ld")
        ldp = k.sem("ldp")
        st = k.sem("stn")
        so = k.sem("so")
        register_fconsts(k, caps, ld)
        onesrow = load_const(k, caps, "ONESROW", ld)
        ytm = sbuf(k, "ytm", [128, 16 * D], BF16)
        efm = sbuf(k, "efm", [128, 16 * L], BF16)
        errow = sbuf(k, "errow", [1, L], F32)
        erbc = sbuf(k, "erbc", [128, 4 * 512], F32)
        prestg = sbuf(k, "prestg", [128, 2 * 512], F32)
        ostg = sbuf(k, "ostg", [128, 2 * 512], F32)
        o2 = sbuf(k, "o2", [128, 2 * 512], F32)
        ps2 = [psum(k, "psA", (128, 512)), psum(k, "psB", (128, 512))]
        psbc = psum(k, "psbc", (128, 512))

        pin = DPool(k, "pin", "sync", 4)
        for npt in range(16):
            pin.dma(ytm[:, npt * D:(npt + 1) * D], yb[npt * 128:(npt + 1) * 128, :])
        for npt in range(16):
            pin.dma(efm[:, npt * L:(npt + 1) * L], EFM[npt * 128:(npt + 1) * 128, :])
        pin.dma(errow[0:1, :], erow)
        M_ALLL = pin.all()
        M_ALL = M_ALLL[0]

        m_bc = None
        for mg in range(NTG):
            m = k.op("tensor", "matmul", psbc[0:128, 0:512], onesrow[0:1, 0:128],
                     errow[0:1, mg * 512:mg * 512 + 512], start=True, stop=True,
                     wait=(M_ALLL if mg == 0 else []) + ([m_bc] if m_bc is not None else []),
                     inc=(st, 1))
            m_bc = k.op("vector", "tensor_copy", erbc[:, mg * 512:mg * 512 + 512],
                        psbc[0:128, 0:512], wait=[m])
        m_bcall = m_bc

        m_mul = [None, None]
        m_add = [None, None]
        m_ost = [None, None]
        idx = 0
        for dt in range(DKT):
            for mg in range(NTG):
                par = idx % 2
                idx += 1
                wld = [m_mul[par]] if m_mul[par] is not None else None
                k.op("sync", "dma_start", out=prestg[:, par * 512:par * 512 + 512],
                     in_=pres[dt * 128:(dt + 1) * 128, mg * 512:mg * 512 + 512],
                     wait=wld, inc=(ldp, 16))
                M_P = (ldp, k.val(ldp))
                m = None
                for npt in range(16):
                    w = []
                    if npt == 0:
                        if idx == 1:
                            w.append(m_bcall)
                        if m_mul[par] is not None:
                            w.append(m_mul[par])
                    m = k.op("tensor", "matmul", ps2[par][0:128, 0:512],
                             ytm[:, npt * D + dt * 128:npt * D + (dt + 1) * 128],
                             efm[:, npt * L + mg * 512:npt * L + mg * 512 + 512],
                             start=(npt == 0), stop=(npt == 15), wait=w or None,
                             inc=(st, 1) if npt == 15 else None)
                m_mul[par] = k.op("vector", "tensor_tensor", ostg[:, par * 512:par * 512 + 512],
                                  ps2[par][0:128, 0:512], erbc[:, mg * 512:mg * 512 + 512],
                                  ALU.mult,
                                  wait=[m] + ([m_add[par]] if m_add[par] is not None else []))
                m_add[par] = k.op("vector", "tensor_tensor", o2[:, par * 512:par * 512 + 512],
                                  ostg[:, par * 512:par * 512 + 512],
                                  prestg[:, par * 512:par * 512 + 512], ALU.add,
                                  wait=[M_P] + ([m_ost[par]] if m_ost[par] is not None else []))
                k.op("sync", "dma_start",
                     out=o_out[dt * 128:(dt + 1) * 128, mg * 512:mg * 512 + 512],
                     in_=o2[:, par * 512:par * 512 + 512], wait=[m_add[par]], inc=(so, 16))
                m_ost[par] = (so, k.val(so))
        k.wait("gpsimd", so, k.val(so))
        k.finalize()
    return nc


def run_ln0(xT_shards, inputs):
    nc = build_ln0()
    cin = const_inputs()
    cT = np.ascontiguousarray(inputs["L0_ffn_centroids"].T)
    wrT = np.ascontiguousarray(inputs["L0_ffn_w_route"].T)
    in_maps = [dict(xT=xT_shards[i], cT=cT, wrT=wrT, **cin) for i in range(NCORES)]
    return run_bass_kernel_spmd(nc, in_maps, core_ids=list(range(NCORES)))


def make_hsel():
    HSELK = np.zeros((6 * 128, 4), np.float32)
    for d in range(768):
        HSELK[d, d // 192] = 1.0
    H2 = np.zeros((4, 6 * 128), np.float32)
    for d in range(768):
        H2[d // 192, d] = 1.0
    return HSELK, H2


def kernel(**inputs):
    import ml_dtypes
    from concourse.bass_utils import run_bass_kernel_spmd as spmd
    NB = NCORES
    cin = const_inputs()
    cc = lambda a: np.ascontiguousarray(a)
    ins = inputs

    def launch(nc, maps):
        return spmd(nc, maps, core_ids=list(range(NB))).results

    x0 = ins["L0_attn_x"]
    xT = [cc(x0[b].T) for b in range(NB)]
    cT = cc(ins["L0_ffn_centroids"].T)
    wrT = cc(ins["L0_ffn_w_route"].T)
    convw = dict(W_in=ins["L0_attn_W_in"],
                 cwx=ins["L0_attn_conv_x_w"].reshape(D, 4),
                 cbx=ins["L0_attn_conv_x_b"].reshape(D, 1),
                 cwz=ins["L0_attn_conv_z_w"].reshape(D, 4),
                 cbz=ins["L0_attn_conv_z_b"].reshape(D, 1))
    r = launch(build_ln0conv(), [dict(xT=xT[b], cT=cT, wrT=wrT, **convw, **cin)
                                 for b in range(NB)])
    xnhi = [r[b]["xnhi"] for b in range(NB)]
    gatesT = [cc(r[b]["gatesT"]) for b in range(NB)]
    xs = [r[b]["xs"] for b in range(NB)]
    zs = [r[b]["zs"] for b in range(NB)]

    HSELK, HSELKT = make_hsel()
    projw = dict(W_q=ins["L0_attn_W_q"], W_k=ins["L0_attn_W_k"], W_v=ins["L0_attn_W_v"],
                 W_g=ins["L0_attn_W_g"],
                 W_gkb=cc(np.concatenate([ins["L0_attn_W_gk"], ins["L0_attn_W_b"]], 1)),
                 dtb_bb=cc(np.concatenate([ins["L0_attn_dt_bias"], ins["L0_attn_b_b"]]).reshape(1, 8)),
                 A_log=cc(ins["L0_attn_A_log"].reshape(4, 1)), HSELK=HSELK, HSELKT=HSELKT)
    r = launch(build_proj(), [dict(xs=xs[b], **projw, **cin) for b in range(NB)])
    pj = r

    gnwbc = cc(np.tile(ins["L0_attn_g_norm_w"][None, :], (128, 1)).astype(np.float32))
    r = launch(build_delta(), [dict(qdm=pj[b]["qdm"], kdm=pj[b]["kdm"], ktm=pj[b]["ktm"],
                                    vtm=pj[b]["vtm"], gtm=pj[b]["gtm"], G=cc(pj[b]["G"]),
                                    beta=cc(pj[b]["beta"]), gnwbc=gnwbc, **cin)
                               for b in range(NB)])
    ogdm = [r[b]["ogdm"] for b in range(NB)]

    b1T = cc(ins["L0_ffn_b1"].reshape(8, 4, 128).transpose(0, 2, 1)).reshape(8 * 128, 4)
    ffnw = dict(w1=ins["L0_ffn_w1"].reshape(8 * D, 512), b1T=b1T, W_so=ins["L0_attn_W_so"])
    r = launch(build_ffnh(), [dict(xn=xnhi[b], og=ogdm[b], gatesT=gatesT[b], **ffnw, **cin)
                              for b in range(NB)])
    ht = [r[b]["ht"] for b in range(NB)]
    yst = [r[b]["yst"] for b in range(NB)]

    residw = dict(W_out=ins["L0_attn_W_out"], w2=ins["L0_ffn_w2"].reshape(8 * 512, D),
                  b2=ins["L0_ffn_b2"])
    r = launch(build_resid(), [dict(yst=yst[b], zs=zs[b], ht=ht[b], xT=xT[b],
                                    gatesT=gatesT[b], **residw, **cin) for b in range(NB)])
    x1T = [r[b]["x1T"] for b in range(NB)]
    xn1 = [r[b]["xn1"] for b in range(NB)]

    phi2 = cc(ins["L1_ffn_phi"].reshape(D, L))
    attw = dict(WkT=cc(ins["L1_attn_Wk"].T), WvT=cc(ins["L1_attn_Wv"].T),
                WrT=cc(ins["L1_attn_Wr"].T), WoT=cc(ins["L1_attn_Wo"].T),
                wcol=ins["L1_attn_w"].reshape(D, 1), ucol=ins["L1_attn_u"].reshape(D, 1),
                mixk=ins["L1_attn_mix_k"].reshape(D, 1),
                mixv=ins["L1_attn_mix_v"].reshape(D, 1),
                mixr=ins["L1_attn_mix_r"].reshape(D, 1), phi2=phi2)
    r = launch(build_l1attn(), [dict(xn1=xn1[b], x1T=x1T[b], **attw, **cin)
                                for b in range(NB)])
    pres = [r[b]["pres"] for b in range(NB)]
    E16 = [r[b]["E16"] for b in range(NB)]
    erinv = [r[b]["erinv"] for b in range(NB)]

    r = launch(build_lb(), [dict(Eg=cc(np.concatenate(
        [E16[b][i * 256:(i + 1) * 256] for b in range(NB)], 0)), **cin)
        for i in range(NB)])
    Dinv16 = np.concatenate([r[i]["Dinv16"] for i in range(NB)], 0)

    r = launch(build_lc(), [dict(E16b=E16[b], Dinv=Dinv16, xn1TM=cc(xn1[b].T), **cin)
                            for b in range(NB)])
    slotsb = [r[b]["slotsb"] for b in range(NB)]

    Wfull = ins["L1_ffn_W"]
    bfull = ins["L1_ffn_b"]
    maps = []
    for i in range(NB):
        sc = np.zeros((D, 2048), dtype=ml_dtypes.bfloat16)
        for nl in range(8):
            n = i * 8 + nl
            for b in range(NB):
                sc[:, (nl * 8 + b) * 32:(nl * 8 + b) * 32 + 32] = \
                    slotsb[b][:, n * 32:(n + 1) * 32]
        maps.append(dict(slots=sc, Wc=Wfull[i * 8:(i + 1) * 8].reshape(8 * D, D),
                         bc=bfull[i * 8:(i + 1) * 8], **cin))
    r = launch(build_ld(), maps)
    yc = [r[i]["yc"] for i in range(NB)]

    maps = []
    for b in range(NB):
        yb = np.zeros((2048, D), dtype=ml_dtypes.bfloat16)
        for n in range(64):
            yb[n * 32:(n + 1) * 32] = yc[n // 8][(n % 8) * 256 + b * 32:(n % 8) * 256 + b * 32 + 32]
        maps.append(dict(yb=yb, EFM=cc(E16[b].T), erow=cc(erinv[b].reshape(1, L)),
                         pres=pres[b], **cin))
    r = launch(build_le(), maps)
    out = np.stack([cc(r[b]["outT"].T) for b in range(NB)], 0)
    return np.asarray(out, np.float32)
